# version 02
# baseline (speedup 1.0000x reference)
import os
import math
import numpy as np
import ml_dtypes
from contextlib import ExitStack

import concourse.bass as bass
import concourse.mybir as mybir
from concourse.bass_utils import run_bass_kernel_spmd
from concourse.bass import broadcast_tensor_aps

F32 = mybir.dt.float32
BF16 = mybir.dt.bfloat16
F16 = mybir.dt.float16
AF = mybir.ActivationFunctionType
ALU = mybir.AluOpType
AX = mybir.AxisListType

D = 2048
T = 2048
NTOK = 1024
NT = 8
H = 16
HKV = 4
HD = 128
DKV = 512
FFN_H = 8192
DIN = 4096
DST = 64
NH = 64
CONV_DIM = 4224
EIN = 8384
CHUNK = 256
NCH = 4
ASCALE = 1.0 / math.sqrt(HD)
F32EPS = float(np.finfo(np.float32).eps)
NEG = -1e30

UPTO = int(os.environ.get("MK_UPTO", "4"))
SKIP3 = os.environ.get("MK_SKIP3", "") == "1"

ENGS = ("sync", "scalar", "vector", "tensor", "gpsimd")
ESEM = {"scalar": "eS", "vector": "eV", "tensor": "eT", "gpsimd": "eG"}
QSEM = {"sync": "qY", "scalar": "qA", "vector": "qV", "tensor": "qT", "gpsimd": "qG"}


class PH:
    __slots__ = ("val",)

    def __init__(self):
        self.val = None


def _nm(x):
    if isinstance(x, str):
        return x
    t = getattr(x, "tensor", None)
    if t is not None:
        return t.name
    return getattr(x, "name")


class P:
    def __init__(self, nc):
        self.nc = nc
        self.es = ExitStack()
        self.streams = {e: [] for e in ENGS}
        self.semh = {}
        self.cnt = {}
        self.qpool = {"sync": [f"qy{i}" for i in range(10)],
                      "scalar": [f"qa{i}" for i in range(4)],
                      "vector": [f"qv{i}" for i in range(2)],
                      "tensor": [f"qt{i}" for i in range(2)],
                      "gpsimd": [f"qg{i}" for i in range(10)]}
        names = ["eS", "eV", "eT", "eG", "cc"] + [x for v in self.qpool.values() for x in v]
        for s in names:
            self.semh[s] = self.es.enter_context(nc.semaphore(s))
            self.cnt[s] = 0
        self.qpi = {k: 0 for k in self.qpool}
        self.qpend = {k: [] for k in self.qpool}
        self.waited = {e: {} for e in ENGS}
        self.buf = {}
        self.scopes = []

    def sb(self, name, shape, dtype=F32, persist=False):
        cm = self.nc.sbuf_tensor(name, list(shape), dtype)
        tgt = self.es if (persist or not self.scopes) else self.scopes[-1]
        return tgt.enter_context(cm)

    def ps(self, name, shape, dtype=F32):
        return self.es.enter_context(self.nc.psum_tensor(name, list(shape), dtype))

    def push_scope(self):
        s = ExitStack()
        self.scopes.append(s)
        return s

    def pop_scope(self):
        self.barrier()
        self.scopes.pop().close()

    def _wait(self, eng, sem, val, ph=None):
        if val <= 0:
            return
        hw = self.waited[eng]
        if hw.get(sem, 0) >= val:
            return
        hw[sem] = val
        h = self.semh[sem]
        if ph is not None:
            ph.val = (self.semh[sem], val)
        self.streams[eng].append(lambda e, h=h, v=val: e.wait_ge(h, v))

    def emit(self, eng, fn, reads=(), writes=(), q=None, sem=None, inc=None):
        need = {}
        for r in reads:
            b = self.buf.get(_nm(r))
            if b and b["w"]:
                s, v, ph = b["w"]
                if s not in need or need[s][0] < v:
                    need[s] = (v, ph)
        for w in writes:
            b = self.buf.get(_nm(w))
            if b:
                if b["w"]:
                    s, v, ph = b["w"]
                    if s not in need or need[s][0] < v:
                        need[s] = (v, ph)
                for s, (v, ph) in b["r"].items():
                    if s not in need or need[s][0] < v:
                        need[s] = (v, ph)
        for s, (v, ph) in need.items():
            self._wait(eng, s, v, ph)
        if sem is None:
            if q:
                pool = self.qpool[eng]
                sem = pool[self.qpi[eng] % len(pool)]
                self.qpi[eng] += 1
            else:
                sem = ESEM[eng]
        if inc is None:
            inc = 16 if q else 1
        self.cnt[sem] += inc
        val = self.cnt[sem]
        h = self.semh[sem]
        self.streams[eng].append(lambda e, fn=fn, h=h, inc=inc: fn(e).then_inc(h, inc))
        ph = None
        if not q:
            ph = PH()
            self.streams[eng].append(ph)
        else:
            pend = self.qpend[eng]
            pend.append((sem, val))
            if len(pend) > len(self.qpool[eng]) - 2:
                s_, v_ = pend.pop(0)
                h_ = self.semh[s_]
                hw = self.waited[eng]
                if hw.get(s_, 0) < v_:
                    hw[s_] = v_
                    self.streams[eng].append(lambda e, h=h_, v=v_: e.wait_ge(h, v))
        edge = (sem, val, ph)
        for w in writes:
            self.buf[_nm(w)] = {"w": edge, "r": {}}
        for r in reads:
            b = self.buf.setdefault(_nm(r), {"w": None, "r": {}})
            pv = b["r"].get(sem)
            if pv is None or pv[0] < val:
                b["r"][sem] = (val, ph)
        return val

    def dma(self, eng, out, in_, extra_reads=()):
        self.emit(eng, lambda e, o=out, i=in_: e.dma_start(out=o, in_=i),
                  reads=[in_] + list(extra_reads), writes=[out], q=QSEM[eng])

    def barrier(self):
        for eng in ("scalar", "vector", "tensor", "gpsimd"):
            sem = ESEM[eng]
            self.cnt[sem] += 1
            h = self.semh[sem]
            self.streams[eng].append(lambda e, h=h: e.nop().then_inc(h, 1))
            self.waited[eng][sem] = self.cnt[sem]
        snap = dict(self.cnt)
        for eng in ENGS:
            for s, v in snap.items():
                self._wait(eng, s, v)
        for b in self.buf.values():
            b["r"] = {}
            if b["w"]:
                s, v, ph = b["w"]
                b["w"] = (s, v, None)

    def finalize(self):
        blk = self.es.enter_context(self.nc.Block())
        for eng in ENGS:
            ops = []
            for f in self.streams[eng]:
                if isinstance(f, PH):
                    if f.val is not None:
                        ops.append(lambda e, h=f.val[0], v=f.val[1]: e.wait_ge(h, v))
                else:
                    ops.append(f)
            getattr(blk, eng)(lambda e, st=ops: [f(e) for f in st])
        self.es.close()



def _host_consts():
    c = {}
    i = np.arange(0, HD, 2, dtype=np.float64)
    inv_freq = 1.0 / (1000000.0 ** (i / HD))
    pos = np.arange(T, dtype=np.float64)
    freqs = pos[:, None] * inv_freq[None, :]
    emb = np.concatenate([freqs, freqs], axis=-1)
    cos = np.cos(emb).astype(np.float32)
    sinA = np.sin(emb).astype(np.float32)
    sinA[:, : HD // 2] *= -1.0
    c["cos_ctx"] = cos
    c["sinA_ctx"] = sinA
    c["ident"] = np.eye(128, dtype=np.float32)
    c["onesr"] = np.ones((1, 512), dtype=np.float32)
    jj = np.arange(128)[:, None]
    ii = np.arange(256)[None, :]
    c["ssdA"] = np.where(ii >= jj, 0.0, NEG).astype(np.float32)
    c["ssdB"] = np.where(ii >= 128 + jj, 0.0, NEG).astype(np.float32)
    e64 = np.zeros((128, 2), dtype=np.float32)
    e64[:64, 0] = 1.0
    e64[64:, 1] = 1.0
    c["E64"] = e64
    c["sel2"] = np.ascontiguousarray(e64.T)
    for hi in (0, 1):
        kk = np.arange(128)[:, None]
        qq = np.arange(512)[None, :]
        m = np.zeros((2, 16, 128, 512), dtype=np.float32)
        for cch in range(2):
            for j in range(16):
                m[cch, j] = ((hi * 1024 + cch * 512 + qq) >= (j * 128 + kk)).astype(np.float32)
        c[f"masks_{hi}"] = m.astype(ml_dtypes.bfloat16)
        c[f"hz_{hi}"] = np.full((128, 1), float(hi), dtype=np.float32)
        sl = slice(hi * 1024, (hi + 1) * 1024)
        c[f"cos_own_{hi}"] = cos[sl].copy()
        c[f"sinA_own_{hi}"] = sinA[sl].copy()
    return c


def _prep_inputs(inp):
    g = {}
    t = lambda a: np.ascontiguousarray(np.asarray(a).T)
    asc = np.ascontiguousarray
    rep = lambda a, n=128: asc(np.broadcast_to(np.asarray(a)[None, :], (n, len(a))))
    g["wqT"] = t(inp["L0_attn_wq"])
    g["wkT"] = t(inp["L0_attn_wk"])
    g["wvT"] = t(inp["L0_attn_wv"])
    g["woT"] = t(inp["L0_attn_wo"])
    g["wq_nat"] = asc(np.asarray(inp["L0_attn_wq"]))
    g["wk_nat"] = asc(np.asarray(inp["L0_attn_wk"]))
    g["wv_nat"] = asc(np.asarray(inp["L0_attn_wv"]))
    g["wfcT"] = t(inp["L0_ffn_W_fc"])
    g["wprojT"] = t(inp["L0_ffn_W_proj"])
    g["inprojT"] = t(inp["L1_attn_in_proj_w"])
    g["outprojT"] = t(inp["L1_attn_out_proj_w"])
    g["conv_w"] = asc(np.asarray(inp["L1_attn_conv_w"]))
    g["conv_b"] = asc(np.asarray(inp["L1_attn_conv_b"]).reshape(CONV_DIM, 1))
    g["rms_w_rep"] = rep(np.asarray(inp["L1_attn_rms_weight"]))
    g["dtb_rep"] = rep(np.asarray(inp["L1_attn_dt_bias"]))
    g["alog_rep"] = rep(np.asarray(inp["L1_attn_A_log"]))
    g["dd_rep"] = rep(np.asarray(inp["L1_attn_D"]))
    g["normw_rep"] = rep(np.asarray(inp["L1_attn_norm_w"]))
    g["w0T"] = t(inp["L1_ffn_W0"])
    g["w1T"] = t(inp["L1_ffn_W1"])
    g["bias0"] = asc(np.asarray(inp["L1_ffn_bias0"]).reshape(FFN_H, 1))
    g["bias1_rep"] = rep(np.asarray(inp["L1_ffn_bias1"]))
    return g



def build_program():
    nc = bass.Bass("TRN2", target_bir_lowering=False, debug=False)
    dram = {}

    def din(name, shape, dtype=F32):
        dram[name] = nc.dram_tensor(name, list(shape), dtype, kind="ExternalInput").ap()
        return dram[name]

    x_own = din("x_own", (NTOK, D))
    x_ctx = din("x_ctx", (T, D))
    for n, s in (("wqT", (D, D)), ("wkT", (D, DKV)), ("wvT", (D, DKV)), ("woT", (D, D)),
                 ("wq_nat", (D, D)), ("wk_nat", (DKV, D)), ("wv_nat", (DKV, D)),
                 ("wfcT", (D, FFN_H)), ("wprojT", (FFN_H, D)),
                 ("inprojT", (D, EIN)), ("outprojT", (DIN, D)),
                 ("conv_w", (CONV_DIM, 4)), ("conv_b", (CONV_DIM, 1)),
                 ("rms_w_rep", (128, D)), ("dtb_rep", (128, NH)), ("alog_rep", (128, NH)),
                 ("dd_rep", (128, NH)), ("normw_rep", (128, DIN)),
                 ("w0T", (D, FFN_H)), ("w1T", (FFN_H, D)),
                 ("bias0", (FFN_H, 1)), ("bias1_rep", (128, D)),
                 ("cos_ctx", (T, HD)), ("sinA_ctx", (T, HD)),
                 ("cos_own", (NTOK, HD)), ("sinA_own", (NTOK, HD)),
                 ("ident", (128, 128)), ("onesr", (1, 512)),
                 ("E64", (128, 2)), ("sel2", (2, 128)),
                 ("ssdA", (128, 256)), ("ssdB", (128, 256)), ("hz", (128, 1))):
        din(n, s)
    din("masks", (2, 16, 128, 512), BF16)

    y_out = nc.dram_tensor("y_out", [NTOK, D], F32, kind="ExternalOutput").ap()
    xr1 = nc.dram_tensor("xr1", [NTOK, D], F32).ap()
    xr2 = nc.dram_tensor("xr2", [NTOK, D], F32).ap()
    xr3 = nc.dram_tensor("xr3", [NTOK, D], F32).ap()
    zspill = nc.dram_tensor("zspill", [NTOK, DIN], BF16).ap()
    w1fc_s = nc.dram_tensor("w1fc_s", [64, 16, 128, 128], BF16).ap()
    w1pr_s = nc.dram_tensor("w1pr_s", [16, 64, 128, 128], BF16).ap()
    ygsp = nc.dram_tensor("ygsp", [8, 32, 128, 128], BF16).ap()
    cch_in = nc.dram_tensor("cch_in", [33, 128, 3], F32).ap()
    cch_out = nc.dram_tensor("cch_out", [2, 33, 128, 3], F32).ap()
    cst_in = nc.dram_tensor("cst_in", [64, DIN], F32).ap()
    cst_out = nc.dram_tensor("cst_out", [2, 64, DIN], F32).ap()

    p = P(nc)
    E = p.emit

    identf = p.sb("identf", (128, 128), persist=True)
    identb = p.sb("identb", (128, 128), BF16, persist=True)
    onesr = p.sb("onesr_s", (1, 512), persist=True)
    onesc = p.sb("onesc_s", (128, 1), BF16, persist=True)
    e64 = p.sb("e64_s", (128, 2), BF16, persist=True)
    sel2 = p.sb("sel2_s", (2, 128), BF16, persist=True)
    hzc = p.sb("hz_s", (128, 1), persist=True)
    stats = p.sb("stats", (128, 96), persist=True)
    mv = p.sb("mv", (128, 2), persist=True)
    cols = p.sb("cols", (128, 24), persist=True)
    m3 = p.sb("m3", (128, 16), persist=True)

    psf = [p.ps(f"psf{i}", [128, 512], F32) for i in range(6)]
    psb = [p.ps(f"psb{i}", [128, 1024], BF16) for i in range(2)]
    _tb = [0]

    p.dma("sync", identf[:], dram["ident"][:, :])
    p.dma("sync", onesr[:], dram["onesr"][:, :])
    p.dma("gpsimd", e64[:], dram["E64"][:, :])
    p.dma("gpsimd", sel2[:], dram["sel2"][:, :])
    p.dma("sync", hzc[:], dram["hz"][:, :])
    p.dma("gpsimd", identb[:], dram["ident"][:, :])
    E("gpsimd", lambda e: e.memset(onesc[:], 1.0), reads=[], writes=[onesc])

    nmc, stc, smx = cols[:, 0:1], cols[:, 1:2], cols[:, 2:3]
    smx4 = cols[:, 4:8]

    def trans(dst_ap, src_ap, pdim=128, fdim=128, f32=False):
        if not f32:
            bank = psb[_tb[0] % 2]
            _tb[0] += 1
            ident = identb
        else:
            bank = psf[5]
            ident = identf
        E("tensor", lambda e, b=bank, s=src_ap, idn=ident, pd=pdim, fd=fdim: e.transpose(
            out=b[0:fd, 0:pd], in_=s, identity=idn[0:pd, 0:pd]),
          reads=[src_ap, ident], writes=[bank])
        E("vector", lambda e, b=bank, d=dst_ap, pd=pdim, fd=fdim: e.tensor_copy(
            out=d, in_=b[0:fd, 0:pd]),
          reads=[bank], writes=[dst_ap])

    def divcol(out_ap, in0_ap, col_ap, rd, wr):
        rcp = cols[:, 23:24]
        E("vector", lambda e, c=col_ap: e.reciprocal(out=rcp, in_=c),
          reads=list(rd), writes=[cols])
        E("vector", lambda e, o=out_ap, i=in0_ap: e.tensor_scalar(
            out=o, in0=i, scalar1=rcp, scalar2=None, op0=ALU.mult),
          reads=list(rd) + [cols], writes=list(wr))

    def adddiv(out_ap, in0_ap, nm_col, std_col, rd, wr):
        E("vector", lambda e, o=out_ap, i=in0_ap, nm=nm_col: e.tensor_scalar(
            out=o, in0=i, scalar1=nm, scalar2=None, op0=ALU.add),
          reads=list(rd) + [cols], writes=list(wr))
        divcol(out_ap, out_ap, std_col, list(wr) + [cols], list(wr))

    def ln_stats(src_ap, width, extra_reads=()):
        nchk = (width + 511) // 512
        for ci in range(nchk):
            w = min(512, width - ci * 512)
            E("vector", lambda e, s=src_ap, ci=ci, w=w: e.bn_stats(
                out=stats[:, ci * 6:(ci + 1) * 6], in_=s[:, ci * 512: ci * 512 + w]),
              reads=[src_ap] + list(extra_reads), writes=[stats])
        E("vector", lambda e, n=nchk: e.bn_aggr(out=mv[:], in_=stats[:, 0:6 * n]),
          reads=[stats], writes=[mv])

    def ln_cols(nm_col, std_col, var_eps=1e-5):
        E("vector", lambda e, o=nm_col: e.tensor_scalar(
            out=o, in0=mv[:, 0:1], scalar1=-1.0, scalar2=None, op0=ALU.mult),
          reads=[mv], writes=[cols])
        E("vector", lambda e, o=std_col, ve=var_eps: e.tensor_scalar(
            out=o, in0=mv[:, 1:2], scalar1=ve, scalar2=None, op0=ALU.add),
          reads=[mv], writes=[cols])
        E("scalar", lambda e, o=std_col: e.activation(out=o, in_=o, func=AF.Sqrt),
          reads=[cols], writes=[cols])

    def rms_std_col(std_col, eps):
        E("vector", lambda e, o=std_col: e.tensor_tensor(
            out=o, in0=mv[:, 0:1], in1=mv[:, 0:1], op=ALU.mult),
          reads=[mv], writes=[cols])
        E("vector", lambda e, o=std_col: e.tensor_tensor(
            out=o, in0=o, in1=mv[:, 1:2], op=ALU.add),
          reads=[mv, cols], writes=[cols])
        E("vector", lambda e, o=std_col, ve=eps: e.tensor_scalar(
            out=o, in0=o, scalar1=ve, scalar2=None, op0=ALU.add),
          reads=[cols], writes=[cols])
        E("scalar", lambda e, o=std_col: e.activation(out=o, in_=o, func=AF.Sqrt),
          reads=[cols], writes=[cols])

    def quant_act(dst_bf, src_f32, tmp_f32, tmp2_f32, nheads, smax, extra=()):
        hd = None
        if nheads > 1:
            E("vector", lambda e, s=src_f32, n=nheads, o=smax: e.reduce_max(
                out=o, in_=s.rearrange("p (h d) -> p h d", h=n),
                axis=AX.X, apply_absolute_value=True),
              reads=[src_f32] + list(extra), writes=[cols])
        else:
            E("vector", lambda e, s=src_f32, o=smax: e.reduce_max(
                out=o, in_=s, axis=AX.X, apply_absolute_value=True),
              reads=[src_f32] + list(extra), writes=[cols])
        E("vector", lambda e, o=smax: e.tensor_scalar(
            out=o, in0=o, scalar1=1.0 / 127.0, scalar2=1e-8,
            op0=ALU.mult, op1=ALU.max),
          reads=[cols], writes=[cols])
        rsm = cols[:, 16:16 + nheads]
        E("vector", lambda e, o=smax, r=rsm: e.reciprocal(out=r, in_=o),
          reads=[cols], writes=[cols])
        w = 2048 // nheads if nheads > 1 else None
        if nheads > 1:
            wd = (dst_bf.tensor.shape[-1] if False else 512) // nheads
        for h in range(nheads):
            if nheads > 1:
                sl = slice(h * 128, (h + 1) * 128)
            else:
                sl = slice(None)
            E("vector", lambda e, s=src_f32, t=tmp_f32, r=rsm, h=h, sl=sl: e.tensor_scalar(
                out=t[:, sl], in0=s[:, sl], scalar1=r[:, h:h + 1] if True else None,
                scalar2=None, op0=ALU.mult),
              reads=[src_f32, cols], writes=[tmp_f32])
        E("vector", lambda e, t=tmp_f32: e.tensor_scalar(
            out=t, in0=t, scalar1=12582912.0, scalar2=12582912.0,
            op0=ALU.add, op1=ALU.subtract),
          reads=[tmp_f32], writes=[tmp_f32])
        E("vector", lambda e, t=tmp_f32: e.tensor_scalar(
            out=t, in0=t, scalar1=-127.0, scalar2=127.0, op0=ALU.max, op1=ALU.min),
          reads=[tmp_f32], writes=[tmp_f32])
        for h in range(nheads):
            if nheads > 1:
                sl = slice(h * 128, (h + 1) * 128)
            else:
                sl = slice(None)
            E("vector", lambda e, t=tmp_f32, d=dst_bf, o=smax, h=h, sl=sl: e.tensor_scalar(
                out=d[:, sl], in0=t[:, sl], scalar1=o[:, h:h + 1],
                scalar2=None, op0=ALU.mult),
              reads=[tmp_f32, cols], writes=[dst_bf])

    def rope(dst_f32, src_f32, costile, sintile, nheads, tmp):
        for h in range(nheads):
            b = h * 128
            for (o0, o1) in ((0, 64), (64, 0)):
                E("vector", lambda e, s=src_f32, t=tmp, b=b, o0=o0, o1=o1, st=sintile:
                  e.tensor_tensor(out=t[:, b + o0:b + o0 + 64],
                                  in0=s[:, b + o1:b + o1 + 64],
                                  in1=st[:, o0:o0 + 64], op=ALU.mult),
                  reads=[src_f32, sintile], writes=[tmp])
            E("vector", lambda e, s=src_f32, d=dst_f32, b=b, ct=costile:
              e.tensor_tensor(out=d[:, b:b + 128], in0=s[:, b:b + 128],
                              in1=ct[:], op=ALU.mult),
              reads=[src_f32, costile], writes=[dst_f32])
        E("vector", lambda e, d=dst_f32, t=tmp: e.tensor_tensor(
            out=d, in0=d, in1=t, op=ALU.add),
          reads=[dst_f32, tmp], writes=[dst_f32])

    def sandwich(otile_ap, xin_dram, xout_dram, tt, xtmp, xnew, extra_reads=()):
        ln_stats(otile_ap, D, extra_reads=extra_reads)
        ln_cols(nmc, stc)
        p.dma("sync", xtmp, xin_dram[tt * 128:(tt + 1) * 128, :])
        adddiv(xnew, otile_ap, nmc, stc, [otile_ap] + list(extra_reads), [xnew])
        E("vector", lambda e, x=xnew, xt=xtmp: e.tensor_tensor(
            out=x, in0=x, in1=xt, op=ALU.add),
          reads=[xnew, xtmp], writes=[xnew])
        p.dma("gpsimd", xout_dram[tt * 128:(tt + 1) * 128, :], xnew)

    def l0_attn():
        out_dram = y_out if UPTO == 1 else xr1
        p.push_scope()
        qhatT = p.sb("qhatT", (128, H * 1024), BF16)
        wq_sr = p.sb("wq_sr", (1, D))
        wk_sr = p.sb("wk_sr", (1, DKV))
        wv_sr = p.sb("wv_sr", (1, DKV))
        wq_rr = p.sb("wq_rr", (1, D))
        wk_rr = p.sb("wk_rr", (1, DKV))
        wv_rr = p.sb("wv_rr", (1, DKV))

        def quant_w_tile(raw_ap, dst_ap, sc_bank, rc_bank, tmpf, tmpb):
            E("vector", lambda e, r=raw_ap, b=rc_bank, t=tmpf: e.tensor_tensor(
                out=t, in0=r, in1=b[:, 0:512], op=ALU.mult),
              reads=[raw_ap, rc_bank], writes=[tmpf])
            E("vector", lambda e, t=tmpf: e.tensor_scalar(
                out=t, in0=t, scalar1=12582912.0, scalar2=12582912.0,
                op0=ALU.add, op1=ALU.subtract),
              reads=[tmpf], writes=[tmpf])
            E("vector", lambda e, d=dst_ap, b=sc_bank, t=tmpf: e.tensor_tensor(
                out=d, in0=t, in1=b[:, 0:512], op=ALU.mult),
              reads=[tmpf, sc_bank], writes=[dst_ap])

        p.push_scope()
        wnat = p.sb("wnat", (128, 2 * D))
        colm = p.sb("colm", (128, 1))
        for nm_, rows, srow, rrow in (("wq_nat", D, wq_sr, wq_rr),
                                      ("wk_nat", DKV, wk_sr, wk_rr),
                                      ("wv_nat", DKV, wv_sr, wv_rr)):
            for ot in range(rows // 128):
                wb = wnat[:, (ot % 2) * D:(ot % 2 + 1) * D]
                p.dma("sync", wb, dram[nm_][ot * 128:(ot + 1) * 128, :])
                E("vector", lambda e, wb=wb: e.reduce_max(
                    out=colm[:], in_=wb, axis=AX.X, apply_absolute_value=True),
                  reads=[wnat], writes=[colm])
                E("vector", lambda e: e.tensor_scalar(
                    out=colm[:], in0=colm[:], scalar1=1.0 / 127.0, scalar2=1e-8,
                    op0=ALU.mult, op1=ALU.max),
                  reads=[colm], writes=[colm])
                E("tensor", lambda e: e.matmul(psf[5][0:1, 0:128], colm[:], identf[:],
                                               start=True, stop=True),
                  reads=[colm, identf], writes=[psf[5]])
                E("vector", lambda e, srow=srow, ot=ot: e.tensor_copy(
                    out=srow[:, ot * 128:(ot + 1) * 128], in_=psf[5][0:1, 0:128]),
                  reads=[psf[5]], writes=[srow])
                E("vector", lambda e, srow=srow, rrow=rrow, ot=ot: e.reciprocal(
                    out=rrow[:, ot * 128:(ot + 1) * 128],
                    in_=srow[:, ot * 128:(ot + 1) * 128]),
                  reads=[srow], writes=[rrow])
        p.pop_scope()
        p.barrier()

        p.push_scope()
        xhatTo = p.sb("xhatTo", (128, 16 * 1024), BF16)
        wqq = p.sb("wqq", (128, 16 * 512), BF16)
        tq1 = p.sb("tq1a", (128, D))
        tq2 = p.sb("tq2a", (128, D))
        xn = p.sb("xna", (128, D))
        xhat = p.sb("xhata", (128, D), BF16)
        wraw = p.sb("wrawa", (128, 1024))
        cosb = p.sb("cosba", (128, HD))
        sinb = p.sb("sinba", (128, HD))
        qhat = p.sb("qhat_b", (128, 512), BF16)
        xtl = p.sb("xtla", (128, 2 * D))
        for tt in range(NT):
            xt = xtl[:, (tt % 2) * D:(tt % 2 + 1) * D]
            p.dma("sync", xt, x_own[tt * 128:(tt + 1) * 128, :])
            ln_stats(xt, D, extra_reads=[xtl])
            ln_cols(nmc, stc)
            adddiv(xn[:], xt, nmc, stc, [xtl], [xn])
            quant_act(xhat[:], xn[:], tq1[:], tq2[:], 1, smx)
            for dt in range(16):
                trans(xhatTo[:, dt * 1024 + tt * 128: dt * 1024 + (tt + 1) * 128],
                      xhat[:, dt * 128:(dt + 1) * 128])
        for oc in range(4):
            E("tensor", lambda e, oc=oc: e.matmul(
                psf[4][:, 0:512], onesr[0:1, 0:128], wq_sr[0:1, oc * 512:(oc + 1) * 512],
                start=True, stop=True),
              reads=[onesr, wq_sr], writes=[psf[4]])
            E("tensor", lambda e, oc=oc: e.matmul(
                psf[3][:, 0:512], onesr[0:1, 0:128], wq_rr[0:1, oc * 512:(oc + 1) * 512],
                start=True, stop=True),
              reads=[onesr, wq_rr], writes=[psf[3]])
            for dt in range(16):
                wb = wraw[:, (dt % 2) * 512:(dt % 2 + 1) * 512]
                p.dma("sync", wb,
                      dram["wqT"][dt * 128:(dt + 1) * 128, oc * 512:(oc + 1) * 512])
                quant_w_tile(wb, wqq[:, dt * 512:(dt + 1) * 512], psf[4], psf[3],
                             tq1[:, 0:512], tq2[:, 0:512])
            for tt in range(NT):
                bank = psf[tt % 4]
                for dt in range(16):
                    E("tensor", lambda e, b=bank, dt=dt, tt=tt: e.matmul(
                        b[:, 0:512],
                        xhatTo[:, dt * 1024 + tt * 128: dt * 1024 + (tt + 1) * 128],
                        wqq[:, dt * 512:(dt + 1) * 512],
                        start=(dt == 0), stop=(dt == 15)),
                      reads=[xhatTo, wqq], writes=[bank])
                p.dma("sync", cosb[:], dram["cos_own"][tt * 128:(tt + 1) * 128, :])
                p.dma("sync", sinb[:], dram["sinA_own"][tt * 128:(tt + 1) * 128, :])
                E("vector", lambda e, b=bank, tq2=tq2: e.tensor_copy(
                    out=tq2[:, 1024:1536], in_=b[:, 0:512]),
                  reads=[bank], writes=[tq2])
                rope(tq2[:, 1536:2048], tq2[:, 1024:1536], cosb[:], sinb[:], 4,
                     tq1[:, 1024:1536])
                quant_act(qhat[:], tq2[:, 1536:2048], tq1[:, 0:512], tq2[:, 0:512],
                          4, smx4)
                for hh in range(4):
                    h = oc * 4 + hh
                    trans(qhatT[:, h * 1024 + tt * 128: h * 1024 + (tt + 1) * 128],
                          qhat[:, hh * 128:(hh + 1) * 128])
        p.pop_scope()
        p.barrier()

        p.push_scope()
        khatT = p.sb("khatT", (128, HKV * T), BF16)
        vhato = p.sb("vhato", (128, 16 * HKV * 132), BF16)
        p.push_scope()
        wkq = p.sb("wkq", (128, 16 * 512), BF16)
        wvq = p.sb("wvq", (128, 16 * 512), BF16)
        xhatTc = p.sb("xhatTc", (128, 16 * 128), BF16)
        tq1 = p.sb("tq1b", (128, D))
        tq2 = p.sb("tq2b", (128, D))
        xn = p.sb("xnb", (128, D))
        xhat = p.sb("xhatb", (128, D), BF16)
        wraw = p.sb("wrawb", (128, 1024))
        cosb = p.sb("cosbb", (128, HD))
        sinb = p.sb("sinbb", (128, HD))
        khat = p.sb("khat_b", (128, 512), BF16)
        vhat = p.sb("vhat_b", (128, 512), BF16)
        xctx = p.sb("xctx", (128, 2 * D))
        for nm_, srow, rrow, dstq in (("wkT", wk_sr, wk_rr, wkq),
                                      ("wvT", wv_sr, wv_rr, wvq)):
            E("tensor", lambda e, srow=srow: e.matmul(
                psf[4][:, 0:512], onesr[0:1, 0:128], srow[0:1, 0:512],
                start=True, stop=True),
              reads=[onesr, srow], writes=[psf[4]])
            E("tensor", lambda e, rrow=rrow: e.matmul(
                psf[3][:, 0:512], onesr[0:1, 0:128], rrow[0:1, 0:512],
                start=True, stop=True),
              reads=[onesr, rrow], writes=[psf[3]])
            for dt in range(16):
                wb = wraw[:, (dt % 2) * 512:(dt % 2 + 1) * 512]
                p.dma("sync", wb, dram[nm_][dt * 128:(dt + 1) * 128, :])
                quant_w_tile(wb, dstq[:, dt * 512:(dt + 1) * 512], psf[4], psf[3],
                             tq1[:, 0:512], tq2[:, 0:512])
        E("gpsimd", lambda e: e.memset(vhato[:], 1.0), reads=[], writes=[vhato])
        for ct in range(16):
            xc = xctx[:, (ct % 2) * D:(ct % 2 + 1) * D]
            p.dma("sync", xc, x_ctx[ct * 128:(ct + 1) * 128, :])
            ln_stats(xc, D, extra_reads=[xctx])
            ln_cols(nmc, stc)
            adddiv(xn[:], xc, nmc, stc, [xctx], [xn])
            quant_act(xhat[:], xn[:], tq1[:], tq2[:], 1, smx)
            for dt in range(16):
                trans(xhatTc[:, dt * 128:(dt + 1) * 128], xhat[:, dt * 128:(dt + 1) * 128])
            for dt in range(16):
                E("tensor", lambda e, dt=dt: e.matmul(
                    psf[0][:, 0:512], xhatTc[:, dt * 128:(dt + 1) * 128],
                    wkq[:, dt * 512:(dt + 1) * 512], start=(dt == 0), stop=(dt == 15)),
                  reads=[xhatTc, wkq], writes=[psf[0]])
            for dt in range(16):
                E("tensor", lambda e, dt=dt: e.matmul(
                    psf[1][:, 0:512], xhatTc[:, dt * 128:(dt + 1) * 128],
                    wvq[:, dt * 512:(dt + 1) * 512], start=(dt == 0), stop=(dt == 15)),
                  reads=[xhatTc, wvq], writes=[psf[1]])
            p.dma("sync", cosb[:], dram["cos_ctx"][ct * 128:(ct + 1) * 128, :])
            p.dma("sync", sinb[:], dram["sinA_ctx"][ct * 128:(ct + 1) * 128, :])
            E("vector", lambda e: e.tensor_copy(out=tq2[:, 1024:1536], in_=psf[0][:, 0:512]),
              reads=[psf[0]], writes=[tq2])
            rope(tq2[:, 1536:2048], tq2[:, 1024:1536], cosb[:], sinb[:], 4,
                 tq1[:, 1024:1536])
            quant_act(khat[:], tq2[:, 1536:2048], tq1[:, 0:512], tq2[:, 0:512], 4, smx4)
            E("vector", lambda e: e.tensor_copy(out=tq2[:, 1024:1536], in_=psf[1][:, 0:512]),
              reads=[psf[1]], writes=[tq2])
            quant_act(vhat[:], tq2[:, 1024:1536], tq1[:, 0:512], tq2[:, 0:512], 4,
                      cols[:, 8:12])
            for kv in range(HKV):
                trans(khatT[:, kv * T + ct * 128: kv * T + (ct + 1) * 128],
                      khat[:, kv * 128:(kv + 1) * 128])
                base = (ct * HKV + kv) * 132
                E("vector", lambda e, b=base, kv=kv: e.tensor_copy(
                    out=vhato[:, b:b + 128], in_=vhat[:, kv * 128:(kv + 1) * 128]),
                  reads=[vhat], writes=[vhato])
        p.pop_scope()
        p.barrier()

        p.push_scope()
        expS = p.sb("expS", (128, 16 * 512), BF16)
        maskb = p.sb("maskb", (128, 16 * 512), BF16)
        atm = [p.sb(f"atm{i}", (128, D), BF16) for i in range(4)]
        wob = p.sb("wob", (128, 16 * 512), BF16)
        otile = [p.sb(f"otile_a{i}", (128, D)) for i in range(4)]
        avp = p.sb("avp", (128, 132))
        xtmp = p.sb("xtmp_a", (128, D))
        xnew = p.sb("xnew_a", (128, D))
        for c in range(2):
            p.dma("scalar", maskb[:].rearrange("p (j q) -> p j q", j=16),
                  dram["masks"][c, :, :, :].rearrange("j pp q -> pp j q"))
            njt = 4 * c + 12
            for h in range(H):
                kv = h // 4
                for j in range(njt):
                    bank = psf[j % 4]
                    esl = expS[:, j * 512:(j + 1) * 512]
                    E("tensor", lambda e, b=bank, h=h, j=j, kv=kv, c=c: e.matmul(
                        b[:, 0:512],
                        khatT[:, kv * T + j * 128: kv * T + (j + 1) * 128],
                        qhatT[:, h * 1024 + c * 512: h * 1024 + (c + 1) * 512],
                        start=True, stop=True),
                      reads=[khatT, qhatT], writes=[bank])
                    E("scalar", lambda e, b=bank, o=esl: e.activation(
                        out=o, in_=b[:, 0:512], func=AF.Exp, scale=ASCALE),
                      reads=[bank], writes=[expS])
                    E("vector", lambda e, o=esl, j=j: e.tensor_tensor(
                        out=o, in0=o, in1=maskb[:, j * 512:(j + 1) * 512], op=ALU.mult),
                      reads=[expS, maskb], writes=[expS])
                for il in range(4):
                    i = c * 4 + il
                    bank = psf[4 + (il % 2)]
                    for j in range(i + 9):
                        E("tensor", lambda e, b=bank, j=j, il=il, kv=kv, i=i: e.matmul(
                            b[:, 0:132],
                            expS[:, j * 512 + il * 128: j * 512 + (il + 1) * 128],
                            vhato[:, (j * HKV + kv) * 132:(j * HKV + kv) * 132 + 132],
                            start=(j == 0), stop=(j == i + 8)),
                          reads=[expS, vhato], writes=[bank])
                    E("vector", lambda e, b=bank: e.tensor_copy(out=avp[:], in_=b[:, 0:132]),
                      reads=[bank], writes=[avp])
                    divcol(atm[il][:, h * 128:(h + 1) * 128], avp[:, 0:128],
                           avp[:, 128:129], [avp], [atm[il]])
            for il in range(4):
                for ft in range(16):
                    trans(atm[il][:, ft * 128:(ft + 1) * 128],
                          atm[il][:, ft * 128:(ft + 1) * 128])
            for dc in range(4):
                p.dma("gpsimd", wob[:].rearrange("p (dt q) -> p dt q", dt=16),
                      dram["woT"][:, dc * 512:(dc + 1) * 512].rearrange(
                          "(dt pp) q -> pp dt q", pp=128))
                for il in range(4):
                    bank = psf[il % 4]
                    for ft in range(16):
                        E("tensor", lambda e, b=bank, ft=ft, il=il: e.matmul(
                            b[:, 0:512], atm[il][:, ft * 128:(ft + 1) * 128],
                            wob[:, ft * 512:(ft + 1) * 512],
                            start=(ft == 0), stop=(ft == 15)),
                          reads=[atm[il], wob], writes=[bank])
                    E("vector", lambda e, b=bank, il=il, dc=dc: e.tensor_copy(
                        out=otile[il][:, dc * 512:(dc + 1) * 512], in_=b[:, 0:512]),
                      reads=[bank], writes=[otile[il]])
            for il in range(4):
                tt = c * 4 + il
                sandwich(otile[il][:], x_own, out_dram, tt, xtmp[:], xnew[:])
        p.pop_scope()
        p.pop_scope()
        p.pop_scope()
        p.barrier()

    def l0_ffn():
        out_dram = y_out if UPTO == 2 else xr2
        xin = xr1 if UPTO >= 1 else x_own
        p.push_scope()
        p.push_scope()
        wbt = p.sb("f_wbt", (128, FFN_H), BF16)
        absb = p.sb("f_absb", (128, FFN_H), BF16)
        sgn = p.sb("f_sgn", (128, FFN_H), BF16)
        w1t = p.sb("f_w1t", (128, FFN_H), BF16)
        srow = p.sb("f_srow", (2, FFN_H), BF16)
        hrow = p.sb("f_hrow", (2, FFN_H), BF16)

        def tern_quant(src_dram, n_tiles, width, store_fn):
            nch = width // 512
            for it_ in range(n_tiles):
                p.dma("gpsimd", wbt[:, 0:width], src_dram[it_ * 128:(it_ + 1) * 128, :])
                E("scalar", lambda e, w=width: e.activation(
                    out=absb[:, 0:w], in_=wbt[:, 0:w], func=AF.Abs),
                  reads=[wbt], writes=[absb])
                E("scalar", lambda e, w=width: e.sign(out=sgn[:, 0:w], in_=wbt[:, 0:w]),
                  reads=[wbt], writes=[sgn])
                for ch in range(nch):
                    E("tensor", lambda e, ch=ch: e.matmul(
                        psf[2][0:2, 0:512], e64[:], absb[:, ch * 512:(ch + 1) * 512],
                        start=True, stop=True),
                      reads=[absb, e64], writes=[psf[2]])
                    E("vector", lambda e, ch=ch: e.tensor_scalar(
                        out=srow[:, ch * 512:(ch + 1) * 512],
                        in0=psf[2][0:2, 0:512], scalar1=1.0 / 64.0, scalar2=1e-8,
                        op0=ALU.mult, op1=ALU.max),
                      reads=[psf[2]], writes=[srow])
                    E("gpsimd", lambda e, ch=ch: e.tensor_scalar(
                        out=hrow[:, ch * 512:(ch + 1) * 512],
                        in0=srow[:, ch * 512:(ch + 1) * 512],
                        scalar1=0.5, scalar2=None, op0=ALU.mult),
                      reads=[srow], writes=[hrow])
                    E("tensor", lambda e, ch=ch: e.matmul(
                        psf[3][:, 0:512], sel2[:], srow[:, ch * 512:(ch + 1) * 512],
                        start=True, stop=True),
                      reads=[sel2, srow], writes=[psf[3]])
                    E("tensor", lambda e, ch=ch: e.matmul(
                        psf[4][:, 0:512], sel2[:], hrow[:, ch * 512:(ch + 1) * 512],
                        start=True, stop=True),
                      reads=[sel2, hrow], writes=[psf[4]])
                    E("vector", lambda e, ch=ch: e.tensor_tensor(
                        out=w1t[:, ch * 512:(ch + 1) * 512],
                        in0=absb[:, ch * 512:(ch + 1) * 512],
                        in1=psf[4][:, 0:512], op=ALU.is_gt),
                      reads=[absb, psf[4]], writes=[w1t])
                    E("vector", lambda e, ch=ch: e.tensor_tensor(
                        out=w1t[:, ch * 512:(ch + 1) * 512],
                        in0=w1t[:, ch * 512:(ch + 1) * 512],
                        in1=psf[3][:, 0:512], op=ALU.mult),
                      reads=[w1t, psf[3]], writes=[w1t])
                    E("vector", lambda e, ch=ch: e.tensor_tensor(
                        out=w1t[:, ch * 512:(ch + 1) * 512],
                        in0=w1t[:, ch * 512:(ch + 1) * 512],
                        in1=sgn[:, ch * 512:(ch + 1) * 512], op=ALU.mult),
                      reads=[w1t, sgn], writes=[w1t])
                store_fn(it_)

        def store_fc(dt):
            p.dma("sync", w1fc_s[:, dt, :, :].rearrange("hc pp cc -> pp hc cc"),
                  w1t[:].rearrange("p (hc cc) -> p hc cc", hc=64))

        def store_pr(ht):
            p.dma("sync", w1pr_s[:, ht, :, :].rearrange("dc pp cc -> pp dc cc"),
                  w1t[:, 0:D].rearrange("p (dc cc) -> p dc cc", dc=16))

        tern_quant(dram["wfcT"], 16, FFN_H, store_fc)
        tern_quant(dram["wprojT"], 64, D, store_pr)
        p.pop_scope()
        p.barrier()

        p.push_scope()
        xn2T = p.sb("xn2T", (128, 16 * 1024), BF16)
        p.push_scope()
        xtl = p.sb("f_xtl", (128, 2 * D))
        xnb = p.sb("f_xnb", (128, D), BF16)
        for tt in range(NT):
            xt = xtl[:, (tt % 2) * D:(tt % 2 + 1) * D]
            p.dma("sync", xt, xin[tt * 128:(tt + 1) * 128, :])
            ln_stats(xt, D, extra_reads=[xtl])
            ln_cols(nmc, stc)
            adddiv(xnb[:], xt, nmc, stc, [xtl], [xnb])
            for dt in range(16):
                trans(xn2T[:, dt * 1024 + tt * 128: dt * 1024 + (tt + 1) * 128],
                      xnb[:, dt * 128:(dt + 1) * 128])
        p.pop_scope()

        hqT = p.sb("hqT", (128, 64 * 512), BF16)
        w1sl = p.sb("w1sl", (128, 2 * 2048), BF16)
        w2sl = p.sb("w2sl", (128, 2 * 8192), BF16)
        rbuf = p.sb("f_rbuf", (128, 512), BF16)
        hq2 = p.sb("f_hq2", (128, 512), BF16)
        stdr = p.sb("f_stdr", (1, 512))
        outTb = p.sb("f_outTb", (128, 512), BF16)
        otile4 = p.sb("f_otile4", (128, 4 * D), BF16)
        xtmp = p.sb("f_xtmp", (128, D))
        xnew = p.sb("f_xnew", (128, D))
        for tc in range(2):
            for hc in range(64):
                sl = w1sl[:, (hc % 2) * 2048:(hc % 2 + 1) * 2048]
                p.dma("sync", sl.rearrange("p (dt cc) -> p dt cc", dt=16),
                      w1fc_s[hc, :, :, :].rearrange("dt pp cc -> pp dt cc"))
                bank = psf[hc % 2]
                for dt in range(16):
                    E("tensor", lambda e, b=bank, sl=sl, dt=dt, tc=tc: e.matmul(
                        b[:, 0:512], sl[:, dt * 128:(dt + 1) * 128],
                        xn2T[:, dt * 1024 + tc * 512: dt * 1024 + (tc + 1) * 512],
                        start=(dt == 0), stop=(dt == 15)),
                      reads=[w1sl, xn2T], writes=[bank])
                E("scalar", lambda e, b=bank: e.activation(
                    out=rbuf[:], in_=b[:, 0:512], func=AF.Relu),
                  reads=[bank], writes=[rbuf])
                E("vector", lambda e, hc=hc: e.tensor_tensor(
                    out=hqT[:, hc * 512:(hc + 1) * 512], in0=rbuf[:], in1=rbuf[:],
                    op=ALU.mult),
                  reads=[rbuf], writes=[hqT])
                E("vector", lambda e, hc=hc: e.tensor_tensor(
                    out=hq2[:], in0=hqT[:, hc * 512:(hc + 1) * 512],
                    in1=hqT[:, hc * 512:(hc + 1) * 512], op=ALU.mult),
                  reads=[hqT], writes=[hq2])
                E("tensor", lambda e, hc=hc: e.matmul(
                    psf[2][0:1, 0:512], onesc[:], hq2[:],
                    start=(hc == 0), stop=(hc == 63)),
                  reads=[onesc, hq2], writes=[psf[2]])
            E("vector", lambda e: e.tensor_scalar(
                out=stdr[:], in0=psf[2][0:1, 0:512], scalar1=1.0 / FFN_H,
                scalar2=F32EPS, op0=ALU.mult, op1=ALU.add),
              reads=[psf[2]], writes=[stdr])
            E("scalar", lambda e: e.activation(out=stdr[:], in_=stdr[:], func=AF.Sqrt),
              reads=[stdr], writes=[stdr])
            E("vector", lambda e: e.reciprocal(out=stdr[:], in_=stdr[:]),
              reads=[stdr], writes=[stdr])
            E("tensor", lambda e: e.matmul(
                psf[3][:, 0:512], onesr[0:1, 0:128], stdr[:], start=True, stop=True),
              reads=[onesr, stdr], writes=[psf[3]])
            for hc in range(64):
                E("vector", lambda e, hc=hc: e.tensor_tensor(
                    out=hqT[:, hc * 512:(hc + 1) * 512],
                    in0=hqT[:, hc * 512:(hc + 1) * 512],
                    in1=psf[3][:, 0:512], op=ALU.mult),
                  reads=[hqT, psf[3]], writes=[hqT])
            for dcol in range(16):
                sl = w2sl[:, (dcol % 2) * 8192:(dcol % 2 + 1) * 8192]
                p.dma("sync", sl.rearrange("p (ht cc) -> p ht cc", ht=64),
                      w1pr_s[dcol, :, :, :].rearrange("ht pp cc -> pp ht cc"))
                bank = psf[dcol % 2]
                for ht in range(64):
                    E("tensor", lambda e, b=bank, sl=sl, ht=ht: e.matmul(
                        b[:, 0:512], sl[:, ht * 128:(ht + 1) * 128],
                        hqT[:, ht * 512:(ht + 1) * 512],
                        start=(ht == 0), stop=(ht == 63)),
                      reads=[w2sl, hqT], writes=[bank])
                E("vector", lambda e, b=bank: e.tensor_copy(out=outTb[:], in_=b[:, 0:512]),
                  reads=[bank], writes=[outTb])
                for q4 in range(4):
                    trans(otile4[:, q4 * D + dcol * 128: q4 * D + (dcol + 1) * 128],
                          outTb[:, q4 * 128:(q4 + 1) * 128])
            for q4 in range(4):
                tt = tc * 4 + q4
                sandwich(otile4[:, q4 * D:(q4 + 1) * D], xin, out_dram, tt,
                         xtmp[:], xnew[:])
        p.pop_scope()
        p.pop_scope()
        p.barrier()

    def l1_attn():
        out_dram = y_out if UPTO == 3 else xr3
        xin = xr2 if UPTO >= 2 else x_own
        p.push_scope()
        p.push_scope()
        xbcs = p.sb("xbcs", (128, 33 * 1024), BF16)
        dtv = p.sb("dtv", (128, 512))
        dtA = p.sb("dtA", (128, 512))
        ltA = p.sb("ltA", (64, NCH * 256))
        negl = p.sb("negl", (128, 8 * 64))
        wtb = p.sb("wtb", (128, 8 * 64))
        pendA = p.sb("pendA", (128, NCH * 64))
        Btok = p.sb("Btok", (128, 8 * 64), BF16)
        Bt_cm = p.sb("Bt_cm", (64, 1024), BF16)
        Ct_cm = p.sb("Ct_cm", (64, 1024), BF16)
        Sst = p.sb("Sst", (64, 2 * DIN))
        ssdAb = p.sb("ssdAb", (128, 256))
        ssdBb = p.sb("ssdBb", (128, 256))
        ddb = p.sb("ddb", (128, NH))
        aneg = p.sb("aneg", (128, NH))
        dtbb = p.sb("dtbb", (128, NH))
        onesc64 = p.sb("onesc64", (64, 256))
        p.dma("sync", ssdAb[:], dram["ssdA"][:, :])
        p.dma("sync", ssdBb[:], dram["ssdB"][:, :])
        p.dma("sync", ddb[:], dram["dd_rep"][:, :])
        p.dma("sync", dtbb[:], dram["dtb_rep"][:, :])
        p.dma("sync", aneg[:], dram["alog_rep"][:, :])
        E("scalar", lambda e: e.activation(out=aneg[:], in_=aneg[:], func=AF.Exp),
          reads=[aneg], writes=[aneg])
        E("vector", lambda e: e.tensor_scalar(
            out=aneg[:], in0=aneg[:], scalar1=-1.0, scalar2=None, op0=ALU.mult),
          reads=[aneg], writes=[aneg])
        E("gpsimd", lambda e: e.memset(onesc64[:], 1.0), reads=[], writes=[onesc64])
        E("gpsimd", lambda e: e.memset(Sst[:, 0:DIN], 0.0), reads=[], writes=[Sst])

        p.push_scope()
        uT = p.sb("uT", (128, 16 * 1024), BF16)
        p.push_scope()
        xtl = p.sb("m_xtl", (128, 2 * D))
        ln3b = p.sb("m_ln3", (128, D))
        ub = p.sb("m_ub", (128, D), BF16)
        rmswb = p.sb("m_rmsw", (128, D))
        p.dma("sync", rmswb[:], dram["rms_w_rep"][:, :])
        for tt in range(NT):
            xt = xtl[:, (tt % 2) * D:(tt % 2 + 1) * D]
            p.dma("sync", xt, xin[tt * 128:(tt + 1) * 128, :])
            ln_stats(xt, D, extra_reads=[xtl])
            ln_cols(nmc, stc)
            E("vector", lambda e, tt=tt: e.tensor_copy(
                out=m3[:, 2 * tt:2 * tt + 1], in_=nmc),
              reads=[cols], writes=[m3])
            E("vector", lambda e, tt=tt: e.tensor_copy(
                out=m3[:, 2 * tt + 1:2 * tt + 2], in_=stc),
              reads=[cols], writes=[m3])
            adddiv(ln3b[:], xt, nmc, stc, [xtl], [ln3b])
            ln_stats(ln3b[:], D)
            rms_std_col(stc, 1e-5)
            divcol(ln3b[:], ln3b[:], stc, [ln3b, cols], [ln3b])
            E("vector", lambda e: e.tensor_tensor(
                out=ub[:], in0=ln3b[:], in1=rmswb[:], op=ALU.mult),
              reads=[ln3b, rmswb], writes=[ub])
            for dt in range(16):
                trans(uT[:, dt * 1024 + tt * 128: dt * 1024 + (tt + 1) * 128],
                      ub[:, dt * 128:(dt + 1) * 128])
        p.pop_scope()

        p.push_scope()
        wzs = p.sb("m_wzs", (128, 2 * 8192), BF16)
        zb = p.sb("m_zb", (128, 512), BF16)
        h3 = p.sb("m_h3", (128, 3))
        wdts = p.sb("m_wdts", (128, 16 * 64), BF16)
        for zc in range(8):
            sl = wzs[:, (zc % 2) * 8192:(zc % 2 + 1) * 8192]
            p.dma("gpsimd", sl.rearrange("p (dt cc) -> p dt cc", dt=16),
                  dram["inprojT"][:, zc * 512:(zc + 1) * 512].rearrange(
                      "(dt pp) cc -> pp dt cc", pp=128))
            for tt in range(NT):
                bank = psf[tt % 2]
                for dt in range(16):
                    E("tensor", lambda e, b=bank, sl=sl, dt=dt, tt=tt: e.matmul(
                        b[:, 0:512],
                        uT[:, dt * 1024 + tt * 128: dt * 1024 + (tt + 1) * 128],
                        sl[:, dt * 512:(dt + 1) * 512],
                        start=(dt == 0), stop=(dt == 15)),
                      reads=[uT, wzs], writes=[bank])
                E("vector", lambda e, b=bank: e.tensor_copy(out=zb[:], in_=b[:, 0:512]),
                  reads=[bank], writes=[zb])
                p.dma("gpsimd", zspill[tt * 128:(tt + 1) * 128, zc * 512:(zc + 1) * 512],
                      zb[:])
        for ct in range(33):
            sl = wzs[:, (ct % 8) * 2048:(ct % 8) * 2048 + 2048]
            p.dma("gpsimd", sl.rearrange("p (dt cc) -> p dt cc", dt=16),
                  dram["inprojT"][:, DIN + ct * 128: DIN + (ct + 1) * 128].rearrange(
                      "(dt pp) cc -> pp dt cc", pp=128))
            for tc in range(2):
                bank = psf[2 + tc]
                for dt in range(16):
                    E("tensor", lambda e, b=bank, sl=sl, dt=dt, tc=tc: e.matmul(
                        b[:, 0:512], sl[:, dt * 128:(dt + 1) * 128],
                        uT[:, dt * 1024 + tc * 512: dt * 1024 + (tc + 1) * 512],
                        start=(dt == 0), stop=(dt == 15)),
                      reads=[uT, wzs], writes=[bank])
                E("vector", lambda e, b=bank, ct=ct, tc=tc: e.tensor_copy(
                    out=xbcs[:, ct * 1024 + tc * 512: ct * 1024 + (tc + 1) * 512],
                    in_=b[:, 0:512]),
                  reads=[bank], writes=[xbcs])
                if tc == 1:
                    E("vector", lambda e, b=bank: e.tensor_copy(
                        out=h3[:], in_=b[:, 509:512]),
                      reads=[bank], writes=[h3])
                    p.dma("gpsimd", cch_in[ct, :, :], h3[:])
        p.dma("gpsimd", wdts[:].rearrange("p (dt cc) -> p dt cc", dt=16),
              dram["inprojT"][:, 8320:8384].rearrange("(dt pp) cc -> pp dt cc", pp=128))
        for tt in range(NT):
            bank = psf[tt % 2]
            for dt in range(16):
                E("tensor", lambda e, b=bank, dt=dt, tt=tt: e.matmul(
                    b[:, 0:64],
                    uT[:, dt * 1024 + tt * 128: dt * 1024 + (tt + 1) * 128],
                    wdts[:, dt * 64:(dt + 1) * 64],
                    start=(dt == 0), stop=(dt == 15)),
                  reads=[uT, wdts], writes=[bank])
            E("vector", lambda e, b=bank, tt=tt: e.tensor_tensor(
                out=dtv[:, tt * 64:(tt + 1) * 64], in0=b[:, 0:64], in1=dtbb[:],
                op=ALU.add),
              reads=[bank, dtbb], writes=[dtv])
        E("scalar", lambda e: e.activation(out=dtA[:], in_=dtv[:], func=AF.Exp),
          reads=[dtv], writes=[dtA])
        E("scalar", lambda e: e.activation(out=dtv[:], in_=dtA[:], func=AF.Ln, bias=1.0),
          reads=[dtA], writes=[dtv])
        for tt in range(NT):
            E("vector", lambda e, tt=tt: e.tensor_tensor(
                out=dtA[:, tt * 64:(tt + 1) * 64], in0=dtv[:, tt * 64:(tt + 1) * 64],
                in1=aneg[:], op=ALU.mult),
              reads=[dtv, aneg], writes=[dtA])
        p.pop_scope()
        p.pop_scope()
        p.barrier()

        E("gpsimd", lambda e: e.collective_compute(
            "AllGather", ALU.bypass, replica_groups=[[0, 1], [2, 3], [4, 5], [6, 7]],
            ins=[cch_in.opt()], outs=[cch_out.opt()]),
          reads=[cch_in], writes=[cch_out], sem="cc", inc=1)
        p.push_scope()
        pd_ = p.sb("m_pd", (128, 1027))
        acc1 = p.sb("m_acc1", (128, 1024))
        acc2 = p.sb("m_acc2", (128, 1024))
        cwb = p.sb("m_cwb", (128, 4))
        cbb = p.sb("m_cbb", (128, 1))
        h3r = p.sb("m_h3r", (128, 3))
        for ct in range(33):
            p.dma("sync", cwb[:], dram["conv_w"][ct * 128:(ct + 1) * 128, :])
            p.dma("sync", cbb[:], dram["conv_b"][ct * 128:(ct + 1) * 128, :])
            p.dma("sync", h3r[:], cch_out[0, ct, :, :])
            E("vector", lambda e: e.tensor_scalar(
                out=pd_[:, 0:3], in0=h3r[:], scalar1=hzc[:], scalar2=None, op0=ALU.mult),
              reads=[h3r, hzc], writes=[pd_])
            E("vector", lambda e, ct=ct: e.tensor_copy(
                out=pd_[:, 3:1027], in_=xbcs[:, ct * 1024:(ct + 1) * 1024]),
              reads=[xbcs], writes=[pd_])
            E("vector", lambda e: e.tensor_scalar(
                out=acc1[:], in0=pd_[:, 0:1024], scalar1=cwb[:, 0:1], scalar2=None,
                op0=ALU.mult),
              reads=[pd_, cwb], writes=[acc1])
            E("vector", lambda e: e.scalar_tensor_tensor(
                out=acc2[:], in0=pd_[:, 1:1025], scalar=cwb[:, 1:2], in1=acc1[:],
                op0=ALU.mult, op1=ALU.add),
              reads=[pd_, cwb, acc1], writes=[acc2])
            E("vector", lambda e: e.scalar_tensor_tensor(
                out=acc1[:], in0=pd_[:, 2:1026], scalar=cwb[:, 2:3], in1=acc2[:],
                op0=ALU.mult, op1=ALU.add),
              reads=[pd_, cwb, acc2], writes=[acc1])
            E("vector", lambda e: e.scalar_tensor_tensor(
                out=acc2[:], in0=pd_[:, 3:1027], scalar=cwb[:, 3:4], in1=acc1[:],
                op0=ALU.mult, op1=ALU.add),
              reads=[pd_, cwb, acc1], writes=[acc2])
            E("scalar", lambda e, ct=ct: e.activation(
                out=xbcs[:, ct * 1024:(ct + 1) * 1024], in_=acc2[:], func=AF.Silu,
                bias=cbb[:]),
              reads=[acc2, cbb], writes=[xbcs])
        p.dma("gpsimd", Bt_cm[:, :], xbcs[0:64, 32 * 1024:33 * 1024])
        p.dma("gpsimd", Ct_cm[:, :], xbcs[64:128, 32 * 1024:33 * 1024])
        p.pop_scope()
        p.barrier()

        p.push_scope()
        dtt = p.sb("m_dtt", (64, 256))
        ltok = p.sb("m_ltok", (128, 128))
        lrow = p.sb("m_lrow", (1, 64))
        for ic in range(NCH):
            for jt in range(2):
                trans(dtt[:, jt * 128:(jt + 1) * 128],
                      dtA[:, (2 * ic + jt) * 64:(2 * ic + jt + 1) * 64],
                      pdim=128, fdim=64, f32=True)
            E("vector", lambda e, ic=ic: e.tensor_tensor_scan(
                out=ltA[:, ic * 256:(ic + 1) * 256], data0=onesc64[:], data1=dtt[:],
                initial=0.0, op0=ALU.mult, op1=ALU.add),
              reads=[onesc64, dtt], writes=[ltA])
            for jt in range(2):
                trans(ltok[:, jt * 64:(jt + 1) * 64],
                      ltA[:, ic * 256 + jt * 128: ic * 256 + (jt + 1) * 128],
                      pdim=64, fdim=128, f32=True)
            E("vector", lambda e, ic=ic: e.tensor_scalar(
                out=negl[:, ic * 128:(ic + 1) * 128], in0=ltok[:], scalar1=-1.0,
                scalar2=None, op0=ALU.mult),
              reads=[ltok], writes=[negl])
            p.dma("gpsimd", lrow[:], ltok[127:128, 64:128])
            E("tensor", lambda e, ic=ic: e.matmul(
                psf[4][:, 0:64], onesr[0:1, 0:128], lrow[0:1, 0:64],
                start=True, stop=True),
              reads=[onesr, lrow], writes=[psf[4]])
            E("scalar", lambda e, ic=ic: e.activation(
                out=pendA[:, ic * 64:(ic + 1) * 64], in_=psf[4][:, 0:64], func=AF.Exp),
              reads=[psf[4]], writes=[pendA])
            for jt in range(2):
                E("vector", lambda e, ic=ic, jt=jt: e.tensor_tensor(
                    out=wtb[:, (2 * ic + jt) * 64:(2 * ic + jt + 1) * 64],
                    in0=psf[4][:, 0:64], in1=ltok[:, jt * 64:(jt + 1) * 64],
                    op=ALU.subtract),
                  reads=[psf[4], ltok], writes=[wtb])
            E("scalar", lambda e, ic=ic: e.activation(
                out=wtb[:, 2 * ic * 64:(2 * ic + 2) * 64],
                in_=wtb[:, 2 * ic * 64:(2 * ic + 2) * 64], func=AF.Exp),
              reads=[wtb], writes=[wtb])
            E("vector", lambda e, ic=ic: e.tensor_tensor(
                out=wtb[:, 2 * ic * 64:(2 * ic + 2) * 64],
                in0=wtb[:, 2 * ic * 64:(2 * ic + 2) * 64],
                in1=dtv[:, 2 * ic * 64:(2 * ic + 2) * 64], op=ALU.mult),
              reads=[wtb, dtv], writes=[wtb])
            for jt in range(2):
                trans(Btok[:, (2 * ic + jt) * 64:(2 * ic + jt + 1) * 64],
                      Bt_cm[:, ic * 256 + jt * 128: ic * 256 + (jt + 1) * 128],
                      pdim=64, fdim=128)
        p.pop_scope()
        p.barrier()

        p.push_scope()
        xtk = p.sb("m_xtk", (128, 2 * DIN), BF16)
        bwb = p.sb("m_bwb", (128, 64), BF16)

        def state_chunk(ic, rd, wr):
            for h in range(NH):
                for jt in range(2):
                    E("vector", lambda e, h=h, jt=jt, ic=ic: e.tensor_scalar(
                        out=bwb[:], in0=Btok[:, (2 * ic + jt) * 64:(2 * ic + jt + 1) * 64],
                        scalar1=wtb[:, (2 * ic + jt) * 64 + h:(2 * ic + jt) * 64 + h + 1],
                        scalar2=None, op0=ALU.mult),
                      reads=[Btok, wtb], writes=[bwb])
                    E("tensor", lambda e, h=h, jt=jt: e.matmul(
                        psf[5][0:64, 0:64], bwb[:],
                        xtk[:, jt * DIN + h * 64: jt * DIN + (h + 1) * 64],
                        start=(jt == 0), stop=(jt == 1)),
                      reads=[bwb, xtk], writes=[psf[5]])
                E("vector", lambda e, h=h, ic=ic, rd=rd, wr=wr: e.scalar_tensor_tensor(
                    out=Sst[:, wr * DIN + h * 64: wr * DIN + (h + 1) * 64],
                    in0=Sst[:, rd * DIN + h * 64: rd * DIN + (h + 1) * 64],
                    scalar=pendA[0:64, ic * 64 + h: ic * 64 + h + 1],
                    in1=psf[5][0:64, 0:64], op0=ALU.mult, op1=ALU.add),
                  reads=[Sst, pendA, psf[5]], writes=[Sst])

        def xtok_chunk(ic):
            for cp in range(32):
                for jt in range(2):
                    trans(xtk[:, jt * DIN + cp * 128: jt * DIN + (cp + 1) * 128],
                          xbcs[:, cp * 1024 + ic * 256 + jt * 128:
                               cp * 1024 + ic * 256 + (jt + 1) * 128])

        for ic in range(NCH):
            xtok_chunk(ic)
            state_chunk(ic, ic % 2, (ic + 1) % 2)
        fin = NCH % 2
        p.dma("gpsimd", cst_in[:, :], Sst[:, fin * DIN:(fin + 1) * DIN])
        E("gpsimd", lambda e: e.collective_compute(
            "AllGather", ALU.bypass, replica_groups=[[0, 1], [2, 3], [4, 5], [6, 7]],
            ins=[cst_in.opt()], outs=[cst_out.opt()]),
          reads=[cst_in], writes=[cst_out], sem="cc", inc=1)
        p.dma("sync", Sst[:, 0:DIN], cst_out[0, :, :])
        E("vector", lambda e: e.tensor_scalar(
            out=Sst[:, 0:DIN], in0=Sst[:, 0:DIN], scalar1=hzc[0:64, :], scalar2=None,
            op0=ALU.mult),
          reads=[Sst, hzc], writes=[Sst])
        p.barrier()

        Gb = p.sb("m_Gb", (128, 2 * 256), BF16)
        lrow2 = p.sb("m_lrow2", (1, 256))
        tmpG = p.sb("m_tmpG", (128, 2 * 256))
        Eb = p.sb("m_Eb", (128, 2 * 256))
        ebc = p.sb("m_ebc", (64, 256))
        chat = p.sb("m_chat", (64, 256))
        ybuf = p.sb("m_ybuf", (128, 2 * DIN), BF16)
        zbl = p.sb("m_zbl", (128, DIN), BF16)
        szb = p.sb("m_szb", (128, DIN), BF16)
        ygb = p.sb("m_ygb", (128, DIN), BF16)
        ygT = p.sb("m_ygT", (128, 32 * 128), BF16)
        nwb = p.sb("m_nwb", (128, DIN), BF16)
        p.dma("gpsimd", nwb[:], dram["normw_rep"][:, :])
        for ic in range(NCH):
            rd = ic % 2
            wr = (ic + 1) % 2
            xtok_chunk(ic)
            for h in range(NH):
                for jt in range(2):
                    bank = psf[jt]
                    E("tensor", lambda e, b=bank, ic=ic, jt=jt: e.matmul(
                        b[:, 0:256], Bt_cm[:, ic * 256 + jt * 128: ic * 256 + (jt + 1) * 128],
                        Ct_cm[:, ic * 256:(ic + 1) * 256], start=True, stop=True),
                      reads=[Bt_cm, Ct_cm], writes=[bank])
                    bank2 = psf[2 + jt]
                    if jt == 0:
                        p.dma("gpsimd", lrow2[:], ltA[h:h + 1, ic * 256:(ic + 1) * 256])
                    E("tensor", lambda e, b=bank2, ic=ic, h=h: e.matmul(
                        b[:, 0:256], onesr[0:1, 0:128], lrow2[0:1, 0:256],
                        start=True, stop=True),
                      reads=[onesr, lrow2], writes=[bank2])
                    ssd = ssdAb if jt == 0 else ssdBb
                    E("vector", lambda e, b=bank2, jt=jt, ssd=ssd: e.tensor_tensor(
                        out=tmpG[:, jt * 256:(jt + 1) * 256], in0=b[:, 0:256], in1=ssd[:],
                        op=ALU.add),
                      reads=[bank2, ssd], writes=[tmpG])
                    E("scalar", lambda e, jt=jt, ic=ic, h=h: e.activation(
                        out=Eb[:, jt * 256:(jt + 1) * 256],
                        in_=tmpG[:, jt * 256:(jt + 1) * 256], func=AF.Exp,
                        bias=negl[:, ic * 128 + jt * 64 + h: ic * 128 + jt * 64 + h + 1]),
                      reads=[tmpG, negl], writes=[Eb])
                    E("vector", lambda e, b=bank, jt=jt, ic=ic, h=h: e.scalar_tensor_tensor(
                        out=Gb[:, jt * 256:(jt + 1) * 256], in0=b[:, 0:256],
                        scalar=dtv[:, (2 * ic + jt) * 64 + h:(2 * ic + jt) * 64 + h + 1],
                        in1=Eb[:, jt * 256:(jt + 1) * 256], op0=ALU.mult, op1=ALU.mult),
                      reads=[bank, dtv, Eb], writes=[Gb])
                E("scalar", lambda e: e.activation(
                    out=ebc[:], in_=psf[2][0:64, 0:256], func=AF.Exp),
                  reads=[psf[2]], writes=[ebc])
                E("vector", lambda e, ic=ic: e.tensor_tensor(
                    out=chat[:], in0=Ct_cm[:, ic * 256:(ic + 1) * 256], in1=ebc[:],
                    op=ALU.mult),
                  reads=[Ct_cm, ebc], writes=[chat])
                for it in range(2):
                    bank = psf[4]
                    for jt in range(it + 1):
                        E("tensor", lambda e, b=bank, jt=jt, it=it, h=h: e.matmul(
                            b[:, 0:64], Gb[:, jt * 256 + it * 128: jt * 256 + (it + 1) * 128],
                            xtk[:, jt * DIN + h * 64: jt * DIN + (h + 1) * 64],
                            start=(jt == 0), stop=False),
                          reads=[Gb, xtk], writes=[bank])
                    E("tensor", lambda e, b=bank, it=it, h=h, rd=rd: e.matmul(
                        b[:, 0:64], chat[0:64, it * 128:(it + 1) * 128],
                        Sst[:, rd * DIN + h * 64: rd * DIN + (h + 1) * 64],
                        start=False, stop=True),
                      reads=[chat, Sst], writes=[bank])
                    E("vector", lambda e, b=bank, it=it, h=h: e.scalar_tensor_tensor(
                        out=ybuf[:, it * DIN + h * 64: it * DIN + (h + 1) * 64],
                        in0=xtk[:, it * DIN + h * 64: it * DIN + (h + 1) * 64],
                        scalar=ddb[:, h:h + 1], in1=b[:, 0:64],
                        op0=ALU.mult, op1=ALU.add),
                      reads=[xtk, ddb, bank], writes=[ybuf])
            state_chunk(ic, rd, wr)
            for it in range(2):
                tile = 2 * ic + it
                p.dma("sync", zbl[:], zspill[tile * 128:(tile + 1) * 128, :])
                E("scalar", lambda e: e.activation(out=szb[:], in_=zbl[:], func=AF.Silu),
                  reads=[zbl], writes=[szb])
                E("vector", lambda e, it=it: e.tensor_tensor(
                    out=ygb[:], in0=ybuf[:, it * DIN:(it + 1) * DIN], in1=szb[:],
                    op=ALU.mult),
                  reads=[ybuf, szb], writes=[ygb])
                ln_stats(ygb[:], DIN)
                rms_std_col(stc, 1e-5)
                divcol(ygb[:], ygb[:], stc, [ygb, cols], [ygb])
                E("vector", lambda e: e.tensor_tensor(
                    out=ygb[:], in0=ygb[:], in1=nwb[:], op=ALU.mult),
                  reads=[ygb, nwb], writes=[ygb])
                for et in range(32):
                    trans(ygT[:, et * 128:(et + 1) * 128], ygb[:, et * 128:(et + 1) * 128])
                p.dma("gpsimd", ygsp[tile, :, :, :].rearrange("et pp cc -> pp et cc"),
                      ygT[:].rearrange("p (et cc) -> p et cc", et=32))
        p.pop_scope()
        p.pop_scope()
        p.barrier()

        p.push_scope()
        ygl = p.sb("m_ygl", (128, 2 * 4096), BF16)
        wop = p.sb("m_wop", (128, 32 * 512), BF16)
        otfull = p.sb("m_otfull", (128, 8 * D))
        oti = p.sb("m_oti", (128, D))
        xtmp = p.sb("m_xtmp", (128, D))
        xnew = p.sb("m_xnew", (128, D))
        for dc in range(4):
            p.dma("gpsimd", wop[:].rearrange("p (et cc) -> p et cc", et=32),
                  dram["outprojT"][:, dc * 512:(dc + 1) * 512].rearrange(
                      "(et pp) cc -> pp et cc", pp=128))
            for tt in range(NT):
                sl = ygl[:, (tt % 2) * 4096:(tt % 2 + 1) * 4096]
                p.dma("sync", sl.rearrange("p (et cc) -> p et cc", et=32),
                      ygsp[tt, :, :, :].rearrange("et pp cc -> pp et cc"))
                bank = psf[tt % 2]
                for et in range(32):
                    E("tensor", lambda e, b=bank, sl=sl, et=et: e.matmul(
                        b[:, 0:512], sl[:, et * 128:(et + 1) * 128],
                        wop[:, et * 512:(et + 1) * 512],
                        start=(et == 0), stop=(et == 31)),
                      reads=[ygl, wop], writes=[bank])
                E("vector", lambda e, b=bank, tt=tt, dc=dc: e.tensor_copy(
                    out=otfull[:, tt * D + dc * 512: tt * D + (dc + 1) * 512],
                    in_=b[:, 0:512]),
                  reads=[bank], writes=[otfull])
        for tt in range(NT):
            p.dma("sync", xtmp[:], xin[tt * 128:(tt + 1) * 128, :])
            E("vector", lambda e, tt=tt: e.tensor_scalar(
                out=xnew[:], in0=xtmp[:], scalar1=m3[:, 2 * tt:2 * tt + 1],
                scalar2=None, op0=ALU.add),
              reads=[xtmp, m3], writes=[xnew])
            divcol(xnew[:], xnew[:], m3[:, 2 * tt + 1:2 * tt + 2], [xnew, m3], [xnew])
            E("vector", lambda e, tt=tt: e.tensor_tensor(
                out=oti[:], in0=otfull[:, tt * D:(tt + 1) * D], in1=xnew[:], op=ALU.add),
              reads=[otfull, xnew], writes=[oti])
            sandwich(oti[:], xin, out_dram, tt, xtmp[:], xnew[:])
        p.pop_scope()
        p.pop_scope()
        p.barrier()

    def l1_ffn():
        out_dram = y_out
        xin = (xr2 if SKIP3 else xr3) if UPTO >= 3 else x_own
        p.push_scope()
        XT = p.sb("l_XT", (128, 16 * 1024), BF16)
        p.push_scope()
        xtl = p.sb("l_xtl", (128, 2 * D))
        xnb = p.sb("l_xnb", (128, D), BF16)
        for tt in range(NT):
            xt = xtl[:, (tt % 2) * D:(tt % 2 + 1) * D]
            p.dma("sync", xt, xin[tt * 128:(tt + 1) * 128, :])
            ln_stats(xt, D, extra_reads=[xtl])
            ln_cols(nmc, stc)
            adddiv(xnb[:], xt, nmc, stc, [xtl], [xnb])
            for dt in range(16):
                trans(XT[:, dt * 1024 + tt * 128: dt * 1024 + (tt + 1) * 128],
                      xnb[:, dt * 128:(dt + 1) * 128])
        p.pop_scope()

        DhT = p.sb("l_DhT", (128, 64 * 512), BF16)
        w0sl = p.sb("l_w0sl", (128, 2 * 2048), BF16)
        w1sl = p.sb("l_w1sl", (128, 2 * 8192), BF16)
        b0c = p.sb("l_b0c", (128, 2))
        outTb = p.sb("l_outTb", (128, 512), BF16)
        otile4 = p.sb("l_otile4", (128, 4 * D), BF16)
        b1b = p.sb("l_b1b", (128, D))
        otf32 = p.sb("l_otf32", (128, D))
        otf16 = p.sb("l_otf16", (128, D), F16)
        xtmp = p.sb("l_xtmp", (128, D))
        xnew = p.sb("l_xnew", (128, D))
        p.dma("sync", b1b[:], dram["bias1_rep"][:, :])
        for tc in range(2):
            for hc in range(64):
                sl = w0sl[:, (hc % 2) * 2048:(hc % 2 + 1) * 2048]
                p.dma("gpsimd", sl.rearrange("p (dt cc) -> p dt cc", dt=16),
                      dram["w0T"][:, hc * 128:(hc + 1) * 128].rearrange(
                          "(dt pp) cc -> pp dt cc", pp=128))
                bc = b0c[:, hc % 2: hc % 2 + 1]
                p.dma("sync", bc, dram["bias0"][hc * 128:(hc + 1) * 128, :])
                bank = psf[hc % 2]
                for dt in range(16):
                    E("tensor", lambda e, b=bank, sl=sl, dt=dt, tc=tc: e.matmul(
                        b[:, 0:512], sl[:, dt * 128:(dt + 1) * 128],
                        XT[:, dt * 1024 + tc * 512: dt * 1024 + (tc + 1) * 512],
                        start=(dt == 0), stop=(dt == 15)),
                      reads=[w0sl, XT], writes=[bank])
                E("scalar", lambda e, b=bank, hc=hc, bc=bc: e.activation(
                    out=DhT[:, hc * 512:(hc + 1) * 512], in_=b[:, 0:512],
                    func=AF.Gelu, bias=bc),
                  reads=[bank, b0c], writes=[DhT])
            for dcol in range(16):
                sl = w1sl[:, (dcol % 2) * 8192:(dcol % 2 + 1) * 8192]
                p.dma("gpsimd", sl.rearrange("p (ht cc) -> p ht cc", ht=64),
                      dram["w1T"][:, dcol * 128:(dcol + 1) * 128].rearrange(
                          "(ht pp) cc -> pp ht cc", pp=128))
                bank = psf[2 + dcol % 2]
                for ht in range(64):
                    E("tensor", lambda e, b=bank, sl=sl, ht=ht: e.matmul(
                        b[:, 0:512], sl[:, ht * 128:(ht + 1) * 128],
                        DhT[:, ht * 512:(ht + 1) * 512],
                        start=(ht == 0), stop=(ht == 63)),
                      reads=[w1sl, DhT], writes=[bank])
                E("vector", lambda e, b=bank: e.tensor_copy(out=outTb[:], in_=b[:, 0:512]),
                  reads=[bank], writes=[outTb])
                for q4 in range(4):
                    trans(otile4[:, q4 * D + dcol * 128: q4 * D + (dcol + 1) * 128],
                          outTb[:, q4 * 128:(q4 + 1) * 128])
            for q4 in range(4):
                tt = tc * 4 + q4
                E("vector", lambda e, q4=q4: e.tensor_tensor(
                    out=otf32[:], in0=otile4[:, q4 * D:(q4 + 1) * D], in1=b1b[:],
                    op=ALU.add),
                  reads=[otile4, b1b], writes=[otf32])
                E("vector", lambda e: e.tensor_copy(out=otf16[:], in_=otf32[:]),
                  reads=[otf32], writes=[otf16])
                E("vector", lambda e: e.tensor_copy(out=otf32[:], in_=otf16[:]),
                  reads=[otf16], writes=[otf32])
                sandwich(otf32[:], xin, out_dram, tt, xtmp[:], xnew[:])
        p.pop_scope()
        p.barrier()

    if UPTO >= 1:
        l0_attn()
    if UPTO >= 2:
        l0_ffn()
    if UPTO >= 3 and not SKIP3:
        l1_attn()
    if UPTO >= 4:
        l1_ffn()
    if UPTO == 0:
        p.push_scope()
        xt0 = p.sb("xt0", (128, D))
        for tt in range(NT):
            p.dma("sync", xt0[:], x_own[tt * 128:(tt + 1) * 128, :])
            p.dma("gpsimd", y_out[tt * 128:(tt + 1) * 128, :], xt0[:])
        p.pop_scope()
    elif UPTO < 4:
        pass

    p.barrier()
    p.finalize()
    return nc


def _in_map(g, c, x, core):
    pr, hi = core // 2, core % 2
    m = dict(g)
    m["x_ctx"] = np.ascontiguousarray(x[pr])
    m["x_own"] = np.ascontiguousarray(x[pr, hi * 1024:(hi + 1) * 1024])
    m["cos_ctx"] = c["cos_ctx"]; m["sinA_ctx"] = c["sinA_ctx"]
    m["cos_own"] = c[f"cos_own_{hi}"]; m["sinA_own"] = c[f"sinA_own_{hi}"]
    m["masks"] = c[f"masks_{hi}"]; m["hz"] = c[f"hz_{hi}"]
    m["ident"] = c["ident"]; m["onesr"] = c["onesr"]
    m["E64"] = c["E64"]; m["sel2"] = c["sel2"]
    m["ssdA"] = c["ssdA"]; m["ssdB"] = c["ssdB"]
    return m


def kernel(**inputs):
    g = _prep_inputs(inputs)
    c = _host_consts()
    x = np.asarray(inputs["L0_attn_x"])
    nc = build_program()
    in_maps = [_in_map(g, c, x, core) for core in range(8)]
    res = run_bass_kernel_spmd(nc, in_maps, core_ids=list(range(8)))
    out = np.empty((4, 2048, 2048), dtype=np.float32)
    for core in range(8):
        pr, hi = core // 2, core % 2
        out[pr, hi * 1024:(hi + 1) * 1024] = res.results[core]["y_out"]
    return out


# version 03
# speedup vs baseline: 1.1494x; 1.1494x over previous
import os
import math
import numpy as np
import ml_dtypes
from contextlib import ExitStack

import concourse.bass as bass
import concourse.mybir as mybir
from concourse.bass_utils import run_bass_kernel_spmd
from concourse.bass import broadcast_tensor_aps

F32 = mybir.dt.float32
BF16 = mybir.dt.bfloat16
F16 = mybir.dt.float16
AF = mybir.ActivationFunctionType
ALU = mybir.AluOpType
AX = mybir.AxisListType

D = 2048
T = 2048
NTOK = 1024
NT = 8
H = 16
HKV = 4
HD = 128
DKV = 512
FFN_H = 8192
DIN = 4096
DST = 64
NH = 64
CONV_DIM = 4224
EIN = 8384
CHUNK = 256
NCH = 4
ASCALE = 1.0 / math.sqrt(HD)
F32EPS = float(np.finfo(np.float32).eps)
NEG = -1e30

UPTO = int(os.environ.get("MK_UPTO", "4"))
SKIP3 = os.environ.get("MK_SKIP3", "") == "1"

ENGS = ("sync", "scalar", "vector", "tensor", "gpsimd")
ESEM = {"scalar": "eS", "vector": "eV", "tensor": "eT", "gpsimd": "eG"}
QSEM = {"sync": "qY", "scalar": "qA", "vector": "qV", "tensor": "qT", "gpsimd": "qG"}


class PH:
    __slots__ = ("val",)

    def __init__(self):
        self.val = None


def _nm(x):
    if isinstance(x, str):
        return x
    t = getattr(x, "tensor", None)
    if t is not None:
        return t.name
    return getattr(x, "name")


class P:
    def __init__(self, nc):
        self.nc = nc
        self.es = ExitStack()
        self.streams = {e: [] for e in ENGS}
        self.semh = {}
        self.cnt = {}
        self.qpool = {"sync": [f"qy{i}" for i in range(10)],
                      "scalar": [f"qa{i}" for i in range(4)],
                      "vector": [f"qv{i}" for i in range(2)],
                      "tensor": [f"qt{i}" for i in range(2)],
                      "gpsimd": [f"qg{i}" for i in range(10)]}
        names = ["eS", "eV", "eT", "eG", "cc"] + [x for v in self.qpool.values() for x in v]
        for s in names:
            self.semh[s] = self.es.enter_context(nc.semaphore(s))
            self.cnt[s] = 0
        self.qpi = {k: 0 for k in self.qpool}
        self.qpend = {k: [] for k in self.qpool}
        self.waited = {e: {} for e in ENGS}
        self.buf = {}
        self.scopes = []

    def sb(self, name, shape, dtype=F32, persist=False):
        cm = self.nc.sbuf_tensor(name, list(shape), dtype)
        tgt = self.es if (persist or not self.scopes) else self.scopes[-1]
        return tgt.enter_context(cm)

    def ps(self, name, shape, dtype=F32):
        return self.es.enter_context(self.nc.psum_tensor(name, list(shape), dtype))

    def push_scope(self):
        s = ExitStack()
        self.scopes.append(s)
        return s

    def pop_scope(self):
        self.barrier()
        self.scopes.pop().close()

    def _wait(self, eng, sem, val, ph=None):
        if val <= 0:
            return
        hw = self.waited[eng]
        if hw.get(sem, 0) >= val:
            return
        hw[sem] = val
        h = self.semh[sem]
        if ph is not None:
            ph.val = (self.semh[sem], val)
        self.streams[eng].append(lambda e, h=h, v=val: e.wait_ge(h, v))

    def emit(self, eng, fn, reads=(), writes=(), q=None, sem=None, inc=None):
        need = {}
        for r in reads:
            b = self.buf.get(_nm(r))
            if b and b["w"]:
                s, v, ph = b["w"]
                if s not in need or need[s][0] < v:
                    need[s] = (v, ph)
        for w in writes:
            b = self.buf.get(_nm(w))
            if b:
                if b["w"]:
                    s, v, ph = b["w"]
                    if s not in need or need[s][0] < v:
                        need[s] = (v, ph)
                for s, (v, ph) in b["r"].items():
                    if s not in need or need[s][0] < v:
                        need[s] = (v, ph)
        for s, (v, ph) in need.items():
            self._wait(eng, s, v, ph)
        if sem is None:
            if q:
                pool = self.qpool[eng]
                sem = pool[self.qpi[eng] % len(pool)]
                self.qpi[eng] += 1
            else:
                sem = ESEM[eng]
        if inc is None:
            inc = 16 if q else 1
        self.cnt[sem] += inc
        val = self.cnt[sem]
        h = self.semh[sem]
        self.streams[eng].append(lambda e, fn=fn, h=h, inc=inc: fn(e).then_inc(h, inc))
        ph = None
        if not q:
            ph = PH()
            self.streams[eng].append(ph)
        else:
            pend = self.qpend[eng]
            pend.append((sem, val))
            if len(pend) > len(self.qpool[eng]) - 2:
                s_, v_ = pend.pop(0)
                h_ = self.semh[s_]
                hw = self.waited[eng]
                if hw.get(s_, 0) < v_:
                    hw[s_] = v_
                    self.streams[eng].append(lambda e, h=h_, v=v_: e.wait_ge(h, v))
        edge = (sem, val, ph)
        for w in writes:
            self.buf[_nm(w)] = {"w": edge, "r": {}}
        for r in reads:
            b = self.buf.setdefault(_nm(r), {"w": None, "r": {}})
            pv = b["r"].get(sem)
            if pv is None or pv[0] < val:
                b["r"][sem] = (val, ph)
        return val

    def dma(self, eng, out, in_, extra_reads=()):
        self.emit(eng, lambda e, o=out, i=in_: e.dma_start(out=o, in_=i),
                  reads=[in_] + list(extra_reads), writes=[out], q=QSEM[eng])

    def barrier(self):
        for eng in ("scalar", "vector", "tensor", "gpsimd"):
            sem = ESEM[eng]
            self.cnt[sem] += 1
            h = self.semh[sem]
            self.streams[eng].append(lambda e, h=h: e.nop().then_inc(h, 1))
            self.waited[eng][sem] = self.cnt[sem]
        snap = dict(self.cnt)
        for eng in ENGS:
            for s, v in snap.items():
                self._wait(eng, s, v)
        for b in self.buf.values():
            b["r"] = {}
            if b["w"]:
                s, v, ph = b["w"]
                b["w"] = (s, v, None)

    def finalize(self):
        blk = self.es.enter_context(self.nc.Block())
        for eng in ENGS:
            ops = []
            for f in self.streams[eng]:
                if isinstance(f, PH):
                    if f.val is not None:
                        ops.append(lambda e, h=f.val[0], v=f.val[1]: e.wait_ge(h, v))
                else:
                    ops.append(f)
            getattr(blk, eng)(lambda e, st=ops: [f(e) for f in st])
        self.es.close()



def _host_consts():
    c = {}
    i = np.arange(0, HD, 2, dtype=np.float64)
    inv_freq = 1.0 / (1000000.0 ** (i / HD))
    pos = np.arange(T, dtype=np.float64)
    freqs = pos[:, None] * inv_freq[None, :]
    emb = np.concatenate([freqs, freqs], axis=-1)
    cos = np.cos(emb).astype(np.float32)
    sinA = np.sin(emb).astype(np.float32)
    sinA[:, : HD // 2] *= -1.0
    c["cos_ctx"] = cos
    c["sinA_ctx"] = sinA
    c["ident"] = np.eye(128, dtype=np.float32)
    c["onesr"] = np.ones((1, 512), dtype=np.float32)
    jj = np.arange(128)[:, None]
    ii = np.arange(256)[None, :]
    c["ssdA"] = np.where(ii >= jj, 0.0, NEG).astype(np.float32)
    c["ssdB"] = np.where(ii >= 128 + jj, 0.0, NEG).astype(np.float32)
    e64 = np.zeros((128, 2), dtype=np.float32)
    e64[:64, 0] = 1.0
    e64[64:, 1] = 1.0
    c["E64"] = e64
    c["sel2"] = np.ascontiguousarray(e64.T)
    for hi in (0, 1):
        kk = np.arange(128)[:, None]
        qq = np.arange(512)[None, :]
        m = np.zeros((2, 16, 128, 512), dtype=np.float32)
        for cch in range(2):
            for j in range(16):
                m[cch, j] = ((hi * 1024 + cch * 512 + qq) >= (j * 128 + kk)).astype(np.float32)
        c[f"masks_{hi}"] = m.astype(ml_dtypes.bfloat16)
        c[f"hz_{hi}"] = np.full((128, 1), float(hi), dtype=np.float32)
        sl = slice(hi * 1024, (hi + 1) * 1024)
        c[f"cos_own_{hi}"] = cos[sl].copy()
        c[f"sinA_own_{hi}"] = sinA[sl].copy()
    return c


def _prep_inputs(inp):
    g = {}
    t = lambda a: np.ascontiguousarray(np.asarray(a).T)
    asc = np.ascontiguousarray
    rep = lambda a, n=128: asc(np.broadcast_to(np.asarray(a)[None, :], (n, len(a))))
    g["wqT"] = t(inp["L0_attn_wq"])
    g["wkT"] = t(inp["L0_attn_wk"])
    g["wvT"] = t(inp["L0_attn_wv"])
    g["woT"] = t(inp["L0_attn_wo"])
    g["wq_nat"] = asc(np.asarray(inp["L0_attn_wq"]))
    g["wk_nat"] = asc(np.asarray(inp["L0_attn_wk"]))
    g["wv_nat"] = asc(np.asarray(inp["L0_attn_wv"]))
    g["wfcT"] = t(inp["L0_ffn_W_fc"])
    g["wprojT"] = t(inp["L0_ffn_W_proj"])
    g["inprojT"] = t(inp["L1_attn_in_proj_w"])
    g["outprojT"] = t(inp["L1_attn_out_proj_w"])
    g["conv_w"] = asc(np.asarray(inp["L1_attn_conv_w"]))
    g["conv_b"] = asc(np.asarray(inp["L1_attn_conv_b"]).reshape(CONV_DIM, 1))
    g["rms_w_rep"] = rep(np.asarray(inp["L1_attn_rms_weight"]))
    g["dtb_rep"] = rep(np.asarray(inp["L1_attn_dt_bias"]))
    g["alog_rep"] = rep(np.asarray(inp["L1_attn_A_log"]))
    g["dd_rep"] = rep(np.asarray(inp["L1_attn_D"]))
    g["normw_rep"] = rep(np.asarray(inp["L1_attn_norm_w"]))
    g["w0T"] = t(inp["L1_ffn_W0"])
    g["w1T"] = t(inp["L1_ffn_W1"])
    g["bias0"] = asc(np.asarray(inp["L1_ffn_bias0"]).reshape(FFN_H, 1))
    g["bias1_rep"] = rep(np.asarray(inp["L1_ffn_bias1"]))
    return g



def build_program():
    nc = bass.Bass("TRN2", target_bir_lowering=False, debug=False)
    dram = {}

    def din(name, shape, dtype=F32):
        dram[name] = nc.dram_tensor(name, list(shape), dtype, kind="ExternalInput").ap()
        return dram[name]

    x_own = din("x_own", (NTOK, D))
    x_ctx = din("x_ctx", (T, D))
    for n, s in (("wqT", (D, D)), ("wkT", (D, DKV)), ("wvT", (D, DKV)), ("woT", (D, D)),
                 ("wq_nat", (D, D)), ("wk_nat", (DKV, D)), ("wv_nat", (DKV, D)),
                 ("wfcT", (D, FFN_H)), ("wprojT", (FFN_H, D)),
                 ("inprojT", (D, EIN)), ("outprojT", (DIN, D)),
                 ("conv_w", (CONV_DIM, 4)), ("conv_b", (CONV_DIM, 1)),
                 ("rms_w_rep", (128, D)), ("dtb_rep", (128, NH)), ("alog_rep", (128, NH)),
                 ("dd_rep", (128, NH)), ("normw_rep", (128, DIN)),
                 ("w0T", (D, FFN_H)), ("w1T", (FFN_H, D)),
                 ("bias0", (FFN_H, 1)), ("bias1_rep", (128, D)),
                 ("cos_ctx", (T, HD)), ("sinA_ctx", (T, HD)),
                 ("cos_own", (NTOK, HD)), ("sinA_own", (NTOK, HD)),
                 ("ident", (128, 128)), ("onesr", (1, 512)),
                 ("E64", (128, 2)), ("sel2", (2, 128)),
                 ("ssdA", (128, 256)), ("ssdB", (128, 256)), ("hz", (128, 1))):
        din(n, s)
    din("masks", (2, 16, 128, 512), BF16)

    y_out = nc.dram_tensor("y_out", [NTOK, D], F32, kind="ExternalOutput").ap()
    xr1 = nc.dram_tensor("xr1", [NTOK, D], F32).ap()
    xr2 = nc.dram_tensor("xr2", [NTOK, D], F32).ap()
    xr3 = nc.dram_tensor("xr3", [NTOK, D], F32).ap()
    zspill = nc.dram_tensor("zspill", [NTOK, DIN], BF16).ap()
    w1fc_s = nc.dram_tensor("w1fc_s", [64, 16, 128, 128], BF16).ap()
    w1pr_s = nc.dram_tensor("w1pr_s", [16, 64, 128, 128], BF16).ap()
    ygsp = nc.dram_tensor("ygsp", [8, 32, 128, 128], BF16).ap()
    cch_in = nc.dram_tensor("cch_in", [33, 128, 3], F32).ap()
    cch_out = nc.dram_tensor("cch_out", [2, 33, 128, 3], F32).ap()
    cst_in = nc.dram_tensor("cst_in", [64, DIN], F32).ap()
    cst_out = nc.dram_tensor("cst_out", [2, 64, DIN], F32).ap()

    p = P(nc)
    E = p.emit

    identf = p.sb("identf", (128, 128), persist=True)
    identb = p.sb("identb", (128, 128), BF16, persist=True)
    onesr = p.sb("onesr_s", (1, 512), persist=True)
    onesc = p.sb("onesc_s", (128, 1), BF16, persist=True)
    e64 = p.sb("e64_s", (128, 2), BF16, persist=True)
    sel2 = p.sb("sel2_s", (2, 128), BF16, persist=True)
    hzc = p.sb("hz_s", (128, 1), persist=True)
    stats = p.sb("stats", (128, 96), persist=True)
    mv = p.sb("mv", (128, 2), persist=True)
    cols = p.sb("cols", (128, 24), persist=True)
    m3 = p.sb("m3", (128, 16), persist=True)

    psf = [p.ps(f"psf{i}", [128, 512], F32) for i in range(6)]
    psb = [p.ps(f"psb{i}", [128, 1024], BF16) for i in range(2)]
    _tb = [0]

    p.dma("sync", identf[:], dram["ident"][:, :])
    p.dma("sync", onesr[:], dram["onesr"][:, :])
    p.dma("gpsimd", e64[:], dram["E64"][:, :])
    p.dma("gpsimd", sel2[:], dram["sel2"][:, :])
    p.dma("sync", hzc[:], dram["hz"][:, :])
    p.dma("gpsimd", identb[:], dram["ident"][:, :])
    E("gpsimd", lambda e: e.memset(onesc[:], 1.0), reads=[], writes=[onesc])

    nmc, stc, smx = cols[:, 0:1], cols[:, 1:2], cols[:, 2:3]
    smx4 = cols[:, 4:8]

    def trans(dst_ap, src_ap, pdim=128, fdim=128, f32=False):
        if not f32:
            bank = psb[_tb[0] % 2]
            _tb[0] += 1
            ident = identb
        else:
            bank = psf[5]
            ident = identf
        E("tensor", lambda e, b=bank, s=src_ap, idn=ident, pd=pdim, fd=fdim: e.transpose(
            out=b[0:fd, 0:pd], in_=s, identity=idn[0:pd, 0:pd]),
          reads=[src_ap, ident], writes=[bank])
        E("vector", lambda e, b=bank, d=dst_ap, pd=pdim, fd=fdim: e.tensor_copy(
            out=d, in_=b[0:fd, 0:pd]),
          reads=[bank], writes=[dst_ap])

    def divcol(out_ap, in0_ap, col_ap, rd, wr):
        rcp = cols[:, 23:24]
        E("vector", lambda e, c=col_ap: e.reciprocal(out=rcp, in_=c),
          reads=list(rd), writes=[cols])
        E("vector", lambda e, o=out_ap, i=in0_ap: e.tensor_scalar(
            out=o, in0=i, scalar1=rcp, scalar2=None, op0=ALU.mult),
          reads=list(rd) + [cols], writes=list(wr))

    def adddiv(out_ap, in0_ap, nm_col, std_col, rd, wr):
        E("vector", lambda e, o=out_ap, i=in0_ap, nm=nm_col: e.tensor_scalar(
            out=o, in0=i, scalar1=nm, scalar2=None, op0=ALU.add),
          reads=list(rd) + [cols], writes=list(wr))
        divcol(out_ap, out_ap, std_col, list(wr) + [cols], list(wr))

    def ln_stats(src_ap, width, extra_reads=()):
        nchk = (width + 511) // 512
        for ci in range(nchk):
            w = min(512, width - ci * 512)
            E("vector", lambda e, s=src_ap, ci=ci, w=w: e.bn_stats(
                out=stats[:, ci * 6:(ci + 1) * 6], in_=s[:, ci * 512: ci * 512 + w]),
              reads=[src_ap] + list(extra_reads), writes=[stats])
        E("vector", lambda e, n=nchk: e.bn_aggr(out=mv[:], in_=stats[:, 0:6 * n]),
          reads=[stats], writes=[mv])

    def ln_cols(nm_col, std_col, var_eps=1e-5):
        E("vector", lambda e, o=nm_col: e.tensor_scalar(
            out=o, in0=mv[:, 0:1], scalar1=-1.0, scalar2=None, op0=ALU.mult),
          reads=[mv], writes=[cols])
        E("vector", lambda e, o=std_col, ve=var_eps: e.tensor_scalar(
            out=o, in0=mv[:, 1:2], scalar1=ve, scalar2=None, op0=ALU.add),
          reads=[mv], writes=[cols])
        E("scalar", lambda e, o=std_col: e.activation(out=o, in_=o, func=AF.Sqrt),
          reads=[cols], writes=[cols])

    def rms_std_col(std_col, eps):
        E("vector", lambda e, o=std_col: e.tensor_tensor(
            out=o, in0=mv[:, 0:1], in1=mv[:, 0:1], op=ALU.mult),
          reads=[mv], writes=[cols])
        E("vector", lambda e, o=std_col: e.tensor_tensor(
            out=o, in0=o, in1=mv[:, 1:2], op=ALU.add),
          reads=[mv, cols], writes=[cols])
        E("vector", lambda e, o=std_col, ve=eps: e.tensor_scalar(
            out=o, in0=o, scalar1=ve, scalar2=None, op0=ALU.add),
          reads=[cols], writes=[cols])
        E("scalar", lambda e, o=std_col: e.activation(out=o, in_=o, func=AF.Sqrt),
          reads=[cols], writes=[cols])

    def quant_act(dst_bf, src_f32, tmp_f32, tmp2_f32, nheads, smax, extra=()):
        hd = None
        if nheads > 1:
            E("vector", lambda e, s=src_f32, n=nheads, o=smax: e.reduce_max(
                out=o, in_=s.rearrange("p (h d) -> p h d", h=n),
                axis=AX.X, apply_absolute_value=True),
              reads=[src_f32] + list(extra), writes=[cols])
        else:
            E("vector", lambda e, s=src_f32, o=smax: e.reduce_max(
                out=o, in_=s, axis=AX.X, apply_absolute_value=True),
              reads=[src_f32] + list(extra), writes=[cols])
        E("vector", lambda e, o=smax: e.tensor_scalar(
            out=o, in0=o, scalar1=1.0 / 127.0, scalar2=1e-8,
            op0=ALU.mult, op1=ALU.max),
          reads=[cols], writes=[cols])
        rsm = cols[:, 16:16 + nheads]
        E("vector", lambda e, o=smax, r=rsm: e.reciprocal(out=r, in_=o),
          reads=[cols], writes=[cols])
        w = 2048 // nheads if nheads > 1 else None
        if nheads > 1:
            wd = (dst_bf.tensor.shape[-1] if False else 512) // nheads
        for h in range(nheads):
            if nheads > 1:
                sl = slice(h * 128, (h + 1) * 128)
            else:
                sl = slice(None)
            E("vector", lambda e, s=src_f32, t=tmp_f32, r=rsm, h=h, sl=sl: e.tensor_scalar(
                out=t[:, sl], in0=s[:, sl], scalar1=r[:, h:h + 1] if True else None,
                scalar2=None, op0=ALU.mult),
              reads=[src_f32, cols], writes=[tmp_f32])
        E("vector", lambda e, t=tmp_f32: e.tensor_scalar(
            out=t, in0=t, scalar1=12582912.0, scalar2=12582912.0,
            op0=ALU.add, op1=ALU.subtract),
          reads=[tmp_f32], writes=[tmp_f32])
        E("vector", lambda e, t=tmp_f32: e.tensor_scalar(
            out=t, in0=t, scalar1=-127.0, scalar2=127.0, op0=ALU.max, op1=ALU.min),
          reads=[tmp_f32], writes=[tmp_f32])
        for h in range(nheads):
            if nheads > 1:
                sl = slice(h * 128, (h + 1) * 128)
            else:
                sl = slice(None)
            E("vector", lambda e, t=tmp_f32, d=dst_bf, o=smax, h=h, sl=sl: e.tensor_scalar(
                out=d[:, sl], in0=t[:, sl], scalar1=o[:, h:h + 1],
                scalar2=None, op0=ALU.mult),
              reads=[tmp_f32, cols], writes=[dst_bf])

    def rope(dst_f32, src_f32, costile, sintile, nheads, tmp):
        for h in range(nheads):
            b = h * 128
            for (o0, o1) in ((0, 64), (64, 0)):
                E("vector", lambda e, s=src_f32, t=tmp, b=b, o0=o0, o1=o1, st=sintile:
                  e.tensor_tensor(out=t[:, b + o0:b + o0 + 64],
                                  in0=s[:, b + o1:b + o1 + 64],
                                  in1=st[:, o0:o0 + 64], op=ALU.mult),
                  reads=[src_f32, sintile], writes=[tmp])
            E("vector", lambda e, s=src_f32, d=dst_f32, b=b, ct=costile:
              e.tensor_tensor(out=d[:, b:b + 128], in0=s[:, b:b + 128],
                              in1=ct[:], op=ALU.mult),
              reads=[src_f32, costile], writes=[dst_f32])
        E("vector", lambda e, d=dst_f32, t=tmp: e.tensor_tensor(
            out=d, in0=d, in1=t, op=ALU.add),
          reads=[dst_f32, tmp], writes=[dst_f32])

    def sandwich(otile_ap, xin_dram, xout_dram, tt, xtmp, xnew, extra_reads=()):
        ln_stats(otile_ap, D, extra_reads=extra_reads)
        ln_cols(nmc, stc)
        p.dma("sync", xtmp, xin_dram[tt * 128:(tt + 1) * 128, :])
        adddiv(xnew, otile_ap, nmc, stc, [otile_ap] + list(extra_reads), [xnew])
        E("vector", lambda e, x=xnew, xt=xtmp: e.tensor_tensor(
            out=x, in0=x, in1=xt, op=ALU.add),
          reads=[xnew, xtmp], writes=[xnew])
        p.dma("gpsimd", xout_dram[tt * 128:(tt + 1) * 128, :], xnew)

    def l0_attn():
        out_dram = y_out if UPTO == 1 else xr1
        p.push_scope()
        qhatT = p.sb("qhatT", (128, H * 1024), BF16)
        wq_sr = p.sb("wq_sr", (1, D))
        wk_sr = p.sb("wk_sr", (1, DKV))
        wv_sr = p.sb("wv_sr", (1, DKV))
        wq_rr = p.sb("wq_rr", (1, D))
        wk_rr = p.sb("wk_rr", (1, DKV))
        wv_rr = p.sb("wv_rr", (1, DKV))

        def quant_w_tile(raw_ap, dst_ap, sc_bank, rc_bank, tmpf, tmpb):
            E("vector", lambda e, r=raw_ap, b=rc_bank, t=tmpf: e.tensor_tensor(
                out=t, in0=r, in1=b[:, 0:512], op=ALU.mult),
              reads=[raw_ap, rc_bank], writes=[tmpf])
            E("vector", lambda e, t=tmpf: e.tensor_scalar(
                out=t, in0=t, scalar1=12582912.0, scalar2=12582912.0,
                op0=ALU.add, op1=ALU.subtract),
              reads=[tmpf], writes=[tmpf])
            E("vector", lambda e, d=dst_ap, b=sc_bank, t=tmpf: e.tensor_tensor(
                out=d, in0=t, in1=b[:, 0:512], op=ALU.mult),
              reads=[tmpf, sc_bank], writes=[dst_ap])

        p.push_scope()
        wnat = p.sb("wnat", (128, 2 * D))
        colm = p.sb("colm", (128, 1))
        for nm_, rows, srow, rrow in (("wq_nat", D, wq_sr, wq_rr),
                                      ("wk_nat", DKV, wk_sr, wk_rr),
                                      ("wv_nat", DKV, wv_sr, wv_rr)):
            for ot in range(rows // 128):
                wb = wnat[:, (ot % 2) * D:(ot % 2 + 1) * D]
                p.dma("sync", wb, dram[nm_][ot * 128:(ot + 1) * 128, :])
                E("vector", lambda e, wb=wb: e.reduce_max(
                    out=colm[:], in_=wb, axis=AX.X, apply_absolute_value=True),
                  reads=[wnat], writes=[colm])
                E("vector", lambda e: e.tensor_scalar(
                    out=colm[:], in0=colm[:], scalar1=1.0 / 127.0, scalar2=1e-8,
                    op0=ALU.mult, op1=ALU.max),
                  reads=[colm], writes=[colm])
                E("tensor", lambda e: e.matmul(psf[5][0:1, 0:128], colm[:], identf[:],
                                               start=True, stop=True),
                  reads=[colm, identf], writes=[psf[5]])
                E("vector", lambda e, srow=srow, ot=ot: e.tensor_copy(
                    out=srow[:, ot * 128:(ot + 1) * 128], in_=psf[5][0:1, 0:128]),
                  reads=[psf[5]], writes=[srow])
                E("vector", lambda e, srow=srow, rrow=rrow, ot=ot: e.reciprocal(
                    out=rrow[:, ot * 128:(ot + 1) * 128],
                    in_=srow[:, ot * 128:(ot + 1) * 128]),
                  reads=[srow], writes=[rrow])
        p.pop_scope()
        p.barrier()

        p.push_scope()
        xhatTo = p.sb("xhatTo", (128, 16 * 1024), BF16)
        wqq = p.sb("wqq", (128, 16 * 512), BF16)
        tq1 = p.sb("tq1a", (128, D))
        tq2 = p.sb("tq2a", (128, D))
        xn = p.sb("xna", (128, D))
        xhat = p.sb("xhata", (128, D), BF16)
        wraw = p.sb("wrawa", (128, 1024))
        cosb = p.sb("cosba", (128, HD))
        sinb = p.sb("sinba", (128, HD))
        qhat = p.sb("qhat_b", (128, 512), BF16)
        xtl = p.sb("xtla", (128, 2 * D))
        for tt in range(NT):
            xt = xtl[:, (tt % 2) * D:(tt % 2 + 1) * D]
            p.dma("sync", xt, x_own[tt * 128:(tt + 1) * 128, :])
            ln_stats(xt, D, extra_reads=[xtl])
            ln_cols(nmc, stc)
            adddiv(xn[:], xt, nmc, stc, [xtl], [xn])
            quant_act(xhat[:], xn[:], tq1[:], tq2[:], 1, smx)
            for dt in range(16):
                trans(xhatTo[:, dt * 1024 + tt * 128: dt * 1024 + (tt + 1) * 128],
                      xhat[:, dt * 128:(dt + 1) * 128])
        for oc in range(4):
            E("tensor", lambda e, oc=oc: e.matmul(
                psf[4][:, 0:512], onesr[0:1, 0:128], wq_sr[0:1, oc * 512:(oc + 1) * 512],
                start=True, stop=True),
              reads=[onesr, wq_sr], writes=[psf[4]])
            E("tensor", lambda e, oc=oc: e.matmul(
                psf[3][:, 0:512], onesr[0:1, 0:128], wq_rr[0:1, oc * 512:(oc + 1) * 512],
                start=True, stop=True),
              reads=[onesr, wq_rr], writes=[psf[3]])
            for dt in range(16):
                wb = wraw[:, (dt % 2) * 512:(dt % 2 + 1) * 512]
                p.dma("sync", wb,
                      dram["wqT"][dt * 128:(dt + 1) * 128, oc * 512:(oc + 1) * 512])
                quant_w_tile(wb, wqq[:, dt * 512:(dt + 1) * 512], psf[4], psf[3],
                             tq1[:, 0:512], tq2[:, 0:512])
            for tt in range(NT):
                bank = psf[tt % 4]
                for dt in range(16):
                    E("tensor", lambda e, b=bank, dt=dt, tt=tt: e.matmul(
                        b[:, 0:512],
                        xhatTo[:, dt * 1024 + tt * 128: dt * 1024 + (tt + 1) * 128],
                        wqq[:, dt * 512:(dt + 1) * 512],
                        start=(dt == 0), stop=(dt == 15)),
                      reads=[xhatTo, wqq], writes=[bank])
                p.dma("sync", cosb[:], dram["cos_own"][tt * 128:(tt + 1) * 128, :])
                p.dma("sync", sinb[:], dram["sinA_own"][tt * 128:(tt + 1) * 128, :])
                E("vector", lambda e, b=bank, tq2=tq2: e.tensor_copy(
                    out=tq2[:, 1024:1536], in_=b[:, 0:512]),
                  reads=[bank], writes=[tq2])
                rope(tq2[:, 1536:2048], tq2[:, 1024:1536], cosb[:], sinb[:], 4,
                     tq1[:, 1024:1536])
                quant_act(qhat[:], tq2[:, 1536:2048], tq1[:, 0:512], tq2[:, 0:512],
                          4, smx4)
                for hh in range(4):
                    h = oc * 4 + hh
                    trans(qhatT[:, h * 1024 + tt * 128: h * 1024 + (tt + 1) * 128],
                          qhat[:, hh * 128:(hh + 1) * 128])
        p.pop_scope()
        p.barrier()

        p.push_scope()
        khatT = p.sb("khatT", (128, HKV * T), BF16)
        vhato = p.sb("vhato", (128, 16 * HKV * 132), BF16)
        p.push_scope()
        wkq = p.sb("wkq", (128, 16 * 512), BF16)
        wvq = p.sb("wvq", (128, 16 * 512), BF16)
        xhatTc = p.sb("xhatTc", (128, 16 * 128), BF16)
        tq1 = p.sb("tq1b", (128, D))
        tq2 = p.sb("tq2b", (128, D))
        xn = p.sb("xnb", (128, D))
        xhat = p.sb("xhatb", (128, D), BF16)
        wraw = p.sb("wrawb", (128, 1024))
        cosb = p.sb("cosbb", (128, HD))
        sinb = p.sb("sinbb", (128, HD))
        khat = p.sb("khat_b", (128, 512), BF16)
        vhat = p.sb("vhat_b", (128, 512), BF16)
        xctx = p.sb("xctx", (128, 2 * D))
        for nm_, srow, rrow, dstq in (("wkT", wk_sr, wk_rr, wkq),
                                      ("wvT", wv_sr, wv_rr, wvq)):
            E("tensor", lambda e, srow=srow: e.matmul(
                psf[4][:, 0:512], onesr[0:1, 0:128], srow[0:1, 0:512],
                start=True, stop=True),
              reads=[onesr, srow], writes=[psf[4]])
            E("tensor", lambda e, rrow=rrow: e.matmul(
                psf[3][:, 0:512], onesr[0:1, 0:128], rrow[0:1, 0:512],
                start=True, stop=True),
              reads=[onesr, rrow], writes=[psf[3]])
            for dt in range(16):
                wb = wraw[:, (dt % 2) * 512:(dt % 2 + 1) * 512]
                p.dma("sync", wb, dram[nm_][dt * 128:(dt + 1) * 128, :])
                quant_w_tile(wb, dstq[:, dt * 512:(dt + 1) * 512], psf[4], psf[3],
                             tq1[:, 0:512], tq2[:, 0:512])
        E("gpsimd", lambda e: e.memset(vhato[:], 1.0), reads=[], writes=[vhato])
        for ct in range(16):
            xc = xctx[:, (ct % 2) * D:(ct % 2 + 1) * D]
            p.dma("sync", xc, x_ctx[ct * 128:(ct + 1) * 128, :])
            ln_stats(xc, D, extra_reads=[xctx])
            ln_cols(nmc, stc)
            adddiv(xn[:], xc, nmc, stc, [xctx], [xn])
            quant_act(xhat[:], xn[:], tq1[:], tq2[:], 1, smx)
            for dt in range(16):
                trans(xhatTc[:, dt * 128:(dt + 1) * 128], xhat[:, dt * 128:(dt + 1) * 128])
            for dt in range(16):
                E("tensor", lambda e, dt=dt: e.matmul(
                    psf[0][:, 0:512], xhatTc[:, dt * 128:(dt + 1) * 128],
                    wkq[:, dt * 512:(dt + 1) * 512], start=(dt == 0), stop=(dt == 15)),
                  reads=[xhatTc, wkq], writes=[psf[0]])
            for dt in range(16):
                E("tensor", lambda e, dt=dt: e.matmul(
                    psf[1][:, 0:512], xhatTc[:, dt * 128:(dt + 1) * 128],
                    wvq[:, dt * 512:(dt + 1) * 512], start=(dt == 0), stop=(dt == 15)),
                  reads=[xhatTc, wvq], writes=[psf[1]])
            p.dma("sync", cosb[:], dram["cos_ctx"][ct * 128:(ct + 1) * 128, :])
            p.dma("sync", sinb[:], dram["sinA_ctx"][ct * 128:(ct + 1) * 128, :])
            E("vector", lambda e: e.tensor_copy(out=tq2[:, 1024:1536], in_=psf[0][:, 0:512]),
              reads=[psf[0]], writes=[tq2])
            rope(tq2[:, 1536:2048], tq2[:, 1024:1536], cosb[:], sinb[:], 4,
                 tq1[:, 1024:1536])
            quant_act(khat[:], tq2[:, 1536:2048], tq1[:, 0:512], tq2[:, 0:512], 4, smx4)
            E("vector", lambda e: e.tensor_copy(out=tq2[:, 1024:1536], in_=psf[1][:, 0:512]),
              reads=[psf[1]], writes=[tq2])
            quant_act(vhat[:], tq2[:, 1024:1536], tq1[:, 0:512], tq2[:, 0:512], 4,
                      cols[:, 8:12])
            for kv in range(HKV):
                trans(khatT[:, kv * T + ct * 128: kv * T + (ct + 1) * 128],
                      khat[:, kv * 128:(kv + 1) * 128])
                base = (ct * HKV + kv) * 132
                E("vector", lambda e, b=base, kv=kv: e.tensor_copy(
                    out=vhato[:, b:b + 128], in_=vhat[:, kv * 128:(kv + 1) * 128]),
                  reads=[vhat], writes=[vhato])
        p.pop_scope()
        p.barrier()

        p.push_scope()
        expS = p.sb("expS", (128, 16 * 512), BF16)
        maskb = p.sb("maskb", (128, 16 * 512), BF16)
        atm = [p.sb(f"atm{i}", (128, D), BF16) for i in range(4)]
        wob = p.sb("wob", (128, 16 * 512), BF16)
        otile = [p.sb(f"otile_a{i}", (128, D)) for i in range(4)]
        avp = p.sb("avp", (128, 132))
        xtmp = p.sb("xtmp_a", (128, D))
        xnew = p.sb("xnew_a", (128, D))
        for c in range(2):
            p.dma("scalar", maskb[:].rearrange("p (j q) -> p j q", j=16),
                  dram["masks"][c, :, :, :].rearrange("j pp q -> pp j q"))
            njt = 4 * c + 12
            for h in range(H):
                kv = h // 4
                for j in range(njt):
                    bank = psf[j % 4]
                    esl = expS[:, j * 512:(j + 1) * 512]
                    E("tensor", lambda e, b=bank, h=h, j=j, kv=kv, c=c: e.matmul(
                        b[:, 0:512],
                        khatT[:, kv * T + j * 128: kv * T + (j + 1) * 128],
                        qhatT[:, h * 1024 + c * 512: h * 1024 + (c + 1) * 512],
                        start=True, stop=True),
                      reads=[khatT, qhatT], writes=[bank])
                    E("scalar", lambda e, b=bank, o=esl: e.activation(
                        out=o, in_=b[:, 0:512], func=AF.Exp, scale=ASCALE),
                      reads=[bank], writes=[expS])
                    E("vector", lambda e, o=esl, j=j: e.tensor_tensor(
                        out=o, in0=o, in1=maskb[:, j * 512:(j + 1) * 512], op=ALU.mult),
                      reads=[expS, maskb], writes=[expS])
                for il in range(4):
                    i = c * 4 + il
                    bank = psf[4 + (il % 2)]
                    for j in range(i + 9):
                        E("tensor", lambda e, b=bank, j=j, il=il, kv=kv, i=i: e.matmul(
                            b[:, 0:132],
                            expS[:, j * 512 + il * 128: j * 512 + (il + 1) * 128],
                            vhato[:, (j * HKV + kv) * 132:(j * HKV + kv) * 132 + 132],
                            start=(j == 0), stop=(j == i + 8)),
                          reads=[expS, vhato], writes=[bank])
                    E("vector", lambda e, b=bank: e.tensor_copy(out=avp[:], in_=b[:, 0:132]),
                      reads=[bank], writes=[avp])
                    divcol(atm[il][:, h * 128:(h + 1) * 128], avp[:, 0:128],
                           avp[:, 128:129], [avp], [atm[il]])
            for il in range(4):
                for ft in range(16):
                    trans(atm[il][:, ft * 128:(ft + 1) * 128],
                          atm[il][:, ft * 128:(ft + 1) * 128])
            for dc in range(4):
                p.dma("gpsimd", wob[:].rearrange("p (dt q) -> p dt q", dt=16),
                      dram["woT"][:, dc * 512:(dc + 1) * 512].rearrange(
                          "(dt pp) q -> pp dt q", pp=128))
                for il in range(4):
                    bank = psf[il % 4]
                    for ft in range(16):
                        E("tensor", lambda e, b=bank, ft=ft, il=il: e.matmul(
                            b[:, 0:512], atm[il][:, ft * 128:(ft + 1) * 128],
                            wob[:, ft * 512:(ft + 1) * 512],
                            start=(ft == 0), stop=(ft == 15)),
                          reads=[atm[il], wob], writes=[bank])
                    E("vector", lambda e, b=bank, il=il, dc=dc: e.tensor_copy(
                        out=otile[il][:, dc * 512:(dc + 1) * 512], in_=b[:, 0:512]),
                      reads=[bank], writes=[otile[il]])
            for il in range(4):
                tt = c * 4 + il
                sandwich(otile[il][:], x_own, out_dram, tt, xtmp[:], xnew[:])
        p.pop_scope()
        p.pop_scope()
        p.pop_scope()
        p.barrier()

    def l0_ffn():
        out_dram = y_out if UPTO == 2 else xr2
        xin = xr1 if UPTO >= 1 else x_own
        p.push_scope()
        p.push_scope()
        wbt = p.sb("f_wbt", (128, FFN_H), BF16)
        absb = p.sb("f_absb", (128, FFN_H), BF16)
        sgn = p.sb("f_sgn", (128, FFN_H), BF16)
        w1t = p.sb("f_w1t", (128, FFN_H), BF16)
        srow = p.sb("f_srow", (2, FFN_H), BF16)
        hrow = p.sb("f_hrow", (2, FFN_H), BF16)

        def tern_quant(src_dram, n_tiles, width, store_fn):
            nch = width // 512
            for it_ in range(n_tiles):
                p.dma("gpsimd", wbt[:, 0:width], src_dram[it_ * 128:(it_ + 1) * 128, :])
                E("scalar", lambda e, w=width: e.activation(
                    out=absb[:, 0:w], in_=wbt[:, 0:w], func=AF.Abs),
                  reads=[wbt], writes=[absb])
                E("scalar", lambda e, w=width: e.sign(out=sgn[:, 0:w], in_=wbt[:, 0:w]),
                  reads=[wbt], writes=[sgn])
                for ch in range(nch):
                    E("tensor", lambda e, ch=ch: e.matmul(
                        psf[2][0:2, 0:512], e64[:], absb[:, ch * 512:(ch + 1) * 512],
                        start=True, stop=True),
                      reads=[absb, e64], writes=[psf[2]])
                    E("vector", lambda e, ch=ch: e.tensor_scalar(
                        out=srow[:, ch * 512:(ch + 1) * 512],
                        in0=psf[2][0:2, 0:512], scalar1=1.0 / 64.0, scalar2=1e-8,
                        op0=ALU.mult, op1=ALU.max),
                      reads=[psf[2]], writes=[srow])
                    E("vector", lambda e, ch=ch: e.tensor_scalar(
                        out=hrow[:, ch * 512:(ch + 1) * 512],
                        in0=srow[:, ch * 512:(ch + 1) * 512],
                        scalar1=0.5, scalar2=None, op0=ALU.mult),
                      reads=[srow], writes=[hrow])
                    E("tensor", lambda e, ch=ch: e.matmul(
                        psf[3][:, 0:512], sel2[:], srow[:, ch * 512:(ch + 1) * 512],
                        start=True, stop=True),
                      reads=[sel2, srow], writes=[psf[3]])
                    E("tensor", lambda e, ch=ch: e.matmul(
                        psf[4][:, 0:512], sel2[:], hrow[:, ch * 512:(ch + 1) * 512],
                        start=True, stop=True),
                      reads=[sel2, hrow], writes=[psf[4]])
                    E("vector", lambda e, ch=ch: e.tensor_tensor(
                        out=w1t[:, ch * 512:(ch + 1) * 512],
                        in0=absb[:, ch * 512:(ch + 1) * 512],
                        in1=psf[4][:, 0:512], op=ALU.is_gt),
                      reads=[absb, psf[4]], writes=[w1t])
                    E("vector", lambda e, ch=ch: e.tensor_tensor(
                        out=w1t[:, ch * 512:(ch + 1) * 512],
                        in0=w1t[:, ch * 512:(ch + 1) * 512],
                        in1=psf[3][:, 0:512], op=ALU.mult),
                      reads=[w1t, psf[3]], writes=[w1t])
                    E("vector", lambda e, ch=ch: e.tensor_tensor(
                        out=w1t[:, ch * 512:(ch + 1) * 512],
                        in0=w1t[:, ch * 512:(ch + 1) * 512],
                        in1=sgn[:, ch * 512:(ch + 1) * 512], op=ALU.mult),
                      reads=[w1t, sgn], writes=[w1t])
                store_fn(it_)

        def store_fc(dt):
            p.dma("sync", w1fc_s[:, dt, :, :].rearrange("hc pp cc -> pp hc cc"),
                  w1t[:].rearrange("p (hc cc) -> p hc cc", hc=64))

        def store_pr(ht):
            p.dma("sync", w1pr_s[:, ht, :, :].rearrange("dc pp cc -> pp dc cc"),
                  w1t[:, 0:D].rearrange("p (dc cc) -> p dc cc", dc=16))

        tern_quant(dram["wfcT"], 16, FFN_H, store_fc)
        tern_quant(dram["wprojT"], 64, D, store_pr)
        p.pop_scope()
        p.barrier()

        p.push_scope()
        xn2T = p.sb("xn2T", (128, 16 * 1024), BF16)
        p.push_scope()
        xtl = p.sb("f_xtl", (128, 2 * D))
        xnb = p.sb("f_xnb", (128, D), BF16)
        for tt in range(NT):
            xt = xtl[:, (tt % 2) * D:(tt % 2 + 1) * D]
            p.dma("sync", xt, xin[tt * 128:(tt + 1) * 128, :])
            ln_stats(xt, D, extra_reads=[xtl])
            ln_cols(nmc, stc)
            adddiv(xnb[:], xt, nmc, stc, [xtl], [xnb])
            for dt in range(16):
                trans(xn2T[:, dt * 1024 + tt * 128: dt * 1024 + (tt + 1) * 128],
                      xnb[:, dt * 128:(dt + 1) * 128])
        p.pop_scope()

        hqT = p.sb("hqT", (128, 64 * 512), BF16)
        w1sl = p.sb("w1sl", (128, 4 * 2048), BF16)
        w2sl = p.sb("w2sl", (128, 2 * 8192), BF16)
        rbuf = p.sb("f_rbuf", (128, 4 * 512), BF16)
        hq2 = p.sb("f_hq2", (128, 512), BF16)
        stdr = p.sb("f_stdr", (1, 512))
        outTb = p.sb("f_outTb", (128, 512), BF16)
        otile4 = p.sb("f_otile4", (128, 4 * D), BF16)
        xtmp = p.sb("f_xtmp", (128, D))
        xnew = p.sb("f_xnew", (128, D))
        for tc in range(2):
            for hc in range(64):
                sl = w1sl[:, (hc % 4) * 2048:(hc % 4 + 1) * 2048]
                p.dma("sync", sl.rearrange("p (dt cc) -> p dt cc", dt=16),
                      w1fc_s[hc, :, :, :].rearrange("dt pp cc -> pp dt cc"))
                bank = psf[(0, 1, 4, 5)[hc % 4]]
                for dt in range(16):
                    E("tensor", lambda e, b=bank, sl=sl, dt=dt, tc=tc: e.matmul(
                        b[:, 0:512], sl[:, dt * 128:(dt + 1) * 128],
                        xn2T[:, dt * 1024 + tc * 512: dt * 1024 + (tc + 1) * 512],
                        start=(dt == 0), stop=(dt == 15)),
                      reads=[w1sl, xn2T], writes=[bank])
                rb = rbuf[:, (hc % 4) * 512:(hc % 4 + 1) * 512]
                E("scalar", lambda e, b=bank, rb=rb: e.activation(
                    out=rb, in_=b[:, 0:512], func=AF.Relu),
                  reads=[bank], writes=[rbuf])
                E("vector", lambda e, hc=hc, rb=rb: e.tensor_tensor(
                    out=hqT[:, hc * 512:(hc + 1) * 512], in0=rb, in1=rb,
                    op=ALU.mult),
                  reads=[rbuf], writes=[hqT])
                E("vector", lambda e, hc=hc: e.tensor_tensor(
                    out=hq2[:], in0=hqT[:, hc * 512:(hc + 1) * 512],
                    in1=hqT[:, hc * 512:(hc + 1) * 512], op=ALU.mult),
                  reads=[hqT], writes=[hq2])
                E("tensor", lambda e, hc=hc: e.matmul(
                    psf[2][0:1, 0:512], onesc[:], hq2[:],
                    start=(hc == 0), stop=(hc == 63)),
                  reads=[onesc, hq2], writes=[psf[2]])
            E("vector", lambda e: e.tensor_scalar(
                out=stdr[:], in0=psf[2][0:1, 0:512], scalar1=1.0 / FFN_H,
                scalar2=F32EPS, op0=ALU.mult, op1=ALU.add),
              reads=[psf[2]], writes=[stdr])
            E("scalar", lambda e: e.activation(out=stdr[:], in_=stdr[:], func=AF.Sqrt),
              reads=[stdr], writes=[stdr])
            E("vector", lambda e: e.reciprocal(out=stdr[:], in_=stdr[:]),
              reads=[stdr], writes=[stdr])
            E("tensor", lambda e: e.matmul(
                psf[3][:, 0:512], onesr[0:1, 0:128], stdr[:], start=True, stop=True),
              reads=[onesr, stdr], writes=[psf[3]])
            for hc in range(64):
                E("vector", lambda e, hc=hc: e.tensor_tensor(
                    out=hqT[:, hc * 512:(hc + 1) * 512],
                    in0=hqT[:, hc * 512:(hc + 1) * 512],
                    in1=psf[3][:, 0:512], op=ALU.mult),
                  reads=[hqT, psf[3]], writes=[hqT])
            for dcol in range(16):
                sl = w2sl[:, (dcol % 2) * 8192:(dcol % 2 + 1) * 8192]
                p.dma("sync", sl.rearrange("p (ht cc) -> p ht cc", ht=64),
                      w1pr_s[dcol, :, :, :].rearrange("ht pp cc -> pp ht cc"))
                bank = psf[(0, 1, 4, 5)[dcol % 4]]
                for ht in range(64):
                    E("tensor", lambda e, b=bank, sl=sl, ht=ht: e.matmul(
                        b[:, 0:512], sl[:, ht * 128:(ht + 1) * 128],
                        hqT[:, ht * 512:(ht + 1) * 512],
                        start=(ht == 0), stop=(ht == 63)),
                      reads=[w2sl, hqT], writes=[bank])
                E("vector", lambda e, b=bank: e.tensor_copy(out=outTb[:], in_=b[:, 0:512]),
                  reads=[bank], writes=[outTb])
                for q4 in range(4):
                    trans(otile4[:, q4 * D + dcol * 128: q4 * D + (dcol + 1) * 128],
                          outTb[:, q4 * 128:(q4 + 1) * 128])
            for q4 in range(4):
                tt = tc * 4 + q4
                sandwich(otile4[:, q4 * D:(q4 + 1) * D], xin, out_dram, tt,
                         xtmp[:], xnew[:])
        p.pop_scope()
        p.pop_scope()
        p.barrier()

    def l1_attn():
        out_dram = y_out if UPTO == 3 else xr3
        xin = xr2 if UPTO >= 2 else x_own
        p.push_scope()
        p.push_scope()
        xbcs = p.sb("xbcs", (128, 33 * 1024), BF16)
        dtv = p.sb("dtv", (128, 512))
        dtA = p.sb("dtA", (128, 512))
        ltA = p.sb("ltA", (64, NCH * 256))
        negl = p.sb("negl", (128, 8 * 64))
        wtb = p.sb("wtb", (128, 8 * 64))
        pendA = p.sb("pendA", (128, NCH * 64))
        Btok = p.sb("Btok", (128, 8 * 64), BF16)
        Bt_cm = p.sb("Bt_cm", (64, 1024), BF16)
        Ct_cm = p.sb("Ct_cm", (64, 1024), BF16)
        Sst = p.sb("Sst", (64, 2 * DIN))
        ssdAb = p.sb("ssdAb", (128, 256))
        ssdBb = p.sb("ssdBb", (128, 256))
        ddb = p.sb("ddb", (128, NH))
        aneg = p.sb("aneg", (128, NH))
        dtbb = p.sb("dtbb", (128, NH))
        onesc64 = p.sb("onesc64", (64, 256))
        p.dma("sync", ssdAb[:], dram["ssdA"][:, :])
        p.dma("sync", ssdBb[:], dram["ssdB"][:, :])
        p.dma("sync", ddb[:], dram["dd_rep"][:, :])
        p.dma("sync", dtbb[:], dram["dtb_rep"][:, :])
        p.dma("sync", aneg[:], dram["alog_rep"][:, :])
        E("scalar", lambda e: e.activation(out=aneg[:], in_=aneg[:], func=AF.Exp),
          reads=[aneg], writes=[aneg])
        E("vector", lambda e: e.tensor_scalar(
            out=aneg[:], in0=aneg[:], scalar1=-1.0, scalar2=None, op0=ALU.mult),
          reads=[aneg], writes=[aneg])
        E("gpsimd", lambda e: e.memset(onesc64[:], 1.0), reads=[], writes=[onesc64])
        E("gpsimd", lambda e: e.memset(Sst[:, 0:DIN], 0.0), reads=[], writes=[Sst])

        p.push_scope()
        uT = p.sb("uT", (128, 16 * 1024), BF16)
        p.push_scope()
        xtl = p.sb("m_xtl", (128, 2 * D))
        ln3b = p.sb("m_ln3", (128, D))
        ub = p.sb("m_ub", (128, D), BF16)
        rmswb = p.sb("m_rmsw", (128, D))
        p.dma("sync", rmswb[:], dram["rms_w_rep"][:, :])
        for tt in range(NT):
            xt = xtl[:, (tt % 2) * D:(tt % 2 + 1) * D]
            p.dma("sync", xt, xin[tt * 128:(tt + 1) * 128, :])
            ln_stats(xt, D, extra_reads=[xtl])
            ln_cols(nmc, stc)
            E("vector", lambda e, tt=tt: e.tensor_copy(
                out=m3[:, 2 * tt:2 * tt + 1], in_=nmc),
              reads=[cols], writes=[m3])
            E("vector", lambda e, tt=tt: e.tensor_copy(
                out=m3[:, 2 * tt + 1:2 * tt + 2], in_=stc),
              reads=[cols], writes=[m3])
            adddiv(ln3b[:], xt, nmc, stc, [xtl], [ln3b])
            ln_stats(ln3b[:], D)
            rms_std_col(stc, 1e-5)
            divcol(ln3b[:], ln3b[:], stc, [ln3b, cols], [ln3b])
            E("vector", lambda e: e.tensor_tensor(
                out=ub[:], in0=ln3b[:], in1=rmswb[:], op=ALU.mult),
              reads=[ln3b, rmswb], writes=[ub])
            for dt in range(16):
                trans(uT[:, dt * 1024 + tt * 128: dt * 1024 + (tt + 1) * 128],
                      ub[:, dt * 128:(dt + 1) * 128])
        p.pop_scope()

        p.push_scope()
        wzs = p.sb("m_wzs", (128, 2 * 8192), BF16)
        zb = p.sb("m_zb", (128, 512), BF16)
        h3 = p.sb("m_h3", (128, 3))
        wdts = p.sb("m_wdts", (128, 16 * 64), BF16)
        for zc in range(8):
            sl = wzs[:, (zc % 2) * 8192:(zc % 2 + 1) * 8192]
            p.dma("gpsimd", sl.rearrange("p (dt cc) -> p dt cc", dt=16),
                  dram["inprojT"][:, zc * 512:(zc + 1) * 512].rearrange(
                      "(dt pp) cc -> pp dt cc", pp=128))
            for tt in range(NT):
                bank = psf[(0, 1, 4, 5)[tt % 4]]
                for dt in range(16):
                    E("tensor", lambda e, b=bank, sl=sl, dt=dt, tt=tt: e.matmul(
                        b[:, 0:512],
                        uT[:, dt * 1024 + tt * 128: dt * 1024 + (tt + 1) * 128],
                        sl[:, dt * 512:(dt + 1) * 512],
                        start=(dt == 0), stop=(dt == 15)),
                      reads=[uT, wzs], writes=[bank])
                E("vector", lambda e, b=bank: e.tensor_copy(out=zb[:], in_=b[:, 0:512]),
                  reads=[bank], writes=[zb])
                p.dma("gpsimd", zspill[tt * 128:(tt + 1) * 128, zc * 512:(zc + 1) * 512],
                      zb[:])
        for ct in range(33):
            sl = wzs[:, (ct % 8) * 2048:(ct % 8) * 2048 + 2048]
            p.dma("gpsimd", sl.rearrange("p (dt cc) -> p dt cc", dt=16),
                  dram["inprojT"][:, DIN + ct * 128: DIN + (ct + 1) * 128].rearrange(
                      "(dt pp) cc -> pp dt cc", pp=128))
            for tc in range(2):
                bank = psf[2 + tc]
                for dt in range(16):
                    E("tensor", lambda e, b=bank, sl=sl, dt=dt, tc=tc: e.matmul(
                        b[:, 0:512], sl[:, dt * 128:(dt + 1) * 128],
                        uT[:, dt * 1024 + tc * 512: dt * 1024 + (tc + 1) * 512],
                        start=(dt == 0), stop=(dt == 15)),
                      reads=[uT, wzs], writes=[bank])
                E("vector", lambda e, b=bank, ct=ct, tc=tc: e.tensor_copy(
                    out=xbcs[:, ct * 1024 + tc * 512: ct * 1024 + (tc + 1) * 512],
                    in_=b[:, 0:512]),
                  reads=[bank], writes=[xbcs])
                if tc == 1:
                    E("vector", lambda e, b=bank: e.tensor_copy(
                        out=h3[:], in_=b[:, 509:512]),
                      reads=[bank], writes=[h3])
                    p.dma("gpsimd", cch_in[ct, :, :], h3[:])
        p.dma("gpsimd", wdts[:].rearrange("p (dt cc) -> p dt cc", dt=16),
              dram["inprojT"][:, 8320:8384].rearrange("(dt pp) cc -> pp dt cc", pp=128))
        for tt in range(NT):
            bank = psf[tt % 2]
            for dt in range(16):
                E("tensor", lambda e, b=bank, dt=dt, tt=tt: e.matmul(
                    b[:, 0:64],
                    uT[:, dt * 1024 + tt * 128: dt * 1024 + (tt + 1) * 128],
                    wdts[:, dt * 64:(dt + 1) * 64],
                    start=(dt == 0), stop=(dt == 15)),
                  reads=[uT, wdts], writes=[bank])
            E("vector", lambda e, b=bank, tt=tt: e.tensor_tensor(
                out=dtv[:, tt * 64:(tt + 1) * 64], in0=b[:, 0:64], in1=dtbb[:],
                op=ALU.add),
              reads=[bank, dtbb], writes=[dtv])
        E("scalar", lambda e: e.activation(out=dtA[:], in_=dtv[:], func=AF.Exp),
          reads=[dtv], writes=[dtA])
        E("scalar", lambda e: e.activation(out=dtv[:], in_=dtA[:], func=AF.Ln, bias=1.0),
          reads=[dtA], writes=[dtv])
        for tt in range(NT):
            E("vector", lambda e, tt=tt: e.tensor_tensor(
                out=dtA[:, tt * 64:(tt + 1) * 64], in0=dtv[:, tt * 64:(tt + 1) * 64],
                in1=aneg[:], op=ALU.mult),
              reads=[dtv, aneg], writes=[dtA])
        p.pop_scope()
        p.pop_scope()
        p.barrier()

        E("gpsimd", lambda e: e.collective_compute(
            "AllGather", ALU.bypass, replica_groups=[[0, 1], [2, 3], [4, 5], [6, 7]],
            ins=[cch_in.opt()], outs=[cch_out.opt()]),
          reads=[cch_in], writes=[cch_out], sem="cc", inc=1)
        p.push_scope()
        pd_ = p.sb("m_pd", (128, 1027))
        acc1 = p.sb("m_acc1", (128, 1024))
        acc2 = p.sb("m_acc2", (128, 1024))
        cwb = p.sb("m_cwb", (128, 4))
        cbb = p.sb("m_cbb", (128, 1))
        h3r = p.sb("m_h3r", (128, 3))
        for ct in range(33):
            p.dma("sync", cwb[:], dram["conv_w"][ct * 128:(ct + 1) * 128, :])
            p.dma("sync", cbb[:], dram["conv_b"][ct * 128:(ct + 1) * 128, :])
            p.dma("sync", h3r[:], cch_out[0, ct, :, :])
            E("vector", lambda e: e.tensor_scalar(
                out=pd_[:, 0:3], in0=h3r[:], scalar1=hzc[:], scalar2=None, op0=ALU.mult),
              reads=[h3r, hzc], writes=[pd_])
            E("vector", lambda e, ct=ct: e.tensor_copy(
                out=pd_[:, 3:1027], in_=xbcs[:, ct * 1024:(ct + 1) * 1024]),
              reads=[xbcs], writes=[pd_])
            E("vector", lambda e: e.tensor_scalar(
                out=acc1[:], in0=pd_[:, 0:1024], scalar1=cwb[:, 0:1], scalar2=None,
                op0=ALU.mult),
              reads=[pd_, cwb], writes=[acc1])
            E("vector", lambda e: e.scalar_tensor_tensor(
                out=acc2[:], in0=pd_[:, 1:1025], scalar=cwb[:, 1:2], in1=acc1[:],
                op0=ALU.mult, op1=ALU.add),
              reads=[pd_, cwb, acc1], writes=[acc2])
            E("vector", lambda e: e.scalar_tensor_tensor(
                out=acc1[:], in0=pd_[:, 2:1026], scalar=cwb[:, 2:3], in1=acc2[:],
                op0=ALU.mult, op1=ALU.add),
              reads=[pd_, cwb, acc2], writes=[acc1])
            E("vector", lambda e: e.scalar_tensor_tensor(
                out=acc2[:], in0=pd_[:, 3:1027], scalar=cwb[:, 3:4], in1=acc1[:],
                op0=ALU.mult, op1=ALU.add),
              reads=[pd_, cwb, acc1], writes=[acc2])
            E("scalar", lambda e, ct=ct: e.activation(
                out=xbcs[:, ct * 1024:(ct + 1) * 1024], in_=acc2[:], func=AF.Silu,
                bias=cbb[:]),
              reads=[acc2, cbb], writes=[xbcs])
        p.dma("gpsimd", Bt_cm[:, :], xbcs[0:64, 32 * 1024:33 * 1024])
        p.dma("gpsimd", Ct_cm[:, :], xbcs[64:128, 32 * 1024:33 * 1024])
        p.pop_scope()
        p.barrier()

        p.push_scope()
        dtt = p.sb("m_dtt", (64, 256))
        ltok = p.sb("m_ltok", (128, 128))
        lrow = p.sb("m_lrow", (1, 64))
        for ic in range(NCH):
            for jt in range(2):
                trans(dtt[:, jt * 128:(jt + 1) * 128],
                      dtA[:, (2 * ic + jt) * 64:(2 * ic + jt + 1) * 64],
                      pdim=128, fdim=64, f32=True)
            E("vector", lambda e, ic=ic: e.tensor_tensor_scan(
                out=ltA[:, ic * 256:(ic + 1) * 256], data0=onesc64[:], data1=dtt[:],
                initial=0.0, op0=ALU.mult, op1=ALU.add),
              reads=[onesc64, dtt], writes=[ltA])
            for jt in range(2):
                trans(ltok[:, jt * 64:(jt + 1) * 64],
                      ltA[:, ic * 256 + jt * 128: ic * 256 + (jt + 1) * 128],
                      pdim=64, fdim=128, f32=True)
            E("vector", lambda e, ic=ic: e.tensor_scalar(
                out=negl[:, ic * 128:(ic + 1) * 128], in0=ltok[:], scalar1=-1.0,
                scalar2=None, op0=ALU.mult),
              reads=[ltok], writes=[negl])
            p.dma("gpsimd", lrow[:], ltok[127:128, 64:128])
            E("tensor", lambda e, ic=ic: e.matmul(
                psf[4][:, 0:64], onesr[0:1, 0:128], lrow[0:1, 0:64],
                start=True, stop=True),
              reads=[onesr, lrow], writes=[psf[4]])
            E("scalar", lambda e, ic=ic: e.activation(
                out=pendA[:, ic * 64:(ic + 1) * 64], in_=psf[4][:, 0:64], func=AF.Exp),
              reads=[psf[4]], writes=[pendA])
            for jt in range(2):
                E("vector", lambda e, ic=ic, jt=jt: e.tensor_tensor(
                    out=wtb[:, (2 * ic + jt) * 64:(2 * ic + jt + 1) * 64],
                    in0=psf[4][:, 0:64], in1=ltok[:, jt * 64:(jt + 1) * 64],
                    op=ALU.subtract),
                  reads=[psf[4], ltok], writes=[wtb])
            E("scalar", lambda e, ic=ic: e.activation(
                out=wtb[:, 2 * ic * 64:(2 * ic + 2) * 64],
                in_=wtb[:, 2 * ic * 64:(2 * ic + 2) * 64], func=AF.Exp),
              reads=[wtb], writes=[wtb])
            E("vector", lambda e, ic=ic: e.tensor_tensor(
                out=wtb[:, 2 * ic * 64:(2 * ic + 2) * 64],
                in0=wtb[:, 2 * ic * 64:(2 * ic + 2) * 64],
                in1=dtv[:, 2 * ic * 64:(2 * ic + 2) * 64], op=ALU.mult),
              reads=[wtb, dtv], writes=[wtb])
            for jt in range(2):
                trans(Btok[:, (2 * ic + jt) * 64:(2 * ic + jt + 1) * 64],
                      Bt_cm[:, ic * 256 + jt * 128: ic * 256 + (jt + 1) * 128],
                      pdim=64, fdim=128)
        p.pop_scope()
        p.barrier()

        p.push_scope()
        xtk = p.sb("m_xtk", (128, 2 * DIN), BF16)
        bwb = p.sb("m_bwb", (128, 64), BF16)

        def state_chunk(ic, rd, wr):
            for h in range(NH):
                for jt in range(2):
                    E("vector", lambda e, h=h, jt=jt, ic=ic: e.tensor_scalar(
                        out=bwb[:], in0=Btok[:, (2 * ic + jt) * 64:(2 * ic + jt + 1) * 64],
                        scalar1=wtb[:, (2 * ic + jt) * 64 + h:(2 * ic + jt) * 64 + h + 1],
                        scalar2=None, op0=ALU.mult),
                      reads=[Btok, wtb], writes=[bwb])
                    E("tensor", lambda e, h=h, jt=jt: e.matmul(
                        psf[5][0:64, 0:64], bwb[:],
                        xtk[:, jt * DIN + h * 64: jt * DIN + (h + 1) * 64],
                        start=(jt == 0), stop=(jt == 1)),
                      reads=[bwb, xtk], writes=[psf[5]])
                E("vector", lambda e, h=h, ic=ic, rd=rd, wr=wr: e.scalar_tensor_tensor(
                    out=Sst[:, wr * DIN + h * 64: wr * DIN + (h + 1) * 64],
                    in0=Sst[:, rd * DIN + h * 64: rd * DIN + (h + 1) * 64],
                    scalar=pendA[0:64, ic * 64 + h: ic * 64 + h + 1],
                    in1=psf[5][0:64, 0:64], op0=ALU.mult, op1=ALU.add),
                  reads=[Sst, pendA, psf[5]], writes=[Sst])

        def xtok_chunk(ic):
            for cp in range(32):
                for jt in range(2):
                    trans(xtk[:, jt * DIN + cp * 128: jt * DIN + (cp + 1) * 128],
                          xbcs[:, cp * 1024 + ic * 256 + jt * 128:
                               cp * 1024 + ic * 256 + (jt + 1) * 128])

        for ic in range(NCH):
            xtok_chunk(ic)
            state_chunk(ic, ic % 2, (ic + 1) % 2)
        fin = NCH % 2
        p.dma("gpsimd", cst_in[:, :], Sst[:, fin * DIN:(fin + 1) * DIN])
        E("gpsimd", lambda e: e.collective_compute(
            "AllGather", ALU.bypass, replica_groups=[[0, 1], [2, 3], [4, 5], [6, 7]],
            ins=[cst_in.opt()], outs=[cst_out.opt()]),
          reads=[cst_in], writes=[cst_out], sem="cc", inc=1)
        p.dma("sync", Sst[:, 0:DIN], cst_out[0, :, :])
        E("vector", lambda e: e.tensor_scalar(
            out=Sst[:, 0:DIN], in0=Sst[:, 0:DIN], scalar1=hzc[0:64, :], scalar2=None,
            op0=ALU.mult),
          reads=[Sst, hzc], writes=[Sst])
        p.barrier()

        Gb = p.sb("m_Gb", (128, 2 * 256), BF16)
        lrow2 = p.sb("m_lrow2", (1, 256))
        tmpG = p.sb("m_tmpG", (128, 2 * 256))
        Eb = p.sb("m_Eb", (128, 2 * 256))
        ebc = p.sb("m_ebc", (64, 256))
        chat = p.sb("m_chat", (64, 256))
        ybuf = p.sb("m_ybuf", (128, 2 * DIN), BF16)
        zbl = p.sb("m_zbl", (128, DIN), BF16)
        szb = p.sb("m_szb", (128, DIN), BF16)
        ygb = p.sb("m_ygb", (128, DIN), BF16)
        ygT = p.sb("m_ygT", (128, 32 * 128), BF16)
        nwb = p.sb("m_nwb", (128, DIN), BF16)
        p.dma("gpsimd", nwb[:], dram["normw_rep"][:, :])
        for ic in range(NCH):
            rd = ic % 2
            wr = (ic + 1) % 2
            xtok_chunk(ic)
            for h in range(NH):
                for jt in range(2):
                    bank = psf[jt]
                    E("tensor", lambda e, b=bank, ic=ic, jt=jt: e.matmul(
                        b[:, 0:256], Bt_cm[:, ic * 256 + jt * 128: ic * 256 + (jt + 1) * 128],
                        Ct_cm[:, ic * 256:(ic + 1) * 256], start=True, stop=True),
                      reads=[Bt_cm, Ct_cm], writes=[bank])
                    bank2 = psf[2 + jt]
                    if jt == 0:
                        p.dma("gpsimd", lrow2[:], ltA[h:h + 1, ic * 256:(ic + 1) * 256])
                    E("tensor", lambda e, b=bank2, ic=ic, h=h: e.matmul(
                        b[:, 0:256], onesr[0:1, 0:128], lrow2[0:1, 0:256],
                        start=True, stop=True),
                      reads=[onesr, lrow2], writes=[bank2])
                    ssd = ssdAb if jt == 0 else ssdBb
                    E("vector", lambda e, b=bank2, jt=jt, ssd=ssd: e.tensor_tensor(
                        out=tmpG[:, jt * 256:(jt + 1) * 256], in0=b[:, 0:256], in1=ssd[:],
                        op=ALU.add),
                      reads=[bank2, ssd], writes=[tmpG])
                    E("scalar", lambda e, jt=jt, ic=ic, h=h: e.activation(
                        out=Eb[:, jt * 256:(jt + 1) * 256],
                        in_=tmpG[:, jt * 256:(jt + 1) * 256], func=AF.Exp,
                        bias=negl[:, ic * 128 + jt * 64 + h: ic * 128 + jt * 64 + h + 1]),
                      reads=[tmpG, negl], writes=[Eb])
                    E("vector", lambda e, b=bank, jt=jt, ic=ic, h=h: e.scalar_tensor_tensor(
                        out=Gb[:, jt * 256:(jt + 1) * 256], in0=b[:, 0:256],
                        scalar=dtv[:, (2 * ic + jt) * 64 + h:(2 * ic + jt) * 64 + h + 1],
                        in1=Eb[:, jt * 256:(jt + 1) * 256], op0=ALU.mult, op1=ALU.mult),
                      reads=[bank, dtv, Eb], writes=[Gb])
                E("scalar", lambda e: e.activation(
                    out=ebc[:], in_=psf[2][0:64, 0:256], func=AF.Exp),
                  reads=[psf[2]], writes=[ebc])
                E("vector", lambda e, ic=ic: e.tensor_tensor(
                    out=chat[:], in0=Ct_cm[:, ic * 256:(ic + 1) * 256], in1=ebc[:],
                    op=ALU.mult),
                  reads=[Ct_cm, ebc], writes=[chat])
                for it in range(2):
                    bank = psf[4]
                    for jt in range(it + 1):
                        E("tensor", lambda e, b=bank, jt=jt, it=it, h=h: e.matmul(
                            b[:, 0:64], Gb[:, jt * 256 + it * 128: jt * 256 + (it + 1) * 128],
                            xtk[:, jt * DIN + h * 64: jt * DIN + (h + 1) * 64],
                            start=(jt == 0), stop=False),
                          reads=[Gb, xtk], writes=[bank])
                    E("tensor", lambda e, b=bank, it=it, h=h, rd=rd: e.matmul(
                        b[:, 0:64], chat[0:64, it * 128:(it + 1) * 128],
                        Sst[:, rd * DIN + h * 64: rd * DIN + (h + 1) * 64],
                        start=False, stop=True),
                      reads=[chat, Sst], writes=[bank])
                    E("vector", lambda e, b=bank, it=it, h=h: e.scalar_tensor_tensor(
                        out=ybuf[:, it * DIN + h * 64: it * DIN + (h + 1) * 64],
                        in0=xtk[:, it * DIN + h * 64: it * DIN + (h + 1) * 64],
                        scalar=ddb[:, h:h + 1], in1=b[:, 0:64],
                        op0=ALU.mult, op1=ALU.add),
                      reads=[xtk, ddb, bank], writes=[ybuf])
            state_chunk(ic, rd, wr)
            for it in range(2):
                tile = 2 * ic + it
                p.dma("sync", zbl[:], zspill[tile * 128:(tile + 1) * 128, :])
                E("scalar", lambda e: e.activation(out=szb[:], in_=zbl[:], func=AF.Silu),
                  reads=[zbl], writes=[szb])
                E("vector", lambda e, it=it: e.tensor_tensor(
                    out=ygb[:], in0=ybuf[:, it * DIN:(it + 1) * DIN], in1=szb[:],
                    op=ALU.mult),
                  reads=[ybuf, szb], writes=[ygb])
                ln_stats(ygb[:], DIN)
                rms_std_col(stc, 1e-5)
                divcol(ygb[:], ygb[:], stc, [ygb, cols], [ygb])
                E("vector", lambda e: e.tensor_tensor(
                    out=ygb[:], in0=ygb[:], in1=nwb[:], op=ALU.mult),
                  reads=[ygb, nwb], writes=[ygb])
                for et in range(32):
                    trans(ygT[:, et * 128:(et + 1) * 128], ygb[:, et * 128:(et + 1) * 128])
                p.dma("gpsimd", ygsp[tile, :, :, :].rearrange("et pp cc -> pp et cc"),
                      ygT[:].rearrange("p (et cc) -> p et cc", et=32))
        p.pop_scope()
        p.pop_scope()
        p.barrier()

        p.push_scope()
        ygl = p.sb("m_ygl", (128, 2 * 4096), BF16)
        wop = p.sb("m_wop", (128, 32 * 512), BF16)
        otfull = p.sb("m_otfull", (128, 8 * D))
        oti = p.sb("m_oti", (128, D))
        xtmp = p.sb("m_xtmp", (128, D))
        xnew = p.sb("m_xnew", (128, D))
        for dc in range(4):
            p.dma("gpsimd", wop[:].rearrange("p (et cc) -> p et cc", et=32),
                  dram["outprojT"][:, dc * 512:(dc + 1) * 512].rearrange(
                      "(et pp) cc -> pp et cc", pp=128))
            for tt in range(NT):
                sl = ygl[:, (tt % 2) * 4096:(tt % 2 + 1) * 4096]
                p.dma("sync", sl.rearrange("p (et cc) -> p et cc", et=32),
                      ygsp[tt, :, :, :].rearrange("et pp cc -> pp et cc"))
                bank = psf[(0, 1, 4, 5)[tt % 4]]
                for et in range(32):
                    E("tensor", lambda e, b=bank, sl=sl, et=et: e.matmul(
                        b[:, 0:512], sl[:, et * 128:(et + 1) * 128],
                        wop[:, et * 512:(et + 1) * 512],
                        start=(et == 0), stop=(et == 31)),
                      reads=[ygl, wop], writes=[bank])
                E("vector", lambda e, b=bank, tt=tt, dc=dc: e.tensor_copy(
                    out=otfull[:, tt * D + dc * 512: tt * D + (dc + 1) * 512],
                    in_=b[:, 0:512]),
                  reads=[bank], writes=[otfull])
        for tt in range(NT):
            p.dma("sync", xtmp[:], xin[tt * 128:(tt + 1) * 128, :])
            E("vector", lambda e, tt=tt: e.tensor_scalar(
                out=xnew[:], in0=xtmp[:], scalar1=m3[:, 2 * tt:2 * tt + 1],
                scalar2=None, op0=ALU.add),
              reads=[xtmp, m3], writes=[xnew])
            divcol(xnew[:], xnew[:], m3[:, 2 * tt + 1:2 * tt + 2], [xnew, m3], [xnew])
            E("vector", lambda e, tt=tt: e.tensor_tensor(
                out=oti[:], in0=otfull[:, tt * D:(tt + 1) * D], in1=xnew[:], op=ALU.add),
              reads=[otfull, xnew], writes=[oti])
            sandwich(oti[:], xin, out_dram, tt, xtmp[:], xnew[:])
        p.pop_scope()
        p.pop_scope()
        p.barrier()

    def l1_ffn():
        out_dram = y_out
        xin = (xr2 if SKIP3 else xr3) if UPTO >= 3 else x_own
        p.push_scope()
        XT = p.sb("l_XT", (128, 16 * 1024), BF16)
        p.push_scope()
        xtl = p.sb("l_xtl", (128, 2 * D))
        xnb = p.sb("l_xnb", (128, D), BF16)
        for tt in range(NT):
            xt = xtl[:, (tt % 2) * D:(tt % 2 + 1) * D]
            p.dma("sync", xt, xin[tt * 128:(tt + 1) * 128, :])
            ln_stats(xt, D, extra_reads=[xtl])
            ln_cols(nmc, stc)
            adddiv(xnb[:], xt, nmc, stc, [xtl], [xnb])
            for dt in range(16):
                trans(XT[:, dt * 1024 + tt * 128: dt * 1024 + (tt + 1) * 128],
                      xnb[:, dt * 128:(dt + 1) * 128])
        p.pop_scope()

        DhT = p.sb("l_DhT", (128, 64 * 512), BF16)
        w0sl = p.sb("l_w0sl", (128, 4 * 2048), BF16)
        w1sl = p.sb("l_w1sl", (128, 2 * 8192), BF16)
        b0c = p.sb("l_b0c", (128, 2))
        outTb = p.sb("l_outTb", (128, 512), BF16)
        otile4 = p.sb("l_otile4", (128, 4 * D), BF16)
        b1b = p.sb("l_b1b", (128, D))
        otf32 = p.sb("l_otf32", (128, D))
        otf16 = p.sb("l_otf16", (128, D), F16)
        xtmp = p.sb("l_xtmp", (128, D))
        xnew = p.sb("l_xnew", (128, D))
        p.dma("sync", b1b[:], dram["bias1_rep"][:, :])
        for tc in range(2):
            for hc in range(64):
                sl = w0sl[:, (hc % 4) * 2048:(hc % 4 + 1) * 2048]
                p.dma("gpsimd", sl.rearrange("p (dt cc) -> p dt cc", dt=16),
                      dram["w0T"][:, hc * 128:(hc + 1) * 128].rearrange(
                          "(dt pp) cc -> pp dt cc", pp=128))
                bc = b0c[:, hc % 2: hc % 2 + 1]
                p.dma("sync", bc, dram["bias0"][hc * 128:(hc + 1) * 128, :])
                bank = psf[(0, 1, 4, 5)[hc % 4]]
                for dt in range(16):
                    E("tensor", lambda e, b=bank, sl=sl, dt=dt, tc=tc: e.matmul(
                        b[:, 0:512], sl[:, dt * 128:(dt + 1) * 128],
                        XT[:, dt * 1024 + tc * 512: dt * 1024 + (tc + 1) * 512],
                        start=(dt == 0), stop=(dt == 15)),
                      reads=[w0sl, XT], writes=[bank])
                E("scalar", lambda e, b=bank, hc=hc, bc=bc: e.activation(
                    out=DhT[:, hc * 512:(hc + 1) * 512], in_=b[:, 0:512],
                    func=AF.Gelu, bias=bc),
                  reads=[bank, b0c], writes=[DhT])
            for dcol in range(16):
                sl = w1sl[:, (dcol % 2) * 8192:(dcol % 2 + 1) * 8192]
                p.dma("gpsimd", sl.rearrange("p (ht cc) -> p ht cc", ht=64),
                      dram["w1T"][:, dcol * 128:(dcol + 1) * 128].rearrange(
                          "(ht pp) cc -> pp ht cc", pp=128))
                bank = psf[(2, 3, 0, 1)[dcol % 4]]
                for ht in range(64):
                    E("tensor", lambda e, b=bank, sl=sl, ht=ht: e.matmul(
                        b[:, 0:512], sl[:, ht * 128:(ht + 1) * 128],
                        DhT[:, ht * 512:(ht + 1) * 512],
                        start=(ht == 0), stop=(ht == 63)),
                      reads=[w1sl, DhT], writes=[bank])
                E("vector", lambda e, b=bank: e.tensor_copy(out=outTb[:], in_=b[:, 0:512]),
                  reads=[bank], writes=[outTb])
                for q4 in range(4):
                    trans(otile4[:, q4 * D + dcol * 128: q4 * D + (dcol + 1) * 128],
                          outTb[:, q4 * 128:(q4 + 1) * 128])
            for q4 in range(4):
                tt = tc * 4 + q4
                E("vector", lambda e, q4=q4: e.tensor_tensor(
                    out=otf32[:], in0=otile4[:, q4 * D:(q4 + 1) * D], in1=b1b[:],
                    op=ALU.add),
                  reads=[otile4, b1b], writes=[otf32])
                E("vector", lambda e: e.tensor_copy(out=otf16[:], in_=otf32[:]),
                  reads=[otf32], writes=[otf16])
                E("vector", lambda e: e.tensor_copy(out=otf32[:], in_=otf16[:]),
                  reads=[otf16], writes=[otf32])
                sandwich(otf32[:], xin, out_dram, tt, xtmp[:], xnew[:])
        p.pop_scope()
        p.barrier()

    if UPTO >= 1:
        l0_attn()
    if UPTO >= 2:
        l0_ffn()
    if UPTO >= 3 and not SKIP3:
        l1_attn()
    if UPTO >= 4:
        l1_ffn()
    if UPTO == 0:
        p.push_scope()
        xt0 = p.sb("xt0", (128, D))
        for tt in range(NT):
            p.dma("sync", xt0[:], x_own[tt * 128:(tt + 1) * 128, :])
            p.dma("gpsimd", y_out[tt * 128:(tt + 1) * 128, :], xt0[:])
        p.pop_scope()
    elif UPTO < 4:
        pass

    p.barrier()
    p.finalize()
    return nc


def _in_map(g, c, x, core):
    pr, hi = core // 2, core % 2
    m = dict(g)
    m["x_ctx"] = np.ascontiguousarray(x[pr])
    m["x_own"] = np.ascontiguousarray(x[pr, hi * 1024:(hi + 1) * 1024])
    m["cos_ctx"] = c["cos_ctx"]; m["sinA_ctx"] = c["sinA_ctx"]
    m["cos_own"] = c[f"cos_own_{hi}"]; m["sinA_own"] = c[f"sinA_own_{hi}"]
    m["masks"] = c[f"masks_{hi}"]; m["hz"] = c[f"hz_{hi}"]
    m["ident"] = c["ident"]; m["onesr"] = c["onesr"]
    m["E64"] = c["E64"]; m["sel2"] = c["sel2"]
    m["ssdA"] = c["ssdA"]; m["ssdB"] = c["ssdB"]
    return m


def kernel(**inputs):
    g = _prep_inputs(inputs)
    c = _host_consts()
    x = np.asarray(inputs["L0_attn_x"])
    nc = build_program()
    in_maps = [_in_map(g, c, x, core) for core in range(8)]
    res = run_bass_kernel_spmd(nc, in_maps, core_ids=list(range(8)))
    out = np.empty((4, 2048, 2048), dtype=np.float32)
    for core in range(8):
        pr, hi = core // 2, core % 2
        out[pr, hi * 1024:(hi + 1) * 1024] = res.results[core]["y_out"]
    return out


# version 07
# speedup vs baseline: 1.1556x; 1.0054x over previous
import os
import math
import numpy as np
import ml_dtypes
from contextlib import ExitStack

import concourse.bass as bass
import concourse.mybir as mybir
from concourse.bass_utils import run_bass_kernel_spmd
from concourse.bass import broadcast_tensor_aps

F32 = mybir.dt.float32
BF16 = mybir.dt.bfloat16
F16 = mybir.dt.float16
AF = mybir.ActivationFunctionType
ALU = mybir.AluOpType
AX = mybir.AxisListType
DR = mybir.MatmulPerfMode.DoubleRow

D = 2048
T = 2048
NTOK = 1024
NT = 8
H = 16
HKV = 4
HD = 128
DKV = 512
FFN_H = 8192
DIN = 4096
DST = 64
NH = 64
CONV_DIM = 4224
EIN = 8384
CHUNK = 256
NCH = 4
ASCALE = 1.0 / math.sqrt(HD)
F32EPS = float(np.finfo(np.float32).eps)
NEG = -1e30

UPTO = int(os.environ.get("MK_UPTO", "4"))
SKIP3 = os.environ.get("MK_SKIP3", "") == "1"

ENGS = ("sync", "scalar", "vector", "tensor", "gpsimd")
ESEM = {"scalar": "eS", "vector": "eV", "tensor": "eT", "gpsimd": "eG"}
QSEM = {"sync": "qY", "scalar": "qA", "vector": "qV", "tensor": "qT", "gpsimd": "qG"}


class PH:
    __slots__ = ("val",)

    def __init__(self):
        self.val = None


def _nm(x):
    if isinstance(x, str):
        return x
    t = getattr(x, "tensor", None)
    if t is not None:
        return t.name
    return getattr(x, "name")


class P:
    def __init__(self, nc):
        self.nc = nc
        self.es = ExitStack()
        self.streams = {e: [] for e in ENGS}
        self.semh = {}
        self.cnt = {}
        self.qpool = {"sync": [f"qy{i}" for i in range(10)],
                      "scalar": [f"qa{i}" for i in range(4)],
                      "vector": [f"qv{i}" for i in range(2)],
                      "tensor": [f"qt{i}" for i in range(2)],
                      "gpsimd": [f"qg{i}" for i in range(10)]}
        names = ["eS", "eV", "eT", "eG", "cc"] + [x for v in self.qpool.values() for x in v]
        for s in names:
            self.semh[s] = self.es.enter_context(nc.semaphore(s))
            self.cnt[s] = 0
        self.qpi = {k: 0 for k in self.qpool}
        self.qpend = {k: [] for k in self.qpool}
        self.waited = {e: {} for e in ENGS}
        self.buf = {}
        self.scopes = []

    def sb(self, name, shape, dtype=F32, persist=False):
        cm = self.nc.sbuf_tensor(name, list(shape), dtype)
        tgt = self.es if (persist or not self.scopes) else self.scopes[-1]
        return tgt.enter_context(cm)

    def ps(self, name, shape, dtype=F32):
        return self.es.enter_context(self.nc.psum_tensor(name, list(shape), dtype))

    def push_scope(self):
        s = ExitStack()
        self.scopes.append(s)
        return s

    def pop_scope(self):
        self.barrier()
        self.scopes.pop().close()

    def _wait(self, eng, sem, val, ph=None):
        if val <= 0:
            return
        hw = self.waited[eng]
        if hw.get(sem, 0) >= val:
            return
        hw[sem] = val
        h = self.semh[sem]
        if ph is not None:
            ph.val = (self.semh[sem], val)
        self.streams[eng].append(lambda e, h=h, v=val: e.wait_ge(h, v))

    def emit(self, eng, fn, reads=(), writes=(), q=None, sem=None, inc=None):
        need = {}
        for r in reads:
            b = self.buf.get(_nm(r))
            if b and b["w"]:
                s, v, ph = b["w"]
                if s not in need or need[s][0] < v:
                    need[s] = (v, ph)
        for w in writes:
            b = self.buf.get(_nm(w))
            if b:
                if b["w"]:
                    s, v, ph = b["w"]
                    if s not in need or need[s][0] < v:
                        need[s] = (v, ph)
                for s, (v, ph) in b["r"].items():
                    if s not in need or need[s][0] < v:
                        need[s] = (v, ph)
        for s, (v, ph) in need.items():
            self._wait(eng, s, v, ph)
        if sem is None:
            if q:
                pool = self.qpool[eng]
                sem = pool[self.qpi[eng] % len(pool)]
                self.qpi[eng] += 1
            else:
                sem = ESEM[eng]
        if inc is None:
            inc = 16 if q else 1
        self.cnt[sem] += inc
        val = self.cnt[sem]
        h = self.semh[sem]
        self.streams[eng].append(lambda e, fn=fn, h=h, inc=inc: fn(e).then_inc(h, inc))
        ph = None
        if not q:
            ph = PH()
            self.streams[eng].append(ph)
        else:
            pend = self.qpend[eng]
            pend.append((sem, val))
            if len(pend) > len(self.qpool[eng]) - 2:
                s_, v_ = pend.pop(0)
                h_ = self.semh[s_]
                hw = self.waited[eng]
                if hw.get(s_, 0) < v_:
                    hw[s_] = v_
                    self.streams[eng].append(lambda e, h=h_, v=v_: e.wait_ge(h, v))
        edge = (sem, val, ph)
        for w in writes:
            self.buf[_nm(w)] = {"w": edge, "r": {}}
        for r in reads:
            b = self.buf.setdefault(_nm(r), {"w": None, "r": {}})
            pv = b["r"].get(sem)
            if pv is None or pv[0] < val:
                b["r"][sem] = (val, ph)
        return val

    def dma(self, eng, out, in_, extra_reads=()):
        self.emit(eng, lambda e, o=out, i=in_: e.dma_start(out=o, in_=i),
                  reads=[in_] + list(extra_reads), writes=[out], q=QSEM[eng])

    def barrier(self):
        for eng in ("scalar", "vector", "tensor", "gpsimd"):
            sem = ESEM[eng]
            self.cnt[sem] += 1
            h = self.semh[sem]
            self.streams[eng].append(lambda e, h=h: e.nop().then_inc(h, 1))
            self.waited[eng][sem] = self.cnt[sem]
        snap = dict(self.cnt)
        for eng in ENGS:
            for s, v in snap.items():
                self._wait(eng, s, v)
        for b in self.buf.values():
            b["r"] = {}
            if b["w"]:
                s, v, ph = b["w"]
                b["w"] = (s, v, None)

    def finalize(self):
        blk = self.es.enter_context(self.nc.Block())
        for eng in ENGS:
            ops = []
            for f in self.streams[eng]:
                if isinstance(f, PH):
                    if f.val is not None:
                        ops.append(lambda e, h=f.val[0], v=f.val[1]: e.wait_ge(h, v))
                else:
                    ops.append(f)
            getattr(blk, eng)(lambda e, st=ops: [f(e) for f in st])
        self.es.close()



def _host_consts():
    c = {}
    i = np.arange(0, HD, 2, dtype=np.float64)
    inv_freq = 1.0 / (1000000.0 ** (i / HD))
    pos = np.arange(T, dtype=np.float64)
    freqs = pos[:, None] * inv_freq[None, :]
    emb = np.concatenate([freqs, freqs], axis=-1)
    cos = np.cos(emb).astype(np.float32)
    sinA = np.sin(emb).astype(np.float32)
    sinA[:, : HD // 2] *= -1.0
    c["cos_ctx"] = cos
    c["sinA_ctx"] = sinA
    c["ident"] = np.eye(128, dtype=np.float32)
    c["onesr"] = np.ones((1, 512), dtype=np.float32)
    jj = np.arange(128)[:, None]
    ii = np.arange(256)[None, :]
    c["ssdA"] = np.where(ii >= jj, 0.0, NEG).astype(np.float32)
    c["ssdB"] = np.where(ii >= 128 + jj, 0.0, NEG).astype(np.float32)
    e64 = np.zeros((128, 2), dtype=np.float32)
    e64[:64, 0] = 1.0
    e64[64:, 1] = 1.0
    c["E64"] = e64
    c["sel2"] = np.ascontiguousarray(e64.T)
    for hi in (0, 1):
        kk = np.arange(128)[:, None]
        qq = np.arange(512)[None, :]
        m = np.zeros((2, 16, 128, 512), dtype=np.float32)
        for cch in range(2):
            for j in range(16):
                m[cch, j] = ((hi * 1024 + cch * 512 + qq) >= (j * 128 + kk)).astype(np.float32)
        c[f"masks_{hi}"] = m.astype(ml_dtypes.bfloat16)
        c[f"hz_{hi}"] = np.full((128, 1), float(hi), dtype=np.float32)
        sl = slice(hi * 1024, (hi + 1) * 1024)
        c[f"cos_own_{hi}"] = cos[sl].copy()
        c[f"sinA_own_{hi}"] = sinA[sl].copy()
    return c


def _prep_inputs(inp):
    g = {}
    t = lambda a: np.ascontiguousarray(np.asarray(a).T)
    asc = np.ascontiguousarray
    rep = lambda a, n=128: asc(np.broadcast_to(np.asarray(a)[None, :], (n, len(a))))
    g["wqT"] = t(inp["L0_attn_wq"])
    g["wkT"] = t(inp["L0_attn_wk"])
    g["wvT"] = t(inp["L0_attn_wv"])
    g["woT"] = t(inp["L0_attn_wo"])
    g["wq_nat"] = asc(np.asarray(inp["L0_attn_wq"]))
    g["wk_nat"] = asc(np.asarray(inp["L0_attn_wk"]))
    g["wv_nat"] = asc(np.asarray(inp["L0_attn_wv"]))
    g["wfcT"] = t(inp["L0_ffn_W_fc"])
    g["wprojT"] = t(inp["L0_ffn_W_proj"])
    g["inprojT"] = t(inp["L1_attn_in_proj_w"])
    g["outprojT"] = t(inp["L1_attn_out_proj_w"])
    g["conv_w"] = asc(np.asarray(inp["L1_attn_conv_w"]))
    g["conv_b"] = asc(np.asarray(inp["L1_attn_conv_b"]).reshape(CONV_DIM, 1))
    g["rms_w_rep"] = rep(np.asarray(inp["L1_attn_rms_weight"]))
    g["dtb_rep"] = rep(np.asarray(inp["L1_attn_dt_bias"]))
    g["alog_rep"] = rep(np.asarray(inp["L1_attn_A_log"]))
    g["dd_rep"] = rep(np.asarray(inp["L1_attn_D"]))
    g["normw_rep"] = rep(np.asarray(inp["L1_attn_norm_w"]))
    g["w0T"] = t(inp["L1_ffn_W0"])
    g["w1T"] = t(inp["L1_ffn_W1"])
    g["bias0"] = asc(np.asarray(inp["L1_ffn_bias0"]).reshape(FFN_H, 1))
    g["bias1_rep"] = rep(np.asarray(inp["L1_ffn_bias1"]))
    return g



def build_program():
    nc = bass.Bass("TRN2", target_bir_lowering=False, debug=False)
    dram = {}

    def din(name, shape, dtype=F32):
        dram[name] = nc.dram_tensor(name, list(shape), dtype, kind="ExternalInput").ap()
        return dram[name]

    x_own = din("x_own", (NTOK, D))
    x_ctx = din("x_ctx", (T, D))
    for n, s in (("wqT", (D, D)), ("wkT", (D, DKV)), ("wvT", (D, DKV)), ("woT", (D, D)),
                 ("wq_nat", (D, D)), ("wk_nat", (DKV, D)), ("wv_nat", (DKV, D)),
                 ("wfcT", (D, FFN_H)), ("wprojT", (FFN_H, D)),
                 ("inprojT", (D, EIN)), ("outprojT", (DIN, D)),
                 ("conv_w", (CONV_DIM, 4)), ("conv_b", (CONV_DIM, 1)),
                 ("rms_w_rep", (128, D)), ("dtb_rep", (128, NH)), ("alog_rep", (128, NH)),
                 ("dd_rep", (128, NH)), ("normw_rep", (128, DIN)),
                 ("w0T", (D, FFN_H)), ("w1T", (FFN_H, D)),
                 ("bias0", (FFN_H, 1)), ("bias1_rep", (128, D)),
                 ("cos_ctx", (T, HD)), ("sinA_ctx", (T, HD)),
                 ("cos_own", (NTOK, HD)), ("sinA_own", (NTOK, HD)),
                 ("ident", (128, 128)), ("onesr", (1, 512)),
                 ("E64", (128, 2)), ("sel2", (2, 128)),
                 ("ssdA", (128, 256)), ("ssdB", (128, 256)), ("hz", (128, 1))):
        din(n, s)
    din("masks", (2, 16, 128, 512), BF16)

    y_out = nc.dram_tensor("y_out", [NTOK, D], F32, kind="ExternalOutput").ap()
    xr1 = nc.dram_tensor("xr1", [NTOK, D], F32).ap()
    xr2 = nc.dram_tensor("xr2", [NTOK, D], F32).ap()
    xr3 = nc.dram_tensor("xr3", [NTOK, D], F32).ap()
    zspill = nc.dram_tensor("zspill", [NTOK, DIN], BF16).ap()
    w1fc_s = nc.dram_tensor("w1fc_s", [64, 16, 128, 128], BF16).ap()
    w1pr_s = nc.dram_tensor("w1pr_s", [16, 64, 128, 128], BF16).ap()
    ygsp = nc.dram_tensor("ygsp", [8, 32, 128, 128], BF16).ap()
    cch_in = nc.dram_tensor("cch_in", [33, 128, 3], F32).ap()
    cch_out = nc.dram_tensor("cch_out", [2, 33, 128, 3], F32).ap()
    cst_in = nc.dram_tensor("cst_in", [64, DIN], F32).ap()
    cst_out = nc.dram_tensor("cst_out", [2, 64, DIN], F32).ap()

    p = P(nc)
    E = p.emit

    identf = p.sb("identf", (128, 128), persist=True)
    identb = p.sb("identb", (128, 128), BF16, persist=True)
    onesr = p.sb("onesr_s", (1, 512), persist=True)
    onesc = p.sb("onesc_s", (128, 1), BF16, persist=True)
    e64 = p.sb("e64_s", (128, 2), BF16, persist=True)
    sel2 = p.sb("sel2_s", (2, 128), BF16, persist=True)
    hzc = p.sb("hz_s", (128, 1), persist=True)
    stats = p.sb("stats", (128, 96), persist=True)
    mv = p.sb("mv", (128, 2), persist=True)
    cols = p.sb("cols", (128, 24), persist=True)
    m3 = p.sb("m3", (128, 16), persist=True)

    psf = [p.ps(f"psf{i}", [128, 512], F32) for i in range(6)]
    psb = [p.ps(f"psb{i}", [128, 1024], BF16) for i in range(2)]
    _tb = [0]

    p.dma("sync", identf[:], dram["ident"][:, :])
    p.dma("sync", onesr[:], dram["onesr"][:, :])
    p.dma("gpsimd", e64[:], dram["E64"][:, :])
    p.dma("gpsimd", sel2[:], dram["sel2"][:, :])
    p.dma("sync", hzc[:], dram["hz"][:, :])
    p.dma("gpsimd", identb[:], dram["ident"][:, :])
    E("gpsimd", lambda e: e.memset(onesc[:], 1.0), reads=[], writes=[onesc])

    nmc, stc, smx = cols[:, 0:1], cols[:, 1:2], cols[:, 2:3]
    smx4 = cols[:, 4:8]

    def trans(dst_ap, src_ap, pdim=128, fdim=128, f32=False):
        if not f32:
            bank = psb[_tb[0] % 2]
            _tb[0] += 1
            ident = identb
        else:
            bank = psf[5]
            ident = identf
        E("tensor", lambda e, b=bank, s=src_ap, idn=ident, pd=pdim, fd=fdim: e.transpose(
            out=b[0:fd, 0:pd], in_=s, identity=idn[0:pd, 0:pd]),
          reads=[src_ap, ident], writes=[bank])
        E("vector", lambda e, b=bank, d=dst_ap, pd=pdim, fd=fdim: e.tensor_copy(
            out=d, in_=b[0:fd, 0:pd]),
          reads=[bank], writes=[dst_ap])

    def divcol(out_ap, in0_ap, col_ap, rd, wr):
        rcp = cols[:, 23:24]
        E("vector", lambda e, c=col_ap: e.reciprocal(out=rcp, in_=c),
          reads=list(rd), writes=[cols])
        E("vector", lambda e, o=out_ap, i=in0_ap: e.tensor_scalar(
            out=o, in0=i, scalar1=rcp, scalar2=None, op0=ALU.mult),
          reads=list(rd) + [cols], writes=list(wr))

    def adddiv(out_ap, in0_ap, nm_col, std_col, rd, wr):
        E("vector", lambda e, o=out_ap, i=in0_ap, nm=nm_col: e.tensor_scalar(
            out=o, in0=i, scalar1=nm, scalar2=None, op0=ALU.add),
          reads=list(rd) + [cols], writes=list(wr))
        divcol(out_ap, out_ap, std_col, list(wr) + [cols], list(wr))

    def ln_stats(src_ap, width, extra_reads=()):
        nchk = (width + 511) // 512
        for ci in range(nchk):
            w = min(512, width - ci * 512)
            E("vector", lambda e, s=src_ap, ci=ci, w=w: e.bn_stats(
                out=stats[:, ci * 6:(ci + 1) * 6], in_=s[:, ci * 512: ci * 512 + w]),
              reads=[src_ap] + list(extra_reads), writes=[stats])
        E("vector", lambda e, n=nchk: e.bn_aggr(out=mv[:], in_=stats[:, 0:6 * n]),
          reads=[stats], writes=[mv])

    def ln_cols(nm_col, std_col, var_eps=1e-5):
        E("vector", lambda e, o=nm_col: e.tensor_scalar(
            out=o, in0=mv[:, 0:1], scalar1=-1.0, scalar2=None, op0=ALU.mult),
          reads=[mv], writes=[cols])
        E("vector", lambda e, o=std_col, ve=var_eps: e.tensor_scalar(
            out=o, in0=mv[:, 1:2], scalar1=ve, scalar2=None, op0=ALU.add),
          reads=[mv], writes=[cols])
        E("scalar", lambda e, o=std_col: e.activation(out=o, in_=o, func=AF.Sqrt),
          reads=[cols], writes=[cols])

    def rms_std_col(std_col, eps):
        E("vector", lambda e, o=std_col: e.tensor_tensor(
            out=o, in0=mv[:, 0:1], in1=mv[:, 0:1], op=ALU.mult),
          reads=[mv], writes=[cols])
        E("vector", lambda e, o=std_col: e.tensor_tensor(
            out=o, in0=o, in1=mv[:, 1:2], op=ALU.add),
          reads=[mv, cols], writes=[cols])
        E("vector", lambda e, o=std_col, ve=eps: e.tensor_scalar(
            out=o, in0=o, scalar1=ve, scalar2=None, op0=ALU.add),
          reads=[cols], writes=[cols])
        E("scalar", lambda e, o=std_col: e.activation(out=o, in_=o, func=AF.Sqrt),
          reads=[cols], writes=[cols])

    def quant_act(dst_bf, src_f32, tmp_f32, tmp2_f32, nheads, smax, extra=()):
        hd = None
        if nheads > 1:
            E("vector", lambda e, s=src_f32, n=nheads, o=smax: e.reduce_max(
                out=o, in_=s.rearrange("p (h d) -> p h d", h=n),
                axis=AX.X, apply_absolute_value=True),
              reads=[src_f32] + list(extra), writes=[cols])
        else:
            E("vector", lambda e, s=src_f32, o=smax: e.reduce_max(
                out=o, in_=s, axis=AX.X, apply_absolute_value=True),
              reads=[src_f32] + list(extra), writes=[cols])
        E("vector", lambda e, o=smax: e.tensor_scalar(
            out=o, in0=o, scalar1=1.0 / 127.0, scalar2=1e-8,
            op0=ALU.mult, op1=ALU.max),
          reads=[cols], writes=[cols])
        rsm = cols[:, 16:16 + nheads]
        E("vector", lambda e, o=smax, r=rsm: e.reciprocal(out=r, in_=o),
          reads=[cols], writes=[cols])
        w = 2048 // nheads if nheads > 1 else None
        if nheads > 1:
            wd = (dst_bf.tensor.shape[-1] if False else 512) // nheads
        for h in range(nheads):
            if nheads > 1:
                sl = slice(h * 128, (h + 1) * 128)
            else:
                sl = slice(None)
            E("vector", lambda e, s=src_f32, t=tmp_f32, r=rsm, h=h, sl=sl: e.tensor_scalar(
                out=t[:, sl], in0=s[:, sl], scalar1=r[:, h:h + 1] if True else None,
                scalar2=None, op0=ALU.mult),
              reads=[src_f32, cols], writes=[tmp_f32])
        E("vector", lambda e, t=tmp_f32: e.tensor_scalar(
            out=t, in0=t, scalar1=12582912.0, scalar2=12582912.0,
            op0=ALU.add, op1=ALU.subtract),
          reads=[tmp_f32], writes=[tmp_f32])
        E("vector", lambda e, t=tmp_f32: e.tensor_scalar(
            out=t, in0=t, scalar1=-127.0, scalar2=127.0, op0=ALU.max, op1=ALU.min),
          reads=[tmp_f32], writes=[tmp_f32])
        for h in range(nheads):
            if nheads > 1:
                sl = slice(h * 128, (h + 1) * 128)
            else:
                sl = slice(None)
            E("scalar", lambda e, t=tmp_f32, d=dst_bf, o=smax, h=h, sl=sl: e.activation(
                out=d[:, sl], in_=t[:, sl], func=AF.Copy, scale=o[:, h:h + 1]),
              reads=[tmp_f32, cols], writes=[dst_bf])

    def rope(dst_f32, src_f32, costile, sintile, nheads, tmp):
        for h in range(nheads):
            b = h * 128
            for (o0, o1) in ((0, 64), (64, 0)):
                E("vector", lambda e, s=src_f32, t=tmp, b=b, o0=o0, o1=o1, st=sintile:
                  e.tensor_tensor(out=t[:, b + o0:b + o0 + 64],
                                  in0=s[:, b + o1:b + o1 + 64],
                                  in1=st[:, o0:o0 + 64], op=ALU.mult),
                  reads=[src_f32, sintile], writes=[tmp])
            E("vector", lambda e, s=src_f32, d=dst_f32, b=b, ct=costile:
              e.tensor_tensor(out=d[:, b:b + 128], in0=s[:, b:b + 128],
                              in1=ct[:], op=ALU.mult),
              reads=[src_f32, costile], writes=[dst_f32])
        E("vector", lambda e, d=dst_f32, t=tmp: e.tensor_tensor(
            out=d, in0=d, in1=t, op=ALU.add),
          reads=[dst_f32, tmp], writes=[dst_f32])

    def sandwich(otile_ap, xin_dram, xout_dram, tt, xtmp, xnew, extra_reads=()):
        ln_stats(otile_ap, D, extra_reads=extra_reads)
        ln_cols(nmc, stc)
        p.dma("sync", xtmp, xin_dram[tt * 128:(tt + 1) * 128, :])
        adddiv(xnew, otile_ap, nmc, stc, [otile_ap] + list(extra_reads), [xnew])
        E("vector", lambda e, x=xnew, xt=xtmp: e.tensor_tensor(
            out=x, in0=x, in1=xt, op=ALU.add),
          reads=[xnew, xtmp], writes=[xnew])
        p.dma("gpsimd", xout_dram[tt * 128:(tt + 1) * 128, :], xnew)

    def l0_attn():
        out_dram = y_out if UPTO == 1 else xr1
        p.push_scope()
        qhatT = p.sb("qhatT", (128, H * 1024), BF16)
        wq_sr = p.sb("wq_sr", (1, D))
        wk_sr = p.sb("wk_sr", (1, DKV))
        wv_sr = p.sb("wv_sr", (1, DKV))
        wq_rr = p.sb("wq_rr", (1, D))
        wk_rr = p.sb("wk_rr", (1, DKV))
        wv_rr = p.sb("wv_rr", (1, DKV))

        def quant_w_tile(raw_ap, dst_ap, sc_bank, rc_bank, tmpf, tmpb):
            E("vector", lambda e, r=raw_ap, b=rc_bank, t=tmpf: e.tensor_tensor(
                out=t, in0=r, in1=b[:, 0:512], op=ALU.mult),
              reads=[raw_ap, rc_bank], writes=[tmpf])
            E("vector", lambda e, t=tmpf: e.tensor_scalar(
                out=t, in0=t, scalar1=12582912.0, scalar2=12582912.0,
                op0=ALU.add, op1=ALU.subtract),
              reads=[tmpf], writes=[tmpf])
            E("vector", lambda e, d=dst_ap, b=sc_bank, t=tmpf: e.tensor_tensor(
                out=d, in0=t, in1=b[:, 0:512], op=ALU.mult),
              reads=[tmpf, sc_bank], writes=[dst_ap])

        p.push_scope()
        wnat = p.sb("wnat", (128, 2 * D))
        colm = p.sb("colm", (128, 1))
        for nm_, rows, srow, rrow in (("wq_nat", D, wq_sr, wq_rr),
                                      ("wk_nat", DKV, wk_sr, wk_rr),
                                      ("wv_nat", DKV, wv_sr, wv_rr)):
            for ot in range(rows // 128):
                wb = wnat[:, (ot % 2) * D:(ot % 2 + 1) * D]
                p.dma("sync", wb, dram[nm_][ot * 128:(ot + 1) * 128, :])
                E("vector", lambda e, wb=wb: e.reduce_max(
                    out=colm[:], in_=wb, axis=AX.X, apply_absolute_value=True),
                  reads=[wnat], writes=[colm])
                E("vector", lambda e: e.tensor_scalar(
                    out=colm[:], in0=colm[:], scalar1=1.0 / 127.0, scalar2=1e-8,
                    op0=ALU.mult, op1=ALU.max),
                  reads=[colm], writes=[colm])
                E("tensor", lambda e: e.matmul(psf[5][0:1, 0:128], colm[:], identf[:],
                                               start=True, stop=True),
                  reads=[colm, identf], writes=[psf[5]])
                E("vector", lambda e, srow=srow, ot=ot: e.tensor_copy(
                    out=srow[:, ot * 128:(ot + 1) * 128], in_=psf[5][0:1, 0:128]),
                  reads=[psf[5]], writes=[srow])
                E("vector", lambda e, srow=srow, rrow=rrow, ot=ot: e.reciprocal(
                    out=rrow[:, ot * 128:(ot + 1) * 128],
                    in_=srow[:, ot * 128:(ot + 1) * 128]),
                  reads=[srow], writes=[rrow])
        p.pop_scope()
        p.barrier()

        p.push_scope()
        xhatTo = p.sb("xhatTo", (128, 16 * 1024), BF16)
        wqq = p.sb("wqq", (128, 16 * 512), BF16)
        tq1 = p.sb("tq1a", (128, D))
        tq2 = p.sb("tq2a", (128, D))
        xn = p.sb("xna", (128, D))
        xhat = p.sb("xhata", (128, D), BF16)
        wraw = p.sb("wrawa", (128, 1024))
        cosb = p.sb("cosba", (128, HD))
        sinb = p.sb("sinba", (128, HD))
        qhat = p.sb("qhat_b", (128, 512), BF16)
        xtl = p.sb("xtla", (128, 2 * D))
        for tt in range(NT):
            xt = xtl[:, (tt % 2) * D:(tt % 2 + 1) * D]
            p.dma("sync", xt, x_own[tt * 128:(tt + 1) * 128, :])
            ln_stats(xt, D, extra_reads=[xtl])
            ln_cols(nmc, stc)
            adddiv(xn[:], xt, nmc, stc, [xtl], [xn])
            quant_act(xhat[:], xn[:], tq1[:], tq2[:], 1, smx)
            for dt in range(16):
                trans(xhatTo[:, tt * 2048 + dt * 128: tt * 2048 + (dt + 1) * 128],
                      xhat[:, dt * 128:(dt + 1) * 128])
        for oc in range(4):
            E("tensor", lambda e, oc=oc: e.matmul(
                psf[4][:, 0:512], onesr[0:1, 0:128], wq_sr[0:1, oc * 512:(oc + 1) * 512],
                start=True, stop=True),
              reads=[onesr, wq_sr], writes=[psf[4]])
            E("tensor", lambda e, oc=oc: e.matmul(
                psf[3][:, 0:512], onesr[0:1, 0:128], wq_rr[0:1, oc * 512:(oc + 1) * 512],
                start=True, stop=True),
              reads=[onesr, wq_rr], writes=[psf[3]])
            for dt in range(16):
                wb = wraw[:, (dt % 2) * 512:(dt % 2 + 1) * 512]
                p.dma("sync", wb,
                      dram["wqT"][dt * 128:(dt + 1) * 128, oc * 512:(oc + 1) * 512])
                quant_w_tile(wb, wqq[:, dt * 512:(dt + 1) * 512], psf[4], psf[3],
                             tq1[:, 0:512], tq2[:, 0:512])
            for tt in range(NT):
                bank = psf[tt % 4]
                for dt in range(16):
                    E("tensor", lambda e, b=bank, dt=dt, tt=tt: e.matmul(
                        b[:, 0:512],
                        xhatTo[:, tt * 2048 + dt * 128: tt * 2048 + (dt + 1) * 128],
                        wqq[:, dt * 512:(dt + 1) * 512],
                        start=(dt == 0), stop=(dt == 15)),
                      reads=[xhatTo, wqq], writes=[bank])
                p.dma("sync", cosb[:], dram["cos_own"][tt * 128:(tt + 1) * 128, :])
                p.dma("sync", sinb[:], dram["sinA_own"][tt * 128:(tt + 1) * 128, :])
                E("vector", lambda e, b=bank, tq2=tq2: e.tensor_copy(
                    out=tq2[:, 1024:1536], in_=b[:, 0:512]),
                  reads=[bank], writes=[tq2])
                rope(tq2[:, 1536:2048], tq2[:, 1024:1536], cosb[:], sinb[:], 4,
                     tq1[:, 1024:1536])
                quant_act(qhat[:], tq2[:, 1536:2048], tq1[:, 0:512], tq2[:, 0:512],
                          4, smx4)
                for hh in range(4):
                    h = oc * 4 + hh
                    trans(qhatT[:, h * 1024 + tt * 128: h * 1024 + (tt + 1) * 128],
                          qhat[:, hh * 128:(hh + 1) * 128])
        p.pop_scope()
        p.barrier()

        p.push_scope()
        khatT = p.sb("khatT", (128, HKV * T), BF16)
        vhato = p.sb("vhato", (128, 16 * HKV * 132), BF16)
        p.push_scope()
        wkq = p.sb("wkq", (128, 16 * 512), BF16)
        wvq = p.sb("wvq", (128, 16 * 512), BF16)
        xhatTc = p.sb("xhatTc", (128, 16 * 128), BF16)
        tq1 = p.sb("tq1b", (128, D))
        tq2 = p.sb("tq2b", (128, D))
        xn = p.sb("xnb", (128, D))
        xhat = p.sb("xhatb", (128, D), BF16)
        wraw = p.sb("wrawb", (128, 1024))
        cosb = p.sb("cosbb", (128, HD))
        sinb = p.sb("sinbb", (128, HD))
        khat = p.sb("khat_b", (128, 512), BF16)
        vhat = p.sb("vhat_b", (128, 512), BF16)
        xctx = p.sb("xctx", (128, 2 * D))
        for nm_, srow, rrow, dstq in (("wkT", wk_sr, wk_rr, wkq),
                                      ("wvT", wv_sr, wv_rr, wvq)):
            E("tensor", lambda e, srow=srow: e.matmul(
                psf[4][:, 0:512], onesr[0:1, 0:128], srow[0:1, 0:512],
                start=True, stop=True),
              reads=[onesr, srow], writes=[psf[4]])
            E("tensor", lambda e, rrow=rrow: e.matmul(
                psf[3][:, 0:512], onesr[0:1, 0:128], rrow[0:1, 0:512],
                start=True, stop=True),
              reads=[onesr, rrow], writes=[psf[3]])
            for dt in range(16):
                wb = wraw[:, (dt % 2) * 512:(dt % 2 + 1) * 512]
                p.dma("sync", wb, dram[nm_][dt * 128:(dt + 1) * 128, :])
                quant_w_tile(wb, dstq[:, dt * 512:(dt + 1) * 512], psf[4], psf[3],
                             tq1[:, 0:512], tq2[:, 0:512])
        E("gpsimd", lambda e: e.memset(vhato[:], 1.0), reads=[], writes=[vhato])
        for ct in range(16):
            xc = xctx[:, (ct % 2) * D:(ct % 2 + 1) * D]
            p.dma("sync", xc, x_ctx[ct * 128:(ct + 1) * 128, :])
            ln_stats(xc, D, extra_reads=[xctx])
            ln_cols(nmc, stc)
            adddiv(xn[:], xc, nmc, stc, [xctx], [xn])
            quant_act(xhat[:], xn[:], tq1[:], tq2[:], 1, smx)
            for dt in range(16):
                trans(xhatTc[:, dt * 128:(dt + 1) * 128], xhat[:, dt * 128:(dt + 1) * 128])
            for dt in range(16):
                E("tensor", lambda e, dt=dt: e.matmul(
                    psf[0][:, 0:512], xhatTc[:, dt * 128:(dt + 1) * 128],
                    wkq[:, dt * 512:(dt + 1) * 512], start=(dt == 0), stop=(dt == 15)),
                  reads=[xhatTc, wkq], writes=[psf[0]])
            for dt in range(16):
                E("tensor", lambda e, dt=dt: e.matmul(
                    psf[1][:, 0:512], xhatTc[:, dt * 128:(dt + 1) * 128],
                    wvq[:, dt * 512:(dt + 1) * 512], start=(dt == 0), stop=(dt == 15)),
                  reads=[xhatTc, wvq], writes=[psf[1]])
            p.dma("sync", cosb[:], dram["cos_ctx"][ct * 128:(ct + 1) * 128, :])
            p.dma("sync", sinb[:], dram["sinA_ctx"][ct * 128:(ct + 1) * 128, :])
            E("vector", lambda e: e.tensor_copy(out=tq2[:, 1024:1536], in_=psf[0][:, 0:512]),
              reads=[psf[0]], writes=[tq2])
            rope(tq2[:, 1536:2048], tq2[:, 1024:1536], cosb[:], sinb[:], 4,
                 tq1[:, 1024:1536])
            quant_act(khat[:], tq2[:, 1536:2048], tq1[:, 0:512], tq2[:, 0:512], 4, smx4)
            E("vector", lambda e: e.tensor_copy(out=tq2[:, 1024:1536], in_=psf[1][:, 0:512]),
              reads=[psf[1]], writes=[tq2])
            quant_act(vhat[:], tq2[:, 1024:1536], tq1[:, 0:512], tq2[:, 0:512], 4,
                      cols[:, 8:12])
            for kv in range(HKV):
                trans(khatT[:, kv * T + ct * 128: kv * T + (ct + 1) * 128],
                      khat[:, kv * 128:(kv + 1) * 128])
                base = (ct * HKV + kv) * 132
                E("vector", lambda e, b=base, kv=kv: e.tensor_copy(
                    out=vhato[:, b:b + 128], in_=vhat[:, kv * 128:(kv + 1) * 128]),
                  reads=[vhat], writes=[vhato])
        p.pop_scope()
        p.barrier()

        p.push_scope()
        expSA = p.sb("expSA", (128, 8 * 512), BF16)
        expSB = p.sb("expSB", (128, 8 * 512), BF16)
        maskb = p.sb("maskb", (128, 16 * 512), BF16)
        atm = [p.sb(f"atm{i}", (128, D), BF16) for i in range(4)]
        wob = p.sb("wob", (128, 16 * 512), BF16)
        otile = [p.sb(f"otile_a{i}", (128, D)) for i in range(4)]
        avp = p.sb("avp", (128, 132))
        xtmp = p.sb("xtmp_a", (128, D))
        xnew = p.sb("xnew_a", (128, D))
        for c in range(2):
            p.dma("scalar", maskb[:].rearrange("p (j q) -> p j q", j=16),
                  dram["masks"][c, :, :, :].rearrange("j pp q -> pp j q"))
            njt = 4 * c + 12
            for h in range(H):
                kv = h // 4
                for j in range(njt):
                    bank = psf[j % 4]
                    ebuf = expSA if j % 2 == 0 else expSB
                    esl = ebuf[:, (j // 2) * 512:(j // 2 + 1) * 512]
                    E("tensor", lambda e, b=bank, h=h, j=j, kv=kv, c=c: e.matmul(
                        b[:, 0:512],
                        khatT[:, kv * T + j * 128: kv * T + (j + 1) * 128],
                        qhatT[:, h * 1024 + c * 512: h * 1024 + (c + 1) * 512],
                        start=True, stop=True),
                      reads=[khatT, qhatT], writes=[bank])
                    E("scalar", lambda e, b=bank, o=esl: e.activation(
                        out=o, in_=b[:, 0:512], func=AF.Exp, scale=ASCALE),
                      reads=[bank], writes=[ebuf])
                    E("vector", lambda e, o=esl, j=j: e.tensor_tensor(
                        out=o, in0=o, in1=maskb[:, j * 512:(j + 1) * 512], op=ALU.mult),
                      reads=[ebuf, maskb], writes=[ebuf])
                for il in range(4):
                    i = c * 4 + il
                    bank = psf[4 + (il % 2)]
                    for j in range(i + 9):
                        ebuf = expSA if j % 2 == 0 else expSB
                        E("tensor", lambda e, b=bank, eb=ebuf, j=j, il=il, kv=kv, i=i: e.matmul(
                            b[:, 0:132],
                            eb[:, (j // 2) * 512 + il * 128: (j // 2) * 512 + (il + 1) * 128],
                            vhato[:, (j * HKV + kv) * 132:(j * HKV + kv) * 132 + 132],
                            start=(j == 0), stop=(j == i + 8)),
                          reads=[ebuf, vhato], writes=[bank])
                    E("vector", lambda e, b=bank: e.tensor_copy(out=avp[:], in_=b[:, 0:132]),
                      reads=[bank], writes=[avp])
                    divcol(atm[il][:, h * 128:(h + 1) * 128], avp[:, 0:128],
                           avp[:, 128:129], [avp], [atm[il]])
            for il in range(4):
                for ft in range(16):
                    trans(atm[il][:, ft * 128:(ft + 1) * 128],
                          atm[il][:, ft * 128:(ft + 1) * 128])
            for dc in range(4):
                p.dma("gpsimd", wob[:].rearrange("p (dt q) -> p dt q", dt=16),
                      dram["woT"][:, dc * 512:(dc + 1) * 512].rearrange(
                          "(dt pp) q -> pp dt q", pp=128))
                for il in range(4):
                    bank = psf[il % 4]
                    for ft in range(16):
                        E("tensor", lambda e, b=bank, ft=ft, il=il: e.matmul(
                            b[:, 0:512], atm[il][:, ft * 128:(ft + 1) * 128],
                            wob[:, ft * 512:(ft + 1) * 512],
                            start=(ft == 0), stop=(ft == 15)),
                          reads=[atm[il], wob], writes=[bank])
                    E("vector", lambda e, b=bank, il=il, dc=dc: e.tensor_copy(
                        out=otile[il][:, dc * 512:(dc + 1) * 512], in_=b[:, 0:512]),
                      reads=[bank], writes=[otile[il]])
            for il in range(4):
                tt = c * 4 + il
                sandwich(otile[il][:], x_own, out_dram, tt, xtmp[:], xnew[:])
        p.pop_scope()
        p.pop_scope()
        p.pop_scope()
        p.barrier()

    def l0_ffn():
        out_dram = y_out if UPTO == 2 else xr2
        xin = xr1 if UPTO >= 1 else x_own
        p.push_scope()
        p.push_scope()
        wbt = p.sb("f_wbt", (128, FFN_H), BF16)
        absb = p.sb("f_absb", (128, FFN_H), BF16)
        sgn = p.sb("f_sgn", (128, FFN_H), BF16)
        w1t = p.sb("f_w1t", (128, FFN_H), BF16)
        srow = p.sb("f_srow", (2, FFN_H), BF16)
        hrow = p.sb("f_hrow", (2, FFN_H), BF16)

        def tern_quant(src_dram, n_tiles, width, store_fn):
            nch = width // 512
            for it_ in range(n_tiles):
                p.dma("gpsimd", wbt[:, 0:width], src_dram[it_ * 128:(it_ + 1) * 128, :])
                E("scalar", lambda e, w=width: e.activation(
                    out=absb[:, 0:w], in_=wbt[:, 0:w], func=AF.Abs),
                  reads=[wbt], writes=[absb])
                E("scalar", lambda e, w=width: e.sign(out=sgn[:, 0:w], in_=wbt[:, 0:w]),
                  reads=[wbt], writes=[sgn])
                for ch in range(nch):
                    E("tensor", lambda e, ch=ch: e.matmul(
                        psf[2][0:2, 0:512], e64[:], absb[:, ch * 512:(ch + 1) * 512],
                        start=True, stop=True),
                      reads=[absb, e64], writes=[psf[2]])
                    E("vector", lambda e, ch=ch: e.tensor_scalar(
                        out=srow[:, ch * 512:(ch + 1) * 512],
                        in0=psf[2][0:2, 0:512], scalar1=1.0 / 64.0, scalar2=1e-8,
                        op0=ALU.mult, op1=ALU.max),
                      reads=[psf[2]], writes=[srow])
                    E("vector", lambda e, ch=ch: e.tensor_scalar(
                        out=hrow[:, ch * 512:(ch + 1) * 512],
                        in0=srow[:, ch * 512:(ch + 1) * 512],
                        scalar1=0.5, scalar2=None, op0=ALU.mult),
                      reads=[srow], writes=[hrow])
                    E("tensor", lambda e, ch=ch: e.matmul(
                        psf[3][:, 0:512], sel2[:], srow[:, ch * 512:(ch + 1) * 512],
                        start=True, stop=True),
                      reads=[sel2, srow], writes=[psf[3]])
                    E("tensor", lambda e, ch=ch: e.matmul(
                        psf[4][:, 0:512], sel2[:], hrow[:, ch * 512:(ch + 1) * 512],
                        start=True, stop=True),
                      reads=[sel2, hrow], writes=[psf[4]])
                    E("vector", lambda e, ch=ch: e.tensor_tensor(
                        out=w1t[:, ch * 512:(ch + 1) * 512],
                        in0=absb[:, ch * 512:(ch + 1) * 512],
                        in1=psf[4][:, 0:512], op=ALU.is_gt),
                      reads=[absb, psf[4]], writes=[w1t])
                    E("vector", lambda e, ch=ch: e.tensor_tensor(
                        out=w1t[:, ch * 512:(ch + 1) * 512],
                        in0=w1t[:, ch * 512:(ch + 1) * 512],
                        in1=psf[3][:, 0:512], op=ALU.mult),
                      reads=[w1t, psf[3]], writes=[w1t])
                    E("vector", lambda e, ch=ch: e.tensor_tensor(
                        out=w1t[:, ch * 512:(ch + 1) * 512],
                        in0=w1t[:, ch * 512:(ch + 1) * 512],
                        in1=sgn[:, ch * 512:(ch + 1) * 512], op=ALU.mult),
                      reads=[w1t, sgn], writes=[w1t])
                store_fn(it_)

        def store_fc(dt):
            p.dma("sync", w1fc_s[:, dt, :, :].rearrange("hc pp cc -> pp hc cc"),
                  w1t[:].rearrange("p (hc cc) -> p hc cc", hc=64))

        def store_pr(ht):
            p.dma("sync", w1pr_s[:, ht, :, :].rearrange("dc pp cc -> pp dc cc"),
                  w1t[:, 0:D].rearrange("p (dc cc) -> p dc cc", dc=16))

        tern_quant(dram["wfcT"], 16, FFN_H, store_fc)
        tern_quant(dram["wprojT"], 64, D, store_pr)
        p.pop_scope()
        p.barrier()

        p.push_scope()
        xn2T = p.sb("xn2T", (128, 16 * 1024), BF16)
        p.push_scope()
        xtl = p.sb("f_xtl", (128, 2 * D))
        xnb = p.sb("f_xnb", (128, D), BF16)
        for tt in range(NT):
            xt = xtl[:, (tt % 2) * D:(tt % 2 + 1) * D]
            p.dma("sync", xt, xin[tt * 128:(tt + 1) * 128, :])
            ln_stats(xt, D, extra_reads=[xtl])
            ln_cols(nmc, stc)
            adddiv(xnb[:], xt, nmc, stc, [xtl], [xnb])
            for dt in range(16):
                base2 = (tt // 4) * 8192 + dt * 512 + (tt % 4) * 128
                trans(xn2T[:, base2:base2 + 128], xnb[:, dt * 128:(dt + 1) * 128])
        p.pop_scope()

        hqT = p.sb("hqT", (128, 64 * 512), BF16)
        w1sl = p.sb("w1sl", (128, 4 * 2048), BF16)
        w2sl = p.sb("w2sl", (128, 2 * 8192), BF16)
        rbuf = p.sb("f_rbuf", (128, 4 * 512), BF16)
        hq2 = p.sb("f_hq2", (128, 512), BF16)
        stdr = p.sb("f_stdr", (1, 512))
        outTb = p.sb("f_outTb", (128, 512), BF16)
        otile4 = p.sb("f_otile4", (128, 4 * D), BF16)
        xtmp = p.sb("f_xtmp", (128, D))
        xnew = p.sb("f_xnew", (128, D))
        for tc in range(2):
            for hc in range(64):
                sl = w1sl[:, (hc % 4) * 2048:(hc % 4 + 1) * 2048]
                p.dma("sync", sl.rearrange("p (dt cc) -> p dt cc", dt=16),
                      w1fc_s[hc, :, :, :].rearrange("dt pp cc -> pp dt cc"))
                bank = psf[(0, 1, 4, 5)[hc % 4]]
                for dt in range(16):
                    E("tensor", lambda e, b=bank, sl=sl, dt=dt, tc=tc: e.matmul(
                        b[:, 0:512], sl[:, dt * 128:(dt + 1) * 128],
                        xn2T[:, tc * 8192 + dt * 512: tc * 8192 + (dt + 1) * 512],
                        start=(dt == 0), stop=(dt == 15)),
                      reads=[w1sl, xn2T], writes=[bank])
                rb = rbuf[:, (hc % 4) * 512:(hc % 4 + 1) * 512]
                E("scalar", lambda e, b=bank, rb=rb: e.activation(
                    out=rb, in_=b[:, 0:512], func=AF.Relu),
                  reads=[bank], writes=[rbuf])
                E("vector", lambda e, hc=hc, rb=rb: e.tensor_tensor(
                    out=hqT[:, hc * 512:(hc + 1) * 512], in0=rb, in1=rb,
                    op=ALU.mult),
                  reads=[rbuf], writes=[hqT])
                E("vector", lambda e, hc=hc: e.tensor_tensor(
                    out=hq2[:], in0=hqT[:, hc * 512:(hc + 1) * 512],
                    in1=hqT[:, hc * 512:(hc + 1) * 512], op=ALU.mult),
                  reads=[hqT], writes=[hq2])
                E("tensor", lambda e, hc=hc: e.matmul(
                    psf[2][0:1, 0:512], onesc[:], hq2[:],
                    start=(hc == 0), stop=(hc == 63)),
                  reads=[onesc, hq2], writes=[psf[2]])
            E("vector", lambda e: e.tensor_scalar(
                out=stdr[:], in0=psf[2][0:1, 0:512], scalar1=1.0 / FFN_H,
                scalar2=F32EPS, op0=ALU.mult, op1=ALU.add),
              reads=[psf[2]], writes=[stdr])
            E("scalar", lambda e: e.activation(out=stdr[:], in_=stdr[:], func=AF.Sqrt),
              reads=[stdr], writes=[stdr])
            E("vector", lambda e: e.reciprocal(out=stdr[:], in_=stdr[:]),
              reads=[stdr], writes=[stdr])
            E("tensor", lambda e: e.matmul(
                psf[3][:, 0:512], onesr[0:1, 0:128], stdr[:], start=True, stop=True),
              reads=[onesr, stdr], writes=[psf[3]])
            for hc in range(64):
                E("vector", lambda e, hc=hc: e.tensor_tensor(
                    out=hqT[:, hc * 512:(hc + 1) * 512],
                    in0=hqT[:, hc * 512:(hc + 1) * 512],
                    in1=psf[3][:, 0:512], op=ALU.mult),
                  reads=[hqT, psf[3]], writes=[hqT])
            for dcol in range(16):
                sl = w2sl[:, (dcol % 2) * 8192:(dcol % 2 + 1) * 8192]
                p.dma("sync", sl.rearrange("p (ht cc) -> p ht cc", ht=64),
                      w1pr_s[dcol, :, :, :].rearrange("ht pp cc -> pp ht cc"))
                bank = psf[(0, 1, 4, 5)[dcol % 4]]
                for ht in range(64):
                    E("tensor", lambda e, b=bank, sl=sl, ht=ht: e.matmul(
                        b[:, 0:512], sl[:, ht * 128:(ht + 1) * 128],
                        hqT[:, ht * 512:(ht + 1) * 512],
                        start=(ht == 0), stop=(ht == 63)),
                      reads=[w2sl, hqT], writes=[bank])
                E("vector", lambda e, b=bank: e.tensor_copy(out=outTb[:], in_=b[:, 0:512]),
                  reads=[bank], writes=[outTb])
                for q4 in range(4):
                    trans(otile4[:, q4 * D + dcol * 128: q4 * D + (dcol + 1) * 128],
                          outTb[:, q4 * 128:(q4 + 1) * 128])
            for q4 in range(4):
                tt = tc * 4 + q4
                sandwich(otile4[:, q4 * D:(q4 + 1) * D], xin, out_dram, tt,
                         xtmp[:], xnew[:])
        p.pop_scope()
        p.pop_scope()
        p.barrier()

    def l1_attn():
        out_dram = y_out if UPTO == 3 else xr3
        xin = xr2 if UPTO >= 2 else x_own
        p.push_scope()
        p.push_scope()
        xbcs = p.sb("xbcs", (128, 33 * 1024), BF16)
        dtv = p.sb("dtv", (128, 512))
        dtA = p.sb("dtA", (128, 512))
        ltA = p.sb("ltA", (64, NCH * 256))
        negl = p.sb("negl", (128, 8 * 64))
        wtb = p.sb("wtb", (128, 8 * 64))
        pendA = p.sb("pendA", (128, NCH * 64))
        Btok = p.sb("Btok", (128, 8 * 64), BF16)
        Bt_cm = p.sb("Bt_cm", (64, 1024), BF16)
        Ct_cm = p.sb("Ct_cm", (64, 1024), BF16)
        Sst = p.sb("Sst", (64, 2 * DIN))
        ssdAb = p.sb("ssdAb", (128, 256))
        ssdBb = p.sb("ssdBb", (128, 256))
        ddb = p.sb("ddb", (128, NH))
        aneg = p.sb("aneg", (128, NH))
        dtbb = p.sb("dtbb", (128, NH))
        onesc64 = p.sb("onesc64", (64, 256))
        p.dma("sync", ssdAb[:], dram["ssdA"][:, :])
        p.dma("sync", ssdBb[:], dram["ssdB"][:, :])
        p.dma("sync", ddb[:], dram["dd_rep"][:, :])
        p.dma("sync", dtbb[:], dram["dtb_rep"][:, :])
        p.dma("sync", aneg[:], dram["alog_rep"][:, :])
        E("scalar", lambda e: e.activation(out=aneg[:], in_=aneg[:], func=AF.Exp),
          reads=[aneg], writes=[aneg])
        E("vector", lambda e: e.tensor_scalar(
            out=aneg[:], in0=aneg[:], scalar1=-1.0, scalar2=None, op0=ALU.mult),
          reads=[aneg], writes=[aneg])
        E("gpsimd", lambda e: e.memset(onesc64[:], 1.0), reads=[], writes=[onesc64])
        E("gpsimd", lambda e: e.memset(Sst[:, 0:DIN], 0.0), reads=[], writes=[Sst])

        p.push_scope()
        uT = p.sb("uT", (128, 16 * 1024), BF16)
        p.push_scope()
        xtl = p.sb("m_xtl", (128, D))
        ln3b = p.sb("m_ln3", (128, D))
        ub = p.sb("m_ub", (128, D), BF16)
        rmswb = p.sb("m_rmsw", (128, D))
        p.dma("sync", rmswb[:], dram["rms_w_rep"][:, :])
        for tt in range(NT):
            xt = xtl[:]
            p.dma("sync", xt, xin[tt * 128:(tt + 1) * 128, :])
            ln_stats(xt, D, extra_reads=[xtl])
            ln_cols(nmc, stc)
            E("vector", lambda e, tt=tt: e.tensor_copy(
                out=m3[:, 2 * tt:2 * tt + 1], in_=nmc),
              reads=[cols], writes=[m3])
            E("vector", lambda e, tt=tt: e.tensor_copy(
                out=m3[:, 2 * tt + 1:2 * tt + 2], in_=stc),
              reads=[cols], writes=[m3])
            adddiv(ln3b[:], xt, nmc, stc, [xtl], [ln3b])
            ln_stats(ln3b[:], D)
            rms_std_col(stc, 1e-5)
            divcol(ln3b[:], ln3b[:], stc, [ln3b, cols], [ln3b])
            E("vector", lambda e: e.tensor_tensor(
                out=ub[:], in0=ln3b[:], in1=rmswb[:], op=ALU.mult),
              reads=[ln3b, rmswb], writes=[ub])
            for dt in range(16):
                trans(uT[:, tt * 2048 + dt * 128: tt * 2048 + (dt + 1) * 128],
                      ub[:, dt * 128:(dt + 1) * 128])
        p.pop_scope()

        p.push_scope()
        uTr = p.sb("uTr", (128, 16 * 1024), BF16)
        wzs = p.sb("m_wzs", (128, 8192), BF16)
        zb = p.sb("m_zb", (128, 512), BF16)
        h3 = p.sb("m_h3", (128, 3))
        wdts = p.sb("m_wdts", (128, 16 * 64), BF16)
        for tt in range(NT):
            for dt in range(16):
                base2 = (tt // 4) * 8192 + dt * 512 + (tt % 4) * 128
                E("vector", lambda e, b2=base2, tt=tt, dt=dt: e.tensor_copy(
                    out=uTr[:, b2:b2 + 128],
                    in_=uT[:, tt * 2048 + dt * 128: tt * 2048 + (dt + 1) * 128]),
                  reads=[uT], writes=[uTr])
        for zc in range(8):
            sl = wzs[:, 0:8192]
            p.dma("gpsimd", sl.rearrange("p (dt cc) -> p dt cc", dt=16),
                  dram["inprojT"][:, zc * 512:(zc + 1) * 512].rearrange(
                      "(dt pp) cc -> pp dt cc", pp=128))
            for tt in range(NT):
                bank = psf[(0, 1, 4, 5)[tt % 4]]
                for dt in range(16):
                    E("tensor", lambda e, b=bank, sl=sl, dt=dt, tt=tt: e.matmul(
                        b[:, 0:512],
                        uT[:, tt * 2048 + dt * 128: tt * 2048 + (dt + 1) * 128],
                        sl[:, dt * 512:(dt + 1) * 512],
                        start=(dt == 0), stop=(dt == 15)),
                      reads=[uT, wzs], writes=[bank])
                E("vector", lambda e, b=bank: e.tensor_copy(out=zb[:], in_=b[:, 0:512]),
                  reads=[bank], writes=[zb])
                p.dma("gpsimd", zspill[tt * 128:(tt + 1) * 128, zc * 512:(zc + 1) * 512],
                      zb[:])
        for ct in range(33):
            sl = wzs[:, (ct % 4) * 2048:(ct % 4) * 2048 + 2048]
            p.dma("gpsimd", sl.rearrange("p (dt cc) -> p dt cc", dt=16),
                  dram["inprojT"][:, DIN + ct * 128: DIN + (ct + 1) * 128].rearrange(
                      "(dt pp) cc -> pp dt cc", pp=128))
            for tc in range(2):
                bank = psf[2 + tc]
                for dt in range(16):
                    E("tensor", lambda e, b=bank, sl=sl, dt=dt, tc=tc: e.matmul(
                        b[:, 0:512], sl[:, dt * 128:(dt + 1) * 128],
                        uTr[:, tc * 8192 + dt * 512: tc * 8192 + (dt + 1) * 512],
                        start=(dt == 0), stop=(dt == 15)),
                      reads=[uTr, wzs], writes=[bank])
                E("vector", lambda e, b=bank, ct=ct, tc=tc: e.tensor_copy(
                    out=xbcs[:, ct * 1024 + tc * 512: ct * 1024 + (tc + 1) * 512],
                    in_=b[:, 0:512]),
                  reads=[bank], writes=[xbcs])
                if tc == 1:
                    E("vector", lambda e, b=bank: e.tensor_copy(
                        out=h3[:], in_=b[:, 509:512]),
                      reads=[bank], writes=[h3])
                    p.dma("gpsimd", cch_in[ct, :, :], h3[:])
        p.dma("gpsimd", wdts[:].rearrange("p (dt cc) -> p dt cc", dt=16),
              dram["inprojT"][:, 8320:8384].rearrange("(dt pp) cc -> pp dt cc", pp=128))
        for tt in range(NT):
            bank = psf[tt % 2]
            for dt in range(16):
                E("tensor", lambda e, b=bank, dt=dt, tt=tt: e.matmul(
                    b[:, 0:64],
                    uT[:, tt * 2048 + dt * 128: tt * 2048 + (dt + 1) * 128],
                    wdts[:, dt * 64:(dt + 1) * 64],
                    start=(dt == 0), stop=(dt == 15)),
                  reads=[uT, wdts], writes=[bank])
            E("vector", lambda e, b=bank, tt=tt: e.tensor_tensor(
                out=dtv[:, tt * 64:(tt + 1) * 64], in0=b[:, 0:64], in1=dtbb[:],
                op=ALU.add),
              reads=[bank, dtbb], writes=[dtv])
        E("scalar", lambda e: e.activation(out=dtA[:], in_=dtv[:], func=AF.Exp),
          reads=[dtv], writes=[dtA])
        E("scalar", lambda e: e.activation(out=dtv[:], in_=dtA[:], func=AF.Ln, bias=1.0),
          reads=[dtA], writes=[dtv])
        for tt in range(NT):
            E("vector", lambda e, tt=tt: e.tensor_tensor(
                out=dtA[:, tt * 64:(tt + 1) * 64], in0=dtv[:, tt * 64:(tt + 1) * 64],
                in1=aneg[:], op=ALU.mult),
              reads=[dtv, aneg], writes=[dtA])
        p.pop_scope()
        p.pop_scope()
        p.barrier()

        E("gpsimd", lambda e: e.collective_compute(
            "AllGather", ALU.bypass, replica_groups=[[0, 1], [2, 3], [4, 5], [6, 7]],
            ins=[cch_in.opt()], outs=[cch_out.opt()]),
          reads=[cch_in], writes=[cch_out], sem="cc", inc=1)
        p.push_scope()
        pd_ = p.sb("m_pd", (128, 1027))
        acc1 = p.sb("m_acc1", (128, 1024))
        acc2 = p.sb("m_acc2", (128, 1024))
        cwb = p.sb("m_cwb", (128, 4))
        cbb = p.sb("m_cbb", (128, 1))
        h3r = p.sb("m_h3r", (128, 3))
        for ct in range(33):
            p.dma("sync", cwb[:], dram["conv_w"][ct * 128:(ct + 1) * 128, :])
            p.dma("sync", cbb[:], dram["conv_b"][ct * 128:(ct + 1) * 128, :])
            p.dma("sync", h3r[:], cch_out[0, ct, :, :])
            E("vector", lambda e: e.tensor_scalar(
                out=pd_[:, 0:3], in0=h3r[:], scalar1=hzc[:], scalar2=None, op0=ALU.mult),
              reads=[h3r, hzc], writes=[pd_])
            E("vector", lambda e, ct=ct: e.tensor_copy(
                out=pd_[:, 3:1027], in_=xbcs[:, ct * 1024:(ct + 1) * 1024]),
              reads=[xbcs], writes=[pd_])
            E("vector", lambda e: e.tensor_scalar(
                out=acc1[:], in0=pd_[:, 0:1024], scalar1=cwb[:, 0:1], scalar2=None,
                op0=ALU.mult),
              reads=[pd_, cwb], writes=[acc1])
            E("vector", lambda e: e.scalar_tensor_tensor(
                out=acc2[:], in0=pd_[:, 1:1025], scalar=cwb[:, 1:2], in1=acc1[:],
                op0=ALU.mult, op1=ALU.add),
              reads=[pd_, cwb, acc1], writes=[acc2])
            E("vector", lambda e: e.scalar_tensor_tensor(
                out=acc1[:], in0=pd_[:, 2:1026], scalar=cwb[:, 2:3], in1=acc2[:],
                op0=ALU.mult, op1=ALU.add),
              reads=[pd_, cwb, acc2], writes=[acc1])
            E("vector", lambda e: e.scalar_tensor_tensor(
                out=acc2[:], in0=pd_[:, 3:1027], scalar=cwb[:, 3:4], in1=acc1[:],
                op0=ALU.mult, op1=ALU.add),
              reads=[pd_, cwb, acc1], writes=[acc2])
            E("scalar", lambda e, ct=ct: e.activation(
                out=xbcs[:, ct * 1024:(ct + 1) * 1024], in_=acc2[:], func=AF.Silu,
                bias=cbb[:]),
              reads=[acc2, cbb], writes=[xbcs])
        p.dma("gpsimd", Bt_cm[:, :], xbcs[0:64, 32 * 1024:33 * 1024])
        p.dma("gpsimd", Ct_cm[:, :], xbcs[64:128, 32 * 1024:33 * 1024])
        p.pop_scope()
        p.barrier()

        p.push_scope()
        dtt = p.sb("m_dtt", (64, 256))
        ltok = p.sb("m_ltok", (128, 128))
        lrow = p.sb("m_lrow", (1, 64))
        for ic in range(NCH):
            for jt in range(2):
                trans(dtt[:, jt * 128:(jt + 1) * 128],
                      dtA[:, (2 * ic + jt) * 64:(2 * ic + jt + 1) * 64],
                      pdim=128, fdim=64, f32=True)
            E("vector", lambda e, ic=ic: e.tensor_tensor_scan(
                out=ltA[:, ic * 256:(ic + 1) * 256], data0=onesc64[:], data1=dtt[:],
                initial=0.0, op0=ALU.mult, op1=ALU.add),
              reads=[onesc64, dtt], writes=[ltA])
            for jt in range(2):
                trans(ltok[:, jt * 64:(jt + 1) * 64],
                      ltA[:, ic * 256 + jt * 128: ic * 256 + (jt + 1) * 128],
                      pdim=64, fdim=128, f32=True)
            E("vector", lambda e, ic=ic: e.tensor_scalar(
                out=negl[:, ic * 128:(ic + 1) * 128], in0=ltok[:], scalar1=-1.0,
                scalar2=None, op0=ALU.mult),
              reads=[ltok], writes=[negl])
            p.dma("gpsimd", lrow[:], ltok[127:128, 64:128])
            E("tensor", lambda e, ic=ic: e.matmul(
                psf[4][:, 0:64], onesr[0:1, 0:128], lrow[0:1, 0:64],
                start=True, stop=True),
              reads=[onesr, lrow], writes=[psf[4]])
            E("scalar", lambda e, ic=ic: e.activation(
                out=pendA[:, ic * 64:(ic + 1) * 64], in_=psf[4][:, 0:64], func=AF.Exp),
              reads=[psf[4]], writes=[pendA])
            for jt in range(2):
                E("vector", lambda e, ic=ic, jt=jt: e.tensor_tensor(
                    out=wtb[:, (2 * ic + jt) * 64:(2 * ic + jt + 1) * 64],
                    in0=psf[4][:, 0:64], in1=ltok[:, jt * 64:(jt + 1) * 64],
                    op=ALU.subtract),
                  reads=[psf[4], ltok], writes=[wtb])
            E("scalar", lambda e, ic=ic: e.activation(
                out=wtb[:, 2 * ic * 64:(2 * ic + 2) * 64],
                in_=wtb[:, 2 * ic * 64:(2 * ic + 2) * 64], func=AF.Exp),
              reads=[wtb], writes=[wtb])
            E("vector", lambda e, ic=ic: e.tensor_tensor(
                out=wtb[:, 2 * ic * 64:(2 * ic + 2) * 64],
                in0=wtb[:, 2 * ic * 64:(2 * ic + 2) * 64],
                in1=dtv[:, 2 * ic * 64:(2 * ic + 2) * 64], op=ALU.mult),
              reads=[wtb, dtv], writes=[wtb])
            for jt in range(2):
                trans(Btok[:, (2 * ic + jt) * 64:(2 * ic + jt + 1) * 64],
                      Bt_cm[:, ic * 256 + jt * 128: ic * 256 + (jt + 1) * 128],
                      pdim=64, fdim=128)
        p.pop_scope()
        p.barrier()

        p.push_scope()
        xtk = p.sb("m_xtk", (128, 2 * DIN), BF16)
        bwb = p.sb("m_bwb", (128, 64), BF16)

        def state_chunk(ic, rd, wr):
            for h in range(NH):
                for jt in range(2):
                    E("vector", lambda e, h=h, jt=jt, ic=ic: e.tensor_scalar(
                        out=bwb[:], in0=Btok[:, (2 * ic + jt) * 64:(2 * ic + jt + 1) * 64],
                        scalar1=wtb[:, (2 * ic + jt) * 64 + h:(2 * ic + jt) * 64 + h + 1],
                        scalar2=None, op0=ALU.mult),
                      reads=[Btok, wtb], writes=[bwb])
                    E("tensor", lambda e, h=h, jt=jt: e.matmul(
                        psf[5][0:64, 0:64], bwb[:],
                        xtk[:, jt * DIN + h * 64: jt * DIN + (h + 1) * 64],
                        start=(jt == 0), stop=(jt == 1)),
                      reads=[bwb, xtk], writes=[psf[5]])
                E("vector", lambda e, h=h, ic=ic, rd=rd, wr=wr: e.scalar_tensor_tensor(
                    out=Sst[:, wr * DIN + h * 64: wr * DIN + (h + 1) * 64],
                    in0=Sst[:, rd * DIN + h * 64: rd * DIN + (h + 1) * 64],
                    scalar=pendA[0:64, ic * 64 + h: ic * 64 + h + 1],
                    in1=psf[5][0:64, 0:64], op0=ALU.mult, op1=ALU.add),
                  reads=[Sst, pendA, psf[5]], writes=[Sst])

        def xtok_chunk(ic):
            for cp in range(32):
                for jt in range(2):
                    trans(xtk[:, jt * DIN + cp * 128: jt * DIN + (cp + 1) * 128],
                          xbcs[:, cp * 1024 + ic * 256 + jt * 128:
                               cp * 1024 + ic * 256 + (jt + 1) * 128])

        for ic in range(NCH):
            xtok_chunk(ic)
            state_chunk(ic, ic % 2, (ic + 1) % 2)
        fin = NCH % 2
        p.dma("gpsimd", cst_in[:, :], Sst[:, fin * DIN:(fin + 1) * DIN])
        E("gpsimd", lambda e: e.collective_compute(
            "AllGather", ALU.bypass, replica_groups=[[0, 1], [2, 3], [4, 5], [6, 7]],
            ins=[cst_in.opt()], outs=[cst_out.opt()]),
          reads=[cst_in], writes=[cst_out], sem="cc", inc=1)
        p.dma("sync", Sst[:, 0:DIN], cst_out[0, :, :])
        E("vector", lambda e: e.tensor_scalar(
            out=Sst[:, 0:DIN], in0=Sst[:, 0:DIN], scalar1=hzc[0:64, :], scalar2=None,
            op0=ALU.mult),
          reads=[Sst, hzc], writes=[Sst])
        p.barrier()

        Gb = p.sb("m_Gb", (128, 2 * 256), BF16)
        lrow2 = p.sb("m_lrow2", (1, 256))
        tmpG = p.sb("m_tmpG", (128, 2 * 256))
        Eb = p.sb("m_Eb", (128, 2 * 256))
        ebc = p.sb("m_ebc", (64, 256))
        chat = p.sb("m_chat", (64, 256))
        ybuf = p.sb("m_ybuf", (128, 2 * DIN), BF16)
        zbl = p.sb("m_zbl", (128, DIN), BF16)
        szb = p.sb("m_szb", (128, DIN), BF16)
        ygb = p.sb("m_ygb", (128, DIN), BF16)
        ygT = p.sb("m_ygT", (128, 32 * 128), BF16)
        nwb = p.sb("m_nwb", (128, DIN), BF16)
        p.dma("gpsimd", nwb[:], dram["normw_rep"][:, :])
        for ic in range(NCH):
            rd = ic % 2
            wr = (ic + 1) % 2
            xtok_chunk(ic)
            for h in range(NH):
                for jt in range(2):
                    bank = psf[jt]
                    E("tensor", lambda e, b=bank, ic=ic, jt=jt: e.matmul(
                        b[:, 0:256], Bt_cm[:, ic * 256 + jt * 128: ic * 256 + (jt + 1) * 128],
                        Ct_cm[:, ic * 256:(ic + 1) * 256], start=True, stop=True),
                      reads=[Bt_cm, Ct_cm], writes=[bank])
                    bank2 = psf[2 + jt]
                    if jt == 0:
                        p.dma("gpsimd", lrow2[:], ltA[h:h + 1, ic * 256:(ic + 1) * 256])
                    E("tensor", lambda e, b=bank2, ic=ic, h=h: e.matmul(
                        b[:, 0:256], onesr[0:1, 0:128], lrow2[0:1, 0:256],
                        start=True, stop=True),
                      reads=[onesr, lrow2], writes=[bank2])
                    ssd = ssdAb if jt == 0 else ssdBb
                    E("vector", lambda e, b=bank2, jt=jt, ssd=ssd: e.tensor_tensor(
                        out=tmpG[:, jt * 256:(jt + 1) * 256], in0=b[:, 0:256], in1=ssd[:],
                        op=ALU.add),
                      reads=[bank2, ssd], writes=[tmpG])
                    E("scalar", lambda e, jt=jt, ic=ic, h=h: e.activation(
                        out=Eb[:, jt * 256:(jt + 1) * 256],
                        in_=tmpG[:, jt * 256:(jt + 1) * 256], func=AF.Exp,
                        bias=negl[:, ic * 128 + jt * 64 + h: ic * 128 + jt * 64 + h + 1]),
                      reads=[tmpG, negl], writes=[Eb])
                    E("vector", lambda e, b=bank, jt=jt, ic=ic, h=h: e.scalar_tensor_tensor(
                        out=Gb[:, jt * 256:(jt + 1) * 256], in0=b[:, 0:256],
                        scalar=dtv[:, (2 * ic + jt) * 64 + h:(2 * ic + jt) * 64 + h + 1],
                        in1=Eb[:, jt * 256:(jt + 1) * 256], op0=ALU.mult, op1=ALU.mult),
                      reads=[bank, dtv, Eb], writes=[Gb])
                E("scalar", lambda e: e.activation(
                    out=ebc[:], in_=psf[2][0:64, 0:256], func=AF.Exp),
                  reads=[psf[2]], writes=[ebc])
                E("vector", lambda e, ic=ic: e.tensor_tensor(
                    out=chat[:], in0=Ct_cm[:, ic * 256:(ic + 1) * 256], in1=ebc[:],
                    op=ALU.mult),
                  reads=[Ct_cm, ebc], writes=[chat])
                for it in range(2):
                    bank = psf[4]
                    for jt in range(it + 1):
                        E("tensor", lambda e, b=bank, jt=jt, it=it, h=h: e.matmul(
                            b[:, 0:64], Gb[:, jt * 256 + it * 128: jt * 256 + (it + 1) * 128],
                            xtk[:, jt * DIN + h * 64: jt * DIN + (h + 1) * 64],
                            start=(jt == 0), stop=False),
                          reads=[Gb, xtk], writes=[bank])
                    E("tensor", lambda e, b=bank, it=it, h=h, rd=rd: e.matmul(
                        b[:, 0:64], chat[0:64, it * 128:(it + 1) * 128],
                        Sst[:, rd * DIN + h * 64: rd * DIN + (h + 1) * 64],
                        start=False, stop=True),
                      reads=[chat, Sst], writes=[bank])
                    E("vector", lambda e, b=bank, it=it, h=h: e.scalar_tensor_tensor(
                        out=ybuf[:, it * DIN + h * 64: it * DIN + (h + 1) * 64],
                        in0=xtk[:, it * DIN + h * 64: it * DIN + (h + 1) * 64],
                        scalar=ddb[:, h:h + 1], in1=b[:, 0:64],
                        op0=ALU.mult, op1=ALU.add),
                      reads=[xtk, ddb, bank], writes=[ybuf])
            state_chunk(ic, rd, wr)
            for it in range(2):
                tile = 2 * ic + it
                p.dma("sync", zbl[:], zspill[tile * 128:(tile + 1) * 128, :])
                E("scalar", lambda e: e.activation(out=szb[:], in_=zbl[:], func=AF.Silu),
                  reads=[zbl], writes=[szb])
                E("vector", lambda e, it=it: e.tensor_tensor(
                    out=ygb[:], in0=ybuf[:, it * DIN:(it + 1) * DIN], in1=szb[:],
                    op=ALU.mult),
                  reads=[ybuf, szb], writes=[ygb])
                ln_stats(ygb[:], DIN)
                rms_std_col(stc, 1e-5)
                divcol(ygb[:], ygb[:], stc, [ygb, cols], [ygb])
                E("vector", lambda e: e.tensor_tensor(
                    out=ygb[:], in0=ygb[:], in1=nwb[:], op=ALU.mult),
                  reads=[ygb, nwb], writes=[ygb])
                for et in range(32):
                    trans(ygT[:, et * 128:(et + 1) * 128], ygb[:, et * 128:(et + 1) * 128])
                p.dma("gpsimd", ygsp[tile, :, :, :].rearrange("et pp cc -> pp et cc"),
                      ygT[:].rearrange("p (et cc) -> p et cc", et=32))
        p.pop_scope()
        p.pop_scope()
        p.barrier()

        p.push_scope()
        ygl = p.sb("m_ygl", (128, 2 * 4096), BF16)
        wop = p.sb("m_wop", (128, 32 * 512), BF16)
        otfull = p.sb("m_otfull", (128, 8 * D))
        oti = p.sb("m_oti", (128, D))
        xtmp = p.sb("m_xtmp", (128, D))
        xnew = p.sb("m_xnew", (128, D))
        for dc in range(4):
            p.dma("gpsimd", wop[:].rearrange("p (et cc) -> p et cc", et=32),
                  dram["outprojT"][:, dc * 512:(dc + 1) * 512].rearrange(
                      "(et pp) cc -> pp et cc", pp=128))
            for tt in range(NT):
                sl = ygl[:, (tt % 2) * 4096:(tt % 2 + 1) * 4096]
                p.dma("sync", sl.rearrange("p (et cc) -> p et cc", et=32),
                      ygsp[tt, :, :, :].rearrange("et pp cc -> pp et cc"))
                bank = psf[(0, 1, 4, 5)[tt % 4]]
                for et in range(32):
                    E("tensor", lambda e, b=bank, sl=sl, et=et: e.matmul(
                        b[:, 0:512], sl[:, et * 128:(et + 1) * 128],
                        wop[:, et * 512:(et + 1) * 512],
                        start=(et == 0), stop=(et == 31)),
                      reads=[ygl, wop], writes=[bank])
                E("vector", lambda e, b=bank, tt=tt, dc=dc: e.tensor_copy(
                    out=otfull[:, tt * D + dc * 512: tt * D + (dc + 1) * 512],
                    in_=b[:, 0:512]),
                  reads=[bank], writes=[otfull])
        for tt in range(NT):
            p.dma("sync", xtmp[:], xin[tt * 128:(tt + 1) * 128, :])
            E("vector", lambda e, tt=tt: e.tensor_scalar(
                out=xnew[:], in0=xtmp[:], scalar1=m3[:, 2 * tt:2 * tt + 1],
                scalar2=None, op0=ALU.add),
              reads=[xtmp, m3], writes=[xnew])
            divcol(xnew[:], xnew[:], m3[:, 2 * tt + 1:2 * tt + 2], [xnew, m3], [xnew])
            E("vector", lambda e, tt=tt: e.tensor_tensor(
                out=oti[:], in0=otfull[:, tt * D:(tt + 1) * D], in1=xnew[:], op=ALU.add),
              reads=[otfull, xnew], writes=[oti])
            sandwich(oti[:], xin, out_dram, tt, xtmp[:], xnew[:])
        p.pop_scope()
        p.pop_scope()
        p.barrier()

    def l1_ffn():
        out_dram = y_out
        xin = (xr2 if SKIP3 else xr3) if UPTO >= 3 else x_own
        p.push_scope()
        XT = p.sb("l_XT", (128, 16 * 1024), BF16)
        p.push_scope()
        xtl = p.sb("l_xtl", (128, 2 * D))
        xnb = p.sb("l_xnb", (128, D), BF16)
        for tt in range(NT):
            xt = xtl[:, (tt % 2) * D:(tt % 2 + 1) * D]
            p.dma("sync", xt, xin[tt * 128:(tt + 1) * 128, :])
            ln_stats(xt, D, extra_reads=[xtl])
            ln_cols(nmc, stc)
            adddiv(xnb[:], xt, nmc, stc, [xtl], [xnb])
            for dt in range(16):
                base2 = (tt // 4) * 8192 + dt * 512 + (tt % 4) * 128
                trans(XT[:, base2:base2 + 128], xnb[:, dt * 128:(dt + 1) * 128])
        p.pop_scope()

        DhT = p.sb("l_DhT", (128, 64 * 512), BF16)
        w0sl = p.sb("l_w0sl", (128, 4 * 2048), BF16)
        w1sl = p.sb("l_w1sl", (128, 2 * 8192), BF16)
        b0c = p.sb("l_b0c", (128, 2))
        outTb = p.sb("l_outTb", (128, 512), BF16)
        otile4 = p.sb("l_otile4", (128, 4 * D), BF16)
        b1b = p.sb("l_b1b", (128, D))
        otf32 = p.sb("l_otf32", (128, D))
        otf16 = p.sb("l_otf16", (128, D), F16)
        xtmp = p.sb("l_xtmp", (128, D))
        xnew = p.sb("l_xnew", (128, D))
        p.dma("sync", b1b[:], dram["bias1_rep"][:, :])
        for tc in range(2):
            for hc in range(64):
                sl = w0sl[:, (hc % 4) * 2048:(hc % 4 + 1) * 2048]
                p.dma("gpsimd", sl.rearrange("p (dt cc) -> p dt cc", dt=16),
                      dram["w0T"][:, hc * 128:(hc + 1) * 128].rearrange(
                          "(dt pp) cc -> pp dt cc", pp=128))
                bc = b0c[:, hc % 2: hc % 2 + 1]
                p.dma("sync", bc, dram["bias0"][hc * 128:(hc + 1) * 128, :])
                bank = psf[(0, 1, 4, 5)[hc % 4]]
                for dt in range(16):
                    E("tensor", lambda e, b=bank, sl=sl, dt=dt, tc=tc: e.matmul(
                        b[:, 0:512], sl[:, dt * 128:(dt + 1) * 128],
                        XT[:, tc * 8192 + dt * 512: tc * 8192 + (dt + 1) * 512],
                        start=(dt == 0), stop=(dt == 15)),
                      reads=[w0sl, XT], writes=[bank])
                E("scalar", lambda e, b=bank, hc=hc, bc=bc: e.activation(
                    out=DhT[:, hc * 512:(hc + 1) * 512], in_=b[:, 0:512],
                    func=AF.Gelu, bias=bc),
                  reads=[bank, b0c], writes=[DhT])
            for dcol in range(16):
                sl = w1sl[:, (dcol % 2) * 8192:(dcol % 2 + 1) * 8192]
                p.dma("gpsimd", sl.rearrange("p (ht cc) -> p ht cc", ht=64),
                      dram["w1T"][:, dcol * 128:(dcol + 1) * 128].rearrange(
                          "(ht pp) cc -> pp ht cc", pp=128))
                bank = psf[(2, 3, 0, 1)[dcol % 4]]
                for ht in range(64):
                    E("tensor", lambda e, b=bank, sl=sl, ht=ht: e.matmul(
                        b[:, 0:512], sl[:, ht * 128:(ht + 1) * 128],
                        DhT[:, ht * 512:(ht + 1) * 512],
                        start=(ht == 0), stop=(ht == 63)),
                      reads=[w1sl, DhT], writes=[bank])
                E("vector", lambda e, b=bank: e.tensor_copy(out=outTb[:], in_=b[:, 0:512]),
                  reads=[bank], writes=[outTb])
                for q4 in range(4):
                    trans(otile4[:, q4 * D + dcol * 128: q4 * D + (dcol + 1) * 128],
                          outTb[:, q4 * 128:(q4 + 1) * 128])
            for q4 in range(4):
                tt = tc * 4 + q4
                E("vector", lambda e, q4=q4: e.tensor_tensor(
                    out=otf32[:], in0=otile4[:, q4 * D:(q4 + 1) * D], in1=b1b[:],
                    op=ALU.add),
                  reads=[otile4, b1b], writes=[otf32])
                E("vector", lambda e: e.tensor_copy(out=otf16[:], in_=otf32[:]),
                  reads=[otf32], writes=[otf16])
                E("vector", lambda e: e.tensor_copy(out=otf32[:], in_=otf16[:]),
                  reads=[otf16], writes=[otf32])
                sandwich(otf32[:], xin, out_dram, tt, xtmp[:], xnew[:])
        p.pop_scope()
        p.barrier()

    if UPTO >= 1:
        l0_attn()
    if UPTO >= 2:
        l0_ffn()
    if UPTO >= 3 and not SKIP3:
        l1_attn()
    if UPTO >= 4:
        l1_ffn()
    if UPTO == 0:
        p.push_scope()
        xt0 = p.sb("xt0", (128, D))
        for tt in range(NT):
            p.dma("sync", xt0[:], x_own[tt * 128:(tt + 1) * 128, :])
            p.dma("gpsimd", y_out[tt * 128:(tt + 1) * 128, :], xt0[:])
        p.pop_scope()
    elif UPTO < 4:
        pass

    p.barrier()
    p.finalize()
    return nc


def _in_map(g, c, x, core):
    pr, hi = core // 2, core % 2
    m = dict(g)
    m["x_ctx"] = np.ascontiguousarray(x[pr])
    m["x_own"] = np.ascontiguousarray(x[pr, hi * 1024:(hi + 1) * 1024])
    m["cos_ctx"] = c["cos_ctx"]; m["sinA_ctx"] = c["sinA_ctx"]
    m["cos_own"] = c[f"cos_own_{hi}"]; m["sinA_own"] = c[f"sinA_own_{hi}"]
    m["masks"] = c[f"masks_{hi}"]; m["hz"] = c[f"hz_{hi}"]
    m["ident"] = c["ident"]; m["onesr"] = c["onesr"]
    m["E64"] = c["E64"]; m["sel2"] = c["sel2"]
    m["ssdA"] = c["ssdA"]; m["ssdB"] = c["ssdB"]
    return m


def kernel(**inputs):
    g = _prep_inputs(inputs)
    c = _host_consts()
    x = np.asarray(inputs["L0_attn_x"])
    nc = build_program()
    in_maps = [_in_map(g, c, x, core) for core in range(8)]
    res = run_bass_kernel_spmd(nc, in_maps, core_ids=list(range(8)))
    out = np.empty((4, 2048, 2048), dtype=np.float32)
    for core in range(8):
        pr, hi = core // 2, core % 2
        out[pr, hi * 1024:(hi + 1) * 1024] = res.results[core]["y_out"]
    return out


# version 11
# speedup vs baseline: 1.1620x; 1.0056x over previous
import os
import math
import numpy as np
import ml_dtypes
from contextlib import ExitStack

import concourse.bass as bass
import concourse.mybir as mybir
from concourse.bass_utils import run_bass_kernel_spmd
from concourse.bass import broadcast_tensor_aps

F32 = mybir.dt.float32
BF16 = mybir.dt.bfloat16
F16 = mybir.dt.float16
AF = mybir.ActivationFunctionType
ALU = mybir.AluOpType
AX = mybir.AxisListType
DR = mybir.MatmulPerfMode.DoubleRow

D = 2048
T = 2048
NTOK = 1024
NT = 8
H = 16
HKV = 4
HD = 128
DKV = 512
FFN_H = 8192
DIN = 4096
DST = 64
NH = 64
CONV_DIM = 4224
EIN = 8384
CHUNK = 256
NCH = 4
ASCALE = 1.0 / math.sqrt(HD)
F32EPS = float(np.finfo(np.float32).eps)
NEG = -1e30

UPTO = int(os.environ.get("MK_UPTO", "4"))
SIMMODE = os.environ.get("BASS_SIM_FP64") == "1"
F_SAME = True
F_PH = True
SKIP3 = os.environ.get("MK_SKIP3", "") == "1"

ENGS = ("sync", "scalar", "vector", "tensor", "gpsimd")
ESEM = {"scalar": "eS", "vector": "eV", "tensor": "eT", "gpsimd": "eG"}
QSEM = {"sync": "qY", "scalar": "qA", "vector": "qV", "tensor": "qT", "gpsimd": "qG"}


class PH:
    __slots__ = ("val",)

    def __init__(self):
        self.val = None


def _nm(x):
    if isinstance(x, str):
        return x
    t = getattr(x, "tensor", None)
    if t is not None:
        return t.name
    return getattr(x, "name")


class P:
    def __init__(self, nc):
        self.nc = nc
        self.es = ExitStack()
        self.streams = {e: [] for e in ENGS}
        self.semh = {}
        self.cnt = {}
        self.qpool = {"sync": [f"qy{i}" for i in range(10)],
                      "scalar": [f"qa{i}" for i in range(4)],
                      "vector": [f"qv{i}" for i in range(2)],
                      "tensor": [f"qt{i}" for i in range(2)],
                      "gpsimd": [f"qg{i}" for i in range(10)]}
        names = ["eS", "eV", "eT", "eG", "cc"] + [x for v in self.qpool.values() for x in v]
        for s in names:
            self.semh[s] = self.es.enter_context(nc.semaphore(s))
            self.cnt[s] = 0
        self.qpi = {k: 0 for k in self.qpool}
        self.qpend = {k: [] for k in self.qpool}
        self.waited = {e: {} for e in ENGS}
        self.buf = {}
        self.scopes = []

    def sb(self, name, shape, dtype=F32, persist=False):
        cm = self.nc.sbuf_tensor(name, list(shape), dtype)
        tgt = self.es if (persist or not self.scopes) else self.scopes[-1]
        return tgt.enter_context(cm)

    def ps(self, name, shape, dtype=F32):
        return self.es.enter_context(self.nc.psum_tensor(name, list(shape), dtype))

    def push_scope(self):
        s = ExitStack()
        self.scopes.append(s)
        return s

    def pop_scope(self):
        self.barrier()
        self.scopes.pop().close()

    def _wait(self, eng, sem, val, ph=None):
        if val <= 0:
            return
        if not F_SAME and sem == ESEM.get(eng) and not getattr(self, "_dmawait", False):
            return
        hw = self.waited[eng]
        if hw.get(sem, 0) >= val:
            return
        hw[sem] = val
        h = self.semh[sem]
        if ph is not None:
            ph.val = (self.semh[sem], val)
        self.streams[eng].append(lambda e, h=h, v=val: e.wait_ge(h, v))

    def emit(self, eng, fn, reads=(), writes=(), q=None, sem=None, inc=None):
        need = {}
        for r in reads:
            b = self.buf.get(_nm(r))
            if b and b["w"]:
                s, v, ph = b["w"]
                if s not in need or need[s][0] < v:
                    need[s] = (v, ph)
        for w in writes:
            b = self.buf.get(_nm(w))
            if b:
                if b["w"]:
                    s, v, ph = b["w"]
                    if s not in need or need[s][0] < v:
                        need[s] = (v, ph)
                for s, (v, ph) in b["r"].items():
                    if s not in need or need[s][0] < v:
                        need[s] = (v, ph)
        self._dmawait = bool(q) or sem == "cc"
        for s, (v, ph) in need.items():
            self._wait(eng, s, v, ph)
        self._dmawait = False
        if sem is None:
            if q:
                pool = self.qpool[eng]
                sem = pool[self.qpi[eng] % len(pool)]
                self.qpi[eng] += 1
            else:
                sem = ESEM[eng]
        if inc is None:
            inc = 16 if q else 1
        self.cnt[sem] += inc
        val = self.cnt[sem]
        h = self.semh[sem]
        self.streams[eng].append(lambda e, fn=fn, h=h, inc=inc: fn(e).then_inc(h, inc))
        ph = None
        if not q:
            ph = PH()
            self.streams[eng].append(ph)
        else:
            pend = self.qpend[eng]
            pend.append((sem, val))
            lim = (len(self.qpool[eng]) - 2) if SIMMODE else 32
            if len(pend) > lim:
                s_, v_ = pend.pop(0)
                h_ = self.semh[s_]
                hw = self.waited[eng]
                if hw.get(s_, 0) < v_:
                    hw[s_] = v_
                    self.streams[eng].append(lambda e, h=h_, v=v_: e.wait_ge(h, v))
        edge = (sem, val, ph)
        for w in writes:
            self.buf[_nm(w)] = {"w": edge, "r": {}}
        for r in reads:
            b = self.buf.setdefault(_nm(r), {"w": None, "r": {}})
            pv = b["r"].get(sem)
            if pv is None or pv[0] < val:
                b["r"][sem] = (val, ph)
        return val

    def dma(self, eng, out, in_, extra_reads=()):
        self.emit(eng, lambda e, o=out, i=in_: e.dma_start(out=o, in_=i),
                  reads=[in_] + list(extra_reads), writes=[out], q=QSEM[eng])

    def barrier(self):
        for eng in ("scalar", "vector", "tensor", "gpsimd"):
            sem = ESEM[eng]
            self.cnt[sem] += 1
            h = self.semh[sem]
            self.streams[eng].append(lambda e, h=h: e.nop().then_inc(h, 1))
            self.waited[eng][sem] = self.cnt[sem]
        snap = dict(self.cnt)
        for eng in ENGS:
            for s, v in snap.items():
                self._wait(eng, s, v)
        for b in self.buf.values():
            b["r"] = {}
            if b["w"]:
                s, v, ph = b["w"]
                b["w"] = (s, v, None)

    def finalize(self):
        blk = self.es.enter_context(self.nc.Block())
        for eng in ENGS:
            ops = []
            for f in self.streams[eng]:
                if isinstance(f, PH):
                    if F_PH and f.val is not None:
                        ops.append(lambda e, h=f.val[0], v=f.val[1]: e.wait_ge(h, v))
                else:
                    ops.append(f)
            getattr(blk, eng)(lambda e, st=ops: [f(e) for f in st])
        self.es.close()



def _host_consts():
    c = {}
    i = np.arange(0, HD, 2, dtype=np.float64)
    inv_freq = 1.0 / (1000000.0 ** (i / HD))
    pos = np.arange(T, dtype=np.float64)
    freqs = pos[:, None] * inv_freq[None, :]
    emb = np.concatenate([freqs, freqs], axis=-1)
    cos = np.cos(emb).astype(np.float32)
    sinA = np.sin(emb).astype(np.float32)
    sinA[:, : HD // 2] *= -1.0
    c["cos_ctx"] = cos
    c["sinA_ctx"] = sinA
    c["ident"] = np.eye(128, dtype=np.float32)
    c["onesr"] = np.ones((1, 512), dtype=np.float32)
    jj = np.arange(128)[:, None]
    ii = np.arange(256)[None, :]
    c["ssdA"] = np.where(ii >= jj, 0.0, NEG).astype(np.float32)
    c["ssdB"] = np.where(ii >= 128 + jj, 0.0, NEG).astype(np.float32)
    e64 = np.zeros((128, 2), dtype=np.float32)
    e64[:64, 0] = 1.0
    e64[64:, 1] = 1.0
    c["E64"] = e64
    c["sel2"] = np.ascontiguousarray(e64.T)
    for hi in (0, 1):
        kk = np.arange(128)[:, None]
        qq = np.arange(512)[None, :]
        m = np.zeros((2, 16, 128, 512), dtype=np.float32)
        for cch in range(2):
            for j in range(16):
                m[cch, j] = ((hi * 1024 + cch * 512 + qq) >= (j * 128 + kk)).astype(np.float32)
        c[f"masks_{hi}"] = m.astype(ml_dtypes.bfloat16)
        c[f"hz_{hi}"] = np.full((128, 1), float(hi), dtype=np.float32)
        sl = slice(hi * 1024, (hi + 1) * 1024)
        c[f"cos_own_{hi}"] = cos[sl].copy()
        c[f"sinA_own_{hi}"] = sinA[sl].copy()
    return c


def _prep_inputs(inp):
    g = {}
    t = lambda a: np.ascontiguousarray(np.asarray(a).T)
    asc = np.ascontiguousarray
    rep = lambda a, n=128: asc(np.broadcast_to(np.asarray(a)[None, :], (n, len(a))))
    g["wqT"] = t(inp["L0_attn_wq"])
    g["wkT"] = t(inp["L0_attn_wk"])
    g["wvT"] = t(inp["L0_attn_wv"])
    g["woT"] = t(inp["L0_attn_wo"])
    g["wq_nat"] = asc(np.asarray(inp["L0_attn_wq"]))
    g["wk_nat"] = asc(np.asarray(inp["L0_attn_wk"]))
    g["wv_nat"] = asc(np.asarray(inp["L0_attn_wv"]))
    g["wfcT"] = t(inp["L0_ffn_W_fc"])
    g["wprojT"] = t(inp["L0_ffn_W_proj"])
    g["inprojT"] = t(inp["L1_attn_in_proj_w"])
    g["outprojT"] = t(inp["L1_attn_out_proj_w"])
    g["conv_w"] = asc(np.asarray(inp["L1_attn_conv_w"]))
    g["conv_b"] = asc(np.asarray(inp["L1_attn_conv_b"]).reshape(CONV_DIM, 1))
    g["rms_w_rep"] = rep(np.asarray(inp["L1_attn_rms_weight"]))
    g["dtb_rep"] = rep(np.asarray(inp["L1_attn_dt_bias"]))
    g["alog_rep"] = rep(np.asarray(inp["L1_attn_A_log"]))
    g["dd_rep"] = rep(np.asarray(inp["L1_attn_D"]))
    g["normw_rep"] = rep(np.asarray(inp["L1_attn_norm_w"]))
    g["w0T"] = t(inp["L1_ffn_W0"])
    g["w1T"] = t(inp["L1_ffn_W1"])
    g["bias0"] = asc(np.asarray(inp["L1_ffn_bias0"]).reshape(FFN_H, 1))
    g["bias1_rep"] = rep(np.asarray(inp["L1_ffn_bias1"]))
    return g



def build_program():
    nc = bass.Bass("TRN2", target_bir_lowering=False, debug=False)
    dram = {}

    def din(name, shape, dtype=F32):
        dram[name] = nc.dram_tensor(name, list(shape), dtype, kind="ExternalInput").ap()
        return dram[name]

    x_own = din("x_own", (NTOK, D))
    x_ctx = din("x_ctx", (T, D))
    for n, s in (("wqT", (D, D)), ("wkT", (D, DKV)), ("wvT", (D, DKV)), ("woT", (D, D)),
                 ("wq_nat", (D, D)), ("wk_nat", (DKV, D)), ("wv_nat", (DKV, D)),
                 ("wfcT", (D, FFN_H)), ("wprojT", (FFN_H, D)),
                 ("inprojT", (D, EIN)), ("outprojT", (DIN, D)),
                 ("conv_w", (CONV_DIM, 4)), ("conv_b", (CONV_DIM, 1)),
                 ("rms_w_rep", (128, D)), ("dtb_rep", (128, NH)), ("alog_rep", (128, NH)),
                 ("dd_rep", (128, NH)), ("normw_rep", (128, DIN)),
                 ("w0T", (D, FFN_H)), ("w1T", (FFN_H, D)),
                 ("bias0", (FFN_H, 1)), ("bias1_rep", (128, D)),
                 ("cos_ctx", (T, HD)), ("sinA_ctx", (T, HD)),
                 ("cos_own", (NTOK, HD)), ("sinA_own", (NTOK, HD)),
                 ("ident", (128, 128)), ("onesr", (1, 512)),
                 ("E64", (128, 2)), ("sel2", (2, 128)),
                 ("ssdA", (128, 256)), ("ssdB", (128, 256)), ("hz", (128, 1))):
        din(n, s)
    din("masks", (2, 16, 128, 512), BF16)

    y_out = nc.dram_tensor("y_out", [NTOK, D], F32, kind="ExternalOutput").ap()
    xr1 = nc.dram_tensor("xr1", [NTOK, D], F32).ap()
    xr2 = nc.dram_tensor("xr2", [NTOK, D], F32).ap()
    xr3 = nc.dram_tensor("xr3", [NTOK, D], F32).ap()
    zspill = nc.dram_tensor("zspill", [NTOK, DIN], BF16).ap()
    w1fc_s = nc.dram_tensor("w1fc_s", [64, 16, 128, 128], BF16).ap()
    w1pr_s = nc.dram_tensor("w1pr_s", [16, 64, 128, 128], BF16).ap()
    ygsp = nc.dram_tensor("ygsp", [8, 32, 128, 128], BF16).ap()
    cch_in = nc.dram_tensor("cch_in", [33, 128, 3], F32).ap()
    cch_out = nc.dram_tensor("cch_out", [2, 33, 128, 3], F32).ap()
    cst_in = nc.dram_tensor("cst_in", [64, DIN], F32).ap()
    cst_out = nc.dram_tensor("cst_out", [2, 64, DIN], F32).ap()

    p = P(nc)
    E = p.emit

    identf = p.sb("identf", (128, 128), persist=True)
    identb = p.sb("identb", (128, 128), BF16, persist=True)
    onesr = p.sb("onesr_s", (1, 512), persist=True)
    onesc = p.sb("onesc_s", (128, 1), BF16, persist=True)
    e64 = p.sb("e64_s", (128, 2), BF16, persist=True)
    sel2 = p.sb("sel2_s", (2, 128), BF16, persist=True)
    hzc = p.sb("hz_s", (128, 1), persist=True)
    stats = p.sb("stats", (128, 96), persist=True)
    mv = p.sb("mv", (128, 2), persist=True)
    cols = p.sb("cols", (128, 24), persist=True)
    m3 = p.sb("m3", (128, 16), persist=True)

    psf = [p.ps(f"psf{i}", [128, 512], F32) for i in range(6)]
    psb = [p.ps(f"psb{i}", [128, 1024], BF16) for i in range(2)]
    _tb = [0]

    p.dma("sync", identf[:], dram["ident"][:, :])
    p.dma("sync", onesr[:], dram["onesr"][:, :])
    p.dma("gpsimd", e64[:], dram["E64"][:, :])
    p.dma("gpsimd", sel2[:], dram["sel2"][:, :])
    p.dma("sync", hzc[:], dram["hz"][:, :])
    p.dma("gpsimd", identb[:], dram["ident"][:, :])
    E("gpsimd", lambda e: e.memset(onesc[:], 1.0), reads=[], writes=[onesc])

    nmc, stc, smx = cols[:, 0:1], cols[:, 1:2], cols[:, 2:3]
    smx4 = cols[:, 4:8]

    def trans8(dst_wide_ap, src_aps, reads_extra=()):
        n = len(src_aps)
        assert n <= 8
        bank = psb[_tb[0] % 2]
        _tb[0] += 1
        for k, sap in enumerate(src_aps):
            E("tensor", lambda e, b=bank, s=sap, k=k: e.transpose(
                out=b[:, k * 128:(k + 1) * 128], in_=s, identity=identb[:]),
              reads=[sap, identb] + list(reads_extra), writes=[bank])
        E("vector", lambda e, b=bank, d=dst_wide_ap, n=n: e.tensor_copy(
            out=d, in_=b[:, 0:n * 128]),
          reads=[bank], writes=[dst_wide_ap])

    def trans(dst_ap, src_ap, pdim=128, fdim=128, f32=False):
        if not f32:
            bank = psb[_tb[0] % 2]
            _tb[0] += 1
            ident = identb
        else:
            bank = psf[5]
            ident = identf
        E("tensor", lambda e, b=bank, s=src_ap, idn=ident, pd=pdim, fd=fdim: e.transpose(
            out=b[0:fd, 0:pd], in_=s, identity=idn[0:pd, 0:pd]),
          reads=[src_ap, ident], writes=[bank])
        E("vector", lambda e, b=bank, d=dst_ap, pd=pdim, fd=fdim: e.tensor_copy(
            out=d, in_=b[0:fd, 0:pd]),
          reads=[bank], writes=[dst_ap])

    def divcol(out_ap, in0_ap, col_ap, rd, wr):
        rcp = cols[:, 23:24]
        E("vector", lambda e, c=col_ap: e.reciprocal(out=rcp, in_=c),
          reads=list(rd), writes=[cols])
        E("vector", lambda e, o=out_ap, i=in0_ap: e.tensor_scalar(
            out=o, in0=i, scalar1=rcp, scalar2=None, op0=ALU.mult),
          reads=list(rd) + [cols], writes=list(wr))

    def adddiv(out_ap, in0_ap, nm_col, std_col, rd, wr):
        E("vector", lambda e, o=out_ap, i=in0_ap, nm=nm_col: e.tensor_scalar(
            out=o, in0=i, scalar1=nm, scalar2=None, op0=ALU.add),
          reads=list(rd) + [cols], writes=list(wr))
        divcol(out_ap, out_ap, std_col, list(wr) + [cols], list(wr))

    def ln_stats(src_ap, width, extra_reads=()):
        nchk = (width + 511) // 512
        for ci in range(nchk):
            w = min(512, width - ci * 512)
            E("vector", lambda e, s=src_ap, ci=ci, w=w: e.bn_stats(
                out=stats[:, ci * 6:(ci + 1) * 6], in_=s[:, ci * 512: ci * 512 + w]),
              reads=[src_ap] + list(extra_reads), writes=[stats])
        E("vector", lambda e, n=nchk: e.bn_aggr(out=mv[:], in_=stats[:, 0:6 * n]),
          reads=[stats], writes=[mv])

    def ln_cols(nm_col, std_col, var_eps=1e-5):
        E("vector", lambda e, o=nm_col: e.tensor_scalar(
            out=o, in0=mv[:, 0:1], scalar1=-1.0, scalar2=None, op0=ALU.mult),
          reads=[mv], writes=[cols])
        E("vector", lambda e, o=std_col, ve=var_eps: e.tensor_scalar(
            out=o, in0=mv[:, 1:2], scalar1=ve, scalar2=None, op0=ALU.add),
          reads=[mv], writes=[cols])
        E("scalar", lambda e, o=std_col: e.activation(out=o, in_=o, func=AF.Sqrt),
          reads=[cols], writes=[cols])

    def rms_std_col(std_col, eps):
        E("vector", lambda e, o=std_col: e.tensor_tensor(
            out=o, in0=mv[:, 0:1], in1=mv[:, 0:1], op=ALU.mult),
          reads=[mv], writes=[cols])
        E("vector", lambda e, o=std_col: e.tensor_tensor(
            out=o, in0=o, in1=mv[:, 1:2], op=ALU.add),
          reads=[mv, cols], writes=[cols])
        E("vector", lambda e, o=std_col, ve=eps: e.tensor_scalar(
            out=o, in0=o, scalar1=ve, scalar2=None, op0=ALU.add),
          reads=[cols], writes=[cols])
        E("scalar", lambda e, o=std_col: e.activation(out=o, in_=o, func=AF.Sqrt),
          reads=[cols], writes=[cols])

    def quant_act(dst_bf, src_f32, tmp_f32, tmp2_f32, nheads, smax, extra=()):
        hd = None
        if nheads > 1:
            E("vector", lambda e, s=src_f32, n=nheads, o=smax: e.reduce_max(
                out=o, in_=s.rearrange("p (h d) -> p h d", h=n),
                axis=AX.X, apply_absolute_value=True),
              reads=[src_f32] + list(extra), writes=[cols])
        else:
            E("vector", lambda e, s=src_f32, o=smax: e.reduce_max(
                out=o, in_=s, axis=AX.X, apply_absolute_value=True),
              reads=[src_f32] + list(extra), writes=[cols])
        E("vector", lambda e, o=smax: e.tensor_scalar(
            out=o, in0=o, scalar1=1.0 / 127.0, scalar2=1e-8,
            op0=ALU.mult, op1=ALU.max),
          reads=[cols], writes=[cols])
        rsm = cols[:, 16:16 + nheads]
        E("vector", lambda e, o=smax, r=rsm: e.reciprocal(out=r, in_=o),
          reads=[cols], writes=[cols])
        w = 2048 // nheads if nheads > 1 else None
        if nheads > 1:
            wd = (dst_bf.tensor.shape[-1] if False else 512) // nheads
        for h in range(nheads):
            if nheads > 1:
                sl = slice(h * 128, (h + 1) * 128)
            else:
                sl = slice(None)
            E("vector", lambda e, s=src_f32, t=tmp_f32, r=rsm, h=h, sl=sl: e.tensor_scalar(
                out=t[:, sl], in0=s[:, sl], scalar1=r[:, h:h + 1] if True else None,
                scalar2=None, op0=ALU.mult),
              reads=[src_f32, cols], writes=[tmp_f32])
        E("vector", lambda e, t=tmp_f32: e.tensor_scalar(
            out=t, in0=t, scalar1=12582912.0, scalar2=12582912.0,
            op0=ALU.add, op1=ALU.subtract),
          reads=[tmp_f32], writes=[tmp_f32])
        E("vector", lambda e, t=tmp_f32: e.tensor_scalar(
            out=t, in0=t, scalar1=-127.0, scalar2=127.0, op0=ALU.max, op1=ALU.min),
          reads=[tmp_f32], writes=[tmp_f32])
        for h in range(nheads):
            if nheads > 1:
                sl = slice(h * 128, (h + 1) * 128)
            else:
                sl = slice(None)
            E("scalar", lambda e, t=tmp_f32, d=dst_bf, o=smax, h=h, sl=sl: e.activation(
                out=d[:, sl], in_=t[:, sl], func=AF.Copy, scale=o[:, h:h + 1]),
              reads=[tmp_f32, cols], writes=[dst_bf])

    def rope(dst_f32, src_f32, costile, sintile, nheads, tmp):
        for h in range(nheads):
            b = h * 128
            for (o0, o1) in ((0, 64), (64, 0)):
                E("vector", lambda e, s=src_f32, t=tmp, b=b, o0=o0, o1=o1, st=sintile:
                  e.tensor_tensor(out=t[:, b + o0:b + o0 + 64],
                                  in0=s[:, b + o1:b + o1 + 64],
                                  in1=st[:, o0:o0 + 64], op=ALU.mult),
                  reads=[src_f32, sintile], writes=[tmp])
            E("vector", lambda e, s=src_f32, d=dst_f32, b=b, ct=costile:
              e.tensor_tensor(out=d[:, b:b + 128], in0=s[:, b:b + 128],
                              in1=ct[:], op=ALU.mult),
              reads=[src_f32, costile], writes=[dst_f32])
        E("vector", lambda e, d=dst_f32, t=tmp: e.tensor_tensor(
            out=d, in0=d, in1=t, op=ALU.add),
          reads=[dst_f32, tmp], writes=[dst_f32])

    def sandwich(otile_ap, xin_dram, xout_dram, tt, xtmp, xnew, extra_reads=()):
        ln_stats(otile_ap, D, extra_reads=extra_reads)
        ln_cols(nmc, stc)
        p.dma("sync", xtmp, xin_dram[tt * 128:(tt + 1) * 128, :])
        adddiv(xnew, otile_ap, nmc, stc, [otile_ap] + list(extra_reads), [xnew])
        E("vector", lambda e, x=xnew, xt=xtmp: e.tensor_tensor(
            out=x, in0=x, in1=xt, op=ALU.add),
          reads=[xnew, xtmp], writes=[xnew])
        p.dma("gpsimd", xout_dram[tt * 128:(tt + 1) * 128, :], xnew)

    def l0_attn():
        out_dram = y_out if UPTO == 1 else xr1
        p.push_scope()
        qhatT = p.sb("qhatT", (128, H * 1024), BF16)
        wq_sr = p.sb("wq_sr", (1, D))
        wk_sr = p.sb("wk_sr", (1, DKV))
        wv_sr = p.sb("wv_sr", (1, DKV))
        wq_rr = p.sb("wq_rr", (1, D))
        wk_rr = p.sb("wk_rr", (1, DKV))
        wv_rr = p.sb("wv_rr", (1, DKV))

        def quant_w_tile(raw_ap, dst_ap, sc_bank, rc_bank, tmpf, tmpb):
            E("vector", lambda e, r=raw_ap, b=rc_bank, t=tmpf: e.tensor_tensor(
                out=t, in0=r, in1=b[:, 0:512], op=ALU.mult),
              reads=[raw_ap, rc_bank], writes=[tmpf])
            E("vector", lambda e, t=tmpf: e.tensor_scalar(
                out=t, in0=t, scalar1=12582912.0, scalar2=12582912.0,
                op0=ALU.add, op1=ALU.subtract),
              reads=[tmpf], writes=[tmpf])
            E("vector", lambda e, d=dst_ap, b=sc_bank, t=tmpf: e.tensor_tensor(
                out=d, in0=t, in1=b[:, 0:512], op=ALU.mult),
              reads=[tmpf, sc_bank], writes=[dst_ap])

        p.push_scope()
        wnat = p.sb("wnat", (128, 2 * D))
        colm = p.sb("colm", (128, 1))
        for nm_, rows, srow, rrow in (("wq_nat", D, wq_sr, wq_rr),
                                      ("wk_nat", DKV, wk_sr, wk_rr),
                                      ("wv_nat", DKV, wv_sr, wv_rr)):
            for ot in range(rows // 128):
                wb = wnat[:, (ot % 2) * D:(ot % 2 + 1) * D]
                p.dma("sync", wb, dram[nm_][ot * 128:(ot + 1) * 128, :])
                E("vector", lambda e, wb=wb: e.reduce_max(
                    out=colm[:], in_=wb, axis=AX.X, apply_absolute_value=True),
                  reads=[wnat], writes=[colm])
                E("vector", lambda e: e.tensor_scalar(
                    out=colm[:], in0=colm[:], scalar1=1.0 / 127.0, scalar2=1e-8,
                    op0=ALU.mult, op1=ALU.max),
                  reads=[colm], writes=[colm])
                E("tensor", lambda e: e.matmul(psf[5][0:1, 0:128], colm[:], identf[:],
                                               start=True, stop=True),
                  reads=[colm, identf], writes=[psf[5]])
                E("vector", lambda e, srow=srow, ot=ot: e.tensor_copy(
                    out=srow[:, ot * 128:(ot + 1) * 128], in_=psf[5][0:1, 0:128]),
                  reads=[psf[5]], writes=[srow])
                E("vector", lambda e, srow=srow, rrow=rrow, ot=ot: e.reciprocal(
                    out=rrow[:, ot * 128:(ot + 1) * 128],
                    in_=srow[:, ot * 128:(ot + 1) * 128]),
                  reads=[srow], writes=[rrow])
        p.pop_scope()
        p.barrier()

        p.push_scope()
        xhatTo = p.sb("xhatTo", (128, 16 * 1024), BF16)
        wqq = p.sb("wqq", (128, 16 * 512), BF16)
        tq1 = p.sb("tq1a", (128, D))
        tq2 = p.sb("tq2a", (128, D))
        xn = p.sb("xna", (128, D))
        xhat = p.sb("xhata", (128, D), BF16)
        wraw = p.sb("wrawa", (128, 1024))
        cosb = p.sb("cosba", (128, HD))
        sinb = p.sb("sinba", (128, HD))
        qhat = p.sb("qhat_b", (128, 512), BF16)
        xtl = p.sb("xtla", (128, 2 * D))
        for tt in range(NT):
            xt = xtl[:, (tt % 2) * D:(tt % 2 + 1) * D]
            p.dma("sync", xt, x_own[tt * 128:(tt + 1) * 128, :])
            ln_stats(xt, D, extra_reads=[xtl])
            ln_cols(nmc, stc)
            adddiv(xn[:], xt, nmc, stc, [xtl], [xn])
            quant_act(xhat[:], xn[:], tq1[:], tq2[:], 1, smx)
            for dh in range(2):
                trans8(xhatTo[:, tt * 2048 + dh * 1024: tt * 2048 + (dh + 1) * 1024],
                       [xhat[:, (dh * 8 + k) * 128:(dh * 8 + k + 1) * 128] for k in range(8)])
        for oc in range(4):
            E("tensor", lambda e, oc=oc: e.matmul(
                psf[4][:, 0:512], onesr[0:1, 0:128], wq_sr[0:1, oc * 512:(oc + 1) * 512],
                start=True, stop=True),
              reads=[onesr, wq_sr], writes=[psf[4]])
            E("tensor", lambda e, oc=oc: e.matmul(
                psf[3][:, 0:512], onesr[0:1, 0:128], wq_rr[0:1, oc * 512:(oc + 1) * 512],
                start=True, stop=True),
              reads=[onesr, wq_rr], writes=[psf[3]])
            for dt in range(16):
                wb = wraw[:, (dt % 2) * 512:(dt % 2 + 1) * 512]
                p.dma("sync", wb,
                      dram["wqT"][dt * 128:(dt + 1) * 128, oc * 512:(oc + 1) * 512])
                quant_w_tile(wb, wqq[:, dt * 512:(dt + 1) * 512], psf[4], psf[3],
                             tq1[:, 0:512], tq2[:, 0:512])
            for tt in range(NT):
                bank = psf[tt % 4]
                for dt in range(16):
                    E("tensor", lambda e, b=bank, dt=dt, tt=tt: e.matmul(
                        b[:, 0:512],
                        xhatTo[:, tt * 2048 + dt * 128: tt * 2048 + (dt + 1) * 128],
                        wqq[:, dt * 512:(dt + 1) * 512],
                        start=(dt == 0), stop=(dt == 15)),
                      reads=[xhatTo, wqq], writes=[bank])
                p.dma("sync", cosb[:], dram["cos_own"][tt * 128:(tt + 1) * 128, :])
                p.dma("sync", sinb[:], dram["sinA_own"][tt * 128:(tt + 1) * 128, :])
                E("vector", lambda e, b=bank, tq2=tq2: e.tensor_copy(
                    out=tq2[:, 1024:1536], in_=b[:, 0:512]),
                  reads=[bank], writes=[tq2])
                rope(tq2[:, 1536:2048], tq2[:, 1024:1536], cosb[:], sinb[:], 4,
                     tq1[:, 1024:1536])
                quant_act(qhat[:], tq2[:, 1536:2048], tq1[:, 0:512], tq2[:, 0:512],
                          4, smx4)
                for hh in range(4):
                    h = oc * 4 + hh
                    trans(qhatT[:, h * 1024 + tt * 128: h * 1024 + (tt + 1) * 128],
                          qhat[:, hh * 128:(hh + 1) * 128])
        p.pop_scope()
        p.barrier()

        p.push_scope()
        khatT = p.sb("khatT", (128, HKV * T), BF16)
        vhato = p.sb("vhato", (128, 16 * HKV * 132), BF16)
        p.push_scope()
        wkq = p.sb("wkq", (128, 16 * 512), BF16)
        wvq = p.sb("wvq", (128, 16 * 512), BF16)
        xhatTc = p.sb("xhatTc", (128, 16 * 128), BF16)
        tq1 = p.sb("tq1b", (128, D))
        tq2 = p.sb("tq2b", (128, D))
        xn = p.sb("xnb", (128, D))
        xhat = p.sb("xhatb", (128, D), BF16)
        wraw = p.sb("wrawb", (128, 1024))
        cosb = p.sb("cosbb", (128, HD))
        sinb = p.sb("sinbb", (128, HD))
        khat = p.sb("khat_b", (128, 512), BF16)
        vhat = p.sb("vhat_b", (128, 512), BF16)
        xctx = p.sb("xctx", (128, 2 * D))
        for nm_, srow, rrow, dstq in (("wkT", wk_sr, wk_rr, wkq),
                                      ("wvT", wv_sr, wv_rr, wvq)):
            E("tensor", lambda e, srow=srow: e.matmul(
                psf[4][:, 0:512], onesr[0:1, 0:128], srow[0:1, 0:512],
                start=True, stop=True),
              reads=[onesr, srow], writes=[psf[4]])
            E("tensor", lambda e, rrow=rrow: e.matmul(
                psf[3][:, 0:512], onesr[0:1, 0:128], rrow[0:1, 0:512],
                start=True, stop=True),
              reads=[onesr, rrow], writes=[psf[3]])
            for dt in range(16):
                wb = wraw[:, (dt % 2) * 512:(dt % 2 + 1) * 512]
                p.dma("sync", wb, dram[nm_][dt * 128:(dt + 1) * 128, :])
                quant_w_tile(wb, dstq[:, dt * 512:(dt + 1) * 512], psf[4], psf[3],
                             tq1[:, 0:512], tq2[:, 0:512])
        E("gpsimd", lambda e: e.memset(vhato[:], 1.0), reads=[], writes=[vhato])
        for ct in range(16):
            xc = xctx[:, (ct % 2) * D:(ct % 2 + 1) * D]
            p.dma("sync", xc, x_ctx[ct * 128:(ct + 1) * 128, :])
            ln_stats(xc, D, extra_reads=[xctx])
            ln_cols(nmc, stc)
            adddiv(xn[:], xc, nmc, stc, [xctx], [xn])
            quant_act(xhat[:], xn[:], tq1[:], tq2[:], 1, smx)
            for dh in range(2):
                trans8(xhatTc[:, dh * 1024:(dh + 1) * 1024],
                       [xhat[:, (dh * 8 + k) * 128:(dh * 8 + k + 1) * 128] for k in range(8)])
            for dt in range(16):
                E("tensor", lambda e, dt=dt: e.matmul(
                    psf[0][:, 0:512], xhatTc[:, dt * 128:(dt + 1) * 128],
                    wkq[:, dt * 512:(dt + 1) * 512], start=(dt == 0), stop=(dt == 15)),
                  reads=[xhatTc, wkq], writes=[psf[0]])
            for dt in range(16):
                E("tensor", lambda e, dt=dt: e.matmul(
                    psf[1][:, 0:512], xhatTc[:, dt * 128:(dt + 1) * 128],
                    wvq[:, dt * 512:(dt + 1) * 512], start=(dt == 0), stop=(dt == 15)),
                  reads=[xhatTc, wvq], writes=[psf[1]])
            p.dma("sync", cosb[:], dram["cos_ctx"][ct * 128:(ct + 1) * 128, :])
            p.dma("sync", sinb[:], dram["sinA_ctx"][ct * 128:(ct + 1) * 128, :])
            E("vector", lambda e: e.tensor_copy(out=tq2[:, 1024:1536], in_=psf[0][:, 0:512]),
              reads=[psf[0]], writes=[tq2])
            rope(tq2[:, 1536:2048], tq2[:, 1024:1536], cosb[:], sinb[:], 4,
                 tq1[:, 1024:1536])
            quant_act(khat[:], tq2[:, 1536:2048], tq1[:, 0:512], tq2[:, 0:512], 4, smx4)
            E("vector", lambda e: e.tensor_copy(out=tq2[:, 1024:1536], in_=psf[1][:, 0:512]),
              reads=[psf[1]], writes=[tq2])
            quant_act(vhat[:], tq2[:, 1024:1536], tq1[:, 0:512], tq2[:, 0:512], 4,
                      cols[:, 8:12])
            for kv in range(HKV):
                trans(khatT[:, kv * T + ct * 128: kv * T + (ct + 1) * 128],
                      khat[:, kv * 128:(kv + 1) * 128])
                base = (ct * HKV + kv) * 132
                E("vector", lambda e, b=base, kv=kv: e.tensor_copy(
                    out=vhato[:, b:b + 128], in_=vhat[:, kv * 128:(kv + 1) * 128]),
                  reads=[vhat], writes=[vhato])
        p.pop_scope()
        p.barrier()

        p.push_scope()
        expSA = p.sb("expSA", (128, 8 * 512), BF16)
        expSB = p.sb("expSB", (128, 8 * 512), BF16)
        maskb = p.sb("maskb", (128, 16 * 512), BF16)
        atm = [p.sb(f"atm{i}", (128, D), BF16) for i in range(4)]
        wob = p.sb("wob", (128, 16 * 512), BF16)
        otile = [p.sb(f"otile_a{i}", (128, D)) for i in range(4)]
        avp = p.sb("avp", (128, 132))
        xtmp = p.sb("xtmp_a", (128, D))
        xnew = p.sb("xnew_a", (128, D))
        for c in range(2):
            p.dma("scalar", maskb[:].rearrange("p (j q) -> p j q", j=16),
                  dram["masks"][c, :, :, :].rearrange("j pp q -> pp j q"))
            njt = 4 * c + 12
            for h in range(H):
                kv = h // 4
                for j in range(njt):
                    bank = psf[j % 4]
                    ebuf = expSA if j % 2 == 0 else expSB
                    esl = ebuf[:, (j // 2) * 512:(j // 2 + 1) * 512]
                    E("tensor", lambda e, b=bank, h=h, j=j, kv=kv, c=c: e.matmul(
                        b[:, 0:512],
                        khatT[:, kv * T + j * 128: kv * T + (j + 1) * 128],
                        qhatT[:, h * 1024 + c * 512: h * 1024 + (c + 1) * 512],
                        start=True, stop=True),
                      reads=[khatT, qhatT], writes=[bank])
                    E("scalar", lambda e, b=bank, o=esl: e.activation(
                        out=o, in_=b[:, 0:512], func=AF.Exp, scale=ASCALE),
                      reads=[bank], writes=[ebuf])
                    E("vector", lambda e, o=esl, j=j: e.tensor_tensor(
                        out=o, in0=o, in1=maskb[:, j * 512:(j + 1) * 512], op=ALU.mult),
                      reads=[ebuf, maskb], writes=[ebuf])
                for il in range(4):
                    i = c * 4 + il
                    bank = psf[4 + (il % 2)]
                    for j in range(i + 9):
                        ebuf = expSA if j % 2 == 0 else expSB
                        E("tensor", lambda e, b=bank, eb=ebuf, j=j, il=il, kv=kv, i=i: e.matmul(
                            b[:, 0:132],
                            eb[:, (j // 2) * 512 + il * 128: (j // 2) * 512 + (il + 1) * 128],
                            vhato[:, (j * HKV + kv) * 132:(j * HKV + kv) * 132 + 132],
                            start=(j == 0), stop=(j == i + 8)),
                          reads=[ebuf, vhato], writes=[bank])
                    E("vector", lambda e, b=bank: e.tensor_copy(out=avp[:], in_=b[:, 0:132]),
                      reads=[bank], writes=[avp])
                    divcol(atm[il][:, h * 128:(h + 1) * 128], avp[:, 0:128],
                           avp[:, 128:129], [avp], [atm[il]])
            for il in range(4):
                for dh in range(2):
                    trans8(atm[il][:, dh * 1024:(dh + 1) * 1024],
                           [atm[il][:, (dh * 8 + k) * 128:(dh * 8 + k + 1) * 128]
                            for k in range(8)])
            for dc in range(4):
                p.dma("gpsimd", wob[:].rearrange("p (dt q) -> p dt q", dt=16),
                      dram["woT"][:, dc * 512:(dc + 1) * 512].rearrange(
                          "(dt pp) q -> pp dt q", pp=128))
                for il in range(4):
                    bank = psf[il % 4]
                    for ft in range(16):
                        E("tensor", lambda e, b=bank, ft=ft, il=il: e.matmul(
                            b[:, 0:512], atm[il][:, ft * 128:(ft + 1) * 128],
                            wob[:, ft * 512:(ft + 1) * 512],
                            start=(ft == 0), stop=(ft == 15)),
                          reads=[atm[il], wob], writes=[bank])
                    E("vector", lambda e, b=bank, il=il, dc=dc: e.tensor_copy(
                        out=otile[il][:, dc * 512:(dc + 1) * 512], in_=b[:, 0:512]),
                      reads=[bank], writes=[otile[il]])
            for il in range(4):
                tt = c * 4 + il
                sandwich(otile[il][:], x_own, out_dram, tt, xtmp[:], xnew[:])
        p.pop_scope()
        p.pop_scope()
        p.pop_scope()
        p.barrier()

    def l0_ffn():
        out_dram = y_out if UPTO == 2 else xr2
        xin = xr1 if UPTO >= 1 else x_own
        p.push_scope()
        p.push_scope()
        wbt = p.sb("f_wbt", (128, FFN_H), BF16)
        absb = p.sb("f_absb", (128, FFN_H), BF16)
        sgn = p.sb("f_sgn", (128, FFN_H), BF16)
        w1t = p.sb("f_w1t", (128, FFN_H), BF16)
        srow = p.sb("f_srow", (2, FFN_H), BF16)
        hrow = p.sb("f_hrow", (2, FFN_H), BF16)

        def tern_quant(src_dram, n_tiles, width, store_fn):
            nch = width // 512
            for it_ in range(n_tiles):
                p.dma("gpsimd", wbt[:, 0:width], src_dram[it_ * 128:(it_ + 1) * 128, :])
                E("scalar", lambda e, w=width: e.activation(
                    out=absb[:, 0:w], in_=wbt[:, 0:w], func=AF.Abs),
                  reads=[wbt], writes=[absb])
                E("scalar", lambda e, w=width: e.sign(out=sgn[:, 0:w], in_=wbt[:, 0:w]),
                  reads=[wbt], writes=[sgn])
                for ch in range(nch):
                    E("tensor", lambda e, ch=ch: e.matmul(
                        psf[2][0:2, 0:512], e64[:], absb[:, ch * 512:(ch + 1) * 512],
                        start=True, stop=True),
                      reads=[absb, e64], writes=[psf[2]])
                    E("vector", lambda e, ch=ch: e.tensor_scalar(
                        out=srow[:, ch * 512:(ch + 1) * 512],
                        in0=psf[2][0:2, 0:512], scalar1=1.0 / 64.0, scalar2=1e-8,
                        op0=ALU.mult, op1=ALU.max),
                      reads=[psf[2]], writes=[srow])
                    E("vector", lambda e, ch=ch: e.tensor_scalar(
                        out=hrow[:, ch * 512:(ch + 1) * 512],
                        in0=srow[:, ch * 512:(ch + 1) * 512],
                        scalar1=0.5, scalar2=None, op0=ALU.mult),
                      reads=[srow], writes=[hrow])
                    E("tensor", lambda e, ch=ch: e.matmul(
                        psf[3][:, 0:512], sel2[:], srow[:, ch * 512:(ch + 1) * 512],
                        start=True, stop=True),
                      reads=[sel2, srow], writes=[psf[3]])
                    E("tensor", lambda e, ch=ch: e.matmul(
                        psf[4][:, 0:512], sel2[:], hrow[:, ch * 512:(ch + 1) * 512],
                        start=True, stop=True),
                      reads=[sel2, hrow], writes=[psf[4]])
                    E("vector", lambda e, ch=ch: e.tensor_tensor(
                        out=w1t[:, ch * 512:(ch + 1) * 512],
                        in0=absb[:, ch * 512:(ch + 1) * 512],
                        in1=psf[4][:, 0:512], op=ALU.is_gt),
                      reads=[absb, psf[4]], writes=[w1t])
                    E("vector", lambda e, ch=ch: e.tensor_tensor(
                        out=w1t[:, ch * 512:(ch + 1) * 512],
                        in0=w1t[:, ch * 512:(ch + 1) * 512],
                        in1=psf[3][:, 0:512], op=ALU.mult),
                      reads=[w1t, psf[3]], writes=[w1t])
                    E("vector", lambda e, ch=ch: e.tensor_tensor(
                        out=w1t[:, ch * 512:(ch + 1) * 512],
                        in0=w1t[:, ch * 512:(ch + 1) * 512],
                        in1=sgn[:, ch * 512:(ch + 1) * 512], op=ALU.mult),
                      reads=[w1t, sgn], writes=[w1t])
                store_fn(it_)

        def store_fc(dt):
            p.dma("sync", w1fc_s[:, dt, :, :].rearrange("hc pp cc -> pp hc cc"),
                  w1t[:].rearrange("p (hc cc) -> p hc cc", hc=64))

        def store_pr(ht):
            p.dma("sync", w1pr_s[:, ht, :, :].rearrange("dc pp cc -> pp dc cc"),
                  w1t[:, 0:D].rearrange("p (dc cc) -> p dc cc", dc=16))

        tern_quant(dram["wfcT"], 16, FFN_H, store_fc)
        tern_quant(dram["wprojT"], 64, D, store_pr)
        p.pop_scope()
        p.barrier()

        p.push_scope()
        xn2T = p.sb("xn2T", (128, 16 * 1024), BF16)
        p.push_scope()
        xtl = p.sb("f_xtl", (128, 2 * D))
        xnb = p.sb("f_xnb", (128, D), BF16)
        for tt in range(NT):
            xt = xtl[:, (tt % 2) * D:(tt % 2 + 1) * D]
            p.dma("sync", xt, xin[tt * 128:(tt + 1) * 128, :])
            ln_stats(xt, D, extra_reads=[xtl])
            ln_cols(nmc, stc)
            adddiv(xnb[:], xt, nmc, stc, [xtl], [xnb])
            for dt in range(16):
                base2 = (tt // 4) * 8192 + dt * 512 + (tt % 4) * 128
                trans(xn2T[:, base2:base2 + 128], xnb[:, dt * 128:(dt + 1) * 128])
        p.pop_scope()

        hqT = p.sb("hqT", (128, 64 * 512), BF16)
        w1sl = p.sb("w1sl", (128, 4 * 2048), BF16)
        w2sl = p.sb("w2sl", (128, 2 * 8192), BF16)
        rbuf = p.sb("f_rbuf", (128, 4 * 512), BF16)
        hq2 = p.sb("f_hq2", (128, 512), BF16)
        stdr = p.sb("f_stdr", (1, 512))
        outTb = p.sb("f_outTb", (128, 512), BF16)
        otile4 = p.sb("f_otile4", (128, 4 * D), BF16)
        xtmp = p.sb("f_xtmp", (128, D))
        xnew = p.sb("f_xnew", (128, D))
        for tc in range(2):
            for hc in range(64):
                sl = w1sl[:, (hc % 4) * 2048:(hc % 4 + 1) * 2048]
                p.dma("sync", sl.rearrange("p (dt cc) -> p dt cc", dt=16),
                      w1fc_s[hc, :, :, :].rearrange("dt pp cc -> pp dt cc"))
                bank = psf[(0, 1, 4, 5)[hc % 4]]
                for dt in range(16):
                    E("tensor", lambda e, b=bank, sl=sl, dt=dt, tc=tc: e.matmul(
                        b[:, 0:512], sl[:, dt * 128:(dt + 1) * 128],
                        xn2T[:, tc * 8192 + dt * 512: tc * 8192 + (dt + 1) * 512],
                        start=(dt == 0), stop=(dt == 15)),
                      reads=[w1sl, xn2T], writes=[bank])
                rb = rbuf[:, (hc % 4) * 512:(hc % 4 + 1) * 512]
                E("scalar", lambda e, b=bank, rb=rb: e.activation(
                    out=rb, in_=b[:, 0:512], func=AF.Relu),
                  reads=[bank], writes=[rbuf])
                E("vector", lambda e, hc=hc, rb=rb: e.tensor_tensor(
                    out=hqT[:, hc * 512:(hc + 1) * 512], in0=rb, in1=rb,
                    op=ALU.mult),
                  reads=[rbuf], writes=[hqT])
                E("vector", lambda e, hc=hc: e.tensor_tensor(
                    out=hq2[:], in0=hqT[:, hc * 512:(hc + 1) * 512],
                    in1=hqT[:, hc * 512:(hc + 1) * 512], op=ALU.mult),
                  reads=[hqT], writes=[hq2])
                E("tensor", lambda e, hc=hc: e.matmul(
                    psf[2][0:1, 0:512], onesc[:], hq2[:],
                    start=(hc == 0), stop=(hc == 63)),
                  reads=[onesc, hq2], writes=[psf[2]])
            E("vector", lambda e: e.tensor_scalar(
                out=stdr[:], in0=psf[2][0:1, 0:512], scalar1=1.0 / FFN_H,
                scalar2=F32EPS, op0=ALU.mult, op1=ALU.add),
              reads=[psf[2]], writes=[stdr])
            E("scalar", lambda e: e.activation(out=stdr[:], in_=stdr[:], func=AF.Sqrt),
              reads=[stdr], writes=[stdr])
            E("vector", lambda e: e.reciprocal(out=stdr[:], in_=stdr[:]),
              reads=[stdr], writes=[stdr])
            E("tensor", lambda e: e.matmul(
                psf[3][:, 0:512], onesr[0:1, 0:128], stdr[:], start=True, stop=True),
              reads=[onesr, stdr], writes=[psf[3]])
            for hc in range(64):
                E("vector", lambda e, hc=hc: e.tensor_tensor(
                    out=hqT[:, hc * 512:(hc + 1) * 512],
                    in0=hqT[:, hc * 512:(hc + 1) * 512],
                    in1=psf[3][:, 0:512], op=ALU.mult),
                  reads=[hqT, psf[3]], writes=[hqT])
            for dcol in range(16):
                sl = w2sl[:, (dcol % 2) * 8192:(dcol % 2 + 1) * 8192]
                p.dma("sync", sl.rearrange("p (ht cc) -> p ht cc", ht=64),
                      w1pr_s[dcol, :, :, :].rearrange("ht pp cc -> pp ht cc"))
                bank = psf[(0, 1, 4, 5)[dcol % 4]]
                for ht in range(64):
                    E("tensor", lambda e, b=bank, sl=sl, ht=ht: e.matmul(
                        b[:, 0:512], sl[:, ht * 128:(ht + 1) * 128],
                        hqT[:, ht * 512:(ht + 1) * 512],
                        start=(ht == 0), stop=(ht == 63)),
                      reads=[w2sl, hqT], writes=[bank])
                E("vector", lambda e, b=bank: e.tensor_copy(out=outTb[:], in_=b[:, 0:512]),
                  reads=[bank], writes=[outTb])
                for q4 in range(4):
                    trans(otile4[:, q4 * D + dcol * 128: q4 * D + (dcol + 1) * 128],
                          outTb[:, q4 * 128:(q4 + 1) * 128])
            for q4 in range(4):
                tt = tc * 4 + q4
                sandwich(otile4[:, q4 * D:(q4 + 1) * D], xin, out_dram, tt,
                         xtmp[:], xnew[:])
        p.pop_scope()
        p.pop_scope()
        p.barrier()

    def l1_attn():
        out_dram = y_out if UPTO == 3 else xr3
        xin = xr2 if UPTO >= 2 else x_own
        p.push_scope()
        p.push_scope()
        xbcs = p.sb("xbcs", (128, 33 * 1024), BF16)
        dtv = p.sb("dtv", (128, 512))
        dtA = p.sb("dtA", (128, 512))
        ltA = p.sb("ltA", (64, NCH * 256))
        negl = p.sb("negl", (128, 8 * 64))
        wtb = p.sb("wtb", (128, 8 * 64))
        pendA = p.sb("pendA", (128, NCH * 64))
        Btok = p.sb("Btok", (128, 8 * 64), BF16)
        Bt_cm = p.sb("Bt_cm", (64, 1024), BF16)
        Ct_cm = p.sb("Ct_cm", (64, 1024), BF16)
        Sst = p.sb("Sst", (64, 2 * DIN))
        ssdAb = p.sb("ssdAb", (128, 256))
        ssdBb = p.sb("ssdBb", (128, 256))
        ddb = p.sb("ddb", (128, NH))
        aneg = p.sb("aneg", (128, NH))
        dtbb = p.sb("dtbb", (128, NH))
        onesc64 = p.sb("onesc64", (64, 256))
        p.dma("sync", ssdAb[:], dram["ssdA"][:, :])
        p.dma("sync", ssdBb[:], dram["ssdB"][:, :])
        p.dma("sync", ddb[:], dram["dd_rep"][:, :])
        p.dma("sync", dtbb[:], dram["dtb_rep"][:, :])
        p.dma("sync", aneg[:], dram["alog_rep"][:, :])
        E("scalar", lambda e: e.activation(out=aneg[:], in_=aneg[:], func=AF.Exp),
          reads=[aneg], writes=[aneg])
        E("vector", lambda e: e.tensor_scalar(
            out=aneg[:], in0=aneg[:], scalar1=-1.0, scalar2=None, op0=ALU.mult),
          reads=[aneg], writes=[aneg])
        E("gpsimd", lambda e: e.memset(onesc64[:], 1.0), reads=[], writes=[onesc64])
        E("gpsimd", lambda e: e.memset(Sst[:, 0:DIN], 0.0), reads=[], writes=[Sst])

        p.push_scope()
        uT = p.sb("uT", (128, 16 * 1024), BF16)
        p.push_scope()
        xtl = p.sb("m_xtl", (128, D))
        ln3b = p.sb("m_ln3", (128, D))
        ub = p.sb("m_ub", (128, D), BF16)
        rmswb = p.sb("m_rmsw", (128, D))
        p.dma("sync", rmswb[:], dram["rms_w_rep"][:, :])
        for tt in range(NT):
            xt = xtl[:]
            p.dma("sync", xt, xin[tt * 128:(tt + 1) * 128, :])
            ln_stats(xt, D, extra_reads=[xtl])
            ln_cols(nmc, stc)
            E("vector", lambda e, tt=tt: e.tensor_copy(
                out=m3[:, 2 * tt:2 * tt + 1], in_=nmc),
              reads=[cols], writes=[m3])
            E("vector", lambda e, tt=tt: e.tensor_copy(
                out=m3[:, 2 * tt + 1:2 * tt + 2], in_=stc),
              reads=[cols], writes=[m3])
            adddiv(ln3b[:], xt, nmc, stc, [xtl], [ln3b])
            ln_stats(ln3b[:], D)
            rms_std_col(stc, 1e-5)
            divcol(ln3b[:], ln3b[:], stc, [ln3b, cols], [ln3b])
            E("vector", lambda e: e.tensor_tensor(
                out=ub[:], in0=ln3b[:], in1=rmswb[:], op=ALU.mult),
              reads=[ln3b, rmswb], writes=[ub])
            for dh in range(2):
                trans8(uT[:, tt * 2048 + dh * 1024: tt * 2048 + (dh + 1) * 1024],
                       [ub[:, (dh * 8 + k) * 128:(dh * 8 + k + 1) * 128] for k in range(8)])
        p.pop_scope()

        p.push_scope()
        uTr = p.sb("uTr", (128, 16 * 1024), BF16)
        wzs = p.sb("m_wzs", (128, 8192), BF16)
        zb = p.sb("m_zb", (128, 512), BF16)
        h3 = p.sb("m_h3", (128, 3))
        wdts = p.sb("m_wdts", (128, 16 * 64), BF16)
        for tt in range(NT):
            for dt in range(16):
                base2 = (tt // 4) * 8192 + dt * 512 + (tt % 4) * 128
                E("vector", lambda e, b2=base2, tt=tt, dt=dt: e.tensor_copy(
                    out=uTr[:, b2:b2 + 128],
                    in_=uT[:, tt * 2048 + dt * 128: tt * 2048 + (dt + 1) * 128]),
                  reads=[uT], writes=[uTr])
        for zc in range(8):
            sl = wzs[:, 0:8192]
            p.dma("gpsimd", sl.rearrange("p (dt cc) -> p dt cc", dt=16),
                  dram["inprojT"][:, zc * 512:(zc + 1) * 512].rearrange(
                      "(dt pp) cc -> pp dt cc", pp=128))
            for tt in range(NT):
                bank = psf[(0, 1, 4, 5)[tt % 4]]
                for dt in range(16):
                    E("tensor", lambda e, b=bank, sl=sl, dt=dt, tt=tt: e.matmul(
                        b[:, 0:512],
                        uT[:, tt * 2048 + dt * 128: tt * 2048 + (dt + 1) * 128],
                        sl[:, dt * 512:(dt + 1) * 512],
                        start=(dt == 0), stop=(dt == 15)),
                      reads=[uT, wzs], writes=[bank])
                E("vector", lambda e, b=bank: e.tensor_copy(out=zb[:], in_=b[:, 0:512]),
                  reads=[bank], writes=[zb])
                p.dma("gpsimd", zspill[tt * 128:(tt + 1) * 128, zc * 512:(zc + 1) * 512],
                      zb[:])
        for ct in range(33):
            sl = wzs[:, (ct % 4) * 2048:(ct % 4) * 2048 + 2048]
            p.dma("gpsimd", sl.rearrange("p (dt cc) -> p dt cc", dt=16),
                  dram["inprojT"][:, DIN + ct * 128: DIN + (ct + 1) * 128].rearrange(
                      "(dt pp) cc -> pp dt cc", pp=128))
            for tc in range(2):
                bank = psf[2 + tc]
                for dt in range(16):
                    E("tensor", lambda e, b=bank, sl=sl, dt=dt, tc=tc: e.matmul(
                        b[:, 0:512], sl[:, dt * 128:(dt + 1) * 128],
                        uTr[:, tc * 8192 + dt * 512: tc * 8192 + (dt + 1) * 512],
                        start=(dt == 0), stop=(dt == 15)),
                      reads=[uTr, wzs], writes=[bank])
                E("vector", lambda e, b=bank, ct=ct, tc=tc: e.tensor_copy(
                    out=xbcs[:, ct * 1024 + tc * 512: ct * 1024 + (tc + 1) * 512],
                    in_=b[:, 0:512]),
                  reads=[bank], writes=[xbcs])
                if tc == 1:
                    E("vector", lambda e, b=bank: e.tensor_copy(
                        out=h3[:], in_=b[:, 509:512]),
                      reads=[bank], writes=[h3])
                    p.dma("gpsimd", cch_in[ct, :, :], h3[:])
        p.dma("gpsimd", wdts[:].rearrange("p (dt cc) -> p dt cc", dt=16),
              dram["inprojT"][:, 8320:8384].rearrange("(dt pp) cc -> pp dt cc", pp=128))
        for tt in range(NT):
            bank = psf[tt % 2]
            for dt in range(16):
                E("tensor", lambda e, b=bank, dt=dt, tt=tt: e.matmul(
                    b[:, 0:64],
                    uT[:, tt * 2048 + dt * 128: tt * 2048 + (dt + 1) * 128],
                    wdts[:, dt * 64:(dt + 1) * 64],
                    start=(dt == 0), stop=(dt == 15)),
                  reads=[uT, wdts], writes=[bank])
            E("vector", lambda e, b=bank, tt=tt: e.tensor_tensor(
                out=dtv[:, tt * 64:(tt + 1) * 64], in0=b[:, 0:64], in1=dtbb[:],
                op=ALU.add),
              reads=[bank, dtbb], writes=[dtv])
        E("scalar", lambda e: e.activation(out=dtA[:], in_=dtv[:], func=AF.Exp),
          reads=[dtv], writes=[dtA])
        E("scalar", lambda e: e.activation(out=dtv[:], in_=dtA[:], func=AF.Ln, bias=1.0),
          reads=[dtA], writes=[dtv])
        for tt in range(NT):
            E("vector", lambda e, tt=tt: e.tensor_tensor(
                out=dtA[:, tt * 64:(tt + 1) * 64], in0=dtv[:, tt * 64:(tt + 1) * 64],
                in1=aneg[:], op=ALU.mult),
              reads=[dtv, aneg], writes=[dtA])
        p.pop_scope()
        p.pop_scope()
        p.barrier()

        E("gpsimd", lambda e: e.collective_compute(
            "AllGather", ALU.bypass, replica_groups=[[0, 1], [2, 3], [4, 5], [6, 7]],
            ins=[cch_in.opt()], outs=[cch_out.opt()]),
          reads=[cch_in], writes=[cch_out], sem="cc", inc=1)
        p.push_scope()
        pd_ = p.sb("m_pd", (128, 1027))
        acc1 = p.sb("m_acc1", (128, 1024))
        acc2 = p.sb("m_acc2", (128, 1024))
        cwb = p.sb("m_cwb", (128, 4))
        cbb = p.sb("m_cbb", (128, 1))
        h3r = p.sb("m_h3r", (128, 3))
        for ct in range(33):
            p.dma("sync", cwb[:], dram["conv_w"][ct * 128:(ct + 1) * 128, :])
            p.dma("sync", cbb[:], dram["conv_b"][ct * 128:(ct + 1) * 128, :])
            p.dma("sync", h3r[:], cch_out[0, ct, :, :])
            E("vector", lambda e: e.tensor_scalar(
                out=pd_[:, 0:3], in0=h3r[:], scalar1=hzc[:], scalar2=None, op0=ALU.mult),
              reads=[h3r, hzc], writes=[pd_])
            E("vector", lambda e, ct=ct: e.tensor_copy(
                out=pd_[:, 3:1027], in_=xbcs[:, ct * 1024:(ct + 1) * 1024]),
              reads=[xbcs], writes=[pd_])
            E("vector", lambda e: e.tensor_scalar(
                out=acc1[:], in0=pd_[:, 0:1024], scalar1=cwb[:, 0:1], scalar2=None,
                op0=ALU.mult),
              reads=[pd_, cwb], writes=[acc1])
            E("vector", lambda e: e.scalar_tensor_tensor(
                out=acc2[:], in0=pd_[:, 1:1025], scalar=cwb[:, 1:2], in1=acc1[:],
                op0=ALU.mult, op1=ALU.add),
              reads=[pd_, cwb, acc1], writes=[acc2])
            E("vector", lambda e: e.scalar_tensor_tensor(
                out=acc1[:], in0=pd_[:, 2:1026], scalar=cwb[:, 2:3], in1=acc2[:],
                op0=ALU.mult, op1=ALU.add),
              reads=[pd_, cwb, acc2], writes=[acc1])
            E("vector", lambda e: e.scalar_tensor_tensor(
                out=acc2[:], in0=pd_[:, 3:1027], scalar=cwb[:, 3:4], in1=acc1[:],
                op0=ALU.mult, op1=ALU.add),
              reads=[pd_, cwb, acc1], writes=[acc2])
            E("scalar", lambda e, ct=ct: e.activation(
                out=xbcs[:, ct * 1024:(ct + 1) * 1024], in_=acc2[:], func=AF.Silu,
                bias=cbb[:]),
              reads=[acc2, cbb], writes=[xbcs])
        p.dma("gpsimd", Bt_cm[:, :], xbcs[0:64, 32 * 1024:33 * 1024])
        p.dma("gpsimd", Ct_cm[:, :], xbcs[64:128, 32 * 1024:33 * 1024])
        p.pop_scope()
        p.barrier()

        p.push_scope()
        dtt = p.sb("m_dtt", (64, 256))
        ltok = p.sb("m_ltok", (128, 128))
        lrow = p.sb("m_lrow", (1, 64))
        for ic in range(NCH):
            for jt in range(2):
                trans(dtt[:, jt * 128:(jt + 1) * 128],
                      dtA[:, (2 * ic + jt) * 64:(2 * ic + jt + 1) * 64],
                      pdim=128, fdim=64, f32=True)
            E("vector", lambda e, ic=ic: e.tensor_tensor_scan(
                out=ltA[:, ic * 256:(ic + 1) * 256], data0=onesc64[:], data1=dtt[:],
                initial=0.0, op0=ALU.mult, op1=ALU.add),
              reads=[onesc64, dtt], writes=[ltA])
            for jt in range(2):
                trans(ltok[:, jt * 64:(jt + 1) * 64],
                      ltA[:, ic * 256 + jt * 128: ic * 256 + (jt + 1) * 128],
                      pdim=64, fdim=128, f32=True)
            E("vector", lambda e, ic=ic: e.tensor_scalar(
                out=negl[:, ic * 128:(ic + 1) * 128], in0=ltok[:], scalar1=-1.0,
                scalar2=None, op0=ALU.mult),
              reads=[ltok], writes=[negl])
            p.dma("gpsimd", lrow[:], ltok[127:128, 64:128])
            E("tensor", lambda e, ic=ic: e.matmul(
                psf[4][:, 0:64], onesr[0:1, 0:128], lrow[0:1, 0:64],
                start=True, stop=True),
              reads=[onesr, lrow], writes=[psf[4]])
            E("scalar", lambda e, ic=ic: e.activation(
                out=pendA[:, ic * 64:(ic + 1) * 64], in_=psf[4][:, 0:64], func=AF.Exp),
              reads=[psf[4]], writes=[pendA])
            for jt in range(2):
                E("vector", lambda e, ic=ic, jt=jt: e.tensor_tensor(
                    out=wtb[:, (2 * ic + jt) * 64:(2 * ic + jt + 1) * 64],
                    in0=psf[4][:, 0:64], in1=ltok[:, jt * 64:(jt + 1) * 64],
                    op=ALU.subtract),
                  reads=[psf[4], ltok], writes=[wtb])
            E("scalar", lambda e, ic=ic: e.activation(
                out=wtb[:, 2 * ic * 64:(2 * ic + 2) * 64],
                in_=wtb[:, 2 * ic * 64:(2 * ic + 2) * 64], func=AF.Exp),
              reads=[wtb], writes=[wtb])
            E("vector", lambda e, ic=ic: e.tensor_tensor(
                out=wtb[:, 2 * ic * 64:(2 * ic + 2) * 64],
                in0=wtb[:, 2 * ic * 64:(2 * ic + 2) * 64],
                in1=dtv[:, 2 * ic * 64:(2 * ic + 2) * 64], op=ALU.mult),
              reads=[wtb, dtv], writes=[wtb])
            for jt in range(2):
                trans(Btok[:, (2 * ic + jt) * 64:(2 * ic + jt + 1) * 64],
                      Bt_cm[:, ic * 256 + jt * 128: ic * 256 + (jt + 1) * 128],
                      pdim=64, fdim=128)
        p.pop_scope()
        p.barrier()

        p.push_scope()
        xtk = p.sb("m_xtk", (128, 2 * DIN), BF16)
        bwb = p.sb("m_bwb", (128, 64), BF16)

        def state_chunk(ic, rd, wr):
            for h in range(NH):
                for jt in range(2):
                    E("vector", lambda e, h=h, jt=jt, ic=ic: e.tensor_scalar(
                        out=bwb[:], in0=Btok[:, (2 * ic + jt) * 64:(2 * ic + jt + 1) * 64],
                        scalar1=wtb[:, (2 * ic + jt) * 64 + h:(2 * ic + jt) * 64 + h + 1],
                        scalar2=None, op0=ALU.mult),
                      reads=[Btok, wtb], writes=[bwb])
                    E("tensor", lambda e, h=h, jt=jt: e.matmul(
                        psf[5][0:64, 0:64], bwb[:],
                        xtk[:, jt * DIN + h * 64: jt * DIN + (h + 1) * 64],
                        start=(jt == 0), stop=(jt == 1)),
                      reads=[bwb, xtk], writes=[psf[5]])
                E("vector", lambda e, h=h, ic=ic, rd=rd, wr=wr: e.scalar_tensor_tensor(
                    out=Sst[:, wr * DIN + h * 64: wr * DIN + (h + 1) * 64],
                    in0=Sst[:, rd * DIN + h * 64: rd * DIN + (h + 1) * 64],
                    scalar=pendA[0:64, ic * 64 + h: ic * 64 + h + 1],
                    in1=psf[5][0:64, 0:64], op0=ALU.mult, op1=ALU.add),
                  reads=[Sst, pendA, psf[5]], writes=[Sst])

        def xtok_chunk(ic):
            for jt in range(2):
                for g in range(4):
                    trans8(xtk[:, jt * DIN + g * 1024: jt * DIN + (g + 1) * 1024],
                           [xbcs[:, (g * 8 + k) * 1024 + ic * 256 + jt * 128:
                                 (g * 8 + k) * 1024 + ic * 256 + (jt + 1) * 128]
                            for k in range(8)])

        for ic in range(NCH):
            xtok_chunk(ic)
            state_chunk(ic, ic % 2, (ic + 1) % 2)
        fin = NCH % 2
        p.dma("gpsimd", cst_in[:, :], Sst[:, fin * DIN:(fin + 1) * DIN])
        E("gpsimd", lambda e: e.collective_compute(
            "AllGather", ALU.bypass, replica_groups=[[0, 1], [2, 3], [4, 5], [6, 7]],
            ins=[cst_in.opt()], outs=[cst_out.opt()]),
          reads=[cst_in], writes=[cst_out], sem="cc", inc=1)
        p.dma("sync", Sst[:, 0:DIN], cst_out[0, :, :])
        E("vector", lambda e: e.tensor_scalar(
            out=Sst[:, 0:DIN], in0=Sst[:, 0:DIN], scalar1=hzc[0:64, :], scalar2=None,
            op0=ALU.mult),
          reads=[Sst, hzc], writes=[Sst])
        p.barrier()

        Gb = p.sb("m_Gb", (128, 2 * 256), BF16)
        lrow2 = p.sb("m_lrow2", (1, 256))
        tmpG = p.sb("m_tmpG", (128, 2 * 256))
        Eb = p.sb("m_Eb", (128, 2 * 256))
        ebc = p.sb("m_ebc", (64, 256))
        chat = p.sb("m_chat", (64, 256))
        ybuf = p.sb("m_ybuf", (128, 2 * DIN), BF16)
        zbl = p.sb("m_zbl", (128, DIN), BF16)
        szb = p.sb("m_szb", (128, DIN), BF16)
        ygb = p.sb("m_ygb", (128, DIN), BF16)
        ygT = p.sb("m_ygT", (128, 32 * 128), BF16)
        nwb = p.sb("m_nwb", (128, DIN), BF16)
        p.dma("gpsimd", nwb[:], dram["normw_rep"][:, :])
        for ic in range(NCH):
            rd = ic % 2
            wr = (ic + 1) % 2
            xtok_chunk(ic)
            for h in range(NH):
                for jt in range(2):
                    bank = psf[jt]
                    E("tensor", lambda e, b=bank, ic=ic, jt=jt: e.matmul(
                        b[:, 0:256], Bt_cm[:, ic * 256 + jt * 128: ic * 256 + (jt + 1) * 128],
                        Ct_cm[:, ic * 256:(ic + 1) * 256], start=True, stop=True),
                      reads=[Bt_cm, Ct_cm], writes=[bank])
                    bank2 = psf[2 + jt]
                    if jt == 0:
                        p.dma("gpsimd", lrow2[:], ltA[h:h + 1, ic * 256:(ic + 1) * 256])
                    E("tensor", lambda e, b=bank2, ic=ic, h=h: e.matmul(
                        b[:, 0:256], onesr[0:1, 0:128], lrow2[0:1, 0:256],
                        start=True, stop=True),
                      reads=[onesr, lrow2], writes=[bank2])
                    ssd = ssdAb if jt == 0 else ssdBb
                    E("vector", lambda e, b=bank2, jt=jt, ssd=ssd: e.tensor_tensor(
                        out=tmpG[:, jt * 256:(jt + 1) * 256], in0=b[:, 0:256], in1=ssd[:],
                        op=ALU.add),
                      reads=[bank2, ssd], writes=[tmpG])
                    E("scalar", lambda e, jt=jt, ic=ic, h=h: e.activation(
                        out=Eb[:, jt * 256:(jt + 1) * 256],
                        in_=tmpG[:, jt * 256:(jt + 1) * 256], func=AF.Exp,
                        bias=negl[:, ic * 128 + jt * 64 + h: ic * 128 + jt * 64 + h + 1]),
                      reads=[tmpG, negl], writes=[Eb])
                    E("vector", lambda e, b=bank, jt=jt, ic=ic, h=h: e.scalar_tensor_tensor(
                        out=Gb[:, jt * 256:(jt + 1) * 256], in0=b[:, 0:256],
                        scalar=dtv[:, (2 * ic + jt) * 64 + h:(2 * ic + jt) * 64 + h + 1],
                        in1=Eb[:, jt * 256:(jt + 1) * 256], op0=ALU.mult, op1=ALU.mult),
                      reads=[bank, dtv, Eb], writes=[Gb])
                E("scalar", lambda e: e.activation(
                    out=ebc[:], in_=psf[2][0:64, 0:256], func=AF.Exp),
                  reads=[psf[2]], writes=[ebc])
                E("vector", lambda e, ic=ic: e.tensor_tensor(
                    out=chat[:], in0=Ct_cm[:, ic * 256:(ic + 1) * 256], in1=ebc[:],
                    op=ALU.mult),
                  reads=[Ct_cm, ebc], writes=[chat])
                for it in range(2):
                    bank = psf[4]
                    for jt in range(it + 1):
                        E("tensor", lambda e, b=bank, jt=jt, it=it, h=h: e.matmul(
                            b[:, 0:64], Gb[:, jt * 256 + it * 128: jt * 256 + (it + 1) * 128],
                            xtk[:, jt * DIN + h * 64: jt * DIN + (h + 1) * 64],
                            start=(jt == 0), stop=False),
                          reads=[Gb, xtk], writes=[bank])
                    E("tensor", lambda e, b=bank, it=it, h=h, rd=rd: e.matmul(
                        b[:, 0:64], chat[0:64, it * 128:(it + 1) * 128],
                        Sst[:, rd * DIN + h * 64: rd * DIN + (h + 1) * 64],
                        start=False, stop=True),
                      reads=[chat, Sst], writes=[bank])
                    E("vector", lambda e, b=bank, it=it, h=h: e.scalar_tensor_tensor(
                        out=ybuf[:, it * DIN + h * 64: it * DIN + (h + 1) * 64],
                        in0=xtk[:, it * DIN + h * 64: it * DIN + (h + 1) * 64],
                        scalar=ddb[:, h:h + 1], in1=b[:, 0:64],
                        op0=ALU.mult, op1=ALU.add),
                      reads=[xtk, ddb, bank], writes=[ybuf])
            state_chunk(ic, rd, wr)
            for it in range(2):
                tile = 2 * ic + it
                p.dma("sync", zbl[:], zspill[tile * 128:(tile + 1) * 128, :])
                E("scalar", lambda e: e.activation(out=szb[:], in_=zbl[:], func=AF.Silu),
                  reads=[zbl], writes=[szb])
                E("vector", lambda e, it=it: e.tensor_tensor(
                    out=ygb[:], in0=ybuf[:, it * DIN:(it + 1) * DIN], in1=szb[:],
                    op=ALU.mult),
                  reads=[ybuf, szb], writes=[ygb])
                ln_stats(ygb[:], DIN)
                rms_std_col(stc, 1e-5)
                divcol(ygb[:], ygb[:], stc, [ygb, cols], [ygb])
                E("vector", lambda e: e.tensor_tensor(
                    out=ygb[:], in0=ygb[:], in1=nwb[:], op=ALU.mult),
                  reads=[ygb, nwb], writes=[ygb])
                for g in range(4):
                    trans8(ygT[:, g * 1024:(g + 1) * 1024],
                           [ygb[:, (g * 8 + k) * 128:(g * 8 + k + 1) * 128]
                            for k in range(8)])
                p.dma("gpsimd", ygsp[tile, :, :, :].rearrange("et pp cc -> pp et cc"),
                      ygT[:].rearrange("p (et cc) -> p et cc", et=32))
        p.pop_scope()
        p.pop_scope()
        p.barrier()

        p.push_scope()
        ygl = p.sb("m_ygl", (128, 2 * 4096), BF16)
        wop = p.sb("m_wop", (128, 32 * 512), BF16)
        otfull = p.sb("m_otfull", (128, 8 * D))
        oti = p.sb("m_oti", (128, D))
        xtmp = p.sb("m_xtmp", (128, D))
        xnew = p.sb("m_xnew", (128, D))
        for dc in range(4):
            p.dma("gpsimd", wop[:].rearrange("p (et cc) -> p et cc", et=32),
                  dram["outprojT"][:, dc * 512:(dc + 1) * 512].rearrange(
                      "(et pp) cc -> pp et cc", pp=128))
            for tt in range(NT):
                sl = ygl[:, (tt % 2) * 4096:(tt % 2 + 1) * 4096]
                p.dma("sync", sl.rearrange("p (et cc) -> p et cc", et=32),
                      ygsp[tt, :, :, :].rearrange("et pp cc -> pp et cc"))
                bank = psf[(0, 1, 4, 5)[tt % 4]]
                for et in range(32):
                    E("tensor", lambda e, b=bank, sl=sl, et=et: e.matmul(
                        b[:, 0:512], sl[:, et * 128:(et + 1) * 128],
                        wop[:, et * 512:(et + 1) * 512],
                        start=(et == 0), stop=(et == 31)),
                      reads=[ygl, wop], writes=[bank])
                E("vector", lambda e, b=bank, tt=tt, dc=dc: e.tensor_copy(
                    out=otfull[:, tt * D + dc * 512: tt * D + (dc + 1) * 512],
                    in_=b[:, 0:512]),
                  reads=[bank], writes=[otfull])
        for tt in range(NT):
            p.dma("sync", xtmp[:], xin[tt * 128:(tt + 1) * 128, :])
            E("vector", lambda e, tt=tt: e.tensor_scalar(
                out=xnew[:], in0=xtmp[:], scalar1=m3[:, 2 * tt:2 * tt + 1],
                scalar2=None, op0=ALU.add),
              reads=[xtmp, m3], writes=[xnew])
            divcol(xnew[:], xnew[:], m3[:, 2 * tt + 1:2 * tt + 2], [xnew, m3], [xnew])
            E("vector", lambda e, tt=tt: e.tensor_tensor(
                out=oti[:], in0=otfull[:, tt * D:(tt + 1) * D], in1=xnew[:], op=ALU.add),
              reads=[otfull, xnew], writes=[oti])
            sandwich(oti[:], xin, out_dram, tt, xtmp[:], xnew[:])
        p.pop_scope()
        p.pop_scope()
        p.barrier()

    def l1_ffn():
        out_dram = y_out
        xin = (xr2 if SKIP3 else xr3) if UPTO >= 3 else x_own
        p.push_scope()
        XT = p.sb("l_XT", (128, 16 * 1024), BF16)
        p.push_scope()
        xtl = p.sb("l_xtl", (128, 2 * D))
        xnb = p.sb("l_xnb", (128, D), BF16)
        for tt in range(NT):
            xt = xtl[:, (tt % 2) * D:(tt % 2 + 1) * D]
            p.dma("sync", xt, xin[tt * 128:(tt + 1) * 128, :])
            ln_stats(xt, D, extra_reads=[xtl])
            ln_cols(nmc, stc)
            adddiv(xnb[:], xt, nmc, stc, [xtl], [xnb])
            for dt in range(16):
                base2 = (tt // 4) * 8192 + dt * 512 + (tt % 4) * 128
                trans(XT[:, base2:base2 + 128], xnb[:, dt * 128:(dt + 1) * 128])
        p.pop_scope()

        DhT = p.sb("l_DhT", (128, 64 * 512), BF16)
        w0sl = p.sb("l_w0sl", (128, 4 * 2048), BF16)
        w1sl = p.sb("l_w1sl", (128, 2 * 8192), BF16)
        b0c = p.sb("l_b0c", (128, 2))
        outTb = p.sb("l_outTb", (128, 512), BF16)
        otile4 = p.sb("l_otile4", (128, 4 * D), BF16)
        b1b = p.sb("l_b1b", (128, D))
        otf32 = p.sb("l_otf32", (128, D))
        otf16 = p.sb("l_otf16", (128, D), F16)
        xtmp = p.sb("l_xtmp", (128, D))
        xnew = p.sb("l_xnew", (128, D))
        p.dma("sync", b1b[:], dram["bias1_rep"][:, :])
        for tc in range(2):
            for hc in range(64):
                sl = w0sl[:, (hc % 4) * 2048:(hc % 4 + 1) * 2048]
                p.dma("gpsimd", sl.rearrange("p (dt cc) -> p dt cc", dt=16),
                      dram["w0T"][:, hc * 128:(hc + 1) * 128].rearrange(
                          "(dt pp) cc -> pp dt cc", pp=128))
                bc = b0c[:, hc % 2: hc % 2 + 1]
                p.dma("sync", bc, dram["bias0"][hc * 128:(hc + 1) * 128, :])
                bank = psf[(0, 1, 4, 5)[hc % 4]]
                for dt in range(16):
                    E("tensor", lambda e, b=bank, sl=sl, dt=dt, tc=tc: e.matmul(
                        b[:, 0:512], sl[:, dt * 128:(dt + 1) * 128],
                        XT[:, tc * 8192 + dt * 512: tc * 8192 + (dt + 1) * 512],
                        start=(dt == 0), stop=(dt == 15)),
                      reads=[w0sl, XT], writes=[bank])
                E("scalar", lambda e, b=bank, hc=hc, bc=bc: e.activation(
                    out=DhT[:, hc * 512:(hc + 1) * 512], in_=b[:, 0:512],
                    func=AF.Gelu, bias=bc),
                  reads=[bank, b0c], writes=[DhT])
            for dcol in range(16):
                sl = w1sl[:, (dcol % 2) * 8192:(dcol % 2 + 1) * 8192]
                p.dma("gpsimd", sl.rearrange("p (ht cc) -> p ht cc", ht=64),
                      dram["w1T"][:, dcol * 128:(dcol + 1) * 128].rearrange(
                          "(ht pp) cc -> pp ht cc", pp=128))
                bank = psf[(2, 3, 0, 1)[dcol % 4]]
                for ht in range(64):
                    E("tensor", lambda e, b=bank, sl=sl, ht=ht: e.matmul(
                        b[:, 0:512], sl[:, ht * 128:(ht + 1) * 128],
                        DhT[:, ht * 512:(ht + 1) * 512],
                        start=(ht == 0), stop=(ht == 63)),
                      reads=[w1sl, DhT], writes=[bank])
                E("vector", lambda e, b=bank: e.tensor_copy(out=outTb[:], in_=b[:, 0:512]),
                  reads=[bank], writes=[outTb])
                for q4 in range(4):
                    trans(otile4[:, q4 * D + dcol * 128: q4 * D + (dcol + 1) * 128],
                          outTb[:, q4 * 128:(q4 + 1) * 128])
            for q4 in range(4):
                tt = tc * 4 + q4
                E("vector", lambda e, q4=q4: e.tensor_tensor(
                    out=otf32[:], in0=otile4[:, q4 * D:(q4 + 1) * D], in1=b1b[:],
                    op=ALU.add),
                  reads=[otile4, b1b], writes=[otf32])
                E("vector", lambda e: e.tensor_copy(out=otf16[:], in_=otf32[:]),
                  reads=[otf32], writes=[otf16])
                E("vector", lambda e: e.tensor_copy(out=otf32[:], in_=otf16[:]),
                  reads=[otf16], writes=[otf32])
                sandwich(otf32[:], xin, out_dram, tt, xtmp[:], xnew[:])
        p.pop_scope()
        p.barrier()

    if UPTO >= 1:
        l0_attn()
    if UPTO >= 2:
        l0_ffn()
    if UPTO >= 3 and not SKIP3:
        l1_attn()
    if UPTO >= 4:
        l1_ffn()
    if UPTO == 0:
        p.push_scope()
        xt0 = p.sb("xt0", (128, D))
        for tt in range(NT):
            p.dma("sync", xt0[:], x_own[tt * 128:(tt + 1) * 128, :])
            p.dma("gpsimd", y_out[tt * 128:(tt + 1) * 128, :], xt0[:])
        p.pop_scope()
    elif UPTO < 4:
        pass

    p.barrier()
    p.finalize()
    return nc


def _in_map(g, c, x, core):
    pr, hi = core // 2, core % 2
    m = dict(g)
    m["x_ctx"] = np.ascontiguousarray(x[pr])
    m["x_own"] = np.ascontiguousarray(x[pr, hi * 1024:(hi + 1) * 1024])
    m["cos_ctx"] = c["cos_ctx"]; m["sinA_ctx"] = c["sinA_ctx"]
    m["cos_own"] = c[f"cos_own_{hi}"]; m["sinA_own"] = c[f"sinA_own_{hi}"]
    m["masks"] = c[f"masks_{hi}"]; m["hz"] = c[f"hz_{hi}"]
    m["ident"] = c["ident"]; m["onesr"] = c["onesr"]
    m["E64"] = c["E64"]; m["sel2"] = c["sel2"]
    m["ssdA"] = c["ssdA"]; m["ssdB"] = c["ssdB"]
    return m


def kernel(**inputs):
    g = _prep_inputs(inputs)
    c = _host_consts()
    x = np.asarray(inputs["L0_attn_x"])
    nc = build_program()
    in_maps = [_in_map(g, c, x, core) for core in range(8)]
    res = run_bass_kernel_spmd(nc, in_maps, core_ids=list(range(8)))
    out = np.empty((4, 2048, 2048), dtype=np.float32)
    for core in range(8):
        pr, hi = core // 2, core % 2
        out[pr, hi * 1024:(hi + 1) * 1024] = res.results[core]["y_out"]
    return out


# version 13 (final)
# speedup vs baseline: 1.1658x; 1.0032x over previous
import os
import math
import numpy as np
import ml_dtypes
from contextlib import ExitStack

import concourse.bass as bass
import concourse.mybir as mybir
from concourse.bass_utils import run_bass_kernel_spmd
from concourse.bass import broadcast_tensor_aps

F32 = mybir.dt.float32
BF16 = mybir.dt.bfloat16
F16 = mybir.dt.float16
AF = mybir.ActivationFunctionType
ALU = mybir.AluOpType
AX = mybir.AxisListType
DR = mybir.MatmulPerfMode.DoubleRow

D = 2048
T = 2048
NTOK = 1024
NT = 8
H = 16
HKV = 4
HD = 128
DKV = 512
FFN_H = 8192
DIN = 4096
DST = 64
NH = 64
CONV_DIM = 4224
EIN = 8384
CHUNK = 256
NCH = 4
ASCALE = 1.0 / math.sqrt(HD)
F32EPS = float(np.finfo(np.float32).eps)
NEG = -1e30

UPTO = int(os.environ.get("MK_UPTO", "4"))
SIMMODE = os.environ.get("BASS_SIM_FP64") == "1"
F_SAME = True
F_PH = True
SKIP3 = os.environ.get("MK_SKIP3", "") == "1"

ENGS = ("sync", "scalar", "vector", "tensor", "gpsimd")
ESEM = {"scalar": "eS", "vector": "eV", "tensor": "eT", "gpsimd": "eG"}
QSEM = {"sync": "qY", "scalar": "qA", "vector": "qV", "tensor": "qT", "gpsimd": "qG"}


class PH:
    __slots__ = ("val",)

    def __init__(self):
        self.val = None


def _nm(x):
    if isinstance(x, str):
        return x
    t = getattr(x, "tensor", None)
    if t is not None:
        return t.name
    return getattr(x, "name")


class P:
    def __init__(self, nc):
        self.nc = nc
        self.es = ExitStack()
        self.streams = {e: [] for e in ENGS}
        self.semh = {}
        self.cnt = {}
        self.qpool = {"sync": [f"qy{i}" for i in range(10)],
                      "scalar": [f"qa{i}" for i in range(4)],
                      "vector": [f"qv{i}" for i in range(2)],
                      "tensor": [f"qt{i}" for i in range(2)],
                      "gpsimd": [f"qg{i}" for i in range(10)]}
        names = ["eS", "eV", "eT", "eG", "cc"] + [x for v in self.qpool.values() for x in v]
        for s in names:
            self.semh[s] = self.es.enter_context(nc.semaphore(s))
            self.cnt[s] = 0
        self.qpi = {k: 0 for k in self.qpool}
        self.qpend = {k: [] for k in self.qpool}
        self.waited = {e: {} for e in ENGS}
        self.buf = {}
        self.scopes = []

    def sb(self, name, shape, dtype=F32, persist=False):
        cm = self.nc.sbuf_tensor(name, list(shape), dtype)
        tgt = self.es if (persist or not self.scopes) else self.scopes[-1]
        return tgt.enter_context(cm)

    def ps(self, name, shape, dtype=F32):
        return self.es.enter_context(self.nc.psum_tensor(name, list(shape), dtype))

    def push_scope(self):
        s = ExitStack()
        self.scopes.append(s)
        return s

    def pop_scope(self):
        self.barrier()
        self.scopes.pop().close()

    def _wait(self, eng, sem, val, ph=None):
        if val <= 0:
            return
        if not F_SAME and sem == ESEM.get(eng) and not getattr(self, "_dmawait", False):
            return
        hw = self.waited[eng]
        if hw.get(sem, 0) >= val:
            return
        hw[sem] = val
        h = self.semh[sem]
        if ph is not None:
            ph.val = (self.semh[sem], val)
        self.streams[eng].append(lambda e, h=h, v=val: e.wait_ge(h, v))

    def emit(self, eng, fn, reads=(), writes=(), q=None, sem=None, inc=None):
        need = {}
        for r in reads:
            b = self.buf.get(_nm(r))
            if b and b["w"]:
                s, v, ph = b["w"]
                if s not in need or need[s][0] < v:
                    need[s] = (v, ph)
        for w in writes:
            b = self.buf.get(_nm(w))
            if b:
                if b["w"]:
                    s, v, ph = b["w"]
                    if s not in need or need[s][0] < v:
                        need[s] = (v, ph)
                for s, (v, ph) in b["r"].items():
                    if s not in need or need[s][0] < v:
                        need[s] = (v, ph)
        self._dmawait = bool(q) or sem == "cc"
        for s, (v, ph) in need.items():
            self._wait(eng, s, v, ph)
        self._dmawait = False
        if sem is None:
            if q:
                pool = self.qpool[eng]
                sem = pool[self.qpi[eng] % len(pool)]
                self.qpi[eng] += 1
            else:
                sem = ESEM[eng]
        if inc is None:
            inc = 16 if q else 1
        self.cnt[sem] += inc
        val = self.cnt[sem]
        h = self.semh[sem]
        self.streams[eng].append(lambda e, fn=fn, h=h, inc=inc: fn(e).then_inc(h, inc))
        ph = None
        if not q:
            ph = PH()
            self.streams[eng].append(ph)
        else:
            pend = self.qpend[eng]
            pend.append((sem, val))
            lim = (len(self.qpool[eng]) - 2) if SIMMODE else 32
            if len(pend) > lim:
                s_, v_ = pend.pop(0)
                h_ = self.semh[s_]
                hw = self.waited[eng]
                if hw.get(s_, 0) < v_:
                    hw[s_] = v_
                    self.streams[eng].append(lambda e, h=h_, v=v_: e.wait_ge(h, v))
        edge = (sem, val, ph)
        for w in writes:
            self.buf[_nm(w)] = {"w": edge, "r": {}}
        for r in reads:
            b = self.buf.setdefault(_nm(r), {"w": None, "r": {}})
            pv = b["r"].get(sem)
            if pv is None or pv[0] < val:
                b["r"][sem] = (val, ph)
        return val

    def dma(self, eng, out, in_, extra_reads=()):
        self.emit(eng, lambda e, o=out, i=in_: e.dma_start(out=o, in_=i),
                  reads=[in_] + list(extra_reads), writes=[out], q=QSEM[eng])

    def barrier(self):
        for eng in ("scalar", "vector", "tensor", "gpsimd"):
            sem = ESEM[eng]
            self.cnt[sem] += 1
            h = self.semh[sem]
            self.streams[eng].append(lambda e, h=h: e.nop().then_inc(h, 1))
            self.waited[eng][sem] = self.cnt[sem]
        snap = dict(self.cnt)
        for eng in ENGS:
            for s, v in snap.items():
                self._wait(eng, s, v)
        for b in self.buf.values():
            b["r"] = {}
            if b["w"]:
                s, v, ph = b["w"]
                b["w"] = (s, v, None)

    def finalize(self):
        blk = self.es.enter_context(self.nc.Block())
        for eng in ENGS:
            ops = []
            for f in self.streams[eng]:
                if isinstance(f, PH):
                    if F_PH and f.val is not None:
                        ops.append(lambda e, h=f.val[0], v=f.val[1]: e.wait_ge(h, v))
                else:
                    ops.append(f)
            getattr(blk, eng)(lambda e, st=ops: [f(e) for f in st])
        self.es.close()



def _host_consts():
    c = {}
    i = np.arange(0, HD, 2, dtype=np.float64)
    inv_freq = 1.0 / (1000000.0 ** (i / HD))
    pos = np.arange(T, dtype=np.float64)
    freqs = pos[:, None] * inv_freq[None, :]
    emb = np.concatenate([freqs, freqs], axis=-1)
    cos = np.cos(emb).astype(np.float32)
    sinA = np.sin(emb).astype(np.float32)
    sinA[:, : HD // 2] *= -1.0
    c["cos_ctx"] = cos
    c["sinA_ctx"] = sinA
    c["ident"] = np.eye(128, dtype=np.float32)
    c["onesr"] = np.ones((1, 512), dtype=np.float32)
    jj = np.arange(128)[:, None]
    ii = np.arange(256)[None, :]
    c["ssdA"] = np.where(ii >= jj, 0.0, NEG).astype(np.float32)
    c["ssdB"] = np.where(ii >= 128 + jj, 0.0, NEG).astype(np.float32)
    e64 = np.zeros((128, 2), dtype=np.float32)
    e64[:64, 0] = 1.0
    e64[64:, 1] = 1.0
    c["E64"] = e64
    c["sel2"] = np.ascontiguousarray(e64.T)
    for hi in (0, 1):
        kk = np.arange(128)[:, None]
        qq = np.arange(512)[None, :]
        m = np.zeros((2, 16, 128, 512), dtype=np.float32)
        for cch in range(2):
            for j in range(16):
                m[cch, j] = ((hi * 1024 + cch * 512 + qq) >= (j * 128 + kk)).astype(np.float32)
        c[f"masks_{hi}"] = m.astype(ml_dtypes.bfloat16)
        c[f"hz_{hi}"] = np.full((128, 1), float(hi), dtype=np.float32)
        sl = slice(hi * 1024, (hi + 1) * 1024)
        c[f"cos_own_{hi}"] = cos[sl].copy()
        c[f"sinA_own_{hi}"] = sinA[sl].copy()
    return c


def _prep_inputs(inp):
    g = {}
    t = lambda a: np.ascontiguousarray(np.asarray(a).T)
    asc = np.ascontiguousarray
    rep = lambda a, n=128: asc(np.broadcast_to(np.asarray(a)[None, :], (n, len(a))))
    g["wqT"] = t(inp["L0_attn_wq"])
    g["wkT"] = t(inp["L0_attn_wk"])
    g["wvT"] = t(inp["L0_attn_wv"])
    g["woT"] = t(inp["L0_attn_wo"])
    g["wq_nat"] = asc(np.asarray(inp["L0_attn_wq"]))
    g["wk_nat"] = asc(np.asarray(inp["L0_attn_wk"]))
    g["wv_nat"] = asc(np.asarray(inp["L0_attn_wv"]))
    g["wfcT"] = t(inp["L0_ffn_W_fc"])
    g["wprojT"] = t(inp["L0_ffn_W_proj"])
    g["inprojT"] = t(inp["L1_attn_in_proj_w"])
    g["outprojT"] = t(inp["L1_attn_out_proj_w"])
    g["conv_w"] = asc(np.asarray(inp["L1_attn_conv_w"]))
    g["conv_b"] = asc(np.asarray(inp["L1_attn_conv_b"]).reshape(CONV_DIM, 1))
    g["rms_w_rep"] = rep(np.asarray(inp["L1_attn_rms_weight"]))
    g["dtb_rep"] = rep(np.asarray(inp["L1_attn_dt_bias"]))
    g["alog_rep"] = rep(np.asarray(inp["L1_attn_A_log"]))
    g["dd_rep"] = rep(np.asarray(inp["L1_attn_D"]))
    g["normw_rep"] = rep(np.asarray(inp["L1_attn_norm_w"]))
    g["w0T"] = t(inp["L1_ffn_W0"])
    g["w1T"] = t(inp["L1_ffn_W1"])
    g["bias0"] = asc(np.asarray(inp["L1_ffn_bias0"]).reshape(FFN_H, 1))
    g["bias1_rep"] = rep(np.asarray(inp["L1_ffn_bias1"]))
    return g



def build_program():
    nc = bass.Bass("TRN2", target_bir_lowering=False, debug=False)
    dram = {}

    def din(name, shape, dtype=F32):
        dram[name] = nc.dram_tensor(name, list(shape), dtype, kind="ExternalInput").ap()
        return dram[name]

    x_own = din("x_own", (NTOK, D))
    x_ctx = din("x_ctx", (T, D))
    for n, s in (("wqT", (D, D)), ("wkT", (D, DKV)), ("wvT", (D, DKV)), ("woT", (D, D)),
                 ("wq_nat", (D, D)), ("wk_nat", (DKV, D)), ("wv_nat", (DKV, D)),
                 ("wfcT", (D, FFN_H)), ("wprojT", (FFN_H, D)),
                 ("inprojT", (D, EIN)), ("outprojT", (DIN, D)),
                 ("conv_w", (CONV_DIM, 4)), ("conv_b", (CONV_DIM, 1)),
                 ("rms_w_rep", (128, D)), ("dtb_rep", (128, NH)), ("alog_rep", (128, NH)),
                 ("dd_rep", (128, NH)), ("normw_rep", (128, DIN)),
                 ("w0T", (D, FFN_H)), ("w1T", (FFN_H, D)),
                 ("bias0", (FFN_H, 1)), ("bias1_rep", (128, D)),
                 ("cos_ctx", (T, HD)), ("sinA_ctx", (T, HD)),
                 ("cos_own", (NTOK, HD)), ("sinA_own", (NTOK, HD)),
                 ("ident", (128, 128)), ("onesr", (1, 512)),
                 ("E64", (128, 2)), ("sel2", (2, 128)),
                 ("ssdA", (128, 256)), ("ssdB", (128, 256)), ("hz", (128, 1))):
        din(n, s)
    din("masks", (2, 16, 128, 512), BF16)

    y_out = nc.dram_tensor("y_out", [NTOK, D], F32, kind="ExternalOutput").ap()
    xr1 = nc.dram_tensor("xr1", [NTOK, D], F32).ap()
    xr2 = nc.dram_tensor("xr2", [NTOK, D], F32).ap()
    xr3 = nc.dram_tensor("xr3", [NTOK, D], F32).ap()
    zspill = nc.dram_tensor("zspill", [NTOK, DIN], BF16).ap()
    w1fc_s = nc.dram_tensor("w1fc_s", [64, 16, 128, 128], BF16).ap()
    w1pr_s = nc.dram_tensor("w1pr_s", [16, 64, 128, 128], BF16).ap()
    ygsp = nc.dram_tensor("ygsp", [8, 32, 128, 128], BF16).ap()
    cch_in = nc.dram_tensor("cch_in", [33, 128, 3], F32).ap()
    cch_out = nc.dram_tensor("cch_out", [2, 33, 128, 3], F32).ap()
    cst_in = nc.dram_tensor("cst_in", [64, DIN], F32).ap()
    cst_out = nc.dram_tensor("cst_out", [2, 64, DIN], F32).ap()

    p = P(nc)
    E = p.emit

    identf = p.sb("identf", (128, 128), persist=True)
    identb = p.sb("identb", (128, 128), BF16, persist=True)
    onesr = p.sb("onesr_s", (1, 512), persist=True)
    onesc = p.sb("onesc_s", (128, 1), BF16, persist=True)
    e64 = p.sb("e64_s", (128, 2), BF16, persist=True)
    sel2 = p.sb("sel2_s", (2, 128), BF16, persist=True)
    hzc = p.sb("hz_s", (128, 1), persist=True)
    stats = p.sb("stats", (128, 96), persist=True)
    mv = p.sb("mv", (128, 2), persist=True)
    cols = p.sb("cols", (128, 24), persist=True)
    m3 = p.sb("m3", (128, 16), persist=True)

    psf = [p.ps(f"psf{i}", [128, 512], F32) for i in range(6)]
    psb = [p.ps(f"psb{i}", [128, 1024], BF16) for i in range(2)]
    _tb = [0]

    p.dma("sync", identf[:], dram["ident"][:, :])
    p.dma("sync", onesr[:], dram["onesr"][:, :])
    p.dma("gpsimd", e64[:], dram["E64"][:, :])
    p.dma("gpsimd", sel2[:], dram["sel2"][:, :])
    p.dma("sync", hzc[:], dram["hz"][:, :])
    p.dma("gpsimd", identb[:], dram["ident"][:, :])
    E("gpsimd", lambda e: e.memset(onesc[:], 1.0), reads=[], writes=[onesc])

    nmc, stc, smx = cols[:, 0:1], cols[:, 1:2], cols[:, 2:3]
    smx4 = cols[:, 4:8]

    def trans8(dst_wide_ap, src_aps, reads_extra=()):
        n = len(src_aps)
        assert n <= 8
        bank = psb[_tb[0] % 2]
        _tb[0] += 1
        for k, sap in enumerate(src_aps):
            E("tensor", lambda e, b=bank, s=sap, k=k: e.transpose(
                out=b[:, k * 128:(k + 1) * 128], in_=s, identity=identb[:]),
              reads=[sap, identb] + list(reads_extra), writes=[bank])
        E("vector", lambda e, b=bank, d=dst_wide_ap, n=n: e.tensor_copy(
            out=d, in_=b[:, 0:n * 128]),
          reads=[bank], writes=[dst_wide_ap])

    def trans(dst_ap, src_ap, pdim=128, fdim=128, f32=False):
        if not f32:
            bank = psb[_tb[0] % 2]
            _tb[0] += 1
            ident = identb
        else:
            bank = psf[5]
            ident = identf
        E("tensor", lambda e, b=bank, s=src_ap, idn=ident, pd=pdim, fd=fdim: e.transpose(
            out=b[0:fd, 0:pd], in_=s, identity=idn[0:pd, 0:pd]),
          reads=[src_ap, ident], writes=[bank])
        E("vector", lambda e, b=bank, d=dst_ap, pd=pdim, fd=fdim: e.tensor_copy(
            out=d, in_=b[0:fd, 0:pd]),
          reads=[bank], writes=[dst_ap])

    def divcol(out_ap, in0_ap, col_ap, rd, wr):
        rcp = cols[:, 23:24]
        E("vector", lambda e, c=col_ap: e.reciprocal(out=rcp, in_=c),
          reads=list(rd), writes=[cols])
        E("vector", lambda e, o=out_ap, i=in0_ap: e.tensor_scalar(
            out=o, in0=i, scalar1=rcp, scalar2=None, op0=ALU.mult),
          reads=list(rd) + [cols], writes=list(wr))

    def adddiv(out_ap, in0_ap, nm_col, std_col, rd, wr):
        E("vector", lambda e, o=out_ap, i=in0_ap, nm=nm_col: e.tensor_scalar(
            out=o, in0=i, scalar1=nm, scalar2=None, op0=ALU.add),
          reads=list(rd) + [cols], writes=list(wr))
        divcol(out_ap, out_ap, std_col, list(wr) + [cols], list(wr))

    def ln_stats(src_ap, width, extra_reads=()):
        nchk = (width + 511) // 512
        for ci in range(nchk):
            w = min(512, width - ci * 512)
            E("vector", lambda e, s=src_ap, ci=ci, w=w: e.bn_stats(
                out=stats[:, ci * 6:(ci + 1) * 6], in_=s[:, ci * 512: ci * 512 + w]),
              reads=[src_ap] + list(extra_reads), writes=[stats])
        E("vector", lambda e, n=nchk: e.bn_aggr(out=mv[:], in_=stats[:, 0:6 * n]),
          reads=[stats], writes=[mv])

    def ln_cols(nm_col, std_col, var_eps=1e-5):
        E("vector", lambda e, o=nm_col: e.tensor_scalar(
            out=o, in0=mv[:, 0:1], scalar1=-1.0, scalar2=None, op0=ALU.mult),
          reads=[mv], writes=[cols])
        E("vector", lambda e, o=std_col, ve=var_eps: e.tensor_scalar(
            out=o, in0=mv[:, 1:2], scalar1=ve, scalar2=None, op0=ALU.add),
          reads=[mv], writes=[cols])
        E("scalar", lambda e, o=std_col: e.activation(out=o, in_=o, func=AF.Sqrt),
          reads=[cols], writes=[cols])

    def rms_std_col(std_col, eps):
        E("vector", lambda e, o=std_col: e.tensor_tensor(
            out=o, in0=mv[:, 0:1], in1=mv[:, 0:1], op=ALU.mult),
          reads=[mv], writes=[cols])
        E("vector", lambda e, o=std_col: e.tensor_tensor(
            out=o, in0=o, in1=mv[:, 1:2], op=ALU.add),
          reads=[mv, cols], writes=[cols])
        E("vector", lambda e, o=std_col, ve=eps: e.tensor_scalar(
            out=o, in0=o, scalar1=ve, scalar2=None, op0=ALU.add),
          reads=[cols], writes=[cols])
        E("scalar", lambda e, o=std_col: e.activation(out=o, in_=o, func=AF.Sqrt),
          reads=[cols], writes=[cols])

    def quant_act(dst_bf, src_f32, tmp_f32, tmp2_f32, nheads, smax, extra=()):
        hd = None
        if nheads > 1:
            E("vector", lambda e, s=src_f32, n=nheads, o=smax: e.reduce_max(
                out=o, in_=s.rearrange("p (h d) -> p h d", h=n),
                axis=AX.X, apply_absolute_value=True),
              reads=[src_f32] + list(extra), writes=[cols])
        else:
            E("vector", lambda e, s=src_f32, o=smax: e.reduce_max(
                out=o, in_=s, axis=AX.X, apply_absolute_value=True),
              reads=[src_f32] + list(extra), writes=[cols])
        E("vector", lambda e, o=smax: e.tensor_scalar(
            out=o, in0=o, scalar1=1.0 / 127.0, scalar2=1e-8,
            op0=ALU.mult, op1=ALU.max),
          reads=[cols], writes=[cols])
        rsm = cols[:, 16:16 + nheads]
        E("vector", lambda e, o=smax, r=rsm: e.reciprocal(out=r, in_=o),
          reads=[cols], writes=[cols])
        w = 2048 // nheads if nheads > 1 else None
        if nheads > 1:
            wd = (dst_bf.tensor.shape[-1] if False else 512) // nheads
        for h in range(nheads):
            if nheads > 1:
                sl = slice(h * 128, (h + 1) * 128)
            else:
                sl = slice(None)
            E("vector", lambda e, s=src_f32, t=tmp_f32, r=rsm, h=h, sl=sl: e.tensor_scalar(
                out=t[:, sl], in0=s[:, sl], scalar1=r[:, h:h + 1] if True else None,
                scalar2=None, op0=ALU.mult),
              reads=[src_f32, cols], writes=[tmp_f32])
        E("vector", lambda e, t=tmp_f32: e.tensor_scalar(
            out=t, in0=t, scalar1=12582912.0, scalar2=12582912.0,
            op0=ALU.add, op1=ALU.subtract),
          reads=[tmp_f32], writes=[tmp_f32])
        E("vector", lambda e, t=tmp_f32: e.tensor_scalar(
            out=t, in0=t, scalar1=-127.0, scalar2=127.0, op0=ALU.max, op1=ALU.min),
          reads=[tmp_f32], writes=[tmp_f32])
        for h in range(nheads):
            if nheads > 1:
                sl = slice(h * 128, (h + 1) * 128)
            else:
                sl = slice(None)
            E("scalar", lambda e, t=tmp_f32, d=dst_bf, o=smax, h=h, sl=sl: e.activation(
                out=d[:, sl], in_=t[:, sl], func=AF.Copy, scale=o[:, h:h + 1]),
              reads=[tmp_f32, cols], writes=[dst_bf])

    def rope(dst_f32, src_f32, costile, sintile, nheads, tmp):
        for h in range(nheads):
            b = h * 128
            for (o0, o1) in ((0, 64), (64, 0)):
                E("vector", lambda e, s=src_f32, t=tmp, b=b, o0=o0, o1=o1, st=sintile:
                  e.tensor_tensor(out=t[:, b + o0:b + o0 + 64],
                                  in0=s[:, b + o1:b + o1 + 64],
                                  in1=st[:, o0:o0 + 64], op=ALU.mult),
                  reads=[src_f32, sintile], writes=[tmp])
            E("vector", lambda e, s=src_f32, d=dst_f32, b=b, ct=costile:
              e.tensor_tensor(out=d[:, b:b + 128], in0=s[:, b:b + 128],
                              in1=ct[:], op=ALU.mult),
              reads=[src_f32, costile], writes=[dst_f32])
        E("vector", lambda e, d=dst_f32, t=tmp: e.tensor_tensor(
            out=d, in0=d, in1=t, op=ALU.add),
          reads=[dst_f32, tmp], writes=[dst_f32])

    def sandwich(otile_ap, xin_dram, xout_dram, tt, xtmp, xnew, extra_reads=()):
        ln_stats(otile_ap, D, extra_reads=extra_reads)
        ln_cols(nmc, stc)
        p.dma("sync", xtmp, xin_dram[tt * 128:(tt + 1) * 128, :])
        adddiv(xnew, otile_ap, nmc, stc, [otile_ap] + list(extra_reads), [xnew])
        E("vector", lambda e, x=xnew, xt=xtmp: e.tensor_tensor(
            out=x, in0=x, in1=xt, op=ALU.add),
          reads=[xnew, xtmp], writes=[xnew])
        p.dma("gpsimd", xout_dram[tt * 128:(tt + 1) * 128, :], xnew)

    def l0_attn():
        out_dram = y_out if UPTO == 1 else xr1
        p.push_scope()
        qhatT = p.sb("qhatT", (128, H * 1024), BF16)
        wq_sr = p.sb("wq_sr", (1, D))
        wk_sr = p.sb("wk_sr", (1, DKV))
        wv_sr = p.sb("wv_sr", (1, DKV))
        wq_rr = p.sb("wq_rr", (1, D))
        wk_rr = p.sb("wk_rr", (1, DKV))
        wv_rr = p.sb("wv_rr", (1, DKV))

        def quant_w_tile(raw_ap, dst_ap, sc_bank, rc_bank, tmpf, tmpb):
            E("vector", lambda e, r=raw_ap, b=rc_bank, t=tmpf: e.tensor_tensor(
                out=t, in0=r, in1=b[:, 0:512], op=ALU.mult),
              reads=[raw_ap, rc_bank], writes=[tmpf])
            E("vector", lambda e, t=tmpf: e.tensor_scalar(
                out=t, in0=t, scalar1=12582912.0, scalar2=12582912.0,
                op0=ALU.add, op1=ALU.subtract),
              reads=[tmpf], writes=[tmpf])
            E("vector", lambda e, d=dst_ap, b=sc_bank, t=tmpf: e.tensor_tensor(
                out=d, in0=t, in1=b[:, 0:512], op=ALU.mult),
              reads=[tmpf, sc_bank], writes=[dst_ap])

        p.push_scope()
        wnat = p.sb("wnat", (128, 2 * D))
        colm = p.sb("colm", (128, 1))
        for nm_, rows, srow, rrow in (("wq_nat", D, wq_sr, wq_rr),
                                      ("wk_nat", DKV, wk_sr, wk_rr),
                                      ("wv_nat", DKV, wv_sr, wv_rr)):
            for ot in range(rows // 128):
                wb = wnat[:, (ot % 2) * D:(ot % 2 + 1) * D]
                p.dma("sync", wb, dram[nm_][ot * 128:(ot + 1) * 128, :])
                E("vector", lambda e, wb=wb: e.reduce_max(
                    out=colm[:], in_=wb, axis=AX.X, apply_absolute_value=True),
                  reads=[wnat], writes=[colm])
                E("vector", lambda e: e.tensor_scalar(
                    out=colm[:], in0=colm[:], scalar1=1.0 / 127.0, scalar2=1e-8,
                    op0=ALU.mult, op1=ALU.max),
                  reads=[colm], writes=[colm])
                E("tensor", lambda e: e.matmul(psf[5][0:1, 0:128], colm[:], identf[:],
                                               start=True, stop=True),
                  reads=[colm, identf], writes=[psf[5]])
                E("vector", lambda e, srow=srow, ot=ot: e.tensor_copy(
                    out=srow[:, ot * 128:(ot + 1) * 128], in_=psf[5][0:1, 0:128]),
                  reads=[psf[5]], writes=[srow])
                E("vector", lambda e, srow=srow, rrow=rrow, ot=ot: e.reciprocal(
                    out=rrow[:, ot * 128:(ot + 1) * 128],
                    in_=srow[:, ot * 128:(ot + 1) * 128]),
                  reads=[srow], writes=[rrow])
        p.pop_scope()
        p.barrier()

        p.push_scope()
        xhatTo = p.sb("xhatTo", (128, 16 * 1024), BF16)
        wqq = p.sb("wqq", (128, 16 * 512), BF16)
        tq1 = p.sb("tq1a", (128, D))
        tq2 = p.sb("tq2a", (128, D))
        xn = p.sb("xna", (128, D))
        xhat = p.sb("xhata", (128, D), BF16)
        wraw = p.sb("wrawa", (128, 1024))
        cosb = p.sb("cosba", (128, HD))
        sinb = p.sb("sinba", (128, HD))
        qhat = p.sb("qhat_b", (128, 512), BF16)
        xtl = p.sb("xtla", (128, 2 * D))
        for tt in range(NT):
            xt = xtl[:, (tt % 2) * D:(tt % 2 + 1) * D]
            p.dma("sync", xt, x_own[tt * 128:(tt + 1) * 128, :])
            ln_stats(xt, D, extra_reads=[xtl])
            ln_cols(nmc, stc)
            adddiv(xn[:], xt, nmc, stc, [xtl], [xn])
            quant_act(xhat[:], xn[:], tq1[:], tq2[:], 1, smx)
            for dh in range(2):
                trans8(xhatTo[:, tt * 2048 + dh * 1024: tt * 2048 + (dh + 1) * 1024],
                       [xhat[:, (dh * 8 + k) * 128:(dh * 8 + k + 1) * 128] for k in range(8)])
        for oc in range(4):
            E("tensor", lambda e, oc=oc: e.matmul(
                psf[4][:, 0:512], onesr[0:1, 0:128], wq_sr[0:1, oc * 512:(oc + 1) * 512],
                start=True, stop=True),
              reads=[onesr, wq_sr], writes=[psf[4]])
            E("tensor", lambda e, oc=oc: e.matmul(
                psf[3][:, 0:512], onesr[0:1, 0:128], wq_rr[0:1, oc * 512:(oc + 1) * 512],
                start=True, stop=True),
              reads=[onesr, wq_rr], writes=[psf[3]])
            for dt in range(16):
                wb = wraw[:, (dt % 2) * 512:(dt % 2 + 1) * 512]
                p.dma("sync", wb,
                      dram["wqT"][dt * 128:(dt + 1) * 128, oc * 512:(oc + 1) * 512])
                quant_w_tile(wb, wqq[:, dt * 512:(dt + 1) * 512], psf[4], psf[3],
                             tq1[:, 0:512], tq2[:, 0:512])
            for tt in range(NT):
                bank = psf[tt % 4]
                for dt in range(16):
                    E("tensor", lambda e, b=bank, dt=dt, tt=tt: e.matmul(
                        b[:, 0:512],
                        xhatTo[:, tt * 2048 + dt * 128: tt * 2048 + (dt + 1) * 128],
                        wqq[:, dt * 512:(dt + 1) * 512],
                        start=(dt == 0), stop=(dt == 15)),
                      reads=[xhatTo, wqq], writes=[bank])
                p.dma("sync", cosb[:], dram["cos_own"][tt * 128:(tt + 1) * 128, :])
                p.dma("sync", sinb[:], dram["sinA_own"][tt * 128:(tt + 1) * 128, :])
                E("vector", lambda e, b=bank, tq2=tq2: e.tensor_copy(
                    out=tq2[:, 1024:1536], in_=b[:, 0:512]),
                  reads=[bank], writes=[tq2])
                rope(tq2[:, 1536:2048], tq2[:, 1024:1536], cosb[:], sinb[:], 4,
                     tq1[:, 1024:1536])
                quant_act(qhat[:], tq2[:, 1536:2048], tq1[:, 0:512], tq2[:, 0:512],
                          4, smx4)
                for hh in range(4):
                    h = oc * 4 + hh
                    trans(qhatT[:, h * 1024 + tt * 128: h * 1024 + (tt + 1) * 128],
                          qhat[:, hh * 128:(hh + 1) * 128])
        p.pop_scope()
        p.barrier()

        p.push_scope()
        khatT = p.sb("khatT", (128, HKV * T), BF16)
        vhato = p.sb("vhato", (128, 16 * HKV * 132), BF16)
        p.push_scope()
        wkq = p.sb("wkq", (128, 16 * 512), BF16)
        wvq = p.sb("wvq", (128, 16 * 512), BF16)
        xhatTc = p.sb("xhatTc", (128, 16 * 128), BF16)
        tq1 = p.sb("tq1b", (128, D))
        tq2 = p.sb("tq2b", (128, D))
        xn = p.sb("xnb", (128, D))
        xhat = p.sb("xhatb", (128, D), BF16)
        wraw = p.sb("wrawb", (128, 1024))
        cosb = p.sb("cosbb", (128, HD))
        sinb = p.sb("sinbb", (128, HD))
        khat = p.sb("khat_b", (128, 512), BF16)
        vhat = p.sb("vhat_b", (128, 512), BF16)
        xctx = p.sb("xctx", (128, 2 * D))
        for nm_, srow, rrow, dstq in (("wkT", wk_sr, wk_rr, wkq),
                                      ("wvT", wv_sr, wv_rr, wvq)):
            E("tensor", lambda e, srow=srow: e.matmul(
                psf[4][:, 0:512], onesr[0:1, 0:128], srow[0:1, 0:512],
                start=True, stop=True),
              reads=[onesr, srow], writes=[psf[4]])
            E("tensor", lambda e, rrow=rrow: e.matmul(
                psf[3][:, 0:512], onesr[0:1, 0:128], rrow[0:1, 0:512],
                start=True, stop=True),
              reads=[onesr, rrow], writes=[psf[3]])
            for dt in range(16):
                wb = wraw[:, (dt % 2) * 512:(dt % 2 + 1) * 512]
                p.dma("sync", wb, dram[nm_][dt * 128:(dt + 1) * 128, :])
                quant_w_tile(wb, dstq[:, dt * 512:(dt + 1) * 512], psf[4], psf[3],
                             tq1[:, 0:512], tq2[:, 0:512])
        E("gpsimd", lambda e: e.memset(vhato[:], 1.0), reads=[], writes=[vhato])
        for ct in range(16):
            xc = xctx[:, (ct % 2) * D:(ct % 2 + 1) * D]
            p.dma("sync", xc, x_ctx[ct * 128:(ct + 1) * 128, :])
            ln_stats(xc, D, extra_reads=[xctx])
            ln_cols(nmc, stc)
            adddiv(xn[:], xc, nmc, stc, [xctx], [xn])
            quant_act(xhat[:], xn[:], tq1[:], tq2[:], 1, smx)
            for dh in range(2):
                trans8(xhatTc[:, dh * 1024:(dh + 1) * 1024],
                       [xhat[:, (dh * 8 + k) * 128:(dh * 8 + k + 1) * 128] for k in range(8)])
            for dt in range(16):
                E("tensor", lambda e, dt=dt: e.matmul(
                    psf[0][:, 0:512], xhatTc[:, dt * 128:(dt + 1) * 128],
                    wkq[:, dt * 512:(dt + 1) * 512], start=(dt == 0), stop=(dt == 15)),
                  reads=[xhatTc, wkq], writes=[psf[0]])
            for dt in range(16):
                E("tensor", lambda e, dt=dt: e.matmul(
                    psf[1][:, 0:512], xhatTc[:, dt * 128:(dt + 1) * 128],
                    wvq[:, dt * 512:(dt + 1) * 512], start=(dt == 0), stop=(dt == 15)),
                  reads=[xhatTc, wvq], writes=[psf[1]])
            p.dma("sync", cosb[:], dram["cos_ctx"][ct * 128:(ct + 1) * 128, :])
            p.dma("sync", sinb[:], dram["sinA_ctx"][ct * 128:(ct + 1) * 128, :])
            E("vector", lambda e: e.tensor_copy(out=tq2[:, 1024:1536], in_=psf[0][:, 0:512]),
              reads=[psf[0]], writes=[tq2])
            rope(tq2[:, 1536:2048], tq2[:, 1024:1536], cosb[:], sinb[:], 4,
                 tq1[:, 1024:1536])
            quant_act(khat[:], tq2[:, 1536:2048], tq1[:, 0:512], tq2[:, 0:512], 4, smx4)
            E("vector", lambda e: e.tensor_copy(out=tq2[:, 1024:1536], in_=psf[1][:, 0:512]),
              reads=[psf[1]], writes=[tq2])
            quant_act(vhat[:], tq2[:, 1024:1536], tq1[:, 0:512], tq2[:, 0:512], 4,
                      cols[:, 8:12])
            for kv in range(HKV):
                trans(khatT[:, kv * T + ct * 128: kv * T + (ct + 1) * 128],
                      khat[:, kv * 128:(kv + 1) * 128])
                base = (ct * HKV + kv) * 132
                E("vector", lambda e, b=base, kv=kv: e.tensor_copy(
                    out=vhato[:, b:b + 128], in_=vhat[:, kv * 128:(kv + 1) * 128]),
                  reads=[vhat], writes=[vhato])
        p.pop_scope()
        p.barrier()

        p.push_scope()
        expSA = p.sb("expSA", (128, 8 * 512), BF16)
        expSB = p.sb("expSB", (128, 8 * 512), BF16)
        maskb = p.sb("maskb", (128, 16 * 512), BF16)
        atm = [p.sb(f"atm{i}", (128, D), BF16) for i in range(4)]
        wob = p.sb("wob", (128, 16 * 512), BF16)
        otile = [p.sb(f"otile_a{i}", (128, D)) for i in range(4)]
        avp = p.sb("avp", (128, 132))
        xtmp = p.sb("xtmp_a", (128, D))
        xnew = p.sb("xnew_a", (128, D))
        for c in range(2):
            p.dma("scalar", maskb[:].rearrange("p (j q) -> p j q", j=16),
                  dram["masks"][c, :, :, :].rearrange("j pp q -> pp j q"))
            njt = 4 * c + 12
            for h in range(H):
                kv = h // 4
                for j in range(njt):
                    bank = psf[j % 4]
                    ebuf = expSA if j % 2 == 0 else expSB
                    esl = ebuf[:, (j // 2) * 512:(j // 2 + 1) * 512]
                    E("tensor", lambda e, b=bank, h=h, j=j, kv=kv, c=c: e.matmul(
                        b[:, 0:512],
                        khatT[:, kv * T + j * 128: kv * T + (j + 1) * 128],
                        qhatT[:, h * 1024 + c * 512: h * 1024 + (c + 1) * 512],
                        start=True, stop=True),
                      reads=[khatT, qhatT], writes=[bank])
                    E("scalar", lambda e, b=bank, o=esl: e.activation(
                        out=o, in_=b[:, 0:512], func=AF.Exp, scale=ASCALE),
                      reads=[bank], writes=[ebuf])
                    E("vector", lambda e, o=esl, j=j: e.tensor_tensor(
                        out=o, in0=o, in1=maskb[:, j * 512:(j + 1) * 512], op=ALU.mult),
                      reads=[ebuf, maskb], writes=[ebuf])
                for il in range(4):
                    i = c * 4 + il
                    bank = psf[4 + (il % 2)]
                    for j in range(i + 9):
                        ebuf = expSA if j % 2 == 0 else expSB
                        E("tensor", lambda e, b=bank, eb=ebuf, j=j, il=il, kv=kv, i=i: e.matmul(
                            b[:, 0:132],
                            eb[:, (j // 2) * 512 + il * 128: (j // 2) * 512 + (il + 1) * 128],
                            vhato[:, (j * HKV + kv) * 132:(j * HKV + kv) * 132 + 132],
                            start=(j == 0), stop=(j == i + 8)),
                          reads=[ebuf, vhato], writes=[bank])
                    E("vector", lambda e, b=bank: e.tensor_copy(out=avp[:], in_=b[:, 0:132]),
                      reads=[bank], writes=[avp])
                    divcol(atm[il][:, h * 128:(h + 1) * 128], avp[:, 0:128],
                           avp[:, 128:129], [avp], [atm[il]])
            for il in range(4):
                for dh in range(2):
                    trans8(atm[il][:, dh * 1024:(dh + 1) * 1024],
                           [atm[il][:, (dh * 8 + k) * 128:(dh * 8 + k + 1) * 128]
                            for k in range(8)])
            for dc in range(4):
                p.dma("gpsimd", wob[:].rearrange("p (dt q) -> p dt q", dt=16),
                      dram["woT"][:, dc * 512:(dc + 1) * 512].rearrange(
                          "(dt pp) q -> pp dt q", pp=128))
                for il in range(4):
                    bank = psf[il % 4]
                    for ft in range(16):
                        E("tensor", lambda e, b=bank, ft=ft, il=il: e.matmul(
                            b[:, 0:512], atm[il][:, ft * 128:(ft + 1) * 128],
                            wob[:, ft * 512:(ft + 1) * 512],
                            start=(ft == 0), stop=(ft == 15)),
                          reads=[atm[il], wob], writes=[bank])
                    E("vector", lambda e, b=bank, il=il, dc=dc: e.tensor_copy(
                        out=otile[il][:, dc * 512:(dc + 1) * 512], in_=b[:, 0:512]),
                      reads=[bank], writes=[otile[il]])
            for il in range(4):
                tt = c * 4 + il
                sandwich(otile[il][:], x_own, out_dram, tt, xtmp[:], xnew[:])
        p.pop_scope()
        p.pop_scope()
        p.pop_scope()
        p.barrier()

    def l0_ffn():
        out_dram = y_out if UPTO == 2 else xr2
        xin = xr1 if UPTO >= 1 else x_own
        p.push_scope()
        p.push_scope()
        wbt = p.sb("f_wbt", (128, FFN_H), BF16)
        absb = p.sb("f_absb", (128, FFN_H), BF16)
        sgn = p.sb("f_sgn", (128, FFN_H), BF16)
        w1t = p.sb("f_w1t", (128, FFN_H), BF16)
        srow = p.sb("f_srow", (2, FFN_H), BF16)
        hrow = p.sb("f_hrow", (2, FFN_H), BF16)

        def tern_quant(src_dram, n_tiles, width, store_fn):
            nch = width // 512
            for it_ in range(n_tiles):
                p.dma("gpsimd", wbt[:, 0:width], src_dram[it_ * 128:(it_ + 1) * 128, :])
                E("scalar", lambda e, w=width: e.activation(
                    out=absb[:, 0:w], in_=wbt[:, 0:w], func=AF.Abs),
                  reads=[wbt], writes=[absb])
                E("scalar", lambda e, w=width: e.sign(out=sgn[:, 0:w], in_=wbt[:, 0:w]),
                  reads=[wbt], writes=[sgn])
                for ch in range(nch):
                    E("tensor", lambda e, ch=ch: e.matmul(
                        psf[2][0:2, 0:512], e64[:], absb[:, ch * 512:(ch + 1) * 512],
                        start=True, stop=True),
                      reads=[absb, e64], writes=[psf[2]])
                    E("vector", lambda e, ch=ch: e.tensor_scalar(
                        out=srow[:, ch * 512:(ch + 1) * 512],
                        in0=psf[2][0:2, 0:512], scalar1=1.0 / 64.0, scalar2=1e-8,
                        op0=ALU.mult, op1=ALU.max),
                      reads=[psf[2]], writes=[srow])
                    E("vector", lambda e, ch=ch: e.tensor_scalar(
                        out=hrow[:, ch * 512:(ch + 1) * 512],
                        in0=srow[:, ch * 512:(ch + 1) * 512],
                        scalar1=0.5, scalar2=None, op0=ALU.mult),
                      reads=[srow], writes=[hrow])
                    E("tensor", lambda e, ch=ch: e.matmul(
                        psf[3][:, 0:512], sel2[:], srow[:, ch * 512:(ch + 1) * 512],
                        start=True, stop=True),
                      reads=[sel2, srow], writes=[psf[3]])
                    E("tensor", lambda e, ch=ch: e.matmul(
                        psf[4][:, 0:512], sel2[:], hrow[:, ch * 512:(ch + 1) * 512],
                        start=True, stop=True),
                      reads=[sel2, hrow], writes=[psf[4]])
                    E("vector", lambda e, ch=ch: e.tensor_tensor(
                        out=w1t[:, ch * 512:(ch + 1) * 512],
                        in0=absb[:, ch * 512:(ch + 1) * 512],
                        in1=psf[4][:, 0:512], op=ALU.is_gt),
                      reads=[absb, psf[4]], writes=[w1t])
                    E("vector", lambda e, ch=ch: e.tensor_tensor(
                        out=w1t[:, ch * 512:(ch + 1) * 512],
                        in0=w1t[:, ch * 512:(ch + 1) * 512],
                        in1=psf[3][:, 0:512], op=ALU.mult),
                      reads=[w1t, psf[3]], writes=[w1t])
                    E("vector", lambda e, ch=ch: e.tensor_tensor(
                        out=w1t[:, ch * 512:(ch + 1) * 512],
                        in0=w1t[:, ch * 512:(ch + 1) * 512],
                        in1=sgn[:, ch * 512:(ch + 1) * 512], op=ALU.mult),
                      reads=[w1t, sgn], writes=[w1t])
                store_fn(it_)

        def store_fc(dt):
            p.dma("sync", w1fc_s[:, dt, :, :].rearrange("hc pp cc -> pp hc cc"),
                  w1t[:].rearrange("p (hc cc) -> p hc cc", hc=64))

        def store_pr(ht):
            p.dma("sync", w1pr_s[:, ht, :, :].rearrange("dc pp cc -> pp dc cc"),
                  w1t[:, 0:D].rearrange("p (dc cc) -> p dc cc", dc=16))

        tern_quant(dram["wfcT"], 16, FFN_H, store_fc)
        tern_quant(dram["wprojT"], 64, D, store_pr)
        p.pop_scope()
        p.barrier()

        p.push_scope()
        xn2T = p.sb("xn2T", (128, 16 * 1024), BF16)
        p.push_scope()
        xtl = p.sb("f_xtl", (128, 2 * D))
        xnb = p.sb("f_xnb", (128, D), BF16)
        for tt in range(NT):
            xt = xtl[:, (tt % 2) * D:(tt % 2 + 1) * D]
            p.dma("sync", xt, xin[tt * 128:(tt + 1) * 128, :])
            ln_stats(xt, D, extra_reads=[xtl])
            ln_cols(nmc, stc)
            adddiv(xnb[:], xt, nmc, stc, [xtl], [xnb])
            for dt in range(16):
                base2 = (tt // 4) * 8192 + dt * 512 + (tt % 4) * 128
                trans(xn2T[:, base2:base2 + 128], xnb[:, dt * 128:(dt + 1) * 128])
        p.pop_scope()

        hqT = p.sb("hqT", (128, 64 * 512), BF16)
        w1sl = p.sb("w1sl", (128, 4 * 2048), BF16)
        w2sl = p.sb("w2sl", (128, 2 * 8192), BF16)
        rbuf = p.sb("f_rbuf", (128, 4 * 512), BF16)
        hq2 = p.sb("f_hq2", (128, 512), BF16)
        stdr = p.sb("f_stdr", (1, 512))
        outTb = p.sb("f_outTb", (128, 512), BF16)
        otile4 = p.sb("f_otile4", (128, 4 * D), BF16)
        xtmp = p.sb("f_xtmp", (128, D))
        xnew = p.sb("f_xnew", (128, D))
        for tc in range(2):
            for hc in range(64):
                sl = w1sl[:, (hc % 4) * 2048:(hc % 4 + 1) * 2048]
                p.dma("sync", sl.rearrange("p (dt cc) -> p dt cc", dt=16),
                      w1fc_s[hc, :, :, :].rearrange("dt pp cc -> pp dt cc"))
                bank = psf[(0, 1, 4, 5)[hc % 4]]
                for dt in range(16):
                    E("tensor", lambda e, b=bank, sl=sl, dt=dt, tc=tc: e.matmul(
                        b[:, 0:512], sl[:, dt * 128:(dt + 1) * 128],
                        xn2T[:, tc * 8192 + dt * 512: tc * 8192 + (dt + 1) * 512],
                        start=(dt == 0), stop=(dt == 15)),
                      reads=[w1sl, xn2T], writes=[bank])
                rb = rbuf[:, (hc % 4) * 512:(hc % 4 + 1) * 512]
                E("scalar", lambda e, b=bank, rb=rb: e.activation(
                    out=rb, in_=b[:, 0:512], func=AF.Relu),
                  reads=[bank], writes=[rbuf])
                E("vector", lambda e, hc=hc, rb=rb: e.tensor_tensor(
                    out=hqT[:, hc * 512:(hc + 1) * 512], in0=rb, in1=rb,
                    op=ALU.mult),
                  reads=[rbuf], writes=[hqT])
                E("vector", lambda e, hc=hc: e.tensor_tensor(
                    out=hq2[:], in0=hqT[:, hc * 512:(hc + 1) * 512],
                    in1=hqT[:, hc * 512:(hc + 1) * 512], op=ALU.mult),
                  reads=[hqT], writes=[hq2])
                E("tensor", lambda e, hc=hc: e.matmul(
                    psf[2][0:1, 0:512], onesc[:], hq2[:],
                    start=(hc == 0), stop=(hc == 63)),
                  reads=[onesc, hq2], writes=[psf[2]])
            E("vector", lambda e: e.tensor_scalar(
                out=stdr[:], in0=psf[2][0:1, 0:512], scalar1=1.0 / FFN_H,
                scalar2=F32EPS, op0=ALU.mult, op1=ALU.add),
              reads=[psf[2]], writes=[stdr])
            E("scalar", lambda e: e.activation(out=stdr[:], in_=stdr[:], func=AF.Sqrt),
              reads=[stdr], writes=[stdr])
            E("vector", lambda e: e.reciprocal(out=stdr[:], in_=stdr[:]),
              reads=[stdr], writes=[stdr])
            E("tensor", lambda e: e.matmul(
                psf[3][:, 0:512], onesr[0:1, 0:128], stdr[:], start=True, stop=True),
              reads=[onesr, stdr], writes=[psf[3]])
            for hc in range(64):
                E("vector", lambda e, hc=hc: e.tensor_tensor(
                    out=hqT[:, hc * 512:(hc + 1) * 512],
                    in0=hqT[:, hc * 512:(hc + 1) * 512],
                    in1=psf[3][:, 0:512], op=ALU.mult),
                  reads=[hqT, psf[3]], writes=[hqT])
            for dcol in range(16):
                sl = w2sl[:, (dcol % 2) * 8192:(dcol % 2 + 1) * 8192]
                p.dma("sync", sl.rearrange("p (ht cc) -> p ht cc", ht=64),
                      w1pr_s[dcol, :, :, :].rearrange("ht pp cc -> pp ht cc"))
                bank = psf[(0, 1, 4, 5)[dcol % 4]]
                for ht in range(64):
                    E("tensor", lambda e, b=bank, sl=sl, ht=ht: e.matmul(
                        b[:, 0:512], sl[:, ht * 128:(ht + 1) * 128],
                        hqT[:, ht * 512:(ht + 1) * 512],
                        start=(ht == 0), stop=(ht == 63)),
                      reads=[w2sl, hqT], writes=[bank])
                E("vector", lambda e, b=bank: e.tensor_copy(out=outTb[:], in_=b[:, 0:512]),
                  reads=[bank], writes=[outTb])
                for q4 in range(4):
                    trans(otile4[:, q4 * D + dcol * 128: q4 * D + (dcol + 1) * 128],
                          outTb[:, q4 * 128:(q4 + 1) * 128])
            for q4 in range(4):
                tt = tc * 4 + q4
                sandwich(otile4[:, q4 * D:(q4 + 1) * D], xin, out_dram, tt,
                         xtmp[:], xnew[:])
        p.pop_scope()
        p.pop_scope()
        p.barrier()

    def l1_attn():
        out_dram = y_out if UPTO == 3 else xr3
        xin = xr2 if UPTO >= 2 else x_own
        p.push_scope()
        p.push_scope()
        xbcs = p.sb("xbcs", (128, 33 * 1024), BF16)
        dtv = p.sb("dtv", (128, 512))
        dtA = p.sb("dtA", (128, 512))
        ltA = p.sb("ltA", (64, NCH * 256))
        negl = p.sb("negl", (128, 8 * 64))
        wtb = p.sb("wtb", (128, 8 * 64))
        pendA = p.sb("pendA", (128, NCH * 64))
        Btok = p.sb("Btok", (128, 8 * 64), BF16)
        Bt_cm = p.sb("Bt_cm", (64, 1024), BF16)
        Ct_cm = p.sb("Ct_cm", (64, 1024), BF16)
        Sst = p.sb("Sst", (64, 2 * DIN))
        ssdAb = p.sb("ssdAb", (128, 256))
        ssdBb = p.sb("ssdBb", (128, 256))
        ddb = p.sb("ddb", (128, NH))
        aneg = p.sb("aneg", (128, NH))
        dtbb = p.sb("dtbb", (128, NH))
        onesc64 = p.sb("onesc64", (64, 256))
        p.dma("sync", ssdAb[:], dram["ssdA"][:, :])
        p.dma("sync", ssdBb[:], dram["ssdB"][:, :])
        p.dma("sync", ddb[:], dram["dd_rep"][:, :])
        p.dma("sync", dtbb[:], dram["dtb_rep"][:, :])
        p.dma("sync", aneg[:], dram["alog_rep"][:, :])
        E("scalar", lambda e: e.activation(out=aneg[:], in_=aneg[:], func=AF.Exp),
          reads=[aneg], writes=[aneg])
        E("vector", lambda e: e.tensor_scalar(
            out=aneg[:], in0=aneg[:], scalar1=-1.0, scalar2=None, op0=ALU.mult),
          reads=[aneg], writes=[aneg])
        E("gpsimd", lambda e: e.memset(onesc64[:], 1.0), reads=[], writes=[onesc64])
        E("gpsimd", lambda e: e.memset(Sst[:, 0:DIN], 0.0), reads=[], writes=[Sst])

        p.push_scope()
        uT = p.sb("uT", (128, 16 * 1024), BF16)
        p.push_scope()
        xtl = p.sb("m_xtl", (128, D))
        ln3b = p.sb("m_ln3", (128, D))
        ub = p.sb("m_ub", (128, D), BF16)
        rmswb = p.sb("m_rmsw", (128, D))
        p.dma("sync", rmswb[:], dram["rms_w_rep"][:, :])
        for tt in range(NT):
            xt = xtl[:]
            p.dma("sync", xt, xin[tt * 128:(tt + 1) * 128, :])
            ln_stats(xt, D, extra_reads=[xtl])
            ln_cols(nmc, stc)
            E("vector", lambda e, tt=tt: e.tensor_copy(
                out=m3[:, 2 * tt:2 * tt + 1], in_=nmc),
              reads=[cols], writes=[m3])
            E("vector", lambda e, tt=tt: e.tensor_copy(
                out=m3[:, 2 * tt + 1:2 * tt + 2], in_=stc),
              reads=[cols], writes=[m3])
            adddiv(ln3b[:], xt, nmc, stc, [xtl], [ln3b])
            ln_stats(ln3b[:], D)
            rms_std_col(stc, 1e-5)
            divcol(ln3b[:], ln3b[:], stc, [ln3b, cols], [ln3b])
            E("vector", lambda e: e.tensor_tensor(
                out=ub[:], in0=ln3b[:], in1=rmswb[:], op=ALU.mult),
              reads=[ln3b, rmswb], writes=[ub])
            for dh in range(2):
                trans8(uT[:, tt * 2048 + dh * 1024: tt * 2048 + (dh + 1) * 1024],
                       [ub[:, (dh * 8 + k) * 128:(dh * 8 + k + 1) * 128] for k in range(8)])
        p.pop_scope()

        p.push_scope()
        uTr = p.sb("uTr", (128, 16 * 1024), BF16)
        wzs = p.sb("m_wzs", (128, 8192), BF16)
        zb = p.sb("m_zb", (128, 512), BF16)
        h3 = p.sb("m_h3", (128, 3))
        wdts = p.sb("m_wdts", (128, 16 * 64), BF16)
        for tt in range(NT):
            for dt in range(16):
                base2 = (tt // 4) * 8192 + dt * 512 + (tt % 4) * 128
                E("vector", lambda e, b2=base2, tt=tt, dt=dt: e.tensor_copy(
                    out=uTr[:, b2:b2 + 128],
                    in_=uT[:, tt * 2048 + dt * 128: tt * 2048 + (dt + 1) * 128]),
                  reads=[uT], writes=[uTr])
        for zc in range(8):
            sl = wzs[:, 0:8192]
            p.dma("gpsimd", sl.rearrange("p (dt cc) -> p dt cc", dt=16),
                  dram["inprojT"][:, zc * 512:(zc + 1) * 512].rearrange(
                      "(dt pp) cc -> pp dt cc", pp=128))
            for tt in range(NT):
                bank = psf[(0, 1, 4, 5)[tt % 4]]
                for dt in range(16):
                    E("tensor", lambda e, b=bank, sl=sl, dt=dt, tt=tt: e.matmul(
                        b[:, 0:512],
                        uT[:, tt * 2048 + dt * 128: tt * 2048 + (dt + 1) * 128],
                        sl[:, dt * 512:(dt + 1) * 512],
                        start=(dt == 0), stop=(dt == 15)),
                      reads=[uT, wzs], writes=[bank])
                E("vector", lambda e, b=bank: e.tensor_copy(out=zb[:], in_=b[:, 0:512]),
                  reads=[bank], writes=[zb])
                p.dma("gpsimd", zspill[tt * 128:(tt + 1) * 128, zc * 512:(zc + 1) * 512],
                      zb[:])
        for ct in range(33):
            sl = wzs[:, (ct % 4) * 2048:(ct % 4) * 2048 + 2048]
            p.dma("gpsimd", sl.rearrange("p (dt cc) -> p dt cc", dt=16),
                  dram["inprojT"][:, DIN + ct * 128: DIN + (ct + 1) * 128].rearrange(
                      "(dt pp) cc -> pp dt cc", pp=128))
            for tc in range(2):
                bank = psf[2 + tc]
                for dt in range(16):
                    E("tensor", lambda e, b=bank, sl=sl, dt=dt, tc=tc: e.matmul(
                        b[:, 0:512], sl[:, dt * 128:(dt + 1) * 128],
                        uTr[:, tc * 8192 + dt * 512: tc * 8192 + (dt + 1) * 512],
                        start=(dt == 0), stop=(dt == 15)),
                      reads=[uTr, wzs], writes=[bank])
                E("vector", lambda e, b=bank, ct=ct, tc=tc: e.tensor_copy(
                    out=xbcs[:, ct * 1024 + tc * 512: ct * 1024 + (tc + 1) * 512],
                    in_=b[:, 0:512]),
                  reads=[bank], writes=[xbcs])
                if tc == 1:
                    E("vector", lambda e, b=bank: e.tensor_copy(
                        out=h3[:], in_=b[:, 509:512]),
                      reads=[bank], writes=[h3])
                    p.dma("gpsimd", cch_in[ct, :, :], h3[:])
        p.dma("gpsimd", wdts[:].rearrange("p (dt cc) -> p dt cc", dt=16),
              dram["inprojT"][:, 8320:8384].rearrange("(dt pp) cc -> pp dt cc", pp=128))
        for tt in range(NT):
            bank = psf[tt % 2]
            for dt in range(16):
                E("tensor", lambda e, b=bank, dt=dt, tt=tt: e.matmul(
                    b[:, 0:64],
                    uT[:, tt * 2048 + dt * 128: tt * 2048 + (dt + 1) * 128],
                    wdts[:, dt * 64:(dt + 1) * 64],
                    start=(dt == 0), stop=(dt == 15)),
                  reads=[uT, wdts], writes=[bank])
            E("vector", lambda e, b=bank, tt=tt: e.tensor_tensor(
                out=dtv[:, tt * 64:(tt + 1) * 64], in0=b[:, 0:64], in1=dtbb[:],
                op=ALU.add),
              reads=[bank, dtbb], writes=[dtv])
        E("scalar", lambda e: e.activation(out=dtA[:], in_=dtv[:], func=AF.Exp),
          reads=[dtv], writes=[dtA])
        E("scalar", lambda e: e.activation(out=dtv[:], in_=dtA[:], func=AF.Ln, bias=1.0),
          reads=[dtA], writes=[dtv])
        for tt in range(NT):
            E("vector", lambda e, tt=tt: e.tensor_tensor(
                out=dtA[:, tt * 64:(tt + 1) * 64], in0=dtv[:, tt * 64:(tt + 1) * 64],
                in1=aneg[:], op=ALU.mult),
              reads=[dtv, aneg], writes=[dtA])
        p.pop_scope()
        p.pop_scope()
        p.barrier()

        E("gpsimd", lambda e: e.collective_compute(
            "AllGather", ALU.bypass, replica_groups=[[0, 1], [2, 3], [4, 5], [6, 7]],
            ins=[cch_in.opt()], outs=[cch_out.opt()]),
          reads=[cch_in], writes=[cch_out], sem="cc", inc=1)
        p.push_scope()
        pd_ = p.sb("m_pd", (128, 1027))
        acc1 = p.sb("m_acc1", (128, 1024))
        acc2 = p.sb("m_acc2", (128, 1024))
        cwb = p.sb("m_cwb", (128, 4))
        cbb = p.sb("m_cbb", (128, 1))
        h3r = p.sb("m_h3r", (128, 3))
        for ct in range(33):
            p.dma("sync", cwb[:], dram["conv_w"][ct * 128:(ct + 1) * 128, :])
            p.dma("sync", cbb[:], dram["conv_b"][ct * 128:(ct + 1) * 128, :])
            p.dma("sync", h3r[:], cch_out[0, ct, :, :])
            E("vector", lambda e: e.tensor_scalar(
                out=pd_[:, 0:3], in0=h3r[:], scalar1=hzc[:], scalar2=None, op0=ALU.mult),
              reads=[h3r, hzc], writes=[pd_])
            E("vector", lambda e, ct=ct: e.tensor_copy(
                out=pd_[:, 3:1027], in_=xbcs[:, ct * 1024:(ct + 1) * 1024]),
              reads=[xbcs], writes=[pd_])
            E("vector", lambda e: e.tensor_scalar(
                out=acc1[:], in0=pd_[:, 0:1024], scalar1=cwb[:, 0:1], scalar2=None,
                op0=ALU.mult),
              reads=[pd_, cwb], writes=[acc1])
            E("vector", lambda e: e.scalar_tensor_tensor(
                out=acc2[:], in0=pd_[:, 1:1025], scalar=cwb[:, 1:2], in1=acc1[:],
                op0=ALU.mult, op1=ALU.add),
              reads=[pd_, cwb, acc1], writes=[acc2])
            E("vector", lambda e: e.scalar_tensor_tensor(
                out=acc1[:], in0=pd_[:, 2:1026], scalar=cwb[:, 2:3], in1=acc2[:],
                op0=ALU.mult, op1=ALU.add),
              reads=[pd_, cwb, acc2], writes=[acc1])
            E("vector", lambda e: e.scalar_tensor_tensor(
                out=acc2[:], in0=pd_[:, 3:1027], scalar=cwb[:, 3:4], in1=acc1[:],
                op0=ALU.mult, op1=ALU.add),
              reads=[pd_, cwb, acc1], writes=[acc2])
            E("scalar", lambda e, ct=ct: e.activation(
                out=xbcs[:, ct * 1024:(ct + 1) * 1024], in_=acc2[:], func=AF.Silu,
                bias=cbb[:]),
              reads=[acc2, cbb], writes=[xbcs])
        p.dma("gpsimd", Bt_cm[:, :], xbcs[0:64, 32 * 1024:33 * 1024])
        p.dma("gpsimd", Ct_cm[:, :], xbcs[64:128, 32 * 1024:33 * 1024])
        p.pop_scope()
        p.barrier()

        p.push_scope()
        dtt = p.sb("m_dtt", (64, 256))
        ltok = p.sb("m_ltok", (128, 128))
        lrow = p.sb("m_lrow", (1, 64))
        for ic in range(NCH):
            for jt in range(2):
                trans(dtt[:, jt * 128:(jt + 1) * 128],
                      dtA[:, (2 * ic + jt) * 64:(2 * ic + jt + 1) * 64],
                      pdim=128, fdim=64, f32=True)
            E("vector", lambda e, ic=ic: e.tensor_tensor_scan(
                out=ltA[:, ic * 256:(ic + 1) * 256], data0=onesc64[:], data1=dtt[:],
                initial=0.0, op0=ALU.mult, op1=ALU.add),
              reads=[onesc64, dtt], writes=[ltA])
            for jt in range(2):
                trans(ltok[:, jt * 64:(jt + 1) * 64],
                      ltA[:, ic * 256 + jt * 128: ic * 256 + (jt + 1) * 128],
                      pdim=64, fdim=128, f32=True)
            E("vector", lambda e, ic=ic: e.tensor_scalar(
                out=negl[:, ic * 128:(ic + 1) * 128], in0=ltok[:], scalar1=-1.0,
                scalar2=None, op0=ALU.mult),
              reads=[ltok], writes=[negl])
            p.dma("gpsimd", lrow[:], ltok[127:128, 64:128])
            E("tensor", lambda e, ic=ic: e.matmul(
                psf[4][:, 0:64], onesr[0:1, 0:128], lrow[0:1, 0:64],
                start=True, stop=True),
              reads=[onesr, lrow], writes=[psf[4]])
            E("scalar", lambda e, ic=ic: e.activation(
                out=pendA[:, ic * 64:(ic + 1) * 64], in_=psf[4][:, 0:64], func=AF.Exp),
              reads=[psf[4]], writes=[pendA])
            for jt in range(2):
                E("vector", lambda e, ic=ic, jt=jt: e.tensor_tensor(
                    out=wtb[:, (2 * ic + jt) * 64:(2 * ic + jt + 1) * 64],
                    in0=psf[4][:, 0:64], in1=ltok[:, jt * 64:(jt + 1) * 64],
                    op=ALU.subtract),
                  reads=[psf[4], ltok], writes=[wtb])
            E("scalar", lambda e, ic=ic: e.activation(
                out=wtb[:, 2 * ic * 64:(2 * ic + 2) * 64],
                in_=wtb[:, 2 * ic * 64:(2 * ic + 2) * 64], func=AF.Exp),
              reads=[wtb], writes=[wtb])
            E("vector", lambda e, ic=ic: e.tensor_tensor(
                out=wtb[:, 2 * ic * 64:(2 * ic + 2) * 64],
                in0=wtb[:, 2 * ic * 64:(2 * ic + 2) * 64],
                in1=dtv[:, 2 * ic * 64:(2 * ic + 2) * 64], op=ALU.mult),
              reads=[wtb, dtv], writes=[wtb])
            for jt in range(2):
                trans(Btok[:, (2 * ic + jt) * 64:(2 * ic + jt + 1) * 64],
                      Bt_cm[:, ic * 256 + jt * 128: ic * 256 + (jt + 1) * 128],
                      pdim=64, fdim=128)
        p.pop_scope()
        p.barrier()

        p.push_scope()
        xtk = p.sb("m_xtk", (128, 2 * DIN), BF16)
        bwb = p.sb("m_bwb", (128, 64), BF16)

        def state_chunk(ic, rd, wr):
            for h in range(NH):
                for jt in range(2):
                    E("vector", lambda e, h=h, jt=jt, ic=ic: e.tensor_scalar(
                        out=bwb[:], in0=Btok[:, (2 * ic + jt) * 64:(2 * ic + jt + 1) * 64],
                        scalar1=wtb[:, (2 * ic + jt) * 64 + h:(2 * ic + jt) * 64 + h + 1],
                        scalar2=None, op0=ALU.mult),
                      reads=[Btok, wtb], writes=[bwb])
                    E("tensor", lambda e, h=h, jt=jt: e.matmul(
                        psf[5][0:64, 0:64], bwb[:],
                        xtk[:, jt * DIN + h * 64: jt * DIN + (h + 1) * 64],
                        start=(jt == 0), stop=(jt == 1)),
                      reads=[bwb, xtk], writes=[psf[5]])
                E("vector", lambda e, h=h, ic=ic, rd=rd, wr=wr: e.scalar_tensor_tensor(
                    out=Sst[:, wr * DIN + h * 64: wr * DIN + (h + 1) * 64],
                    in0=Sst[:, rd * DIN + h * 64: rd * DIN + (h + 1) * 64],
                    scalar=pendA[0:64, ic * 64 + h: ic * 64 + h + 1],
                    in1=psf[5][0:64, 0:64], op0=ALU.mult, op1=ALU.add),
                  reads=[Sst, pendA, psf[5]], writes=[Sst])

        def xtok_chunk(ic):
            for jt in range(2):
                for g in range(4):
                    trans8(xtk[:, jt * DIN + g * 1024: jt * DIN + (g + 1) * 1024],
                           [xbcs[:, (g * 8 + k) * 1024 + ic * 256 + jt * 128:
                                 (g * 8 + k) * 1024 + ic * 256 + (jt + 1) * 128]
                            for k in range(8)])

        for ic in range(NCH):
            xtok_chunk(ic)
            state_chunk(ic, ic % 2, (ic + 1) % 2)
        fin = NCH % 2
        p.dma("gpsimd", cst_in[:, :], Sst[:, fin * DIN:(fin + 1) * DIN])
        E("gpsimd", lambda e: e.collective_compute(
            "AllGather", ALU.bypass, replica_groups=[[0, 1], [2, 3], [4, 5], [6, 7]],
            ins=[cst_in.opt()], outs=[cst_out.opt()]),
          reads=[cst_in], writes=[cst_out], sem="cc", inc=1)
        p.dma("sync", Sst[:, 0:DIN], cst_out[0, :, :])
        E("vector", lambda e: e.tensor_scalar(
            out=Sst[:, 0:DIN], in0=Sst[:, 0:DIN], scalar1=hzc[0:64, :], scalar2=None,
            op0=ALU.mult),
          reads=[Sst, hzc], writes=[Sst])
        p.barrier()

        Gb = p.sb("m_Gb", (128, 2 * 256), BF16)
        cbG = p.sb("m_cbb2", (128, 2 * 256))
        lrow2 = p.sb("m_lrow2", (1, 256))
        tmpG = p.sb("m_tmpG", (128, 2 * 256))
        Eb = p.sb("m_Eb", (128, 2 * 256))
        ebc = p.sb("m_ebc", (64, 256))
        chat = p.sb("m_chat", (64, 256))
        ybuf = p.sb("m_ybuf", (128, 2 * DIN), BF16)
        zbl = p.sb("m_zbl", (128, DIN), BF16)
        szb = p.sb("m_szb", (128, DIN), BF16)
        ygb = p.sb("m_ygb", (128, DIN), BF16)
        ygT = p.sb("m_ygT", (128, 32 * 128), BF16)
        nwb = p.sb("m_nwb", (128, DIN), BF16)
        p.dma("gpsimd", nwb[:], dram["normw_rep"][:, :])
        for ic in range(NCH):
            rd = ic % 2
            wr = (ic + 1) % 2
            xtok_chunk(ic)
            for jt in range(2):
                bank = psf[jt]
                E("tensor", lambda e, b=bank, ic=ic, jt=jt: e.matmul(
                    b[:, 0:256], Bt_cm[:, ic * 256 + jt * 128: ic * 256 + (jt + 1) * 128],
                    Ct_cm[:, ic * 256:(ic + 1) * 256], start=True, stop=True),
                  reads=[Bt_cm, Ct_cm], writes=[bank])
                E("vector", lambda e, b=bank, jt=jt: e.tensor_copy(
                    out=cbG[:, jt * 256:(jt + 1) * 256], in_=b[:, 0:256]),
                  reads=[bank], writes=[cbG])
            for h in range(NH):
                for jt in range(2):
                    bank2 = psf[2 + jt]
                    if jt == 0:
                        p.dma("gpsimd", lrow2[:], ltA[h:h + 1, ic * 256:(ic + 1) * 256])
                    E("tensor", lambda e, b=bank2, ic=ic, h=h: e.matmul(
                        b[:, 0:256], onesr[0:1, 0:128], lrow2[0:1, 0:256],
                        start=True, stop=True),
                      reads=[onesr, lrow2], writes=[bank2])
                    ssd = ssdAb if jt == 0 else ssdBb
                    E("vector", lambda e, b=bank2, jt=jt, ssd=ssd: e.tensor_tensor(
                        out=tmpG[:, jt * 256:(jt + 1) * 256], in0=b[:, 0:256], in1=ssd[:],
                        op=ALU.add),
                      reads=[bank2, ssd], writes=[tmpG])
                    E("scalar", lambda e, jt=jt, ic=ic, h=h: e.activation(
                        out=Eb[:, jt * 256:(jt + 1) * 256],
                        in_=tmpG[:, jt * 256:(jt + 1) * 256], func=AF.Exp,
                        bias=negl[:, ic * 128 + jt * 64 + h: ic * 128 + jt * 64 + h + 1]),
                      reads=[tmpG, negl], writes=[Eb])
                    E("vector", lambda e, jt=jt, ic=ic, h=h: e.scalar_tensor_tensor(
                        out=Gb[:, jt * 256:(jt + 1) * 256], in0=cbG[:, jt * 256:(jt + 1) * 256],
                        scalar=dtv[:, (2 * ic + jt) * 64 + h:(2 * ic + jt) * 64 + h + 1],
                        in1=Eb[:, jt * 256:(jt + 1) * 256], op0=ALU.mult, op1=ALU.mult),
                      reads=[cbG, dtv, Eb], writes=[Gb])
                E("scalar", lambda e: e.activation(
                    out=ebc[:], in_=psf[2][0:64, 0:256], func=AF.Exp),
                  reads=[psf[2]], writes=[ebc])
                E("vector", lambda e, ic=ic: e.tensor_tensor(
                    out=chat[:], in0=Ct_cm[:, ic * 256:(ic + 1) * 256], in1=ebc[:],
                    op=ALU.mult),
                  reads=[Ct_cm, ebc], writes=[chat])
                for it in range(2):
                    bank = psf[4]
                    for jt in range(it + 1):
                        E("tensor", lambda e, b=bank, jt=jt, it=it, h=h: e.matmul(
                            b[:, 0:64], Gb[:, jt * 256 + it * 128: jt * 256 + (it + 1) * 128],
                            xtk[:, jt * DIN + h * 64: jt * DIN + (h + 1) * 64],
                            start=(jt == 0), stop=False),
                          reads=[Gb, xtk], writes=[bank])
                    E("tensor", lambda e, b=bank, it=it, h=h, rd=rd: e.matmul(
                        b[:, 0:64], chat[0:64, it * 128:(it + 1) * 128],
                        Sst[:, rd * DIN + h * 64: rd * DIN + (h + 1) * 64],
                        start=False, stop=True),
                      reads=[chat, Sst], writes=[bank])
                    E("vector", lambda e, b=bank, it=it, h=h: e.scalar_tensor_tensor(
                        out=ybuf[:, it * DIN + h * 64: it * DIN + (h + 1) * 64],
                        in0=xtk[:, it * DIN + h * 64: it * DIN + (h + 1) * 64],
                        scalar=ddb[:, h:h + 1], in1=b[:, 0:64],
                        op0=ALU.mult, op1=ALU.add),
                      reads=[xtk, ddb, bank], writes=[ybuf])
            state_chunk(ic, rd, wr)
            for it in range(2):
                tile = 2 * ic + it
                p.dma("sync", zbl[:], zspill[tile * 128:(tile + 1) * 128, :])
                E("scalar", lambda e: e.activation(out=szb[:], in_=zbl[:], func=AF.Silu),
                  reads=[zbl], writes=[szb])
                E("vector", lambda e, it=it: e.tensor_tensor(
                    out=ygb[:], in0=ybuf[:, it * DIN:(it + 1) * DIN], in1=szb[:],
                    op=ALU.mult),
                  reads=[ybuf, szb], writes=[ygb])
                ln_stats(ygb[:], DIN)
                rms_std_col(stc, 1e-5)
                divcol(ygb[:], ygb[:], stc, [ygb, cols], [ygb])
                E("vector", lambda e: e.tensor_tensor(
                    out=ygb[:], in0=ygb[:], in1=nwb[:], op=ALU.mult),
                  reads=[ygb, nwb], writes=[ygb])
                for g in range(4):
                    trans8(ygT[:, g * 1024:(g + 1) * 1024],
                           [ygb[:, (g * 8 + k) * 128:(g * 8 + k + 1) * 128]
                            for k in range(8)])
                p.dma("gpsimd", ygsp[tile, :, :, :].rearrange("et pp cc -> pp et cc"),
                      ygT[:].rearrange("p (et cc) -> p et cc", et=32))
        p.pop_scope()
        p.pop_scope()
        p.barrier()

        p.push_scope()
        ygl = p.sb("m_ygl", (128, 2 * 4096), BF16)
        wop = p.sb("m_wop", (128, 32 * 512), BF16)
        otfull = p.sb("m_otfull", (128, 8 * D))
        oti = p.sb("m_oti", (128, D))
        xtmp = p.sb("m_xtmp", (128, D))
        xnew = p.sb("m_xnew", (128, D))
        for dc in range(4):
            p.dma("gpsimd", wop[:].rearrange("p (et cc) -> p et cc", et=32),
                  dram["outprojT"][:, dc * 512:(dc + 1) * 512].rearrange(
                      "(et pp) cc -> pp et cc", pp=128))
            for tt in range(NT):
                sl = ygl[:, (tt % 2) * 4096:(tt % 2 + 1) * 4096]
                p.dma("sync", sl.rearrange("p (et cc) -> p et cc", et=32),
                      ygsp[tt, :, :, :].rearrange("et pp cc -> pp et cc"))
                bank = psf[(0, 1, 4, 5)[tt % 4]]
                for et in range(32):
                    E("tensor", lambda e, b=bank, sl=sl, et=et: e.matmul(
                        b[:, 0:512], sl[:, et * 128:(et + 1) * 128],
                        wop[:, et * 512:(et + 1) * 512],
                        start=(et == 0), stop=(et == 31)),
                      reads=[ygl, wop], writes=[bank])
                E("vector", lambda e, b=bank, tt=tt, dc=dc: e.tensor_copy(
                    out=otfull[:, tt * D + dc * 512: tt * D + (dc + 1) * 512],
                    in_=b[:, 0:512]),
                  reads=[bank], writes=[otfull])
        for tt in range(NT):
            p.dma("sync", xtmp[:], xin[tt * 128:(tt + 1) * 128, :])
            E("vector", lambda e, tt=tt: e.tensor_scalar(
                out=xnew[:], in0=xtmp[:], scalar1=m3[:, 2 * tt:2 * tt + 1],
                scalar2=None, op0=ALU.add),
              reads=[xtmp, m3], writes=[xnew])
            divcol(xnew[:], xnew[:], m3[:, 2 * tt + 1:2 * tt + 2], [xnew, m3], [xnew])
            E("vector", lambda e, tt=tt: e.tensor_tensor(
                out=oti[:], in0=otfull[:, tt * D:(tt + 1) * D], in1=xnew[:], op=ALU.add),
              reads=[otfull, xnew], writes=[oti])
            sandwich(oti[:], xin, out_dram, tt, xtmp[:], xnew[:])
        p.pop_scope()
        p.pop_scope()
        p.barrier()

    def l1_ffn():
        out_dram = y_out
        xin = (xr2 if SKIP3 else xr3) if UPTO >= 3 else x_own
        p.push_scope()
        XT = p.sb("l_XT", (128, 16 * 1024), BF16)
        p.push_scope()
        xtl = p.sb("l_xtl", (128, 2 * D))
        xnb = p.sb("l_xnb", (128, D), BF16)
        for tt in range(NT):
            xt = xtl[:, (tt % 2) * D:(tt % 2 + 1) * D]
            p.dma("sync", xt, xin[tt * 128:(tt + 1) * 128, :])
            ln_stats(xt, D, extra_reads=[xtl])
            ln_cols(nmc, stc)
            adddiv(xnb[:], xt, nmc, stc, [xtl], [xnb])
            for dt in range(16):
                base2 = (tt // 4) * 8192 + dt * 512 + (tt % 4) * 128
                trans(XT[:, base2:base2 + 128], xnb[:, dt * 128:(dt + 1) * 128])
        p.pop_scope()

        DhT = p.sb("l_DhT", (128, 64 * 512), BF16)
        w0sl = p.sb("l_w0sl", (128, 4 * 2048), BF16)
        w1sl = p.sb("l_w1sl", (128, 2 * 8192), BF16)
        b0c = p.sb("l_b0c", (128, 2))
        outTb = p.sb("l_outTb", (128, 512), BF16)
        otile4 = p.sb("l_otile4", (128, 4 * D), BF16)
        b1b = p.sb("l_b1b", (128, D))
        otf32 = p.sb("l_otf32", (128, D))
        otf16 = p.sb("l_otf16", (128, D), F16)
        xtmp = p.sb("l_xtmp", (128, D))
        xnew = p.sb("l_xnew", (128, D))
        p.dma("sync", b1b[:], dram["bias1_rep"][:, :])
        for tc in range(2):
            for hc in range(64):
                sl = w0sl[:, (hc % 4) * 2048:(hc % 4 + 1) * 2048]
                p.dma("gpsimd", sl.rearrange("p (dt cc) -> p dt cc", dt=16),
                      dram["w0T"][:, hc * 128:(hc + 1) * 128].rearrange(
                          "(dt pp) cc -> pp dt cc", pp=128))
                bc = b0c[:, hc % 2: hc % 2 + 1]
                p.dma("sync", bc, dram["bias0"][hc * 128:(hc + 1) * 128, :])
                bank = psf[(0, 1, 4, 5)[hc % 4]]
                for dt in range(16):
                    E("tensor", lambda e, b=bank, sl=sl, dt=dt, tc=tc: e.matmul(
                        b[:, 0:512], sl[:, dt * 128:(dt + 1) * 128],
                        XT[:, tc * 8192 + dt * 512: tc * 8192 + (dt + 1) * 512],
                        start=(dt == 0), stop=(dt == 15)),
                      reads=[w0sl, XT], writes=[bank])
                E("scalar", lambda e, b=bank, hc=hc, bc=bc: e.activation(
                    out=DhT[:, hc * 512:(hc + 1) * 512], in_=b[:, 0:512],
                    func=AF.Gelu, bias=bc),
                  reads=[bank, b0c], writes=[DhT])
            for dcol in range(16):
                sl = w1sl[:, (dcol % 2) * 8192:(dcol % 2 + 1) * 8192]
                p.dma("gpsimd", sl.rearrange("p (ht cc) -> p ht cc", ht=64),
                      dram["w1T"][:, dcol * 128:(dcol + 1) * 128].rearrange(
                          "(ht pp) cc -> pp ht cc", pp=128))
                bank = psf[(2, 3, 0, 1)[dcol % 4]]
                for ht in range(64):
                    E("tensor", lambda e, b=bank, sl=sl, ht=ht: e.matmul(
                        b[:, 0:512], sl[:, ht * 128:(ht + 1) * 128],
                        DhT[:, ht * 512:(ht + 1) * 512],
                        start=(ht == 0), stop=(ht == 63)),
                      reads=[w1sl, DhT], writes=[bank])
                E("vector", lambda e, b=bank: e.tensor_copy(out=outTb[:], in_=b[:, 0:512]),
                  reads=[bank], writes=[outTb])
                for q4 in range(4):
                    trans(otile4[:, q4 * D + dcol * 128: q4 * D + (dcol + 1) * 128],
                          outTb[:, q4 * 128:(q4 + 1) * 128])
            for q4 in range(4):
                tt = tc * 4 + q4
                E("vector", lambda e, q4=q4: e.tensor_tensor(
                    out=otf32[:], in0=otile4[:, q4 * D:(q4 + 1) * D], in1=b1b[:],
                    op=ALU.add),
                  reads=[otile4, b1b], writes=[otf32])
                E("vector", lambda e: e.tensor_copy(out=otf16[:], in_=otf32[:]),
                  reads=[otf32], writes=[otf16])
                E("vector", lambda e: e.tensor_copy(out=otf32[:], in_=otf16[:]),
                  reads=[otf16], writes=[otf32])
                sandwich(otf32[:], xin, out_dram, tt, xtmp[:], xnew[:])
        p.pop_scope()
        p.barrier()

    if UPTO >= 1:
        l0_attn()
    if UPTO >= 2:
        l0_ffn()
    if UPTO >= 3 and not SKIP3:
        l1_attn()
    if UPTO >= 4:
        l1_ffn()
    if UPTO == 0:
        p.push_scope()
        xt0 = p.sb("xt0", (128, D))
        for tt in range(NT):
            p.dma("sync", xt0[:], x_own[tt * 128:(tt + 1) * 128, :])
            p.dma("gpsimd", y_out[tt * 128:(tt + 1) * 128, :], xt0[:])
        p.pop_scope()
    elif UPTO < 4:
        pass

    p.barrier()
    p.finalize()
    return nc


def _in_map(g, c, x, core):
    pr, hi = core // 2, core % 2
    m = dict(g)
    m["x_ctx"] = np.ascontiguousarray(x[pr])
    m["x_own"] = np.ascontiguousarray(x[pr, hi * 1024:(hi + 1) * 1024])
    m["cos_ctx"] = c["cos_ctx"]; m["sinA_ctx"] = c["sinA_ctx"]
    m["cos_own"] = c[f"cos_own_{hi}"]; m["sinA_own"] = c[f"sinA_own_{hi}"]
    m["masks"] = c[f"masks_{hi}"]; m["hz"] = c[f"hz_{hi}"]
    m["ident"] = c["ident"]; m["onesr"] = c["onesr"]
    m["E64"] = c["E64"]; m["sel2"] = c["sel2"]
    m["ssdA"] = c["ssdA"]; m["ssdB"] = c["ssdB"]
    return m


def kernel(**inputs):
    g = _prep_inputs(inputs)
    c = _host_consts()
    x = np.asarray(inputs["L0_attn_x"])
    nc = build_program()
    in_maps = [_in_map(g, c, x, core) for core in range(8)]
    res = run_bass_kernel_spmd(nc, in_maps, core_ids=list(range(8)))
    out = np.empty((4, 2048, 2048), dtype=np.float32)
    for core in range(8):
        pr, hi = core // 2, core % 2
        out[pr, hi * 1024:(hi + 1) * 1024] = res.results[core]["y_out"]
    return out
